# version 20
# baseline (speedup 1.0000x reference)
import numpy as np
import ml_dtypes
import concourse.bass as bass
import concourse.mybir as mybir
from concourse.bass_core_rust import IndirectOffsetOnAxis
from concourse.bass_utils import run_bass_kernel_spmd

F32 = mybir.dt.float32
BF16 = mybir.dt.bfloat16
I32 = mybir.dt.int32
AF = mybir.ActivationFunctionType
ALU = mybir.AluOpType
AX = mybir.AxisListType.X

NCORES = 8
T = 4096
D = 1024
CAP = 5120
NSLOT = 5632
TRASH = NSLOT - 1
NTILE = T // 128
GTILE = (NCORES * T) // 128

_L0D_SEL = list(range(10))
_BATCH_HALF = 99

ENGC = ("tensor", "vector", "scalar", "gpsimd", "sync")
DMAQ = ("sync", "gpsimd", "scalar")


class Ctx:

    def __init__(self, nc):
        self.nc = nc
        self.csem = {n: nc.alloc_semaphore(f"C_{n}") for n in ENGC}
        self.dsem = {n: nc.alloc_semaphore(f"D_{n}") for n in DMAQ}
        self.ccnt = {n: 0 for n in ENGC}
        self.dcnt = {n: 0 for n in DMAQ}
        self.wr = {}
        self.rd = {}
        self.mark = {n: {} for n in ENGC}
        self.sched = {n: [] for n in ENGC}
        self.dslot = {}

    def _sem(self, key):
        if key[0] == "c":
            return self.csem[key[1]]
        if key[0] == "s":
            return self.dslot[key[1]][0]
        return self.dsem[key[1]]

    def _need(self, waits, eng, key, cnt):
        if key == ("c", eng) and eng == "tensor":
            return
        m = self.mark[eng]
        if m.get(key, -1) >= cnt:
            return
        m[key] = cnt
        waits.append((key, cnt))

    def _bump(self, key, cnt):
        if key[0] == "d":
            return self.dcnt[key[1]]
        return cnt

    def _deps(self, eng, reads, writes):
        waits = []
        for t in reads:
            wv = self.wr.get(t)
            if wv:
                self._need(waits, eng, wv[0], self._bump(wv[0], wv[1]))
        for t in writes:
            wv = self.wr.get(t)
            if wv:
                self._need(waits, eng, wv[0], self._bump(wv[0], wv[1]))
            for (k, c) in self.rd.get(t, ()):
                self._need(waits, eng, k, self._bump(k, c))
        return waits

    def op(self, eng, fn, reads=(), writes=(), dma=False, dkey=None):
        waits = self._deps(eng, reads, writes)
        if dma:
            if dkey is not None:
                if dkey not in self.dslot:
                    self.dslot[dkey] = [self.nc.alloc_semaphore(f"S_{dkey}"), 0]
                ent = self.dslot[dkey]
                ent[1] += 16
                cnt = ent[1]
                key = ("s", dkey)
                iv = 16
            else:
                self.dcnt[eng] += 16
                cnt = self.dcnt[eng]
                key = ("d", eng)
                iv = 16
        else:
            self.ccnt[eng] += 1
            cnt = self.ccnt[eng]
            key = ("c", eng)
            iv = 1
        self.sched[eng].append((waits, fn, self._sem(key), iv))
        for t in writes:
            self.wr[t] = (key, cnt)
            self.rd[t] = []
        for t in reads:
            self.rd.setdefault(t, []).append((key, cnt))
        return cnt

    def mm(self, pairs, out_ap, reads=(), writes=(), start=True, stop=True):
        for lhsT, rhs in pairs:
            assert lhsT.base_partition() == rhs.base_partition(), \
                (lhsT.tensor.name, lhsT.base_partition(), rhs.tensor.name,
                 rhs.base_partition())
        def fn(e, pairs=tuple(pairs), out_ap=out_ap, start=start, stop=stop):
            n = len(pairs)
            ins = None
            for i, (lhsT, rhs) in enumerate(pairs):
                ins = e.matmul(out_ap, lhsT, rhs,
                               start=(start and i == 0),
                               stop=(stop and i == n - 1))
            return ins
        return self.op("tensor", fn, reads, writes)

    def barrier(self):
        targets = [(("c", n), self.ccnt[n]) for n in ENGC if self.ccnt[n] > 0]
        targets += [(("d", n), self.dcnt[n]) for n in DMAQ if self.dcnt[n] > 0]
        targets += [(("s", k), v[1]) for k, v in self.dslot.items() if v[1] > 0]
        for eng in ENGC:
            waits = []
            for key, cnt in targets:
                if key == ("c", eng):
                    continue
                self._need(waits, eng, key, cnt)
            if waits:
                self.sched[eng].append((waits, None, None, 0))

    def end_wait(self, eng, key_eng):
        key = ("d", key_eng)
        self.sched[eng].append(([(key, self.dcnt[key_eng])], None, None, 0))
        self.mark[eng][key] = self.dcnt[key_eng]

    def replay(self, block):
        def make(name):
            def run(eng):
                for waits, fn, sem, iv in self.sched[name]:
                    for key, cnt in waits:
                        eng.wait_ge(self._sem(key), cnt)
                    if fn is not None:
                        fn(eng).then_inc(sem, iv)
            return run
        block.tensor(make("tensor"))
        block.vector(make("vector"))
        block.scalar(make("scalar"))
        block.gpsimd(make("gpsimd"))
        block.sync(make("sync"))


class Pool:

    def __init__(self, nc):
        self.nc = nc
        self.guards = []

    def sb(self, name, shape, dt):
        g = self.nc.sbuf_tensor(name, shape, dt)
        h = g.__enter__()
        self.guards.append(g)
        return h

    def ps(self, name, shape, dt=F32):
        g = self.nc.psum_tensor(name, shape, dt)
        h = g.__enter__()
        self.guards.append(g)
        return h

    def free(self):
        for g in reversed(self.guards):
            g.__exit__(None, None, None)
        self.guards = []



def _consts():
    bf = ml_dtypes.bfloat16
    c = {}
    c["IDENT"] = np.eye(128, dtype=bf)
    c["IDENTF"] = np.eye(128, dtype=np.float32)
    k = np.arange(128)[:, None]
    i = np.arange(128)[None, :]
    c["UTI"] = (k <= i).astype(bf)
    c["UTS"] = (k < i).astype(bf)
    c["ONESB"] = np.ones((128, 128), dtype=bf)
    c["ONESF"] = np.ones((128, 128), dtype=np.float32)
    c["IOTAB"] = np.ascontiguousarray(
        np.broadcast_to(np.arange(128, dtype=np.float32)[None, :], (128, 128)))
    c["IOTAC"] = np.ascontiguousarray(np.arange(128, dtype=np.float32)[:, None])
    P = np.zeros((128, 128), dtype=np.float32)
    for r in range(128):
        if r % 64 < 32:
            P[r, r + 32] = -1.0
        else:
            P[r, r - 32] = 1.0
    c["ROTT"] = np.ascontiguousarray(P.T.astype(bf))
    inv = 1.0 / (10000.0 ** (np.arange(0, 64, 2, dtype=np.float64) / 64))
    t = np.arange(T, dtype=np.float64)
    fr = np.outer(inv, t)
    c["COS"] = np.ascontiguousarray(np.tile(np.cos(fr), (4, 1)).astype(bf))
    c["SIN"] = np.ascontiguousarray(np.tile(np.sin(fr), (4, 1)).astype(bf))
    return c


def build_nc(debug_taps=()):
    nc = bass.Bass("TRN2", target_bir_lowering=False)

    xin = nc.dram_tensor("x0", [T, D], F32, kind="ExternalInput").ap()
    w = {}
    for n in ("wqt", "wkt", "wvt", "wpt", "aqt", "akt", "avt", "aot"):
        w[n] = nc.dram_tensor(n, [D, D], F32, kind="ExternalInput").ap()
    w["bp"] = nc.dram_tensor("bp", [1, D], F32, kind="ExternalInput").ap()
    w["rw"] = nc.dram_tensor("rw", [D, 8], F32, kind="ExternalInput").ap()
    w["rb"] = nc.dram_tensor("rb", [1, 8], F32, kind="ExternalInput").ap()
    w["w1s"] = nc.dram_tensor("w1s", [10, D, 512], F32, kind="ExternalInput").ap()
    w["w3s"] = nc.dram_tensor("w3s", [10, D, 512], F32, kind="ExternalInput").ap()
    w["w2s"] = nc.dram_tensor("w2s", [10, 512, D], F32, kind="ExternalInput").ap()
    w["wg"] = nc.dram_tensor("wg", [D, 8], F32, kind="ExternalInput").ap()
    w["w1e"] = nc.dram_tensor("w1e", [D, 4096], F32, kind="ExternalInput").ap()
    w["b1e"] = nc.dram_tensor("b1e", [128, 32], F32, kind="ExternalInput").ap()
    w["w2e"] = nc.dram_tensor("w2e", [4096, D], F32, kind="ExternalInput").ap()
    w["b2e"] = nc.dram_tensor("b2e", [1, D], F32, kind="ExternalInput").ap()
    w["esel"] = nc.dram_tensor("esel", [1, 8], F32, kind="ExternalInput").ap()
    w["esel16"] = nc.dram_tensor("esel16", [1, 128], F32, kind="ExternalInput").ap()
    out = nc.dram_tensor("out", [T, D], F32, kind="ExternalOutput").ap()

    dr = {}
    dr["xn0"] = nc.dram_tensor("xn0_d", [T, D], BF16).ap()
    dr["x1"] = nc.dram_tensor("x1_d", [T, D], F32).ap()
    dr["agx_in"] = nc.dram_tensor("agx_in", [T, D], BF16)
    dr["agx_out"] = nc.dram_tensor("agx_out", [NCORES * T, D], BF16)
    dr["aglg_in"] = nc.dram_tensor("aglg_in", [T, 8], F32)
    dr["aglg_out"] = nc.dram_tensor("aglg_out", [NCORES * T, 8], F32)
    dr["disp"] = nc.dram_tensor("disp_d", [NSLOT, D], BF16).ap()
    dr["eout"] = nc.dram_tensor("eout_d", [NSLOT, D], BF16).ap()
    dr["rsin"] = nc.dram_tensor("rsin_d", [NCORES * T, D], BF16)
    dr["rsout"] = nc.dram_tensor("rsout_d", [T, D], BF16)
    dr["bounce"] = nc.dram_tensor("bounce", [2, 256], F32).ap()
    dr["gfm"] = nc.dram_tensor("gfm_d", [8, T], F32).ap()
    dr["ytm1d"] = nc.dram_tensor("ytm1_d", [128, NTILE * D], BF16).ap()
    dr["xa"] = nc.dram_tensor("xa_d", [T, D], F32).ap()
    for n in ("aqt", "akt", "avt", "aot"):
        dr[n + "b"] = nc.dram_tensor(n + "_b16", [D, D], BF16).ap()

    consts = _consts()
    capd = {k: nc.inline_tensor(v, k) for k, v in consts.items()}

    taps = {}
    tapspec = {
        "xn0": ([T, D], BF16), "x1p": ([T, D], F32), "x1": ([T, D], F32),
        "gatesfm": ([8, T], F32), "logits": ([T, 8], F32),
        "mmat": ([128, GTILE], F32), "posmat": ([128, GTILE], F32),
        "slotmat": ([128, GTILE], F32), "gsmat": ([128, GTILE], F32),
        "offsrow": ([1, GTILE], F32), "roffif": ([1, GTILE], F32),
        "regdump": ([1, GTILE], I32),
        "disp": ([NSLOT, D], BF16),
        "eoutd": ([NSLOT, D], BF16), "rsin": ([NCORES * T, D], BF16),
        "rsout": ([T, D], BF16), "ytm0": ([T, D], BF16),
        "ytm1": ([T, D], BF16), "agx": ([NCORES * T, D], BF16),
    }
    for name in debug_taps:
        shp, dt = tapspec[name]
        taps[name] = nc.dram_tensor("tap_" + name, shp, dt, kind="ExternalOutput").ap()

    ctx = Ctx(nc)
    with nc.Block() as block:
        _emit(ctx, nc, xin, w, out, dr, capd, taps)
        ctx.replay(block)
    return nc



def _emit(ctx, nc, xin, w, out, dr, capd, taps):
    op = ctx.op
    pers = Pool(nc)
    sb = {}
    for n, shp, dt in [("IDENT", [128, 128], BF16), ("IDENTF", [128, 128], F32),
                       ("UTI", [128, 128], BF16),
                       ("UTS", [128, 128], BF16), ("ONESB", [128, 128], BF16),
                       ("ONESF", [128, 128], F32), ("IOTAB", [128, 128], F32),
                       ("IOTAC", [128, 1], F32), ("ROTT", [128, 128], BF16)]:
        sb[n] = pers.sb(n.lower(), shp, dt)
        op("sync", lambda e, n=n: e.dma_start(out=sb[n][:], in_=capd[n].ap()),
           writes=[n], dma=True)
    for n, src, shp in [("bp", "bp", [1, D]), ("rb", "rb", [1, 8]),
                        ("b2e", "b2e", [1, D]), ("b1e", "b1e", [128, 32]),
                        ("esel", "esel", [1, 8])]:
        sb[n] = pers.sb(n + "_sb", shp, F32)
        op("sync", lambda e, n=n, src=src: e.dma_start(out=sb[n][:], in_=w[src]),
           writes=[n], dma=True)
    for n, src in [("rwsb", "rw"), ("wgsb", "wg")]:
        sb[n] = pers.sb(n, [128, 64], BF16)
        src_ap = bass.AP(w[src].tensor, 0, [[8, 128], [1024, 8], [1, 8]])
        op("gpsimd", lambda e, n=n, src_ap=src_ap: e.dma_start(
            out=sb[n][:].rearrange("p (c e) -> p c e", e=8), in_=src_ap),
            writes=[n], dma=True)
    for n in ("M", "G1", "POS", "SLOT", "GS", "CUMS", "KTMP"):
        sb[n] = pers.sb(n, [128, GTILE], F32)
    sb["SLOTI"] = pers.sb("SLOTI", [128, GTILE], I32)
    sb["Mb"] = pers.sb("Mb", [128, GTILE], BF16)
    sb["tot"] = pers.sb("tot", [1, GTILE], F32)
    sb["offs"] = pers.sb("offs", [1, GTILE], F32)
    sb["rmin"] = pers.sb("rmin", [1, GTILE], F32)
    sb["roffi"] = pers.sb("roffi", [1, GTILE], I32)
    sb["tot2"] = pers.sb("tot2", [128, 2], BF16)
    sb["offs2"] = pers.sb("offs2", [128, 2], F32)
    sb["scs"] = pers.sb("scs", [1, 2], F32)
    sb["sc1"] = pers.sb("sc1", [128, 8], F32)
    sb["sc2"] = pers.sb("sc2", [128, 8], F32)
    sb["sc3"] = pers.sb("sc3", [128, 8], F32)
    sb["fc"] = pers.sb("fc", [128, 8], F32)
    sb["eselb"] = pers.sb("eselb", [128, 8], F32)
    sb["cEPS"] = pers.sb("cEPS", [128, 1], F32)
    sb["cM12"] = pers.sb("cM12", [128, 1], F32)
    op("vector", lambda e: e.memset(sb["cEPS"][:], 1e-5), writes=["cEPS"])
    op("vector", lambda e: e.memset(sb["cM12"][:], -12.0), writes=["cM12"])
    psepool = Pool(nc)
    pse = psepool.ps("pse", [128, 8])
    ctx.mm([(sb["ONESF"][0:1, :], sb["esel"][:])], pse[:],
           reads=["ONESF", "esel"], writes=["pse"])
    op("vector", lambda e: e.tensor_copy(sb["eselb"][:], pse[:]),
       reads=["pse"], writes=["eselb"])
    ctx.barrier()
    psepool.free()

    cvt = [pers.sb(f"cvt{i}", [128, D], BF16) for i in range(2)]
    for n in ("aqt", "akt", "avt", "aot"):
        for c in range(8):
            q = c % 2
            src = bass.AP(w[n].tensor, c * 128 * D, [[D, 128], [1, D]])
            dst = bass.AP(dr[n + "b"].tensor, c * 128 * D, [[D, 128], [1, D]])
            op("gpsimd", lambda e, src=src, q=q: e.dma_start(
                out=cvt[q][:], in_=src),
                writes=[f"cv{q}"], dma=True, dkey=f"cv{q}")
            op("sync", lambda e, dst=dst, q=q: e.dma_start(
                out=dst, in_=cvt[q][:]),
                reads=[f"cv{q}"], writes=[f"{n}b_d"], dma=True)

    _l0_ln(ctx, nc, sb, xin, dr, taps)
    ctx.barrier()
    _l0_attn(ctx, nc, sb, xin, w, dr, taps)
    ctx.barrier()
    _l0_moe(ctx, nc, sb, w, dr, taps)
    ctx.barrier()

    _l1_pre(ctx, nc, sb, w, dr, taps)
    early, eb = _l1_attn_pre(ctx, nc, sb, dr, capd)
    _l1_routing(ctx, nc, sb, w, dr, taps)
    ctx.barrier()
    rg = [list(range(NCORES))]
    op("gpsimd", lambda e: e.collective_compute(
        "AllGather", ALU.bypass, replica_groups=rg,
        ins=[dr["agx_in"].ap().opt()], outs=[dr["agx_out"].ap().opt()]),
        reads=[f"agxin{j}" for j in range(NTILE)], writes=["agxout"])
    if "agx" in taps:
        op("sync", lambda e: e.dma_start(out=taps["agx"], in_=dr["agx_out"].ap()),
           reads=["agxout"], dma=True)
    pd = Pool(nc)
    dtags = _l1_dispatch(ctx, nc, sb, dr, taps, pd)
    apools = _l1_attn(ctx, nc, sb, w, dr, capd, taps, early, eb)
    ctx.barrier()
    for pp in apools:
        pp.free()
    pd.free()
    early.free()
    etags = _l1_ffn(ctx, nc, sb, w, dr, dtags, taps)
    _l1_combine_final(ctx, nc, sb, w, out, dr, etags, taps)

    ctx.end_wait("sync", "sync")
    ctx.end_wait("sync", "gpsimd")
    ctx.end_wait("sync", "scalar")


def _ln_tile(ctx, sb, xt, xnb, sq):
    op = ctx.op
    F = sb["fc"]
    op("vector", lambda e: e.reduce_sum(F[:, 0:1], xt[:], AX),
       reads=["xt"], writes=["f0"])
    op("vector", lambda e: e.tensor_scalar(F[:, 1:2], F[:, 0:1], 1.0 / D, None, ALU.mult),
       reads=["f0"], writes=["f1"])
    op("scalar", lambda e: e.activation(sq[:], xt[:], AF.Square, accum_out=F[:, 2:3]),
       reads=["xt"], writes=["sq", "f2"])
    op("vector", lambda e: e.tensor_mul(F[:, 3:4], F[:, 1:2], F[:, 1:2]),
       reads=["f1"], writes=["f3"])
    op("vector", lambda e: e.scalar_tensor_tensor(F[:, 3:4], F[:, 2:3], 1.0 / D,
                                                  F[:, 3:4], ALU.mult, ALU.subtract),
       reads=["f2", "f3"], writes=["f3"])
    op("scalar", lambda e: e.activation(F[:, 4:5], F[:, 3:4], AF.Sqrt,
                                        bias=sb["cEPS"][:]),
       reads=["f3", "cEPS"], writes=["f4"])
    op("vector", lambda e: e.reciprocal(F[:, 5:6], F[:, 4:5]),
       reads=["f4"], writes=["f5"])
    op("vector", lambda e: e.tensor_scalar(xnb[:], xt[:], F[:, 1:2], F[:, 5:6],
                                           ALU.subtract, ALU.mult),
       reads=["xt", "f1", "f5"], writes=["xnb"])



def _l0_ln(ctx, nc, sb, xin, dr, taps):
    op = ctx.op
    p = Pool(nc)
    NB = 2
    xt = [p.sb(f"l0a_xt{i}", [128, D], F32) for i in range(NB)]
    xnb = [p.sb(f"l0a_xnb{i}", [128, D], BF16) for i in range(NB)]
    xnf = [p.sb(f"l0a_xnf{i}", [128, D], BF16) for i in range(NB)]
    sq = [p.sb(f"l0a_sq{i}", [128, D], F32) for i in range(NB)]
    fcb = [p.sb(f"l0a_fc{i}", [128, 8], F32) for i in range(NB)]
    sc1 = [p.sb(f"l0a_sc1{i}", [128, 8], F32) for i in range(NB)]
    sc2 = [p.sb(f"l0a_sc2{i}", [128, 8], F32) for i in range(NB)]
    sc3 = [p.sb(f"l0a_sc3{i}", [128, 8], F32) for i in range(NB)]
    gst = [p.sb(f"l0a_gst{i}", [8, 128], F32) for i in range(NB)]
    pt = [p.ps(f"l0a_pt{i}", [128, 128], BF16) for i in range(NB)]
    plg = [p.ps(f"l0a_plg{i}", [128, 8]) for i in range(NB)]
    pg = [p.ps(f"l0a_pg{i}", [8, 128]) for i in range(NB)]

    def front(j):
        q = j % NB
        op("sync", lambda e, j=j, q=q: e.dma_start(
            out=xt[q][:], in_=xin[j * 128:(j + 1) * 128, :]),
            writes=[f"b_xt{q}"], dma=True, dkey=f"b_xt{q}")
        _ln_tile_r(ctx, sb, xt[q], xnb[q], sq[q], fcb[q], f"b{q}")

    def back(j):
        q = j % NB
        F = fcb[q]
        op("sync", lambda e, j=j, q=q: e.dma_start(
            out=dr["xn0"][j * 128:(j + 1) * 128, :], in_=xnb[q][:]),
            reads=[f"b_xnb{q}"], writes=[f"xn0_{j}"], dma=True)
        for cc in range(8):
            op("tensor", lambda e, cc=cc, q=q: e.transpose(
                pt[q][:], xnb[q][:, cc * 128:(cc + 1) * 128], sb["IDENT"][:]),
               reads=[f"b_xnb{q}", "IDENT"], writes=[f"b_pt{q}"])
            op("scalar", lambda e, cc=cc, q=q: e.copy(
                xnf[q][:, cc * 128:(cc + 1) * 128], pt[q][:]),
               reads=[f"b_pt{q}"], writes=[f"b_xnf{q}_{cc}"])
        ctx.mm([(xnf[q][:, cc * 128:(cc + 1) * 128], sb["rwsb"][:, cc * 8:(cc + 1) * 8])
                for cc in range(8)], plg[q][:],
               reads=[f"b_xnf{q}_{cc}" for cc in range(8)] + ["rwsb"],
               writes=[f"b_plg{q}"], stop=False)
        ctx.mm([(sb["ONESF"][0:1, :], sb["rb"][:])], plg[q][:],
               reads=["ONESF", "rb"], writes=[f"b_plg{q}"], start=False)
        op("vector", lambda e, q=q: e.reduce_max(F[:, 0:1], plg[q][:], AX),
           reads=[f"b_plg{q}"], writes=[f"b{q}_g0"])
        op("vector", lambda e, q=q: e.tensor_scalar(sc1[q][:], plg[q][:], F[:, 0:1],
                                                    None, ALU.is_ge),
           reads=[f"b_plg{q}", f"b{q}_g0"], writes=[f"b_sc1{q}"])
        op("vector", lambda e, q=q: e.scalar_tensor_tensor(
            sc2[q][:], sc1[q][:], -1e30, plg[q][:], ALU.mult, ALU.add),
           reads=[f"b_sc1{q}", f"b_plg{q}"], writes=[f"b_sc2{q}"])
        op("vector", lambda e, q=q: e.reduce_max(F[:, 1:2], sc2[q][:], AX),
           reads=[f"b_sc2{q}"], writes=[f"b{q}_g1"])
        op("vector", lambda e, q=q: e.tensor_scalar(sc1[q][:], plg[q][:], F[:, 1:2],
                                                    None, ALU.is_ge),
           reads=[f"b_plg{q}", f"b{q}_g1"], writes=[f"b_sc1{q}"])
        op("vector", lambda e, q=q: e.tensor_scalar(F[:, 2:3], F[:, 0:1], -1.0,
                                                    None, ALU.mult),
           reads=[f"b{q}_g0"], writes=[f"b{q}_g2"])
        op("scalar", lambda e, q=q: e.activation(sc2[q][:], plg[q][:], AF.Exp,
                                                 bias=F[:, 2:3]),
           reads=[f"b_plg{q}", f"b{q}_g2"], writes=[f"b_sc2{q}"])
        op("vector", lambda e, q=q: e.tensor_mul(sc3[q][:], sc2[q][:], sc1[q][:]),
           reads=[f"b_sc2{q}", f"b_sc1{q}"], writes=[f"b_sc3{q}"])
        op("vector", lambda e, q=q: e.reduce_sum(F[:, 3:4], sc3[q][:], AX),
           reads=[f"b_sc3{q}"], writes=[f"b{q}_g3"])
        op("vector", lambda e, q=q: e.reciprocal(F[:, 4:5], F[:, 3:4]),
           reads=[f"b{q}_g3"], writes=[f"b{q}_g4"])
        op("vector", lambda e, q=q: e.tensor_scalar(sc3[q][:], sc3[q][:], F[:, 4:5],
                                                    None, ALU.mult),
           reads=[f"b_sc3{q}", f"b{q}_g4"], writes=[f"b_sc3{q}"])
        op("tensor", lambda e, q=q: e.transpose(pg[q][:], sc3[q][:], sb["IDENTF"][:]),
           reads=[f"b_sc3{q}", "IDENTF"], writes=[f"b_pg{q}"])
        op("vector", lambda e, q=q: e.tensor_copy(gst[q][:], pg[q][:]),
           reads=[f"b_pg{q}"], writes=[f"b_gst{q}"])
        gap = bass.AP(dr["gfm"].tensor, j * 128, [[T, 8], [1, 128]])
        op("sync", lambda e, gap=gap, q=q: e.dma_start(out=gap, in_=gst[q][:]),
           reads=[f"b_gst{q}"], writes=["gfm_d"], dma=True)

    front(0)
    for j in range(NTILE):
        if j + 1 < NTILE:
            front(j + 1)
        back(j)
    if "xn0" in taps:
        op("sync", lambda e: e.dma_start(out=taps["xn0"], in_=dr["xn0"]),
           reads=[f"xn0_{j}" for j in range(NTILE)], dma=True)
    if "gatesfm" in taps:
        op("sync", lambda e: e.dma_start(out=taps["gatesfm"], in_=dr["gfm"]),
           reads=["gfm_d"], dma=True)
    ctx.barrier()
    p.free()



def _l0_attn(ctx, nc, sb, xin, w, dr, taps):
    op = ctx.op
    ypool = Pool(nc)
    ytm = ypool.sb("ytm", [128, NTILE * D], BF16)

    p = Pool(nc)
    qg = p.sb("l0b_q", [128, T], BF16)
    kg = p.sb("l0b_k", [128, T], BF16)
    ktg = p.sb("l0b_kt", [128, T], BF16)
    vex = p.sb("l0b_vex", [128, NTILE * 130], BF16)
    xst = p.sb("l0b_xst", [128, 8 * T], BF16)
    phs = p.sb("l0b_phs", [128, T], BF16)
    ssb = p.sb("l0b_s", [128, 72], BF16)
    asb2 = [p.sb(f"l0b_a{i}", [128, 128], BF16) for i in range(2)]
    wq = p.sb("l0b_wq", [128, 1024], BF16)
    wk = p.sb("l0b_wk", [128, 1024], BF16)
    wv = p.sb("l0b_wv", [128, 1024], BF16)
    dcol = p.sb("l0b_dcol", [128, 4], F32)
    pbank = [p.ps(f"l0b_pb{i}", [128, 512]) for i in range(4)]
    pS2 = [p.ps(f"l0b_pS{i}", [64, 72]) for i in range(2)]
    pT2 = [p.ps(f"l0b_pT{i}", [128, 128], BF16) for i in range(2)]

    xn0tags = [f"xn0_{j}" for j in range(NTILE)]
    for tb in range(8):
        for cc in range(8):
            op("sync", lambda e, cc=cc, tb=tb: e.dma_start_transpose(
                xst[:, cc * T + tb * 512:cc * T + tb * 512 + 512],
                dr["xn0"][tb * 512:(tb + 1) * 512, cc * 128:(cc + 1) * 128]),
                reads=xn0tags[tb * 4:tb * 4 + 4], writes=[f"xst{tb}"], dma=True)
    for g in range(8):
        for name, wt, buf in (("wq", "wqt", wq), ("wk", "wkt", wk), ("wv", "wvt", wv)):
            src = bass.AP(w[wt].tensor, g * 128, [[1024, 128], [1024 * 128, 8], [1, 128]])
            op("gpsimd", lambda e, src=src, buf=buf: e.dma_start(
                out=buf[:].rearrange("p (c f) -> p c f", f=128), in_=src),
                writes=[name], dma=True, dkey=name)
        vones = bass.AP(vex, 64, [[NTILE * 130, 128], [130, NTILE], [65, 2], [1, 1]])
        op("vector", lambda e, vones=vones: e.memset(vones, 1.0),
           reads=["vex"], writes=["vex_ones"])

        pj = 0
        for tb in range(8):
            xr = [f"xst{tb}"]
            for name, buf, dst, dtag in (("wq", wq, qg, "qg"), ("wk", wk, kg, "kg")):
                pb_ = pbank[pj % 4]
                ptag = f"pbank{pj % 4}"
                pj += 1
                ctx.mm([(buf[:, cc * 128:(cc + 1) * 128],
                         xst[:, cc * T + tb * 512:cc * T + tb * 512 + 512])
                        for cc in range(8)], pb_[:],
                       reads=[name] + xr, writes=[ptag])
                op("scalar", lambda e, dst=dst, tb=tb, pb_=pb_: e.copy(
                    dst[:, tb * 512:(tb + 1) * 512], pb_[:]),
                   reads=[ptag], writes=[dtag])
            for ts_ in range(4):
                pb_ = pbank[pj % 4]
                ptag = f"pbank{pj % 4}"
                pj += 1
                ctx.mm([(xst[:, cc * T + tb * 512 + ts_ * 128:
                             cc * T + tb * 512 + ts_ * 128 + 128],
                         wv[:, cc * 128:(cc + 1) * 128]) for cc in range(8)],
                       pb_[:, 0:128], reads=["wv"] + xr, writes=[ptag])
                ti = tb * 4 + ts_
                for hh in range(2):
                    op("vector", lambda e, ti=ti, hh=hh, pb_=pb_: e.tensor_copy(
                        vex[:, ti * 130 + hh * 65:ti * 130 + hh * 65 + 64],
                        pb_[:, hh * 64:hh * 64 + 64]),
                        reads=[ptag], writes=["vex"])
        for dtag, buf in (("qg", qg), ("kg", kg)):
            op("vector", lambda e, buf=buf: e.tensor_scalar(phs[:], buf[:], 0.0,
                                                            None, ALU.min),
               reads=[dtag], writes=["phs"])
            op("scalar", lambda e: e.activation(phs[:], phs[:], AF.Exp),
               reads=["phs"], writes=["phs"])
            op("vector", lambda e, buf=buf: e.scalar_tensor_tensor(
                buf[:], buf[:], 0.0, phs[:], ALU.max, ALU.add),
               reads=[dtag, "phs"], writes=[dtag])
        for c in range(NTILE):
            tp_ = c % 2
            op("tensor", lambda e, c=c, tp_=tp_: e.transpose(
                pT2[tp_][:], kg[:, c * 128:(c + 1) * 128], sb["IDENT"][:]),
               reads=["kg", "IDENT"], writes=[f"pT{tp_}"])
            op("vector", lambda e, c=c, tp_=tp_: e.tensor_copy(
                ktg[:, c * 128:(c + 1) * 128], pT2[tp_][:]),
               reads=[f"pT{tp_}"], writes=["ktg"])
        for c in range(NTILE):
            for hh in range(2):
                b = hh * 64
                ctx.mm([(kg[b:b + 64, c * 128:(c + 1) * 128],
                         qg[b:b + 64, c * 128:(c + 1) * 128])],
                       pbank[hh][:, 0:128], reads=["kg", "qg"], writes=[f"pbank{hh}"])
            for hh in range(2):
                op("vector", lambda e, hh=hh: e.tensor_mul(
                    asb2[hh][:], pbank[hh][:, 0:128], sb["UTI"][:]),
                   reads=[f"pbank{hh}", "UTI"], writes=[f"asb{hh}"])
            for hh in range(2):
                b = hh * 64
                pairs = [(asb2[hh][:], vex[:, c * 130 + hh * 65:c * 130 + hh * 65 + 65])]
                rd = [f"asb{hh}", "vex", "vex_ones"]
                if c > 0:
                    pairs.append((qg[b:b + 64, c * 128:(c + 1) * 128],
                                  ssb[b:b + 64, 0:65]))
                    rd += ["qg", f"ssb{hh}"]
                ctx.mm(pairs, pbank[2 + hh][:, 0:65], reads=rd,
                       writes=[f"pbank{2 + hh}"])
                ctx.mm([(ktg[:, c * 128 + b:c * 128 + b + 64],
                         vex[:, c * 130 + hh * 65:c * 130 + hh * 65 + 65])],
                       pS2[hh][0:64, 0:65], reads=["ktg", "vex", "vex_ones"],
                       writes=[f"pS2{hh}"], start=(c == 0), stop=(c == NTILE - 1))
            for hh in range(2):
                b = hh * 64
                op("vector", lambda e, hh=hh: e.tensor_scalar(
                    dcol[:, 2 * hh:2 * hh + 1], pbank[2 + hh][:, 64:65],
                    1e-6, None, ALU.add),
                   reads=[f"pbank{2 + hh}"], writes=[f"dc0{hh}"])
                op("vector", lambda e, hh=hh: e.reciprocal(
                    dcol[:, 2 * hh + 1:2 * hh + 2], dcol[:, 2 * hh:2 * hh + 1]),
                   reads=[f"dc0{hh}"], writes=[f"dc1{hh}"])
                op("vector", lambda e, c=c, g=g, hh=hh: e.tensor_scalar(
                    ytm[:, c * D + g * 128 + hh * 64:c * D + g * 128 + hh * 64 + 64],
                    pbank[2 + hh][:, 0:64], dcol[:, 2 * hh + 1:2 * hh + 2],
                    None, ALU.mult),
                    reads=[f"pbank{2 + hh}", f"dc1{hh}"], writes=["ytm"])
                if c < NTILE - 1:
                    op("vector", lambda e, hh=hh, b=b: e.tensor_copy(
                        ssb[b:b + 64, 0:65], pS2[hh][0:64, 0:65]),
                       reads=[f"pS2{hh}"], writes=[f"ssb{hh}"])
    if "ytm0" in taps:
        op("sync", lambda e: e.dma_start(
            out=taps["ytm0"].rearrange("(j p) d -> p (j d)", p=128), in_=ytm[:]),
            reads=["ytm"], dma=True)
    ctx.barrier()
    p.free()

    p = Pool(nc)
    yf = p.sb("l0c_yf", [128, 1024], BF16)
    x0t = p.sb("l0c_x0", [128, D], F32)
    x1t = p.sb("l0c_x1", [128, D], F32)
    wp = p.sb("l0c_wp", [128, 8192], BF16)
    pT = p.ps("l0c_pT", [128, 128], BF16)
    pU = p.ps("l0c_pU", [128, 512])
    src = bass.AP(w["wpt"].tensor, 0, [[1024, 128], [1024 * 128, 8], [1, 1024]])
    op("gpsimd", lambda e, src=src: e.dma_start(
        out=wp[:].rearrange("p (c f) -> p c f", f=1024), in_=src),
        writes=["wp"], dma=True, dkey="wp")
    for j in range(NTILE):
        r0 = j * 128
        for cc in range(8):
            op("tensor", lambda e, j=j, cc=cc: e.transpose(
                pT[:], ytm[:, j * D + cc * 128:j * D + (cc + 1) * 128], sb["IDENT"][:]),
               reads=["ytm", "IDENT"], writes=["l0c_pT"])
            op("vector", lambda e, cc=cc: e.tensor_copy(
                yf[:, cc * 128:(cc + 1) * 128], pT[:]),
               reads=["l0c_pT"], writes=[f"yf{cc}"])
        op("sync", lambda e, r0=r0: e.dma_start(out=x0t[:], in_=xin[r0:r0 + 128, :]),
           writes=["x0t"], dma=True)
        for ih in range(2):
            ctx.mm([(yf[:, cc * 128:(cc + 1) * 128],
                     wp[:, cc * 1024 + ih * 512:cc * 1024 + ih * 512 + 512])
                    for cc in range(8)], pU[:],
                   reads=[f"yf{cc}" for cc in range(8)] + ["wp"],
                   writes=["l0c_pU"], stop=False)
            ctx.mm([(sb["ONESF"][0:1, :], sb["bp"][0:1, ih * 512:ih * 512 + 512])], pU[:],
                   reads=["ONESF", "bp"], writes=["l0c_pU"], start=False)
            op("vector", lambda e, ih=ih: e.tensor_add(
                x1t[:, ih * 512:(ih + 1) * 512], pU[:], x0t[:, ih * 512:(ih + 1) * 512]),
                reads=["l0c_pU", "x0t"], writes=["x1t"])
        op("sync", lambda e, r0=r0: e.dma_start(out=dr["x1"][r0:r0 + 128, :], in_=x1t[:]),
           reads=["x1t"], writes=[f"x1_{j}"], dma=True)
    if "x1p" in taps:
        op("sync", lambda e: e.dma_start(out=taps["x1p"], in_=dr["x1"]),
           reads=[f"x1_{j}" for j in range(NTILE)], dma=True)
    ctx.barrier()
    p.free()
    ypool.free()



def _l0_moe(ctx, nc, sb, w, dr, taps):
    op = ctx.op
    p = Pool(nc)
    w1p = p.sb("l0d_w1", [128, 5 * 4096], BF16)
    w3p = p.sb("l0d_w3", [128, 5 * 4096], BF16)
    w2p = p.sb("l0d_w2", [128, 5 * 4096], BF16)
    xnd2 = [p.sb(f"l0d_xn{i}", [128, 2048], BF16) for i in range(2)]
    hsb2 = [p.sb(f"l0d_h{i}", [128, 1024], BF16) for i in range(2)]
    silu2 = [p.sb(f"l0d_si{i}", [128, 256], BF16) for i in range(2)]
    gG2 = [p.sb(f"l0d_g{i}", [128, 5 * 256], F32) for i in range(2)]
    x1t = p.sb("l0d_x1", [128, 2 * D], F32)
    o1 = [p.sb(f"l0d_o1{i}", [128, 512], F32) for i in range(2)]
    pa2 = [p.ps(f"l0d_pa{i}", [128, 256]) for i in range(2)]
    pb2 = [p.ps(f"l0d_pb{i}", [128, 256]) for i in range(2)]
    py = [p.ps(f"l0d_py{i}", [128, 512]) for i in range(4)]
    xn0tags = [f"xn0_{j}" for j in range(NTILE)]

    def load_xnd(tb):
        xb = tb % 2
        for cc in range(8):
            op("sync", lambda e, cc=cc, tb=tb, xb=xb: e.dma_start_transpose(
                xnd2[xb][:, cc * 256:(cc + 1) * 256],
                dr["xn0"][tb * 256:tb * 256 + 256, cc * 128:(cc + 1) * 128]),
                reads=xn0tags[tb * 2:tb * 2 + 2], writes=[f"xnd{xb}_{cc}"], dma=True,
                dkey=f"xnd{xb}")

    def load_gates(ps_, tb):
        xb = tb % 2
        nr = 5 if ps_ == 0 else 3
        for e5 in range(nr):
            ex = ps_ * 5 + e5
            src = bass.AP(dr["gfm"].tensor, ex * T + tb * 256,
                          [[0, 128], [1, 256]])
            wt = [f"gG{xb}_{e5}"] + ([f"gGt{xb}"] if e5 == nr - 1 else [])
            op("scalar", lambda e, src=src, xb=xb, e5=e5: e.dma_start(
                out=gG2[xb][:, e5 * 256:(e5 + 1) * 256], in_=src),
                reads=["gfm_d"], writes=wt, dma=True, dkey=f"gG{xb}")

    for ps_ in range(2):
        for e5 in range(5):
            ex = ps_ * 5 + e5
            for name, wsrc, buf, inner, nch in (("w1p", "w1s", w1p, 512, 8),
                                                ("w3p", "w3s", w3p, 512, 8),
                                                ("w2p", "w2s", w2p, 1024, 4)):
                src = bass.AP(w[wsrc].tensor, ex * D * 512,
                              [[inner, 128], [inner * 128, nch], [1, inner]])
                op("gpsimd", lambda e, src=src, buf=buf, e5=e5, inner=inner:
                   e.dma_start(out=buf[:, e5 * 4096:(e5 + 1) * 4096]
                               .rearrange("p (c f) -> p c f", f=inner), in_=src),
                   writes=[name + str(e5)], dma=True, dkey=name + str(e5))
        if ps_ == 0:
            load_xnd(0)
        load_gates(ps_, 0)
        for tb in range(16):
            t0 = tb * 256
            xb = tb % 2
            xr = [f"xnd{xb}_{cc}" for cc in range(8)]
            if tb + 1 < 16 or ps_ == 0:
                load_xnd(tb + 1 if tb + 1 < 16 else 0)
            if tb + 1 < 16:
                load_gates(ps_, tb + 1)
            elif ps_ == 0:
                load_gates(1, 0)
            for e5 in range(5):
                ex = ps_ * 5 + e5
                hp = e5 % 2
                hsb = hsb2[hp]
                for ft in range(4):
                    fp = ft % 2
                    ctx.mm([(w1p[:, e5 * 4096 + cc * 512 + ft * 128:
                             e5 * 4096 + cc * 512 + ft * 128 + 128],
                             xnd2[xb][:, cc * 256:(cc + 1) * 256]) for cc in range(8)],
                           pa2[fp][:], reads=[f"w1p{e5}"] + xr,
                           writes=[f"l0d_pa{fp}"])
                    ctx.mm([(w3p[:, e5 * 4096 + cc * 512 + ft * 128:
                             e5 * 4096 + cc * 512 + ft * 128 + 128],
                             xnd2[xb][:, cc * 256:(cc + 1) * 256]) for cc in range(8)],
                           pb2[fp][:], reads=[f"w3p{e5}"] + xr,
                           writes=[f"l0d_pb{fp}"])
                    op("scalar", lambda e, fp=fp: e.activation(
                        silu2[fp][:], pa2[fp][:], AF.Silu),
                       reads=[f"l0d_pa{fp}"], writes=[f"silu{fp}"])
                    if ex < 8:
                        op("vector", lambda e, ft=ft, e5=e5, fp=fp, hsb=hsb, xb=xb:
                            e.tensor_mul(
                                hsb[:, ft * 256:(ft + 1) * 256], pb2[fp][:],
                                gG2[xb][:, e5 * 256:(e5 + 1) * 256]),
                            reads=[f"l0d_pb{fp}", f"gG{xb}_{e5}", f"gGt{xb}"],
                            writes=[f"hsb{hp}"])
                        op("vector", lambda e, ft=ft, fp=fp, hsb=hsb: e.tensor_mul(
                            hsb[:, ft * 256:(ft + 1) * 256],
                            hsb[:, ft * 256:(ft + 1) * 256], silu2[fp][:]),
                            reads=[f"hsb{hp}", f"silu{fp}"], writes=[f"hsb{hp}"])
                    else:
                        op("vector", lambda e, ft=ft, fp=fp, hsb=hsb: e.tensor_mul(
                            hsb[:, ft * 256:(ft + 1) * 256], pb2[fp][:], silu2[fp][:]),
                            reads=[f"l0d_pb{fp}", f"silu{fp}"], writes=[f"hsb{hp}"])
                for tsub in range(2):
                    for mh in range(2):
                        ctx.mm([(hsb[:, fc * 256 + tsub * 128:fc * 256 + tsub * 128 + 128],
                                 w2p[:, e5 * 4096 + fc * 1024 + mh * 512:
                                     e5 * 4096 + fc * 1024 + mh * 512 + 512])
                                for fc in range(4)], py[tsub * 2 + mh][:],
                               reads=[f"hsb{hp}", f"w2p{e5}"],
                               writes=[f"l0d_py{tsub * 2 + mh}"],
                               start=(e5 == 0), stop=(e5 == 4))
            x1src = bass.AP(dr["x1"].tensor, t0 * D, [[D, 128], [128 * D, 2], [1, D]])
            op("sync", lambda e, x1src=x1src: e.dma_start(
                out=x1t[:].rearrange("p (k m) -> p k m", m=D), in_=x1src),
                reads=[f"x1_{tb * 2}", f"x1_{tb * 2 + 1}"], writes=["l0d_x1t"], dma=True, dkey="l0d_x1t")
            for tsub in range(2):
                for mh in range(2):
                    ob = mh
                    op("vector", lambda e, tsub=tsub, mh=mh, ob=ob: e.tensor_add(
                        o1[ob][:], py[tsub * 2 + mh][:],
                        x1t[:, tsub * D + mh * 512:tsub * D + mh * 512 + 512]),
                        reads=[f"l0d_py{tsub * 2 + mh}", "l0d_x1t"],
                        writes=[f"l0d_o1{ob}"])
                    op("sync", lambda e, t0=t0, tsub=tsub, mh=mh, ob=ob: e.dma_start(
                        out=dr["x1"][t0 + tsub * 128:t0 + tsub * 128 + 128,
                                     mh * 512:mh * 512 + 512], in_=o1[ob][:]),
                        reads=[f"l0d_o1{ob}"], writes=[f"x1_{tb * 2 + tsub}"], dma=True)
        ctx.barrier()
    if "x1" in taps:
        op("sync", lambda e: e.dma_start(out=taps["x1"], in_=dr["x1"]),
           reads=[f"x1_{j}" for j in range(NTILE)], dma=True)
    ctx.barrier()
    p.free()


def _l1_pre(ctx, nc, sb, w, dr, taps):
    op = ctx.op
    p = Pool(nc)
    NB = 2
    xt = [p.sb(f"l1a_xt{i}", [128, D], F32) for i in range(NB)]
    xnb = [p.sb(f"l1a_xnb{i}", [128, D], BF16) for i in range(NB)]
    xnf = [p.sb(f"l1a_xnf{i}", [128, D], BF16) for i in range(NB)]
    sq = [p.sb(f"l1a_sq{i}", [128, D], F32) for i in range(NB)]
    lg = [p.sb(f"l1a_lg{i}", [128, 8], F32) for i in range(NB)]
    fcb = [p.sb(f"l1a_fc{i}", [128, 8], F32) for i in range(NB)]
    pt = [p.ps(f"l1a_pt{i}", [128, 128], BF16) for i in range(NB)]
    plg = [p.ps(f"l1a_plg{i}", [128, 8]) for i in range(NB)]

    def front(j):
        q = j % NB
        op("sync", lambda e, j=j, q=q: e.dma_start(out=xt[q][:],
                                                   in_=dr["x1"][j * 128:(j + 1) * 128, :]),
           reads=[f"x1_{j}"], writes=[f"a_xt{q}"], dma=True, dkey=f"a_xt{q}")
        _ln_tile_r(ctx, sb, xt[q], xnb[q], sq[q], fcb[q], f"a{q}")

    def back(j):
        q = j % NB
        op("sync", lambda e, j=j, q=q: e.dma_start(
            out=dr["agx_in"][j * 128:(j + 1) * 128, :], in_=xnb[q][:]),
            reads=[f"a_xnb{q}"], writes=[f"agxin{j}"], dma=True)
        for cc in range(8):
            op("tensor", lambda e, cc=cc, q=q: e.transpose(
                pt[q][:], xnb[q][:, cc * 128:(cc + 1) * 128], sb["IDENT"][:]),
               reads=[f"a_xnb{q}", "IDENT"], writes=[f"a_pt{q}"])
            op("scalar", lambda e, cc=cc, q=q: e.copy(
                xnf[q][:, cc * 128:(cc + 1) * 128], pt[q][:]),
               reads=[f"a_pt{q}"], writes=[f"a_xnf{q}_{cc}"])
        ctx.mm([(xnf[q][:, cc * 128:(cc + 1) * 128], sb["wgsb"][:, cc * 8:(cc + 1) * 8])
                for cc in range(8)], plg[q][:],
               reads=[f"a_xnf{q}_{cc}" for cc in range(8)] + ["wgsb"],
               writes=[f"a_plg{q}"])
        op("vector", lambda e, q=q: e.tensor_copy(lg[q][:], plg[q][:]),
           reads=[f"a_plg{q}"], writes=[f"a_lg{q}"])
        op("sync", lambda e, j=j, q=q: e.dma_start(
            out=dr["aglg_in"][j * 128:(j + 1) * 128, :], in_=lg[q][:]),
            reads=[f"a_lg{q}"], writes=[f"aglgin{j}"], dma=True)

    front(0)
    for j in range(NTILE):
        if j + 1 < NTILE:
            front(j + 1)
        back(j)
    ctx.barrier()
    p.free()
    rg = [list(range(NCORES))]
    op("gpsimd", lambda e: e.collective_compute(
        "AllGather", ALU.bypass, replica_groups=rg,
        ins=[dr["aglg_in"].ap().opt()], outs=[dr["aglg_out"].ap().opt()]),
        reads=[f"aglgin{j}" for j in range(NTILE)], writes=["aglgout"])
    if "logits" in taps:
        op("sync", lambda e: e.dma_start(out=taps["logits"], in_=dr["aglg_in"].ap()),
           reads=[f"aglgin{j}" for j in range(NTILE)], dma=True)


def _ln_tile_r(ctx, sb, xt, xnb, sq, F, pfx):
    op = ctx.op
    op("vector", lambda e: e.reduce_sum(F[:, 0:1], xt[:], AX),
       reads=[f"{pfx}_xt" if False else f"{pfx[0]}_xt{pfx[1]}"], writes=[f"{pfx}_f0"])
    op("vector", lambda e: e.tensor_scalar(F[:, 1:2], F[:, 0:1], 1.0 / D, None,
                                           ALU.mult),
       reads=[f"{pfx}_f0"], writes=[f"{pfx}_f1"])
    op("scalar", lambda e: e.activation(sq[:], xt[:], AF.Square, accum_out=F[:, 2:3]),
       reads=[f"{pfx[0]}_xt{pfx[1]}"], writes=[f"{pfx}_sq", f"{pfx}_f2"])
    op("vector", lambda e: e.tensor_mul(F[:, 3:4], F[:, 1:2], F[:, 1:2]),
       reads=[f"{pfx}_f1"], writes=[f"{pfx}_f3"])
    op("vector", lambda e: e.scalar_tensor_tensor(F[:, 3:4], F[:, 2:3], 1.0 / D,
                                                  F[:, 3:4], ALU.mult, ALU.subtract),
       reads=[f"{pfx}_f2", f"{pfx}_f3"], writes=[f"{pfx}_f3"])
    op("scalar", lambda e: e.activation(F[:, 4:5], F[:, 3:4], AF.Sqrt,
                                        bias=sb["cEPS"][:]),
       reads=[f"{pfx}_f3", "cEPS"], writes=[f"{pfx}_f4"])
    op("vector", lambda e: e.reciprocal(F[:, 5:6], F[:, 4:5]),
       reads=[f"{pfx}_f4"], writes=[f"{pfx}_f5"])
    op("vector", lambda e: e.tensor_scalar(xnb[:], xt[:], F[:, 1:2], F[:, 5:6],
                                           ALU.subtract, ALU.mult),
       reads=[f"{pfx[0]}_xt{pfx[1]}", f"{pfx}_f1", f"{pfx}_f5"],
       writes=[f"{pfx[0]}_xnb{pfx[1]}"])



def _batch_done(ctx, nc, sb, dr, taps, p):
    ctx.barrier()
    p.free()


def _l1_routing(ctx, nc, sb, w, dr, taps):
    op = ctx.op
    p = Pool(nc)
    lgt = p.sb("l1r_lg", [128, 128], F32)
    pc = p.ps("l1r_pc", [128, GTILE])
    pt1 = p.ps("l1r_pt1", [1, GTILE])
    pb2 = p.ps("l1r_pb2", [128, 2])
    pcs = p.ps("l1r_pcs", [1, 2])
    pbc = p.ps("l1r_pbc", [128, 1])
    F = sb["fc"]

    es16 = p.sb("l1r_es16", [1, 128], F32)
    esb16 = p.sb("l1r_esb16", [128, 128], F32)
    lgt2 = [p.sb(f"l1r_lg2{i}", [128, 128], F32) for i in range(2)]
    LL = [p.sb(f"l1r_LL{i}", [128, 128], F32) for i in range(2)]
    EX = [p.sb(f"l1r_EX{i}", [128, 128], F32) for i in range(2)]
    MYP = [p.sb(f"l1r_MYP{i}", [128, 128], F32) for i in range(2)]
    M1 = [p.sb(f"l1r_M1{i}", [128, 16], F32) for i in range(2)]
    S16 = [p.sb(f"l1r_S16{i}", [128, 16], F32) for i in range(2)]
    MY16 = [p.sb(f"l1r_MY16{i}", [128, 16], F32) for i in range(2)]
    peb = p.ps("l1r_peb", [128, 128])
    op("sync", lambda e: e.dma_start(out=es16[:], in_=w["esel16"]),
       writes=["es16"], dma=True)
    ctx.mm([(sb["ONESF"][0:1, :], es16[:])], peb[:], reads=["ONESF", "es16"],
           writes=["peb"])
    op("vector", lambda e: e.tensor_copy(esb16[:], peb[:]),
       reads=["peb"], writes=["esb16"])
    for jj in range(16):
        q = jj % 2
        src = bass.AP(dr["aglg_out"].ap().tensor, jj * 16 * 128 * 8,
                      [[8, 128], [128 * 8, 16], [1, 8]])
        op("sync", lambda e, src=src, q=q: e.dma_start(
            out=lgt2[q][:].rearrange("p (j e) -> p j e", e=8), in_=src),
            reads=["aglgout"], writes=[f"lgt{q}"], dma=True, dkey=f"lgt{q}")
        l3 = lgt2[q][:].rearrange("p (j e) -> p j e", e=8)
        op("vector", lambda e, q=q, l3=l3: e.reduce_max(M1[q][:], l3, AX),
           reads=[f"lgt{q}"], writes=[f"rM1{q}"])
        m1b = M1[q][:].rearrange("p (j o) -> p j o", o=1).to_broadcast([128, 16, 8])
        ll3 = LL[q][:].rearrange("p (j e) -> p j e", e=8)
        op("vector", lambda e, q=q, l3=l3, m1b=m1b, ll3=ll3: e.tensor_tensor(
            ll3, l3, m1b, ALU.subtract),
           reads=[f"lgt{q}", f"rM1{q}"], writes=[f"rLL{q}"])
        op("scalar", lambda e, q=q: e.activation(EX[q][:], LL[q][:], AF.Exp),
           reads=[f"rLL{q}"], writes=[f"rEX{q}"])
        e3 = EX[q][:].rearrange("p (j e) -> p j e", e=8)
        op("vector", lambda e, q=q, e3=e3: e.reduce_sum(S16[q][:], e3, AX),
           reads=[f"rEX{q}"], writes=[f"rS16{q}"])
        op("vector", lambda e, jj=jj, q=q: e.reciprocal(
            sb["G1"][:, jj * 16:(jj + 1) * 16], S16[q][:]),
           reads=[f"rS16{q}"], writes=["G1"])
        op("vector", lambda e, q=q: e.tensor_mul(MYP[q][:], lgt2[q][:], esb16[:]),
           reads=[f"lgt{q}", "esb16"], writes=[f"rMYP{q}"])
        p3 = MYP[q][:].rearrange("p (j e) -> p j e", e=8)
        op("vector", lambda e, q=q, p3=p3: e.reduce_sum(MY16[q][:], p3, AX),
           reads=[f"rMYP{q}"], writes=[f"rMY16{q}"])
        op("vector", lambda e, jj=jj, q=q: e.tensor_tensor(
            sb["M"][:, jj * 16:(jj + 1) * 16], MY16[q][:], M1[q][:], ALU.is_ge),
           reads=[f"rMY16{q}", f"rM1{q}"], writes=["M"])
        op("vector", lambda e: e.tensor_copy(sb["Mb"][:], sb["M"][:]),
       reads=["M"], writes=["Mb"])
    ctx.mm([(sb["UTI"][:], sb["Mb"][:])], pc[:], reads=["UTI", "Mb"], writes=["l1r_pc"])
    op("vector", lambda e: e.tensor_copy(sb["CUMS"][:], pc[:]),
       reads=["l1r_pc"], writes=["CUMS"])
    ctx.mm([(sb["ONESB"][:, 0:1], sb["Mb"][:])], pt1[:], reads=["ONESB", "Mb"],
           writes=["ptot"])
    op("vector", lambda e: e.tensor_copy(sb["tot"][:], pt1[:]),
       reads=["ptot"], writes=["tot"])
    op("sync", lambda e: e.dma_start(out=dr["bounce"][0:1, :], in_=sb["tot"][:]),
       reads=["tot"], writes=["bounce0"], dma=True)
    b_src = bass.AP(dr["bounce"].tensor, 0, [[1, 128], [128, 2]])
    op("gpsimd", lambda e, b_src=b_src: e.dma_start(
        out=sb["tot2"][:], in_=b_src, allow_slow_non_contiguous=True),
        reads=["bounce0"], writes=["tot2"], dma=True)
    ctx.mm([(sb["UTS"][:], sb["tot2"][:])], pb2[:], reads=["UTS", "tot2"],
           writes=["pb2"])
    ctx.mm([(sb["ONESB"][:, 0:1], sb["tot2"][:])], pcs[:], reads=["ONESB", "tot2"],
           writes=["pcs"])
    op("vector", lambda e: e.tensor_copy(sb["scs"][:], pcs[:]),
       reads=["pcs"], writes=["scs"])
    ctx.mm([(sb["ONESF"][0:1, :], sb["scs"][0:1, 0:1])], pbc[:],
           reads=["ONESF", "scs"], writes=["pbc"])
    op("vector", lambda e: e.tensor_copy(sb["offs2"][:], pb2[:]),
       reads=["pb2"], writes=["offs2"])
    op("vector", lambda e: e.tensor_add(sb["offs2"][:, 1:2], sb["offs2"][:, 1:2], pbc[:]),
       reads=["offs2", "pbc"], writes=["offs2"])
    b_dst = bass.AP(dr["bounce"].tensor, 256, [[1, 128], [128, 2]])
    op("gpsimd", lambda e, b_dst=b_dst: e.dma_start(
        out=b_dst, in_=sb["offs2"][:], allow_slow_non_contiguous=True),
        reads=["offs2"], writes=["bounce1"], dma=True)
    op("sync", lambda e: e.dma_start(out=sb["offs"][:], in_=dr["bounce"][1:2, :]),
       reads=["bounce1"], writes=["offs"], dma=True)
    ctx.mm([(sb["ONESF"][0:1, :], sb["offs"][:])], pc[:], reads=["ONESF", "offs"],
           writes=["l1r_pc"])
    op("vector", lambda e: e.tensor_add(sb["POS"][:], sb["CUMS"][:], pc[:]),
       reads=["CUMS", "l1r_pc"], writes=["POS"])
    op("vector", lambda e: e.scalar_tensor_tensor(sb["SLOT"][:], sb["M"][:], -33000.0,
                                                  sb["POS"][:], ALU.mult, ALU.add),
       reads=["M", "POS"], writes=["SLOT"])
    op("vector", lambda e: e.tensor_scalar(sb["SLOT"][:], sb["SLOT"][:], 32999.0,
                                           None, ALU.add),
       reads=["SLOT"], writes=["SLOT"])
    op("vector", lambda e: e.tensor_scalar(sb["SLOT"][:], sb["SLOT"][:], float(TRASH),
                                           None, ALU.min),
       reads=["SLOT"], writes=["SLOT"])
    op("vector", lambda e: e.tensor_copy(sb["SLOTI"][:], sb["SLOT"][:]),
       reads=["SLOT"], writes=["SLOTI"])
    op("vector", lambda e: e.tensor_scalar(sb["KTMP"][:], sb["POS"][:], float(CAP),
                                           None, ALU.is_le),
       reads=["POS"], writes=["KTMP"])
    op("vector", lambda e: e.tensor_mul(sb["GS"][:], sb["G1"][:], sb["M"][:]),
       reads=["G1", "M"], writes=["GS"])
    op("vector", lambda e: e.tensor_mul(sb["GS"][:], sb["GS"][:], sb["KTMP"][:]),
       reads=["GS", "KTMP"], writes=["GS"])
    op("vector", lambda e: e.tensor_scalar(sb["rmin"][:], sb["offs"][:], float(CAP),
                                           None, ALU.min),
       reads=["offs"], writes=["rmin"])
    op("vector", lambda e: e.tensor_copy(sb["roffi"][:], sb["rmin"][:]),
       reads=["rmin"], writes=["roffi"])
    for tn, buf in (("mmat", "M"), ("posmat", "POS"), ("slotmat", "SLOT"),
                    ("gsmat", "GS")):
        if tn in taps:
            op("sync", lambda e, tn=tn, buf=buf: e.dma_start(
                out=taps[tn], in_=sb[buf][:]), reads=[buf], dma=True)
    if "offsrow" in taps:
        op("sync", lambda e: e.dma_start(out=taps["offsrow"], in_=sb["offs"][:]),
           reads=["offs"], dma=True)
    if "roffif" in taps:
        op("vector", lambda e: e.tensor_copy(sb["rmin"][:], sb["roffi"][:]),
           reads=["roffi"], writes=["rmin"])
        op("sync", lambda e: e.dma_start(out=taps["roffif"], in_=sb["rmin"][:]),
           reads=["rmin"], dma=True)
    ctx.barrier()
    p.free()



def _l1_dispatch(ctx, nc, sb, dr, taps, p):
    op = ctx.op
    NB = 6
    xnt = [p.sb(f"l1m_xnt{i}", [128, D], BF16) for i in range(NB)]
    zz = p.sb("l1m_zz", [128, 8], BF16)

    dtags = []

    def front(j):
        b = j % NB
        op("gpsimd", lambda e, j=j, b=b: e.dma_start(
            out=xnt[b][:], in_=dr["agx_out"].ap()[j * 128:(j + 1) * 128, :]),
            reads=["agxout"], writes=[f"xnt{b}"], dma=True, dkey=f"xnt{b}")

    def back(j):
        b = j % NB
        tg = f"dsc{j}"
        dtags.append(tg)
        op("gpsimd", lambda e, b=b, j=j: e.indirect_dma_start(
            out=dr["disp"],
            out_offset=IndirectOffsetOnAxis(ap=sb["SLOTI"][:, j:j + 1], axis=0),
            in_=xnt[b][:], in_offset=None),
            reads=[f"xnt{b}", "SLOTI"], writes=[tg], dma=True)

    for jp in range(4):
        front(jp)
    for j in range(GTILE):
        back(j)
        if j + 4 < GTILE:
            front(j + 4)
    op("gpsimd", lambda e: e.dma_start(out=zz[:, 0:8], in_=dr["disp"][0:128, 0:8]),
       reads=dtags, writes=["dfence"], dma=True)
    if "disp" in taps:
        op("sync", lambda e: e.dma_start(out=taps["disp"], in_=dr["disp"]),
           reads=dtags + ["dfence"], dma=True)
    return dtags


def _l1_ffn(ctx, nc, sb, w, dr, dtags, taps):
    op = ctx.op
    p = Pool(nc)
    w1sb = p.sb("l1f_w1", [128, 8 * 4096], BF16)
    w2sb = p.sb("l1f_w2", [128, 32 * 1024], BF16)
    dspf = p.sb("l1f_dsp", [128, 8 * 512], BF16)
    hsb = p.sb("l1f_h", [128, 32 * 512], BF16)
    eo = p.sb("l1f_eo", [128, D], BF16)
    ph = p.ps("l1f_ph", [128, 512])
    pe = [p.ps(f"l1f_pe{i}", [128, 512]) for i in range(2)]
    src = bass.AP(w["w1e"].tensor, 0, [[4096, 128], [4096 * 128, 8], [1, 4096]])
    op("gpsimd", lambda e, src=src: e.dma_start(
        out=w1sb[:].rearrange("p (c f) -> p c f", f=4096), in_=src),
        writes=["w1sb"], dma=True, dkey="w1sb")
    src2 = bass.AP(w["w2e"].tensor, 0, [[1024, 128], [1024 * 128, 32], [1, 1024]])
    op("gpsimd", lambda e, src2=src2: e.dma_start(
        out=w2sb[:].rearrange("p (c f) -> p c f", f=1024), in_=src2),
        writes=["w2sb"], dma=True, dkey="w2sb")

    etags = []
    for cb in range(NSLOT // 512):
        c0 = cb * 512
        for mc in range(8):
            op("sync", lambda e, mc=mc, c0=c0: e.dma_start_transpose(
                dspf[:, mc * 512:(mc + 1) * 512],
                dr["disp"][c0:c0 + 512, mc * 128:(mc + 1) * 128]),
                reads=dtags + ["dfence"], writes=[f"dspf{mc}"], dma=True, dkey="dspf")
        for ft in range(32):
            ctx.mm([(w1sb[:, mc * 4096 + ft * 128:mc * 4096 + ft * 128 + 128],
                     dspf[:, mc * 512:(mc + 1) * 512]) for mc in range(8)],
                   ph[:], reads=["w1sb"] + [f"dspf{mc}" for mc in range(8)],
                   writes=["l1f_ph"])
            op("scalar", lambda e, ft=ft: e.activation(
                hsb[:, ft * 512:(ft + 1) * 512], ph[:], AF.Relu,
                bias=sb["b1e"][:, ft:ft + 1]),
                reads=["l1f_ph", "b1e"], writes=["hsb"])
        for ct in range(4):
            for mh in range(2):
                pid = mh
                ctx.mm([(hsb[:, fc * 512 + ct * 128:fc * 512 + ct * 128 + 128],
                         w2sb[:, fc * 1024 + mh * 512:fc * 1024 + mh * 512 + 512])
                        for fc in range(32)], pe[pid][:],
                       reads=["hsb", "w2sb"], writes=[f"l1f_pe{pid}"], stop=False)
                ctx.mm([(sb["ONESF"][0:1, :], sb["b2e"][0:1, mh * 512:mh * 512 + 512])],
                       pe[pid][:], reads=["ONESF", "b2e"], writes=[f"l1f_pe{pid}"],
                       start=False)
                op("vector", lambda e, mh=mh, pid=pid: e.tensor_copy(
                    eo[:, mh * 512:(mh + 1) * 512], pe[pid][:]),
                    reads=[f"l1f_pe{pid}"], writes=["eo"])
            tg = f"eot{cb}_{ct}"
            etags.append(tg)
            op("sync", lambda e, c0=c0, ct=ct: e.dma_start(
                out=dr["eout"][c0 + ct * 128:c0 + ct * 128 + 128, :], in_=eo[:]),
                reads=["eo"], writes=[tg], dma=True)
    if "eoutd" in taps:
        op("sync", lambda e: e.dma_start(out=taps["eoutd"], in_=dr["eout"]),
           reads=etags, dma=True)
    ctx.barrier()
    p.free()
    return etags


def _qperm(jp):
    q, rem = divmod(jp, 64)
    c, k = divmod(rem, 8)
    return c * 32 + q * 8 + k


def _l1_combine_final(ctx, nc, sb, w, out, dr, etags, taps):
    op = ctx.op
    p = Pool(nc)
    NB = 6
    Esb = [p.sb(f"l1g_E{i}", [128, D], BF16) for i in range(NB)]
    ocm = [p.sb(f"l1g_oc{i}", [128, D], BF16) for i in range(NB)]
    aos = p.sb("l1z_ao", [128, 8192], BF16)
    yf = p.sb("l1z_yf", [128, 1024], BF16)
    ytb = [p.sb(f"l1z_yt{i}", [128, 1024], BF16) for i in range(2)]
    x1t = [p.sb(f"l1z_x1{i}", [128, D], F32) for i in range(2)]
    xat = [p.sb(f"l1z_xa{i}", [128, D], F32) for i in range(2)]
    fa = [p.sb(f"l1z_fa{i}", [128, D], F32) for i in range(2)]
    rst = [p.sb(f"l1z_rs{i}", [128, D], BF16) for i in range(2)]
    ot = [p.sb(f"l1z_o{i}", [128, D], F32) for i in range(2)]
    pT = p.ps("l1z_pT", [128, 128], BF16)
    pF = p.ps("l1z_pF", [128, 512])
    src = bass.AP(dr["aotb"].tensor, 0, [[1024, 128], [1024 * 128, 8], [1, 1024]])
    op("sync", lambda e, src=src: e.dma_start(
        out=aos[:].rearrange("p (c f) -> p c f", f=1024), in_=src),
        reads=["aotb_d"], writes=["aos"], dma=True, dkey="aos")

    rtags = [[] for _ in range(4)]

    def cfront(j):
        b = j % NB
        jo = _qperm(j)
        op("gpsimd", lambda e, b=b, jo=jo: e.indirect_dma_start(
            out=Esb[b][:], out_offset=None, in_=dr["eout"],
            in_offset=IndirectOffsetOnAxis(ap=sb["SLOTI"][:, jo:jo + 1], axis=0)),
            reads=["SLOTI"] + etags, writes=[f"g_E{b}"], dma=True, dkey=f"g_E{b}")

    def cback(j):
        b = j % NB
        jo = _qperm(j)
        for hh in range(2):
            op("vector", lambda e, jo=jo, b=b, hh=hh: e.tensor_scalar(
                ocm[b][:, hh * 512:hh * 512 + 512],
                Esb[b][:, hh * 512:hh * 512 + 512], sb["GS"][:, jo:jo + 1],
                None, ALU.mult),
                reads=[f"g_E{b}", "GS"], writes=[f"g_oc{b}"])
        tg = f"rsw{j}"
        rtags[j // 64].append(tg)
        op("sync", lambda e, j=j, b=b: e.dma_start(
            out=dr["rsin"].ap()[j * 128:(j + 1) * 128, :], in_=ocm[b][:]),
            reads=[f"g_oc{b}"], writes=[tg], dma=True, dkey=f"grs{b}")

    def proj(jj):
        q = jj % 2
        r0 = jj * 128
        op("sync", lambda e, jj=jj, q=q: e.dma_start(
            out=ytb[q][:], in_=dr["ytm1d"][:, jj * 1024:(jj + 1) * 1024]),
            reads=["ytm1d"], writes=[f"z_yt{q}"], dma=True, dkey=f"z_yt{q}")
        for cc in range(8):
            op("tensor", lambda e, cc=cc, q=q: e.transpose(
                pT[:], ytb[q][:, cc * 128:(cc + 1) * 128], sb["IDENT"][:]),
               reads=[f"z_yt{q}", "IDENT"], writes=["l1z_pT"])
            op("vector", lambda e, cc=cc: e.tensor_copy(
                yf[:, cc * 128:(cc + 1) * 128], pT[:]),
               reads=["l1z_pT"], writes=[f"zyf{cc}"])
        op("sync", lambda e, r0=r0, q=q: e.dma_start(
            out=x1t[q][:], in_=dr["x1"][r0:r0 + 128, :]),
            reads=[f"x1_{jj}"], writes=[f"z_x1{q}"], dma=True, dkey=f"z_x1{q}")
        for ih in range(2):
            ctx.mm([(yf[:, cc * 128:(cc + 1) * 128],
                     aos[:, cc * 1024 + ih * 512:cc * 1024 + ih * 512 + 512])
                    for cc in range(8)], pF[:],
                   reads=[f"zyf{cc}" for cc in range(8)] + ["aos"], writes=["z_pF"])
            op("vector", lambda e, ih=ih, q=q: e.tensor_add(
                xat[q][:, ih * 512:(ih + 1) * 512], pF[:],
                x1t[q][:, ih * 512:(ih + 1) * 512]),
                reads=["z_pF", f"z_x1{q}"], writes=[f"z_xa{q}"])
        op("sync", lambda e, r0=r0, q=q: e.dma_start(
            out=dr["xa"][r0:r0 + 128, :], in_=xat[q][:]),
            reads=[f"z_xa{q}"], writes=[f"xa_{jj}"], dma=True)

    rg = [list(range(NCORES))]

    def rs_q(q):
        r0 = q * 8192
        op("gpsimd", lambda e, q=q, r0=r0: e.collective_compute(
            "ReduceScatter", ALU.add, replica_groups=rg,
            ins=[dr["rsin"].ap()[r0:r0 + 8192, :].opt()],
            outs=[dr["rsout"].ap()[q * 1024:(q + 1) * 1024, :].opt()]),
            reads=rtags[q], writes=[f"rsout{q}"])

    def fadd(q, k):
        jj = q * 8 + k
        z = jj % 2
        r0 = jj * 128
        op("scalar", lambda e, r0=r0, z=z: e.dma_start(
            out=fa[z][:], in_=dr["xa"][r0:r0 + 128, :]),
            reads=[f"xa_{jj}"], writes=[f"z_fa{z}"], dma=True, dkey=f"z_fa{z}")
        op("scalar", lambda e, r0=r0, z=z, q=q: e.dma_start(
            out=rst[z][:], in_=dr["rsout"].ap()[r0:r0 + 128, :]),
            reads=[f"rsout{q}"], writes=[f"z_rs{z}"], dma=True, dkey=f"z_rs{z}")
        op("vector", lambda e, z=z: e.tensor_add(ot[z][:], fa[z][:], rst[z][:]),
           reads=[f"z_fa{z}", f"z_rs{z}"], writes=[f"z_ot{z}"])
        op("scalar", lambda e, r0=r0, z=z: e.dma_start(
            out=out[r0:r0 + 128, :], in_=ot[z][:]),
            reads=[f"z_ot{z}"], writes=["outd"], dma=True)

    for jp in range(4):
        cfront(jp)
    pj = 0
    for j in range(GTILE):
        cback(j)
        if j + 4 < GTILE:
            cfront(j + 4)
        if j % 8 == 7 and pj < NTILE:
            proj(pj)
            pj += 1
    while pj < NTILE:
        proj(pj)
        pj += 1
    for q in range(4):
        rs_q(q)
        for k in range(8):
            fadd(q, k)
    ctx.barrier()
    p.free()



def _l1_attn_pre(ctx, nc, sb, dr, capd):
    op = ctx.op
    early = Pool(nc)
    eb = {"cs": early.sb("cos_sb", [128, T], BF16),
          "sn": early.sb("sin_sb", [128, T], BF16),
          "wqs": early.sb("l1t_wq", [128, 2048], BF16),
          "wks": early.sb("l1t_wk", [128, 2048], BF16),
          "wvs": early.sb("l1t_wv", [128, 2048], BF16)}
    op("sync", lambda e: e.dma_start(out=eb["cs"][:], in_=capd["COS"].ap()),
       writes=["COS"], dma=True)
    op("sync", lambda e: e.dma_start(out=eb["sn"][:], in_=capd["SIN"].ap()),
       writes=["SIN"], dma=True)
    for name, wt in (("wqs", "aqt"), ("wks", "akt"), ("wvs", "avt")):
        src = bass.AP(dr[wt + "b"].tensor, 0,
                      [[1024, 128], [1024 * 128, 8], [1, 256]])
        op("sync", lambda e, src=src, name=name: e.dma_start(
            out=eb[name][:].rearrange("p (c f) -> p c f", f=256), in_=src),
            reads=[f"{wt}b_d"], writes=[name], dma=True, dkey=name)
    return early, eb


def _l1_attn(ctx, nc, sb, w, dr, capd, taps, early, eb):
    op = ctx.op
    ypool = Pool(nc)
    ytm1 = ypool.sb("ytm1", [128, NTILE * D], BF16)
    cs = eb["cs"]
    sn = eb["sn"]

    p = Pool(nc)
    qq = p.sb("l1t_q", [128, 2 * T], BF16)
    kq = p.sb("l1t_k", [128, 2 * T], BF16)
    vex = p.sb("l1t_v", [128, NTILE * 260], BF16)
    xsq = p.sb("l1t_xs", [128, T], BF16)
    wqs, wks, wvs = eb["wqs"], eb["wks"], eb["wvs"]
    qraw = p.sb("l1t_qraw", [128, 512], BF16)
    t1 = p.sb("l1t_t1", [128, 512], BF16)
    t2 = p.sb("l1t_t2", [128, 512], BF16)
    Psb2 = [p.sb(f"l1t_P{i}", [128, 512], BF16) for i in range(2)]
    dcol = p.sb("l1t_dcol", [128, 1], F32)
    pP = p.ps("l1t_pP", [128, 512])
    pR = p.ps("l1t_pR", [128, 512])
    pA2 = [p.ps(f"l1t_pA{i}", [128, 512]) for i in range(2)]
    pO = [p.ps(f"l1t_pO{i}", [128, 72]) for i in range(4)]

    agxtags = [f"agxin{j}" for j in range(NTILE)]
    for qt in range(4):
        if qt > 0:
            for name, wt, buf in (("wqs", "aqt", wqs), ("wks", "akt", wks),
                                  ("wvs", "avt", wvs)):
                src = bass.AP(dr[wt + "b"].tensor, qt * 256,
                              [[1024, 128], [1024 * 128, 8], [1, 256]])
                op("sync", lambda e, src=src, buf=buf: e.dma_start(
                    out=buf[:].rearrange("p (c f) -> p c f", f=256), in_=src),
                    reads=[f"{wt}b_d"], writes=[name], dma=True, dkey=name)
        vones = bass.AP(vex, 64, [[NTILE * 260, 128], [260, NTILE], [65, 4], [1, 1]])
        op("vector", lambda e, vones=vones: e.memset(vones, 1.0),
           reads=["vex"], writes=["vex_ones"])
        for tb in range(8):
            xr = [f"xq{cc}" for cc in range(8)]
            for cc in range(8):
                op("sync", lambda e, cc=cc, tb=tb: e.dma_start_transpose(
                    xsq[:, cc * 512:(cc + 1) * 512],
                    dr["agx_in"][tb * 512:(tb + 1) * 512, cc * 128:(cc + 1) * 128]),
                    reads=agxtags[tb * 4:tb * 4 + 4], writes=[xr[cc]], dma=True,
                    dkey=xr[cc])
            for name, wbuf, dst, dtag in (("wqs", wqs, qq, "qqb"),
                                          ("wks", wks, kq, "kqb")):
                for it in range(2):
                    ctx.mm([(wbuf[:, cc * 256 + it * 128:cc * 256 + it * 128 + 128],
                             xsq[:, cc * 512:(cc + 1) * 512]) for cc in range(8)],
                           pP[:], reads=[name] + xr, writes=["l1t_pP"])
                    op("scalar", lambda e: e.copy(qraw[:], pP[:]),
                       reads=["l1t_pP"], writes=["qraw"])
                    ctx.mm([(sb["ROTT"][:], qraw[:])], pR[:],
                           reads=["ROTT", "qraw"], writes=["l1t_pR"])
                    c0 = tb * 512
                    op("vector", lambda e, c0=c0: e.tensor_mul(
                        t1[:], qraw[:], cs[:, c0:c0 + 512]),
                        reads=["qraw", "COS"], writes=["t1"])
                    op("vector", lambda e, c0=c0: e.tensor_mul(
                        t2[:], pR[:], sn[:, c0:c0 + 512]),
                        reads=["l1t_pR", "SIN"], writes=["t2"])
                    op("vector", lambda e, dst=dst, it=it, c0=c0: e.tensor_add(
                        dst[:, it * T + c0:it * T + c0 + 512], t1[:], t2[:]),
                        reads=["t1", "t2"], writes=[dtag])
            for ts_ in range(4):
                ctx.mm([(xsq[:, cc * 512 + ts_ * 128:cc * 512 + ts_ * 128 + 128],
                         wvs[:, cc * 256:(cc + 1) * 256]) for cc in range(8)],
                       pP[:, 0:256], reads=["wvs"] + xr, writes=["l1t_pP"])
                ti = tb * 4 + ts_
                for hh in range(4):
                    op("vector", lambda e, ti=ti, hh=hh: e.tensor_copy(
                        vex[:, ti * 260 + hh * 65:ti * 260 + hh * 65 + 64],
                        pP[:, hh * 64:hh * 64 + 64]),
                        reads=["l1t_pP"], writes=["vex"])
        for hl in range(4):
            h = qt * 4 + hl
            rb = (hl % 2) * 64
            itq = hl // 2
            for qb in range(8):
                q0 = qb * 512
                nkt = (qb + 1) * 4
                def emit_qk(kt):
                    pr = kt % 2
                    ctx.mm([(kq[rb:rb + 64, itq * T + kt * 128:itq * T + kt * 128 + 128],
                             qq[rb:rb + 64, itq * T + q0:itq * T + q0 + 512])],
                           pA2[pr][:], reads=["kqb", "qqb"], writes=[f"l1t_pA{pr}"])

                emit_qk(0)
                for kt in range(nkt):
                    pr = kt % 2
                    pA = pA2[pr]
                    Psb = Psb2[pr]
                    op("scalar", lambda e, pA=pA, Psb=Psb: e.activation(
                        Psb[:], pA[:], AF.Exp, bias=sb["cM12"][:], scale=0.125),
                       reads=[f"l1t_pA{pr}", "cM12"], writes=[f"Psb{pr}"])
                    jloc = kt * 128 - q0
                    if jloc >= 0:
                        if jloc > 0:
                            op("vector", lambda e, jloc=jloc, Psb=Psb: e.memset(
                                Psb[:, 0:jloc], 0.0), writes=[f"Psb{pr}"])
                        op("vector", lambda e, jloc=jloc, Psb=Psb: e.tensor_mul(
                            Psb[:, jloc:jloc + 128], Psb[:, jloc:jloc + 128],
                            sb["UTI"][:]),
                            reads=["UTI"], writes=[f"Psb{pr}"])
                    if kt + 1 < nkt:
                        emit_qk(kt + 1)
                    for s in range(4):
                        ctx.mm([(Psb[:, s * 128:(s + 1) * 128],
                                 vex[:, kt * 260 + hl * 65:kt * 260 + hl * 65 + 65])],
                               pO[s][:, 0:65], reads=[f"Psb{pr}", "vex", "vex_ones"],
                               writes=[f"l1t_pO{s}"],
                               start=(kt == 0), stop=(kt == nkt - 1))
                for s in range(4):
                    c = qb * 4 + s
                    op("vector", lambda e, s=s: e.reciprocal(
                        dcol[:], pO[s][:, 64:65]),
                        reads=[f"l1t_pO{s}"], writes=["l1t_dc"])
                    op("vector", lambda e, s=s, c=c, h=h: e.tensor_scalar(
                        ytm1[:, c * D + h * 64:c * D + h * 64 + 64],
                        pO[s][:, 0:64], dcol[:], None, ALU.mult),
                        reads=[f"l1t_pO{s}", "l1t_dc"], writes=["ytm1"])
    op("sync", lambda e: e.dma_start(out=dr["ytm1d"], in_=ytm1[:]),
       reads=["ytm1"], writes=["ytm1d"], dma=True)
    return [p, ypool]



def kernel(**inputs):
    ins = {k: np.ascontiguousarray(np.asarray(v)) for k, v in inputs.items()}
    x = ins["L0_attn_x"].astype(np.float32, copy=False)

    def tr(a):
        return np.ascontiguousarray(a.T)

    common = {
        "wqt": tr(ins["L0_attn_Wq"]), "wkt": tr(ins["L0_attn_Wk"]),
        "wvt": tr(ins["L0_attn_Wv"]), "wpt": tr(ins["L0_attn_Wp"]),
        "bp": ins["L0_attn_bp"].reshape(1, D),
        "rw": ins["L0_ffn_router_w"], "rb": ins["L0_ffn_router_b"].reshape(1, 8),
        "w1s": np.concatenate([ins["L0_ffn_routed_w1"], ins["L0_ffn_shared_w1"]], 0),
        "w3s": np.concatenate([ins["L0_ffn_routed_w3"], ins["L0_ffn_shared_w3"]], 0),
        "w2s": np.concatenate([ins["L0_ffn_routed_w2"], ins["L0_ffn_shared_w2"]], 0),
        "aqt": tr(ins["L1_attn_Wq"]), "akt": tr(ins["L1_attn_Wk"]),
        "avt": tr(ins["L1_attn_Wv"]), "aot": tr(ins["L1_attn_Wo"]),
        "wg": ins["L1_ffn_wg"],
    }
    common = {k: np.ascontiguousarray(v) for k, v in common.items()}
    nc = build_nc()
    in_maps = []
    for c in range(NCORES):
        m = dict(common)
        m["x0"] = x[c]
        m["w1e"] = ins["L1_ffn_w1"][c]
        m["b1e"] = np.ascontiguousarray(ins["L1_ffn_b1"][c].reshape(32, 128).T)
        m["w2e"] = ins["L1_ffn_w2"][c]
        m["b2e"] = ins["L1_ffn_b2"][c].reshape(1, D)
        es = np.zeros((1, 8), np.float32)
        es[0, c] = 1.0
        m["esel"] = es
        m["esel16"] = np.ascontiguousarray(np.tile(es, (1, 16)))
        in_maps.append(m)
    res = run_bass_kernel_spmd(nc, in_maps, core_ids=list(range(NCORES)))
    return np.stack([r["out"] for r in res.results], axis=0)


# version 23
# speedup vs baseline: 1.0598x; 1.0598x over previous
import numpy as np
import ml_dtypes
import concourse.bass as bass
import concourse.mybir as mybir
from concourse.bass_core_rust import IndirectOffsetOnAxis
from concourse.bass_utils import run_bass_kernel_spmd

F32 = mybir.dt.float32
BF16 = mybir.dt.bfloat16
I32 = mybir.dt.int32
AF = mybir.ActivationFunctionType
ALU = mybir.AluOpType
AX = mybir.AxisListType.X

NCORES = 8
T = 4096
D = 1024
CAP = 5120
NSLOT = 5632
TRASH = NSLOT - 1
NTILE = T // 128
GTILE = (NCORES * T) // 128

_L0D_SEL = list(range(10))
_BATCH_HALF = 99

ENGC = ("tensor", "vector", "scalar", "gpsimd", "sync")
DMAQ = ("sync", "gpsimd", "scalar")


class Ctx:

    def __init__(self, nc):
        self.nc = nc
        self.csem = {n: nc.alloc_semaphore(f"C_{n}") for n in ENGC}
        self.dsem = {n: nc.alloc_semaphore(f"D_{n}") for n in DMAQ}
        self.ccnt = {n: 0 for n in ENGC}
        self.dcnt = {n: 0 for n in DMAQ}
        self.wr = {}
        self.rd = {}
        self.mark = {n: {} for n in ENGC}
        self.sched = {n: [] for n in ENGC}
        self.dslot = {}

    def _sem(self, key):
        if key[0] == "c":
            return self.csem[key[1]]
        if key[0] == "s":
            return self.dslot[key[1]][0]
        return self.dsem[key[1]]

    def _need(self, waits, eng, key, cnt):
        if key == ("c", eng) and eng == "tensor":
            return
        m = self.mark[eng]
        if m.get(key, -1) >= cnt:
            return
        m[key] = cnt
        waits.append((key, cnt))

    def _bump(self, key, cnt):
        if key[0] == "d":
            return self.dcnt[key[1]]
        return cnt

    def _deps(self, eng, reads, writes):
        waits = []
        for t in reads:
            wv = self.wr.get(t)
            if wv:
                self._need(waits, eng, wv[0], self._bump(wv[0], wv[1]))
        for t in writes:
            wv = self.wr.get(t)
            if wv:
                self._need(waits, eng, wv[0], self._bump(wv[0], wv[1]))
            for (k, c) in self.rd.get(t, ()):
                self._need(waits, eng, k, self._bump(k, c))
        return waits

    def op(self, eng, fn, reads=(), writes=(), dma=False, dkey=None):
        waits = self._deps(eng, reads, writes)
        if dma:
            if dkey is not None:
                if dkey not in self.dslot:
                    self.dslot[dkey] = [self.nc.alloc_semaphore(f"S_{dkey}"), 0]
                ent = self.dslot[dkey]
                ent[1] += 16
                cnt = ent[1]
                key = ("s", dkey)
                iv = 16
            else:
                self.dcnt[eng] += 16
                cnt = self.dcnt[eng]
                key = ("d", eng)
                iv = 16
        else:
            self.ccnt[eng] += 1
            cnt = self.ccnt[eng]
            key = ("c", eng)
            iv = 1
        self.sched[eng].append((waits, fn, self._sem(key), iv))
        for t in writes:
            self.wr[t] = (key, cnt)
            self.rd[t] = []
        for t in reads:
            self.rd.setdefault(t, []).append((key, cnt))
        return cnt

    def mm(self, pairs, out_ap, reads=(), writes=(), start=True, stop=True):
        for lhsT, rhs in pairs:
            assert lhsT.base_partition() == rhs.base_partition(), \
                (lhsT.tensor.name, lhsT.base_partition(), rhs.tensor.name,
                 rhs.base_partition())
        def fn(e, pairs=tuple(pairs), out_ap=out_ap, start=start, stop=stop):
            n = len(pairs)
            ins = None
            for i, (lhsT, rhs) in enumerate(pairs):
                ins = e.matmul(out_ap, lhsT, rhs,
                               start=(start and i == 0),
                               stop=(stop and i == n - 1))
            return ins
        return self.op("tensor", fn, reads, writes)

    def barrier(self):
        targets = [(("c", n), self.ccnt[n]) for n in ENGC if self.ccnt[n] > 0]
        targets += [(("d", n), self.dcnt[n]) for n in DMAQ if self.dcnt[n] > 0]
        targets += [(("s", k), v[1]) for k, v in self.dslot.items() if v[1] > 0]
        for eng in ENGC:
            waits = []
            for key, cnt in targets:
                if key == ("c", eng):
                    continue
                self._need(waits, eng, key, cnt)
            if waits:
                self.sched[eng].append((waits, None, None, 0))

    def end_wait(self, eng, key_eng):
        key = ("d", key_eng)
        self.sched[eng].append(([(key, self.dcnt[key_eng])], None, None, 0))
        self.mark[eng][key] = self.dcnt[key_eng]

    def replay(self, block):
        def make(name):
            def run(eng):
                for waits, fn, sem, iv in self.sched[name]:
                    for key, cnt in waits:
                        eng.wait_ge(self._sem(key), cnt)
                    if fn is not None:
                        fn(eng).then_inc(sem, iv)
            return run
        block.tensor(make("tensor"))
        block.vector(make("vector"))
        block.scalar(make("scalar"))
        block.gpsimd(make("gpsimd"))
        block.sync(make("sync"))


class Pool:

    def __init__(self, nc):
        self.nc = nc
        self.guards = []

    def sb(self, name, shape, dt):
        g = self.nc.sbuf_tensor(name, shape, dt)
        h = g.__enter__()
        self.guards.append(g)
        return h

    def ps(self, name, shape, dt=F32):
        g = self.nc.psum_tensor(name, shape, dt)
        h = g.__enter__()
        self.guards.append(g)
        return h

    def free(self):
        for g in reversed(self.guards):
            g.__exit__(None, None, None)
        self.guards = []



def _consts():
    bf = ml_dtypes.bfloat16
    c = {}
    c["IDENT"] = np.eye(128, dtype=bf)
    c["IDENTF"] = np.eye(128, dtype=np.float32)
    k = np.arange(128)[:, None]
    i = np.arange(128)[None, :]
    c["UTI"] = (k <= i).astype(bf)
    c["UTS"] = (k < i).astype(bf)
    c["ONESB"] = np.ones((128, 128), dtype=bf)
    c["ONESF"] = np.ones((128, 128), dtype=np.float32)
    c["IOTAB"] = np.ascontiguousarray(
        np.broadcast_to(np.arange(128, dtype=np.float32)[None, :], (128, 128)))
    c["IOTAC"] = np.ascontiguousarray(np.arange(128, dtype=np.float32)[:, None])
    P = np.zeros((128, 128), dtype=np.float32)
    for r in range(128):
        if r % 64 < 32:
            P[r, r + 32] = -1.0
        else:
            P[r, r - 32] = 1.0
    c["ROTT"] = np.ascontiguousarray(P.T.astype(bf))
    inv = 1.0 / (10000.0 ** (np.arange(0, 64, 2, dtype=np.float64) / 64))
    t = np.arange(T, dtype=np.float64)
    fr = np.outer(inv, t)
    c["COS"] = np.ascontiguousarray(np.tile(np.cos(fr), (4, 1)).astype(bf))
    c["SIN"] = np.ascontiguousarray(np.tile(np.sin(fr), (4, 1)).astype(bf))
    cc = np.arange(256)
    pb = (cc % 32 // 8) * 8192 + (cc // 32) * 1024 + (cc % 8) * 128
    c["PBC"] = np.ascontiguousarray(pb.astype(np.float32)[:, None])
    return c


def build_nc(debug_taps=()):
    nc = bass.Bass("TRN2", target_bir_lowering=False)

    xin = nc.dram_tensor("x0", [T, D], F32, kind="ExternalInput").ap()
    w = {}
    for n in ("wqt", "wkt", "wvt", "wpt", "aqt", "akt", "avt", "aot"):
        w[n] = nc.dram_tensor(n, [D, D], F32, kind="ExternalInput").ap()
    w["bp"] = nc.dram_tensor("bp", [1, D], F32, kind="ExternalInput").ap()
    w["rw"] = nc.dram_tensor("rw", [D, 8], F32, kind="ExternalInput").ap()
    w["rb"] = nc.dram_tensor("rb", [1, 8], F32, kind="ExternalInput").ap()
    w["w1s"] = nc.dram_tensor("w1s", [10, D, 512], F32, kind="ExternalInput").ap()
    w["w3s"] = nc.dram_tensor("w3s", [10, D, 512], F32, kind="ExternalInput").ap()
    w["w2s"] = nc.dram_tensor("w2s", [10, 512, D], F32, kind="ExternalInput").ap()
    w["wg"] = nc.dram_tensor("wg", [D, 8], F32, kind="ExternalInput").ap()
    w["w1e"] = nc.dram_tensor("w1e", [D, 4096], F32, kind="ExternalInput").ap()
    w["b1e"] = nc.dram_tensor("b1e", [128, 32], F32, kind="ExternalInput").ap()
    w["w2e"] = nc.dram_tensor("w2e", [4096, D], F32, kind="ExternalInput").ap()
    w["b2e"] = nc.dram_tensor("b2e", [1, D], F32, kind="ExternalInput").ap()
    w["esel"] = nc.dram_tensor("esel", [1, 8], F32, kind="ExternalInput").ap()
    w["esel16"] = nc.dram_tensor("esel16", [1, 128], F32, kind="ExternalInput").ap()
    out = nc.dram_tensor("out", [T, D], F32, kind="ExternalOutput").ap()

    dr = {}
    dr["xn0"] = nc.dram_tensor("xn0_d", [T, D], BF16).ap()
    dr["x1"] = nc.dram_tensor("x1_d", [T, D], F32).ap()
    dr["agx_in"] = nc.dram_tensor("agx_in", [T, D], BF16)
    dr["agx_out"] = nc.dram_tensor("agx_out", [NCORES * T + 128, D], BF16)
    dr["aglg_in"] = nc.dram_tensor("aglg_in", [T, 8], F32)
    dr["aglg_out"] = nc.dram_tensor("aglg_out", [NCORES * T, 8], F32)
    dr["disp"] = nc.dram_tensor("disp_d", [NSLOT, D], BF16).ap()
    dr["eout"] = nc.dram_tensor("eout_d", [NSLOT, D], BF16).ap()
    dr["rsin"] = nc.dram_tensor("rsin_d", [NCORES * T + 128, D], BF16)
    dr["rsout"] = nc.dram_tensor("rsout_d", [T, D], BF16)
    dr["tbl"] = nc.dram_tensor("tbl_d", [256, 260], F32).ap()
    dr["bounce"] = nc.dram_tensor("bounce", [2, 256], F32).ap()
    dr["gfm"] = nc.dram_tensor("gfm_d", [8, T], F32).ap()
    dr["ytm1d"] = nc.dram_tensor("ytm1_d", [128, NTILE * D], BF16).ap()
    dr["xa"] = nc.dram_tensor("xa_d", [T, D], F32).ap()
    for n in ("aqt", "akt", "avt", "aot"):
        dr[n + "b"] = nc.dram_tensor(n + "_b16", [D, D], BF16).ap()

    consts = _consts()
    capd = {k: nc.inline_tensor(v, k) for k, v in consts.items()}

    taps = {}
    tapspec = {
        "xn0": ([T, D], BF16), "x1p": ([T, D], F32), "x1": ([T, D], F32),
        "gatesfm": ([8, T], F32), "logits": ([T, 8], F32),
        "mmat": ([128, GTILE], F32), "posmat": ([128, GTILE], F32),
        "slotmat": ([128, GTILE], F32), "gsmat": ([128, GTILE], F32),
        "toki": ([128, NSLOT // 128], I32), "tokp": ([128, NSLOT // 128], I32),
        "gsl": ([128, NSLOT // 128], F32), "tblm": ([256, 260], F32),
        "dbgm": ([128, 256], F32), "dbgc": ([128, 8], F32),
        "offsrow": ([1, GTILE], F32), "roffif": ([1, GTILE], F32),
        "regdump": ([1, GTILE], I32),
        "disp": ([NSLOT, D], BF16),
        "eoutd": ([NSLOT, D], BF16), "rsin": ([NCORES * T, D], BF16),
        "rsout": ([T, D], BF16), "ytm0": ([T, D], BF16),
        "ytm1": ([T, D], BF16), "agx": ([NCORES * T, D], BF16),
    }
    for name in debug_taps:
        shp, dt = tapspec[name]
        taps[name] = nc.dram_tensor("tap_" + name, shp, dt, kind="ExternalOutput").ap()

    ctx = Ctx(nc)
    with nc.Block() as block:
        _emit(ctx, nc, xin, w, out, dr, capd, taps)
        ctx.replay(block)
    return nc



def _emit(ctx, nc, xin, w, out, dr, capd, taps):
    op = ctx.op
    pers = Pool(nc)
    sb = {}
    for n, shp, dt in [("IDENT", [128, 128], BF16), ("IDENTF", [128, 128], F32),
                       ("UTI", [128, 128], BF16),
                       ("UTS", [128, 128], BF16), ("ONESB", [128, 128], BF16),
                       ("ONESF", [128, 128], F32), ("IOTAB", [128, 128], F32),
                       ("IOTAC", [128, 1], F32), ("ROTT", [128, 128], BF16)]:
        sb[n] = pers.sb(n.lower(), shp, dt)
        op("sync", lambda e, n=n: e.dma_start(out=sb[n][:], in_=capd[n].ap()),
           writes=[n], dma=True)
    for n, src, shp in [("bp", "bp", [1, D]), ("rb", "rb", [1, 8]),
                        ("b2e", "b2e", [1, D]), ("b1e", "b1e", [128, 32]),
                        ("esel", "esel", [1, 8])]:
        sb[n] = pers.sb(n + "_sb", shp, F32)
        op("sync", lambda e, n=n, src=src: e.dma_start(out=sb[n][:], in_=w[src]),
           writes=[n], dma=True)
    for n, src in [("rwsb", "rw"), ("wgsb", "wg")]:
        sb[n] = pers.sb(n, [128, 64], BF16)
        src_ap = bass.AP(w[src].tensor, 0, [[8, 128], [1024, 8], [1, 8]])
        op("gpsimd", lambda e, n=n, src_ap=src_ap: e.dma_start(
            out=sb[n][:].rearrange("p (c e) -> p c e", e=8), in_=src_ap),
            writes=[n], dma=True)
    for n in ("M", "G1", "POS", "SLOT", "GS", "CUMS", "KTMP"):
        sb[n] = pers.sb(n, [128, GTILE], F32)
    NST = NSLOT // 128
    sb["TOKI"] = pers.sb("TOKI", [128, NST], I32)
    sb["TOKP"] = pers.sb("TOKP", [128, NST], I32)
    sb["GSL"] = pers.sb("GSL", [128, NST], F32)
    sb["Mb"] = pers.sb("Mb", [128, GTILE], BF16)
    sb["tot"] = pers.sb("tot", [1, GTILE], F32)
    sb["offs"] = pers.sb("offs", [1, GTILE], F32)
    sb["rmin"] = pers.sb("rmin", [1, GTILE], F32)
    sb["roffi"] = pers.sb("roffi", [1, GTILE], I32)
    sb["tot2"] = pers.sb("tot2", [128, 2], BF16)
    sb["offs2"] = pers.sb("offs2", [128, 2], F32)
    sb["scs"] = pers.sb("scs", [1, 2], F32)
    sb["sc1"] = pers.sb("sc1", [128, 8], F32)
    sb["sc2"] = pers.sb("sc2", [128, 8], F32)
    sb["sc3"] = pers.sb("sc3", [128, 8], F32)
    sb["fc"] = pers.sb("fc", [128, 8], F32)
    sb["eselb"] = pers.sb("eselb", [128, 8], F32)
    sb["cEPS"] = pers.sb("cEPS", [128, 1], F32)
    sb["cM12"] = pers.sb("cM12", [128, 1], F32)
    op("vector", lambda e: e.memset(sb["cEPS"][:], 1e-5), writes=["cEPS"])
    op("vector", lambda e: e.memset(sb["cM12"][:], -12.0), writes=["cM12"])
    psepool = Pool(nc)
    pse = psepool.ps("pse", [128, 8])
    ctx.mm([(sb["ONESF"][0:1, :], sb["esel"][:])], pse[:],
           reads=["ONESF", "esel"], writes=["pse"])
    op("vector", lambda e: e.tensor_copy(sb["eselb"][:], pse[:]),
       reads=["pse"], writes=["eselb"])
    ctx.barrier()
    psepool.free()

    cvt = [pers.sb(f"cvt{i}", [128, D], BF16) for i in range(2)]
    for n in ("aqt", "akt", "avt", "aot"):
        for c in range(8):
            q = c % 2
            src = bass.AP(w[n].tensor, c * 128 * D, [[D, 128], [1, D]])
            dst = bass.AP(dr[n + "b"].tensor, c * 128 * D, [[D, 128], [1, D]])
            op("gpsimd", lambda e, src=src, q=q: e.dma_start(
                out=cvt[q][:], in_=src),
                writes=[f"cv{q}"], dma=True, dkey=f"cv{q}")
            op("sync", lambda e, dst=dst, q=q: e.dma_start(
                out=dst, in_=cvt[q][:]),
                reads=[f"cv{q}"], writes=[f"{n}b_d"], dma=True)

    _l0_ln(ctx, nc, sb, xin, dr, taps)
    ctx.barrier()
    _l0_attn(ctx, nc, sb, xin, w, dr, taps)
    ctx.barrier()
    _l0_moe(ctx, nc, sb, w, dr, taps)
    ctx.barrier()

    _l1_pre(ctx, nc, sb, w, dr, taps)
    early, eb = _l1_attn_pre(ctx, nc, sb, dr, capd)
    _l1_routing(ctx, nc, sb, w, dr, capd, taps)
    ctx.barrier()
    rg = [list(range(NCORES))]
    op("gpsimd", lambda e: e.collective_compute(
        "AllGather", ALU.bypass, replica_groups=rg,
        ins=[dr["agx_in"].ap().opt()],
        outs=[dr["agx_out"].ap()[0:NCORES * T, :].opt()]),
        reads=[f"agxin{j}" for j in range(NTILE)], writes=["agxout"])
    if "agx" in taps:
        op("sync", lambda e: e.dma_start(
            out=taps["agx"], in_=dr["agx_out"].ap()[0:NCORES * T, :]),
            reads=["agxout"], dma=True)
    pd = Pool(nc)
    dtags = _l1_dispatch(ctx, nc, sb, dr, taps, pd)
    apools = _l1_attn(ctx, nc, sb, w, dr, capd, taps, early, eb)
    ctx.barrier()
    for pp in apools:
        pp.free()
    pd.free()
    early.free()
    etags = _l1_ffn(ctx, nc, sb, w, dr, dtags, taps)
    _l1_combine_final(ctx, nc, sb, w, out, dr, etags, taps)

    ctx.end_wait("sync", "sync")
    ctx.end_wait("sync", "gpsimd")
    ctx.end_wait("sync", "scalar")


def _ln_tile(ctx, sb, xt, xnb, sq):
    op = ctx.op
    F = sb["fc"]
    op("vector", lambda e: e.reduce_sum(F[:, 0:1], xt[:], AX),
       reads=["xt"], writes=["f0"])
    op("vector", lambda e: e.tensor_scalar(F[:, 1:2], F[:, 0:1], 1.0 / D, None, ALU.mult),
       reads=["f0"], writes=["f1"])
    op("scalar", lambda e: e.activation(sq[:], xt[:], AF.Square, accum_out=F[:, 2:3]),
       reads=["xt"], writes=["sq", "f2"])
    op("vector", lambda e: e.tensor_mul(F[:, 3:4], F[:, 1:2], F[:, 1:2]),
       reads=["f1"], writes=["f3"])
    op("vector", lambda e: e.scalar_tensor_tensor(F[:, 3:4], F[:, 2:3], 1.0 / D,
                                                  F[:, 3:4], ALU.mult, ALU.subtract),
       reads=["f2", "f3"], writes=["f3"])
    op("scalar", lambda e: e.activation(F[:, 4:5], F[:, 3:4], AF.Sqrt,
                                        bias=sb["cEPS"][:]),
       reads=["f3", "cEPS"], writes=["f4"])
    op("vector", lambda e: e.reciprocal(F[:, 5:6], F[:, 4:5]),
       reads=["f4"], writes=["f5"])
    op("vector", lambda e: e.tensor_scalar(xnb[:], xt[:], F[:, 1:2], F[:, 5:6],
                                           ALU.subtract, ALU.mult),
       reads=["xt", "f1", "f5"], writes=["xnb"])



def _l0_ln(ctx, nc, sb, xin, dr, taps):
    op = ctx.op
    p = Pool(nc)
    NB = 2
    xt = [p.sb(f"l0a_xt{i}", [128, D], F32) for i in range(NB)]
    xnb = [p.sb(f"l0a_xnb{i}", [128, D], BF16) for i in range(NB)]
    xnf = [p.sb(f"l0a_xnf{i}", [128, D], BF16) for i in range(NB)]
    sq = [p.sb(f"l0a_sq{i}", [128, D], F32) for i in range(NB)]
    fcb = [p.sb(f"l0a_fc{i}", [128, 8], F32) for i in range(NB)]
    sc1 = [p.sb(f"l0a_sc1{i}", [128, 8], F32) for i in range(NB)]
    sc2 = [p.sb(f"l0a_sc2{i}", [128, 8], F32) for i in range(NB)]
    sc3 = [p.sb(f"l0a_sc3{i}", [128, 8], F32) for i in range(NB)]
    gst = [p.sb(f"l0a_gst{i}", [8, 128], F32) for i in range(NB)]
    pt = [p.ps(f"l0a_pt{i}", [128, 128], BF16) for i in range(NB)]
    plg = [p.ps(f"l0a_plg{i}", [128, 8]) for i in range(NB)]
    pg = [p.ps(f"l0a_pg{i}", [8, 128]) for i in range(NB)]

    def front(j):
        q = j % NB
        op("sync", lambda e, j=j, q=q: e.dma_start(
            out=xt[q][:], in_=xin[j * 128:(j + 1) * 128, :]),
            writes=[f"b_xt{q}"], dma=True, dkey=f"b_xt{q}")
        _ln_tile_r(ctx, sb, xt[q], xnb[q], sq[q], fcb[q], f"b{q}")

    def back(j):
        q = j % NB
        F = fcb[q]
        op("sync", lambda e, j=j, q=q: e.dma_start(
            out=dr["xn0"][j * 128:(j + 1) * 128, :], in_=xnb[q][:]),
            reads=[f"b_xnb{q}"], writes=[f"xn0_{j}"], dma=True)
        for cc in range(8):
            op("tensor", lambda e, cc=cc, q=q: e.transpose(
                pt[q][:], xnb[q][:, cc * 128:(cc + 1) * 128], sb["IDENT"][:]),
               reads=[f"b_xnb{q}", "IDENT"], writes=[f"b_pt{q}"])
            op("scalar", lambda e, cc=cc, q=q: e.copy(
                xnf[q][:, cc * 128:(cc + 1) * 128], pt[q][:]),
               reads=[f"b_pt{q}"], writes=[f"b_xnf{q}_{cc}"])
        ctx.mm([(xnf[q][:, cc * 128:(cc + 1) * 128], sb["rwsb"][:, cc * 8:(cc + 1) * 8])
                for cc in range(8)], plg[q][:],
               reads=[f"b_xnf{q}_{cc}" for cc in range(8)] + ["rwsb"],
               writes=[f"b_plg{q}"], stop=False)
        ctx.mm([(sb["ONESF"][0:1, :], sb["rb"][:])], plg[q][:],
               reads=["ONESF", "rb"], writes=[f"b_plg{q}"], start=False)
        op("vector", lambda e, q=q: e.reduce_max(F[:, 0:1], plg[q][:], AX),
           reads=[f"b_plg{q}"], writes=[f"b{q}_g0"])
        op("vector", lambda e, q=q: e.tensor_scalar(sc1[q][:], plg[q][:], F[:, 0:1],
                                                    None, ALU.is_ge),
           reads=[f"b_plg{q}", f"b{q}_g0"], writes=[f"b_sc1{q}"])
        op("vector", lambda e, q=q: e.scalar_tensor_tensor(
            sc2[q][:], sc1[q][:], -1e30, plg[q][:], ALU.mult, ALU.add),
           reads=[f"b_sc1{q}", f"b_plg{q}"], writes=[f"b_sc2{q}"])
        op("vector", lambda e, q=q: e.reduce_max(F[:, 1:2], sc2[q][:], AX),
           reads=[f"b_sc2{q}"], writes=[f"b{q}_g1"])
        op("vector", lambda e, q=q: e.tensor_scalar(sc1[q][:], plg[q][:], F[:, 1:2],
                                                    None, ALU.is_ge),
           reads=[f"b_plg{q}", f"b{q}_g1"], writes=[f"b_sc1{q}"])
        op("vector", lambda e, q=q: e.tensor_scalar(F[:, 2:3], F[:, 0:1], -1.0,
                                                    None, ALU.mult),
           reads=[f"b{q}_g0"], writes=[f"b{q}_g2"])
        op("scalar", lambda e, q=q: e.activation(sc2[q][:], plg[q][:], AF.Exp,
                                                 bias=F[:, 2:3]),
           reads=[f"b_plg{q}", f"b{q}_g2"], writes=[f"b_sc2{q}"])
        op("vector", lambda e, q=q: e.tensor_mul(sc3[q][:], sc2[q][:], sc1[q][:]),
           reads=[f"b_sc2{q}", f"b_sc1{q}"], writes=[f"b_sc3{q}"])
        op("vector", lambda e, q=q: e.reduce_sum(F[:, 3:4], sc3[q][:], AX),
           reads=[f"b_sc3{q}"], writes=[f"b{q}_g3"])
        op("vector", lambda e, q=q: e.reciprocal(F[:, 4:5], F[:, 3:4]),
           reads=[f"b{q}_g3"], writes=[f"b{q}_g4"])
        op("vector", lambda e, q=q: e.tensor_scalar(sc3[q][:], sc3[q][:], F[:, 4:5],
                                                    None, ALU.mult),
           reads=[f"b_sc3{q}", f"b{q}_g4"], writes=[f"b_sc3{q}"])
        op("tensor", lambda e, q=q: e.transpose(pg[q][:], sc3[q][:], sb["IDENTF"][:]),
           reads=[f"b_sc3{q}", "IDENTF"], writes=[f"b_pg{q}"])
        op("vector", lambda e, q=q: e.tensor_copy(gst[q][:], pg[q][:]),
           reads=[f"b_pg{q}"], writes=[f"b_gst{q}"])
        gap = bass.AP(dr["gfm"].tensor, j * 128, [[T, 8], [1, 128]])
        op("sync", lambda e, gap=gap, q=q: e.dma_start(out=gap, in_=gst[q][:]),
           reads=[f"b_gst{q}"], writes=["gfm_d"], dma=True)

    front(0)
    for j in range(NTILE):
        if j + 1 < NTILE:
            front(j + 1)
        back(j)
    if "xn0" in taps:
        op("sync", lambda e: e.dma_start(out=taps["xn0"], in_=dr["xn0"]),
           reads=[f"xn0_{j}" for j in range(NTILE)], dma=True)
    if "gatesfm" in taps:
        op("sync", lambda e: e.dma_start(out=taps["gatesfm"], in_=dr["gfm"]),
           reads=["gfm_d"], dma=True)
    ctx.barrier()
    p.free()



def _l0_attn(ctx, nc, sb, xin, w, dr, taps):
    op = ctx.op
    ypool = Pool(nc)
    ytm = ypool.sb("ytm", [128, NTILE * D], BF16)

    p = Pool(nc)
    qg = p.sb("l0b_q", [128, T], BF16)
    kg = p.sb("l0b_k", [128, T], BF16)
    ktg = p.sb("l0b_kt", [128, T], BF16)
    vex = p.sb("l0b_vex", [128, NTILE * 130], BF16)
    xst = p.sb("l0b_xst", [128, 8 * T], BF16)
    phs = p.sb("l0b_phs", [128, T], BF16)
    ssb = p.sb("l0b_s", [128, 72], BF16)
    asb2 = [p.sb(f"l0b_a{i}", [128, 128], BF16) for i in range(2)]
    wq = p.sb("l0b_wq", [128, 1024], BF16)
    wk = p.sb("l0b_wk", [128, 1024], BF16)
    wv = p.sb("l0b_wv", [128, 1024], BF16)
    dcol = p.sb("l0b_dcol", [128, 4], F32)
    pbank = [p.ps(f"l0b_pb{i}", [128, 512]) for i in range(4)]
    pS2 = [p.ps(f"l0b_pS{i}", [64, 72]) for i in range(2)]
    pT2 = [p.ps(f"l0b_pT{i}", [128, 128], BF16) for i in range(2)]

    xn0tags = [f"xn0_{j}" for j in range(NTILE)]
    for tb in range(8):
        for cc in range(8):
            op("sync", lambda e, cc=cc, tb=tb: e.dma_start_transpose(
                xst[:, cc * T + tb * 512:cc * T + tb * 512 + 512],
                dr["xn0"][tb * 512:(tb + 1) * 512, cc * 128:(cc + 1) * 128]),
                reads=xn0tags[tb * 4:tb * 4 + 4], writes=[f"xst{tb}"], dma=True)
    for g in range(8):
        for name, wt, buf in (("wq", "wqt", wq), ("wk", "wkt", wk), ("wv", "wvt", wv)):
            src = bass.AP(w[wt].tensor, g * 128, [[1024, 128], [1024 * 128, 8], [1, 128]])
            op("gpsimd", lambda e, src=src, buf=buf: e.dma_start(
                out=buf[:].rearrange("p (c f) -> p c f", f=128), in_=src),
                writes=[name], dma=True, dkey=name)
        vones = bass.AP(vex, 64, [[NTILE * 130, 128], [130, NTILE], [65, 2], [1, 1]])
        op("vector", lambda e, vones=vones: e.memset(vones, 1.0),
           reads=["vex"], writes=["vex_ones"])

        pj = 0
        for tb in range(8):
            xr = [f"xst{tb}"]
            for name, buf, dst, dtag in (("wq", wq, qg, "qg"), ("wk", wk, kg, "kg")):
                pb_ = pbank[pj % 4]
                ptag = f"pbank{pj % 4}"
                pj += 1
                ctx.mm([(buf[:, cc * 128:(cc + 1) * 128],
                         xst[:, cc * T + tb * 512:cc * T + tb * 512 + 512])
                        for cc in range(8)], pb_[:],
                       reads=[name] + xr, writes=[ptag])
                op("scalar", lambda e, dst=dst, tb=tb, pb_=pb_: e.copy(
                    dst[:, tb * 512:(tb + 1) * 512], pb_[:]),
                   reads=[ptag], writes=[dtag])
            for ts_ in range(4):
                pb_ = pbank[pj % 4]
                ptag = f"pbank{pj % 4}"
                pj += 1
                ctx.mm([(xst[:, cc * T + tb * 512 + ts_ * 128:
                             cc * T + tb * 512 + ts_ * 128 + 128],
                         wv[:, cc * 128:(cc + 1) * 128]) for cc in range(8)],
                       pb_[:, 0:128], reads=["wv"] + xr, writes=[ptag])
                ti = tb * 4 + ts_
                for hh in range(2):
                    op("scalar", lambda e, ti=ti, hh=hh, pb_=pb_: e.copy(
                        vex[:, ti * 130 + hh * 65:ti * 130 + hh * 65 + 64],
                        pb_[:, hh * 64:hh * 64 + 64]),
                        reads=[ptag], writes=["vex"])
        for dtag, buf in (("qg", qg), ("kg", kg)):
            op("vector", lambda e, buf=buf: e.tensor_scalar(phs[:], buf[:], 0.0,
                                                            None, ALU.min),
               reads=[dtag], writes=["phs"])
            op("scalar", lambda e: e.activation(phs[:], phs[:], AF.Exp),
               reads=["phs"], writes=["phs"])
            op("vector", lambda e, buf=buf: e.scalar_tensor_tensor(
                buf[:], buf[:], 0.0, phs[:], ALU.max, ALU.add),
               reads=[dtag, "phs"], writes=[dtag])
        for c in range(NTILE):
            tp_ = c % 2
            op("tensor", lambda e, c=c, tp_=tp_: e.transpose(
                pT2[tp_][:], kg[:, c * 128:(c + 1) * 128], sb["IDENT"][:]),
               reads=["kg", "IDENT"], writes=[f"pT{tp_}"])
            op("vector", lambda e, c=c, tp_=tp_: e.tensor_copy(
                ktg[:, c * 128:(c + 1) * 128], pT2[tp_][:]),
               reads=[f"pT{tp_}"], writes=["ktg"])
        for c in range(NTILE):
            for hh in range(2):
                b = hh * 64
                ctx.mm([(kg[b:b + 64, c * 128:(c + 1) * 128],
                         qg[b:b + 64, c * 128:(c + 1) * 128])],
                       pbank[hh][:, 0:128], reads=["kg", "qg"], writes=[f"pbank{hh}"])
            for hh in range(2):
                op("vector", lambda e, hh=hh: e.tensor_mul(
                    asb2[hh][:], pbank[hh][:, 0:128], sb["UTI"][:]),
                   reads=[f"pbank{hh}", "UTI"], writes=[f"asb{hh}"])
            for hh in range(2):
                b = hh * 64
                pairs = [(asb2[hh][:], vex[:, c * 130 + hh * 65:c * 130 + hh * 65 + 65])]
                rd = [f"asb{hh}", "vex", "vex_ones"]
                if c > 0:
                    pairs.append((qg[b:b + 64, c * 128:(c + 1) * 128],
                                  ssb[b:b + 64, 0:65]))
                    rd += ["qg", f"ssb{hh}"]
                ctx.mm(pairs, pbank[2 + hh][:, 0:65], reads=rd,
                       writes=[f"pbank{2 + hh}"])
                ctx.mm([(ktg[:, c * 128 + b:c * 128 + b + 64],
                         vex[:, c * 130 + hh * 65:c * 130 + hh * 65 + 65])],
                       pS2[hh][0:64, 0:65], reads=["ktg", "vex", "vex_ones"],
                       writes=[f"pS2{hh}"], start=(c == 0), stop=(c == NTILE - 1))
            for hh in range(2):
                b = hh * 64
                op("vector", lambda e, hh=hh: e.tensor_scalar(
                    dcol[:, 2 * hh:2 * hh + 1], pbank[2 + hh][:, 64:65],
                    1e-6, None, ALU.add),
                   reads=[f"pbank{2 + hh}"], writes=[f"dc0{hh}"])
                op("vector", lambda e, hh=hh: e.reciprocal(
                    dcol[:, 2 * hh + 1:2 * hh + 2], dcol[:, 2 * hh:2 * hh + 1]),
                   reads=[f"dc0{hh}"], writes=[f"dc1{hh}"])
                op("scalar", lambda e, c=c, g=g, hh=hh: e.mul(
                    ytm[:, c * D + g * 128 + hh * 64:c * D + g * 128 + hh * 64 + 64],
                    pbank[2 + hh][:, 0:64], dcol[:, 2 * hh + 1:2 * hh + 2]),
                    reads=[f"pbank{2 + hh}", f"dc1{hh}"], writes=["ytm"])
                if c < NTILE - 1:
                    op("vector", lambda e, hh=hh, b=b: e.tensor_copy(
                        ssb[b:b + 64, 0:65], pS2[hh][0:64, 0:65]),
                       reads=[f"pS2{hh}"], writes=[f"ssb{hh}"])
    if "ytm0" in taps:
        op("sync", lambda e: e.dma_start(
            out=taps["ytm0"].rearrange("(j p) d -> p (j d)", p=128), in_=ytm[:]),
            reads=["ytm"], dma=True)
    ctx.barrier()
    p.free()

    p = Pool(nc)
    yf = p.sb("l0c_yf", [128, 1024], BF16)
    x0t = p.sb("l0c_x0", [128, D], F32)
    x1t = p.sb("l0c_x1", [128, D], F32)
    wp = p.sb("l0c_wp", [128, 8192], BF16)
    pT = p.ps("l0c_pT", [128, 128], BF16)
    pU = p.ps("l0c_pU", [128, 512])
    src = bass.AP(w["wpt"].tensor, 0, [[1024, 128], [1024 * 128, 8], [1, 1024]])
    op("gpsimd", lambda e, src=src: e.dma_start(
        out=wp[:].rearrange("p (c f) -> p c f", f=1024), in_=src),
        writes=["wp"], dma=True, dkey="wp")
    for j in range(NTILE):
        r0 = j * 128
        for cc in range(8):
            op("tensor", lambda e, j=j, cc=cc: e.transpose(
                pT[:], ytm[:, j * D + cc * 128:j * D + (cc + 1) * 128], sb["IDENT"][:]),
               reads=["ytm", "IDENT"], writes=["l0c_pT"])
            op("vector", lambda e, cc=cc: e.tensor_copy(
                yf[:, cc * 128:(cc + 1) * 128], pT[:]),
               reads=["l0c_pT"], writes=[f"yf{cc}"])
        op("sync", lambda e, r0=r0: e.dma_start(out=x0t[:], in_=xin[r0:r0 + 128, :]),
           writes=["x0t"], dma=True)
        for ih in range(2):
            ctx.mm([(yf[:, cc * 128:(cc + 1) * 128],
                     wp[:, cc * 1024 + ih * 512:cc * 1024 + ih * 512 + 512])
                    for cc in range(8)], pU[:],
                   reads=[f"yf{cc}" for cc in range(8)] + ["wp"],
                   writes=["l0c_pU"], stop=False)
            ctx.mm([(sb["ONESF"][0:1, :], sb["bp"][0:1, ih * 512:ih * 512 + 512])], pU[:],
                   reads=["ONESF", "bp"], writes=["l0c_pU"], start=False)
            op("vector", lambda e, ih=ih: e.tensor_add(
                x1t[:, ih * 512:(ih + 1) * 512], pU[:], x0t[:, ih * 512:(ih + 1) * 512]),
                reads=["l0c_pU", "x0t"], writes=["x1t"])
        op("sync", lambda e, r0=r0: e.dma_start(out=dr["x1"][r0:r0 + 128, :], in_=x1t[:]),
           reads=["x1t"], writes=[f"x1_{j}"], dma=True)
    if "x1p" in taps:
        op("sync", lambda e: e.dma_start(out=taps["x1p"], in_=dr["x1"]),
           reads=[f"x1_{j}" for j in range(NTILE)], dma=True)
    ctx.barrier()
    p.free()
    ypool.free()



def _l0_moe(ctx, nc, sb, w, dr, taps):
    op = ctx.op
    p = Pool(nc)
    w1p = p.sb("l0d_w1", [128, 5 * 4096], BF16)
    w3p = p.sb("l0d_w3", [128, 5 * 4096], BF16)
    w2p = p.sb("l0d_w2", [128, 5 * 4096], BF16)
    xnd2 = [p.sb(f"l0d_xn{i}", [128, 2048], BF16) for i in range(2)]
    hsb2 = [p.sb(f"l0d_h{i}", [128, 1024], BF16) for i in range(2)]
    silu2 = [p.sb(f"l0d_si{i}", [128, 256], BF16) for i in range(2)]
    gG2 = [p.sb(f"l0d_g{i}", [128, 5 * 256], F32) for i in range(2)]
    x1t = p.sb("l0d_x1", [128, 2 * D], F32)
    o1 = [p.sb(f"l0d_o1{i}", [128, 512], F32) for i in range(2)]
    pa2 = [p.ps(f"l0d_pa{i}", [128, 256]) for i in range(2)]
    pb2 = [p.ps(f"l0d_pb{i}", [128, 256]) for i in range(2)]
    py = [p.ps(f"l0d_py{i}", [128, 512]) for i in range(4)]
    xn0tags = [f"xn0_{j}" for j in range(NTILE)]

    def load_xnd(tb):
        xb = tb % 2
        for cc in range(8):
            op("sync", lambda e, cc=cc, tb=tb, xb=xb: e.dma_start_transpose(
                xnd2[xb][:, cc * 256:(cc + 1) * 256],
                dr["xn0"][tb * 256:tb * 256 + 256, cc * 128:(cc + 1) * 128]),
                reads=xn0tags[tb * 2:tb * 2 + 2], writes=[f"xnd{xb}_{cc}"], dma=True,
                dkey=f"xnd{xb}")

    def load_gates(ps_, tb):
        xb = tb % 2
        nr = 5 if ps_ == 0 else 3
        for e5 in range(nr):
            ex = ps_ * 5 + e5
            src = bass.AP(dr["gfm"].tensor, ex * T + tb * 256,
                          [[0, 128], [1, 256]])
            wt = [f"gG{xb}_{e5}"] + ([f"gGt{xb}"] if e5 == nr - 1 else [])
            op("scalar", lambda e, src=src, xb=xb, e5=e5: e.dma_start(
                out=gG2[xb][:, e5 * 256:(e5 + 1) * 256], in_=src),
                reads=["gfm_d"], writes=wt, dma=True, dkey=f"gG{xb}")

    for ps_ in range(2):
        for e5 in range(5):
            ex = ps_ * 5 + e5
            for name, wsrc, buf, inner, nch in (("w1p", "w1s", w1p, 512, 8),
                                                ("w3p", "w3s", w3p, 512, 8),
                                                ("w2p", "w2s", w2p, 1024, 4)):
                src = bass.AP(w[wsrc].tensor, ex * D * 512,
                              [[inner, 128], [inner * 128, nch], [1, inner]])
                op("gpsimd", lambda e, src=src, buf=buf, e5=e5, inner=inner:
                   e.dma_start(out=buf[:, e5 * 4096:(e5 + 1) * 4096]
                               .rearrange("p (c f) -> p c f", f=inner), in_=src),
                   writes=[name + str(e5)], dma=True, dkey=name + str(e5))
        if ps_ == 0:
            load_xnd(0)
        load_gates(ps_, 0)
        for tb in range(16):
            t0 = tb * 256
            xb = tb % 2
            xr = [f"xnd{xb}_{cc}" for cc in range(8)]
            if tb + 1 < 16 or ps_ == 0:
                load_xnd(tb + 1 if tb + 1 < 16 else 0)
            if tb + 1 < 16:
                load_gates(ps_, tb + 1)
            elif ps_ == 0:
                load_gates(1, 0)
            for e5 in range(5):
                ex = ps_ * 5 + e5
                hp = e5 % 2
                hsb = hsb2[hp]
                for ft in range(4):
                    fp = ft % 2
                    ctx.mm([(w1p[:, e5 * 4096 + cc * 512 + ft * 128:
                             e5 * 4096 + cc * 512 + ft * 128 + 128],
                             xnd2[xb][:, cc * 256:(cc + 1) * 256]) for cc in range(8)],
                           pa2[fp][:], reads=[f"w1p{e5}"] + xr,
                           writes=[f"l0d_pa{fp}"])
                    ctx.mm([(w3p[:, e5 * 4096 + cc * 512 + ft * 128:
                             e5 * 4096 + cc * 512 + ft * 128 + 128],
                             xnd2[xb][:, cc * 256:(cc + 1) * 256]) for cc in range(8)],
                           pb2[fp][:], reads=[f"w3p{e5}"] + xr,
                           writes=[f"l0d_pb{fp}"])
                    op("scalar", lambda e, fp=fp: e.activation(
                        silu2[fp][:], pa2[fp][:], AF.Silu),
                       reads=[f"l0d_pa{fp}"], writes=[f"silu{fp}"])
                    if ex < 8:
                        op("vector", lambda e, ft=ft, e5=e5, fp=fp, hsb=hsb, xb=xb:
                            e.tensor_mul(
                                hsb[:, ft * 256:(ft + 1) * 256], pb2[fp][:],
                                gG2[xb][:, e5 * 256:(e5 + 1) * 256]),
                            reads=[f"l0d_pb{fp}", f"gG{xb}_{e5}", f"gGt{xb}"],
                            writes=[f"hsb{hp}"])
                        op("vector", lambda e, ft=ft, fp=fp, hsb=hsb: e.tensor_mul(
                            hsb[:, ft * 256:(ft + 1) * 256],
                            hsb[:, ft * 256:(ft + 1) * 256], silu2[fp][:]),
                            reads=[f"hsb{hp}", f"silu{fp}"], writes=[f"hsb{hp}"])
                    else:
                        op("vector", lambda e, ft=ft, fp=fp, hsb=hsb: e.tensor_mul(
                            hsb[:, ft * 256:(ft + 1) * 256], pb2[fp][:], silu2[fp][:]),
                            reads=[f"l0d_pb{fp}", f"silu{fp}"], writes=[f"hsb{hp}"])
                for tsub in range(2):
                    for mh in range(2):
                        ctx.mm([(hsb[:, fc * 256 + tsub * 128:fc * 256 + tsub * 128 + 128],
                                 w2p[:, e5 * 4096 + fc * 1024 + mh * 512:
                                     e5 * 4096 + fc * 1024 + mh * 512 + 512])
                                for fc in range(4)], py[tsub * 2 + mh][:],
                               reads=[f"hsb{hp}", f"w2p{e5}"],
                               writes=[f"l0d_py{tsub * 2 + mh}"],
                               start=(e5 == 0), stop=(e5 == 4))
            x1src = bass.AP(dr["x1"].tensor, t0 * D, [[D, 128], [128 * D, 2], [1, D]])
            op("sync", lambda e, x1src=x1src: e.dma_start(
                out=x1t[:].rearrange("p (k m) -> p k m", m=D), in_=x1src),
                reads=[f"x1_{tb * 2}", f"x1_{tb * 2 + 1}"], writes=["l0d_x1t"], dma=True, dkey="l0d_x1t")
            for tsub in range(2):
                for mh in range(2):
                    ob = mh
                    op("vector", lambda e, tsub=tsub, mh=mh, ob=ob: e.tensor_add(
                        o1[ob][:], py[tsub * 2 + mh][:],
                        x1t[:, tsub * D + mh * 512:tsub * D + mh * 512 + 512]),
                        reads=[f"l0d_py{tsub * 2 + mh}", "l0d_x1t"],
                        writes=[f"l0d_o1{ob}"])
                    op("sync", lambda e, t0=t0, tsub=tsub, mh=mh, ob=ob: e.dma_start(
                        out=dr["x1"][t0 + tsub * 128:t0 + tsub * 128 + 128,
                                     mh * 512:mh * 512 + 512], in_=o1[ob][:]),
                        reads=[f"l0d_o1{ob}"], writes=[f"x1_{tb * 2 + tsub}"], dma=True)
        ctx.barrier()
    if "x1" in taps:
        op("sync", lambda e: e.dma_start(out=taps["x1"], in_=dr["x1"]),
           reads=[f"x1_{j}" for j in range(NTILE)], dma=True)
    ctx.barrier()
    p.free()


def _l1_pre(ctx, nc, sb, w, dr, taps):
    op = ctx.op
    p = Pool(nc)
    NB = 2
    xt = [p.sb(f"l1a_xt{i}", [128, D], F32) for i in range(NB)]
    xnb = [p.sb(f"l1a_xnb{i}", [128, D], BF16) for i in range(NB)]
    xnf = [p.sb(f"l1a_xnf{i}", [128, D], BF16) for i in range(NB)]
    sq = [p.sb(f"l1a_sq{i}", [128, D], F32) for i in range(NB)]
    lg = [p.sb(f"l1a_lg{i}", [128, 8], F32) for i in range(NB)]
    fcb = [p.sb(f"l1a_fc{i}", [128, 8], F32) for i in range(NB)]
    pt = [p.ps(f"l1a_pt{i}", [128, 128], BF16) for i in range(NB)]
    plg = [p.ps(f"l1a_plg{i}", [128, 8]) for i in range(NB)]

    def front(j):
        q = j % NB
        op("sync", lambda e, j=j, q=q: e.dma_start(out=xt[q][:],
                                                   in_=dr["x1"][j * 128:(j + 1) * 128, :]),
           reads=[f"x1_{j}"], writes=[f"a_xt{q}"], dma=True, dkey=f"a_xt{q}")
        _ln_tile_r(ctx, sb, xt[q], xnb[q], sq[q], fcb[q], f"a{q}")

    def back(j):
        q = j % NB
        op("sync", lambda e, j=j, q=q: e.dma_start(
            out=dr["agx_in"][j * 128:(j + 1) * 128, :], in_=xnb[q][:]),
            reads=[f"a_xnb{q}"], writes=[f"agxin{j}"], dma=True)
        for cc in range(8):
            op("tensor", lambda e, cc=cc, q=q: e.transpose(
                pt[q][:], xnb[q][:, cc * 128:(cc + 1) * 128], sb["IDENT"][:]),
               reads=[f"a_xnb{q}", "IDENT"], writes=[f"a_pt{q}"])
            op("scalar", lambda e, cc=cc, q=q: e.copy(
                xnf[q][:, cc * 128:(cc + 1) * 128], pt[q][:]),
               reads=[f"a_pt{q}"], writes=[f"a_xnf{q}_{cc}"])
        ctx.mm([(xnf[q][:, cc * 128:(cc + 1) * 128], sb["wgsb"][:, cc * 8:(cc + 1) * 8])
                for cc in range(8)], plg[q][:],
               reads=[f"a_xnf{q}_{cc}" for cc in range(8)] + ["wgsb"],
               writes=[f"a_plg{q}"])
        op("vector", lambda e, q=q: e.tensor_copy(lg[q][:], plg[q][:]),
           reads=[f"a_plg{q}"], writes=[f"a_lg{q}"])
        op("sync", lambda e, j=j, q=q: e.dma_start(
            out=dr["aglg_in"][j * 128:(j + 1) * 128, :], in_=lg[q][:]),
            reads=[f"a_lg{q}"], writes=[f"aglgin{j}"], dma=True)

    front(0)
    for j in range(NTILE):
        if j + 1 < NTILE:
            front(j + 1)
        back(j)
    ctx.barrier()
    p.free()
    rg = [list(range(NCORES))]
    op("gpsimd", lambda e: e.collective_compute(
        "AllGather", ALU.bypass, replica_groups=rg,
        ins=[dr["aglg_in"].ap().opt()], outs=[dr["aglg_out"].ap().opt()]),
        reads=[f"aglgin{j}" for j in range(NTILE)], writes=["aglgout"])
    if "logits" in taps:
        op("sync", lambda e: e.dma_start(out=taps["logits"], in_=dr["aglg_in"].ap()),
           reads=[f"aglgin{j}" for j in range(NTILE)], dma=True)


def _ln_tile_r(ctx, sb, xt, xnb, sq, F, pfx):
    op = ctx.op
    op("vector", lambda e: e.reduce_sum(F[:, 0:1], xt[:], AX),
       reads=[f"{pfx}_xt" if False else f"{pfx[0]}_xt{pfx[1]}"], writes=[f"{pfx}_f0"])
    op("vector", lambda e: e.tensor_scalar(F[:, 1:2], F[:, 0:1], 1.0 / D, None,
                                           ALU.mult),
       reads=[f"{pfx}_f0"], writes=[f"{pfx}_f1"])
    op("scalar", lambda e: e.activation(sq[:], xt[:], AF.Square, accum_out=F[:, 2:3]),
       reads=[f"{pfx[0]}_xt{pfx[1]}"], writes=[f"{pfx}_sq", f"{pfx}_f2"])
    op("vector", lambda e: e.tensor_mul(F[:, 3:4], F[:, 1:2], F[:, 1:2]),
       reads=[f"{pfx}_f1"], writes=[f"{pfx}_f3"])
    op("vector", lambda e: e.scalar_tensor_tensor(F[:, 3:4], F[:, 2:3], 1.0 / D,
                                                  F[:, 3:4], ALU.mult, ALU.subtract),
       reads=[f"{pfx}_f2", f"{pfx}_f3"], writes=[f"{pfx}_f3"])
    op("scalar", lambda e: e.activation(F[:, 4:5], F[:, 3:4], AF.Sqrt,
                                        bias=sb["cEPS"][:]),
       reads=[f"{pfx}_f3", "cEPS"], writes=[f"{pfx}_f4"])
    op("vector", lambda e: e.reciprocal(F[:, 5:6], F[:, 4:5]),
       reads=[f"{pfx}_f4"], writes=[f"{pfx}_f5"])
    op("vector", lambda e: e.tensor_scalar(xnb[:], xt[:], F[:, 1:2], F[:, 5:6],
                                           ALU.subtract, ALU.mult),
       reads=[f"{pfx[0]}_xt{pfx[1]}", f"{pfx}_f1", f"{pfx}_f5"],
       writes=[f"{pfx[0]}_xnb{pfx[1]}"])



def _batch_done(ctx, nc, sb, dr, taps, p):
    ctx.barrier()
    p.free()


def _l1_routing(ctx, nc, sb, w, dr, capd, taps):
    op = ctx.op
    p = Pool(nc)
    lgt = p.sb("l1r_lg", [128, 128], F32)
    pc = p.ps("l1r_pc", [128, GTILE])
    pt1 = p.ps("l1r_pt1", [1, GTILE])
    pb2 = p.ps("l1r_pb2", [128, 2])
    pcs = p.ps("l1r_pcs", [1, 2])
    pbc = p.ps("l1r_pbc", [128, 1])
    F = sb["fc"]

    es16 = p.sb("l1r_es16", [1, 128], F32)
    esb16 = p.sb("l1r_esb16", [128, 128], F32)
    lgt2 = [p.sb(f"l1r_lg2{i}", [128, 128], F32) for i in range(2)]
    LL = [p.sb(f"l1r_LL{i}", [128, 128], F32) for i in range(2)]
    EX = [p.sb(f"l1r_EX{i}", [128, 128], F32) for i in range(2)]
    MYP = [p.sb(f"l1r_MYP{i}", [128, 128], F32) for i in range(2)]
    M1 = [p.sb(f"l1r_M1{i}", [128, 16], F32) for i in range(2)]
    S16 = [p.sb(f"l1r_S16{i}", [128, 16], F32) for i in range(2)]
    MY16 = [p.sb(f"l1r_MY16{i}", [128, 16], F32) for i in range(2)]
    peb = p.ps("l1r_peb", [128, 128])
    op("sync", lambda e: e.dma_start(out=es16[:], in_=w["esel16"]),
       writes=["es16"], dma=True)
    ctx.mm([(sb["ONESF"][0:1, :], es16[:])], peb[:], reads=["ONESF", "es16"],
           writes=["peb"])
    op("vector", lambda e: e.tensor_copy(esb16[:], peb[:]),
       reads=["peb"], writes=["esb16"])
    for jj in range(16):
        q = jj % 2
        src = bass.AP(dr["aglg_out"].ap().tensor, jj * 16 * 128 * 8,
                      [[8, 128], [128 * 8, 16], [1, 8]])
        op("sync", lambda e, src=src, q=q: e.dma_start(
            out=lgt2[q][:].rearrange("p (j e) -> p j e", e=8), in_=src),
            reads=["aglgout"], writes=[f"lgt{q}"], dma=True, dkey=f"lgt{q}")
        l3 = lgt2[q][:].rearrange("p (j e) -> p j e", e=8)
        op("vector", lambda e, q=q, l3=l3: e.reduce_max(M1[q][:], l3, AX),
           reads=[f"lgt{q}"], writes=[f"rM1{q}"])
        m1b = M1[q][:].rearrange("p (j o) -> p j o", o=1).to_broadcast([128, 16, 8])
        ll3 = LL[q][:].rearrange("p (j e) -> p j e", e=8)
        op("vector", lambda e, q=q, l3=l3, m1b=m1b, ll3=ll3: e.tensor_tensor(
            ll3, l3, m1b, ALU.subtract),
           reads=[f"lgt{q}", f"rM1{q}"], writes=[f"rLL{q}"])
        op("scalar", lambda e, q=q: e.activation(EX[q][:], LL[q][:], AF.Exp),
           reads=[f"rLL{q}"], writes=[f"rEX{q}"])
        e3 = EX[q][:].rearrange("p (j e) -> p j e", e=8)
        op("vector", lambda e, q=q, e3=e3: e.reduce_sum(S16[q][:], e3, AX),
           reads=[f"rEX{q}"], writes=[f"rS16{q}"])
        op("vector", lambda e, jj=jj, q=q: e.reciprocal(
            sb["G1"][:, jj * 16:(jj + 1) * 16], S16[q][:]),
           reads=[f"rS16{q}"], writes=["G1"])
        op("vector", lambda e, q=q: e.tensor_mul(MYP[q][:], lgt2[q][:], esb16[:]),
           reads=[f"lgt{q}", "esb16"], writes=[f"rMYP{q}"])
        p3 = MYP[q][:].rearrange("p (j e) -> p j e", e=8)
        op("vector", lambda e, q=q, p3=p3: e.reduce_sum(MY16[q][:], p3, AX),
           reads=[f"rMYP{q}"], writes=[f"rMY16{q}"])
        op("vector", lambda e, jj=jj, q=q: e.tensor_tensor(
            sb["M"][:, jj * 16:(jj + 1) * 16], MY16[q][:], M1[q][:], ALU.is_ge),
           reads=[f"rMY16{q}", f"rM1{q}"], writes=["M"])
        op("vector", lambda e: e.tensor_copy(sb["Mb"][:], sb["M"][:]),
       reads=["M"], writes=["Mb"])
    ctx.mm([(sb["UTI"][:], sb["Mb"][:])], pc[:], reads=["UTI", "Mb"], writes=["l1r_pc"])
    op("vector", lambda e: e.tensor_copy(sb["CUMS"][:], pc[:]),
       reads=["l1r_pc"], writes=["CUMS"])
    ctx.mm([(sb["ONESB"][:, 0:1], sb["Mb"][:])], pt1[:], reads=["ONESB", "Mb"],
           writes=["ptot"])
    op("vector", lambda e: e.tensor_copy(sb["tot"][:], pt1[:]),
       reads=["ptot"], writes=["tot"])
    op("sync", lambda e: e.dma_start(out=dr["bounce"][0:1, :], in_=sb["tot"][:]),
       reads=["tot"], writes=["bounce0"], dma=True)
    b_src = bass.AP(dr["bounce"].tensor, 0, [[1, 128], [128, 2]])
    op("gpsimd", lambda e, b_src=b_src: e.dma_start(
        out=sb["tot2"][:], in_=b_src, allow_slow_non_contiguous=True),
        reads=["bounce0"], writes=["tot2"], dma=True)
    ctx.mm([(sb["UTS"][:], sb["tot2"][:])], pb2[:], reads=["UTS", "tot2"],
           writes=["pb2"])
    ctx.mm([(sb["ONESB"][:, 0:1], sb["tot2"][:])], pcs[:], reads=["ONESB", "tot2"],
           writes=["pcs"])
    op("vector", lambda e: e.tensor_copy(sb["scs"][:], pcs[:]),
       reads=["pcs"], writes=["scs"])
    ctx.mm([(sb["ONESF"][0:1, :], sb["scs"][0:1, 0:1])], pbc[:],
           reads=["ONESF", "scs"], writes=["pbc"])
    op("vector", lambda e: e.tensor_copy(sb["offs2"][:], pb2[:]),
       reads=["pb2"], writes=["offs2"])
    op("vector", lambda e: e.tensor_add(sb["offs2"][:, 1:2], sb["offs2"][:, 1:2], pbc[:]),
       reads=["offs2", "pbc"], writes=["offs2"])
    b_dst = bass.AP(dr["bounce"].tensor, 256, [[1, 128], [128, 2]])
    op("gpsimd", lambda e, b_dst=b_dst: e.dma_start(
        out=b_dst, in_=sb["offs2"][:], allow_slow_non_contiguous=True),
        reads=["offs2"], writes=["bounce1"], dma=True)
    op("sync", lambda e: e.dma_start(out=sb["offs"][:], in_=dr["bounce"][1:2, :]),
       reads=["bounce1"], writes=["offs"], dma=True)
    ctx.mm([(sb["ONESF"][0:1, :], sb["offs"][:])], pc[:], reads=["ONESF", "offs"],
           writes=["l1r_pc"])
    op("vector", lambda e: e.tensor_add(sb["POS"][:], sb["CUMS"][:], pc[:]),
       reads=["CUMS", "l1r_pc"], writes=["POS"])
    op("vector", lambda e: e.scalar_tensor_tensor(sb["SLOT"][:], sb["M"][:], -33000.0,
                                                  sb["POS"][:], ALU.mult, ALU.add),
       reads=["M", "POS"], writes=["SLOT"])
    op("vector", lambda e: e.tensor_scalar(sb["SLOT"][:], sb["SLOT"][:], 32999.0,
                                           None, ALU.add),
       reads=["SLOT"], writes=["SLOT"])
    op("vector", lambda e: e.tensor_scalar(sb["SLOT"][:], sb["SLOT"][:], float(TRASH),
                                           None, ALU.min),
       reads=["SLOT"], writes=["SLOT"])
    op("vector", lambda e: e.tensor_scalar(sb["KTMP"][:], sb["POS"][:], float(CAP),
                                           None, ALU.is_le),
       reads=["POS"], writes=["KTMP"])
    op("vector", lambda e: e.tensor_mul(sb["GS"][:], sb["G1"][:], sb["M"][:]),
       reads=["G1", "M"], writes=["GS"])
    op("vector", lambda e: e.tensor_mul(sb["GS"][:], sb["GS"][:], sb["KTMP"][:]),
       reads=["GS", "KTMP"], writes=["GS"])
    op("vector", lambda e: e.tensor_scalar(sb["rmin"][:], sb["offs"][:], float(CAP),
                                           None, ALU.min),
       reads=["offs"], writes=["rmin"])
    op("vector", lambda e: e.tensor_copy(sb["roffi"][:], sb["rmin"][:]),
       reads=["rmin"], writes=["roffi"])

    NST = NSLOT // 128
    tpos = p.sb("l1r_tpos", [128, 128], F32)
    OFFSB = p.sb("l1r_offsb", [128, GTILE], F32)
    SCB = p.sb("l1r_scb", [128, NST], F32)
    TT2 = [p.sb(f"l1r_TT{i}", [128, 260], F32) for i in range(2)]
    tmA = [p.sb(f"l1r_tmA{i}", [128, GTILE], F32) for i in range(2)]
    tmB = [p.sb(f"l1r_tmB{i}", [128, 128], F32) for i in range(2)]
    tmC = [p.sb(f"l1r_tmC{i}", [128, 128], F32) for i in range(2)]
    cnt2 = [p.sb(f"l1r_cnt{i}", [128, 8], F32) for i in range(2)]
    coli = [p.sb(f"l1r_coli{i}", [128, 1], I32) for i in range(2)]
    for half, srcn in ((0, "POS"), (1, "GS")):
        for ct in range(2):
            op("tensor", lambda e, srcn=srcn, ct=ct: e.transpose(
                pc[:, 0:128], sb[srcn][:, ct * 128:(ct + 1) * 128], sb["IDENTF"][:]),
               reads=[srcn, "IDENTF"], writes=["l1r_pc"])
            op("vector", lambda e: e.tensor_copy(tpos[:], pc[:, 0:128]),
               reads=["l1r_pc"], writes=["l1r_tpos"])
            op("sync", lambda e, half=half, ct=ct: e.dma_start(
                out=dr["tbl"][ct * 128:(ct + 1) * 128, half * 128:(half + 1) * 128],
                in_=tpos[:]),
                reads=["l1r_tpos"], writes=["tbl_d"], dma=True)
    op("sync", lambda e: e.dma_start(out=dr["tbl"][:, 256:257], in_=capd["PBC"].ap(),
                                     allow_slow_non_contiguous=True),
       writes=["tbl_d"], dma=True)
    ctx.mm([(sb["ONESF"][0:1, :], sb["offs"][:])], pc[:], reads=["ONESF", "offs"],
           writes=["l1r_pc"])
    op("vector", lambda e: e.tensor_copy(OFFSB[:], pc[:]),
       reads=["l1r_pc"], writes=["l1r_offsb"])
    for st in range(NST):
        op("vector", lambda e, st=st: e.tensor_scalar(
            SCB[:, st:st + 1], sb["IOTAC"][:], float(st * 128), None, ALU.add),
           reads=["IOTAC"], writes=["l1r_scb"])
    for st in range(NST):
        q = st % 2
        cn = cnt2[q]
        op("vector", lambda e, st=st, q=q: e.tensor_scalar(
            tmA[q][:], OFFSB[:], SCB[:, st:st + 1], None, ALU.is_le),
           reads=["l1r_offsb", "l1r_scb"], writes=[f"l1r_A{q}"])
        op("vector", lambda e, q=q, cn=cn: e.reduce_sum(cn[:, 0:1], tmA[q][:], AX),
           reads=[f"l1r_A{q}"], writes=[f"l1r_c0{q}"])
        op("vector", lambda e, q=q, cn=cn: e.tensor_scalar(
            cn[:, 1:2], cn[:, 0:1], -1.0, None, ALU.add),
           reads=[f"l1r_c0{q}"], writes=[f"l1r_c1{q}"])
        op("vector", lambda e, q=q, cn=cn: e.tensor_copy(coli[q][:], cn[:, 1:2]),
           reads=[f"l1r_c1{q}"], writes=[f"l1r_coli{q}"])
        op("gpsimd", lambda e, q=q: e.indirect_dma_start(
            out=TT2[q][:], out_offset=None, in_=dr["tbl"],
            in_offset=IndirectOffsetOnAxis(ap=coli[q][:], axis=0)),
            reads=[f"l1r_coli{q}", "tbl_d"], writes=[f"l1r_TT{q}"], dma=True,
            dkey=f"ttg{q}")
        op("vector", lambda e, st=st, q=q: e.tensor_scalar(
            tmB[q][:], TT2[q][:, 0:128], SCB[:, st:st + 1], None, ALU.is_le),
           reads=[f"l1r_TT{q}", "l1r_scb"], writes=[f"l1r_B{q}"])
        op("vector", lambda e, q=q, cn=cn: e.reduce_sum(cn[:, 2:3], tmB[q][:], AX),
           reads=[f"l1r_B{q}"], writes=[f"l1r_c2{q}"])
        op("vector", lambda e, q=q, cn=cn: e.scalar_tensor_tensor(
            cn[:, 3:4], cn[:, 1:2], 128.0, cn[:, 2:3], ALU.mult, ALU.add),
           reads=[f"l1r_c1{q}", f"l1r_c2{q}"], writes=[f"l1r_c3{q}"])
        op("vector", lambda e, st=st, q=q, cn=cn: e.tensor_copy(
            sb["TOKI"][:, st:st + 1], cn[:, 3:4]),
           reads=[f"l1r_c3{q}"], writes=["TOKI"])
        op("vector", lambda e, q=q, cn=cn: e.tensor_add(
            cn[:, 4:5], TT2[q][:, 256:257], cn[:, 2:3]),
           reads=[f"l1r_TT{q}", f"l1r_c2{q}"], writes=[f"l1r_c4{q}"])
        op("vector", lambda e, st=st, q=q, cn=cn: e.tensor_copy(
            sb["TOKP"][:, st:st + 1], cn[:, 4:5]),
           reads=[f"l1r_c4{q}"], writes=["TOKP"])
        op("vector", lambda e, st=st, q=q, cn=cn: e.tensor_scalar(
            cn[:, 5:6], SCB[:, st:st + 1], 1.0, None, ALU.add),
           reads=["l1r_scb"], writes=[f"l1r_c5{q}"])
        op("vector", lambda e, q=q, cn=cn: e.tensor_scalar(
            tmC[q][:], TT2[q][:, 0:128], cn[:, 5:6], None, ALU.is_le),
           reads=[f"l1r_TT{q}", f"l1r_c5{q}"], writes=[f"l1r_C{q}"])
        op("vector", lambda e, q=q: e.tensor_sub(tmC[q][:], tmC[q][:], tmB[q][:]),
           reads=[f"l1r_C{q}", f"l1r_B{q}"], writes=[f"l1r_C{q}"])
        op("vector", lambda e, q=q: e.tensor_mul(
            tmC[q][:], tmC[q][:], TT2[q][:, 128:256]),
           reads=[f"l1r_C{q}", f"l1r_TT{q}"], writes=[f"l1r_C{q}"])
        op("vector", lambda e, st=st, q=q: e.reduce_sum(
            sb["GSL"][:, st:st + 1], tmC[q][:], AX),
           reads=[f"l1r_C{q}"], writes=["GSL"])
        if st == 0 and "dbgm" in taps:
            op("sync", lambda e, q=q: e.dma_start(
                out=taps["dbgm"][:, 0:128], in_=tmB[q][:]),
               reads=[f"l1r_B{q}"], dma=True)
            op("sync", lambda e, q=q: e.dma_start(
                out=taps["dbgm"][:, 128:256], in_=tmC[q][:]),
               reads=[f"l1r_C{q}"], dma=True)
            op("sync", lambda e, q=q, cn=cn: e.dma_start(
                out=taps["dbgc"], in_=cn[:]),
               reads=[f"l1r_c5{q}"], dma=True)
    for tn, buf in (("mmat", "M"), ("posmat", "POS"), ("slotmat", "SLOT"),
                    ("gsmat", "GS"), ("toki", "TOKI"), ("tokp", "TOKP"),
                    ("gsl", "GSL")):
        if tn in taps:
            op("sync", lambda e, tn=tn, buf=buf: e.dma_start(
                out=taps[tn], in_=sb[buf][:]), reads=[buf], dma=True)
    if "tblm" in taps:
        op("sync", lambda e: e.dma_start(out=taps["tblm"], in_=dr["tbl"]),
           reads=["tbl_d"], dma=True)
    if "offsrow" in taps:
        op("sync", lambda e: e.dma_start(out=taps["offsrow"], in_=sb["offs"][:]),
           reads=["offs"], dma=True)
    if "roffif" in taps:
        op("vector", lambda e: e.tensor_copy(sb["rmin"][:], sb["roffi"][:]),
           reads=["roffi"], writes=["rmin"])
        op("sync", lambda e: e.dma_start(out=taps["roffif"], in_=sb["rmin"][:]),
           reads=["rmin"], dma=True)
    ctx.barrier()
    p.free()



def _l1_dispatch(ctx, nc, sb, dr, taps, p):
    op = ctx.op
    zz = p.sb("l1m_zz", [128, 8], BF16)
    zt = p.sb("l1m_zero", [128, D], BF16)
    op("vector", lambda e: e.memset(zt[:], 0.0), writes=["l1m_zt"])
    op("gpsimd", lambda e: e.dma_start(
        out=dr["agx_out"].ap()[NCORES * T:NCORES * T + 128, :], in_=zt[:]),
        reads=["l1m_zt"], writes=["agz"], dma=True)
    xnt = [p.sb(f"l1m_xnt{i}", [128, D], BF16) for i in range(3)]
    dtags = []
    for st in range(NSLOT // 128):
        b = st % 3
        op("gpsimd", lambda e, st=st, b=b: e.indirect_dma_start(
            out=xnt[b][:], out_offset=None,
            in_=dr["agx_out"].ap(),
            in_offset=IndirectOffsetOnAxis(ap=sb["TOKI"][:, st:st + 1], axis=0)),
            reads=["agxout", "TOKI", "agz"], writes=[f"xnt{b}"], dma=True,
            dkey=f"xnt{b}")
        tg = f"dsc{st}"
        dtags.append(tg)
        op("gpsimd", lambda e, st=st, b=b: e.dma_start(
            out=dr["disp"][st * 128:(st + 1) * 128, :], in_=xnt[b][:]),
            reads=[f"xnt{b}"], writes=[tg], dma=True)
    op("gpsimd", lambda e: e.dma_start(out=zz[:, 0:8], in_=dr["disp"][0:128, 0:8]),
       reads=dtags, writes=["dfence"], dma=True)
    if "disp" in taps:
        op("sync", lambda e: e.dma_start(out=taps["disp"], in_=dr["disp"]),
           reads=dtags + ["dfence"], dma=True)
    for j in range(NCORES * T // 128 + 1):
        op("gpsimd", lambda e, j=j: e.dma_start(
            out=dr["rsin"].ap()[j * 128:(j + 1) * 128, :], in_=zt[:]),
            reads=["l1m_zt"], writes=["rzero"], dma=True)
    return dtags


def _l1_ffn(ctx, nc, sb, w, dr, dtags, taps):
    op = ctx.op
    p = Pool(nc)
    w1sb = p.sb("l1f_w1", [128, 8 * 4096], BF16)
    w2sb = p.sb("l1f_w2", [128, 32 * 1024], BF16)
    dspf = p.sb("l1f_dsp", [128, 8 * 512], BF16)
    hsb = p.sb("l1f_h", [128, 32 * 512], BF16)
    eo = [p.sb(f"l1f_eo{i}", [128, D], BF16) for i in range(3)]
    oc = [p.sb(f"l1f_oc{i}", [128, D], BF16) for i in range(3)]
    ph = p.ps("l1f_ph", [128, 512])
    pe = [p.ps(f"l1f_pe{i}", [128, 512]) for i in range(2)]
    src = bass.AP(w["w1e"].tensor, 0, [[4096, 128], [4096 * 128, 8], [1, 4096]])
    op("gpsimd", lambda e, src=src: e.dma_start(
        out=w1sb[:].rearrange("p (c f) -> p c f", f=4096), in_=src),
        writes=["w1sb"], dma=True, dkey="w1sb")
    src2 = bass.AP(w["w2e"].tensor, 0, [[1024, 128], [1024 * 128, 32], [1, 1024]])
    op("gpsimd", lambda e, src2=src2: e.dma_start(
        out=w2sb[:].rearrange("p (c f) -> p c f", f=1024), in_=src2),
        writes=["w2sb"], dma=True, dkey="w2sb")

    etags = []
    for cb in range(NSLOT // 512):
        c0 = cb * 512
        for mc in range(8):
            op("sync", lambda e, mc=mc, c0=c0: e.dma_start_transpose(
                dspf[:, mc * 512:(mc + 1) * 512],
                dr["disp"][c0:c0 + 512, mc * 128:(mc + 1) * 128]),
                reads=dtags + ["dfence"], writes=[f"dspf{mc}"], dma=True, dkey="dspf")
        for ft in range(32):
            ctx.mm([(w1sb[:, mc * 4096 + ft * 128:mc * 4096 + ft * 128 + 128],
                     dspf[:, mc * 512:(mc + 1) * 512]) for mc in range(8)],
                   ph[:], reads=["w1sb"] + [f"dspf{mc}" for mc in range(8)],
                   writes=["l1f_ph"])
            op("scalar", lambda e, ft=ft: e.activation(
                hsb[:, ft * 512:(ft + 1) * 512], ph[:], AF.Relu,
                bias=sb["b1e"][:, ft:ft + 1]),
                reads=["l1f_ph", "b1e"], writes=["hsb"])
        for ct in range(4):
            st = cb * 4 + ct
            r = st % 3
            for mh in range(2):
                pid = mh
                ctx.mm([(hsb[:, fc * 512 + ct * 128:fc * 512 + ct * 128 + 128],
                         w2sb[:, fc * 1024 + mh * 512:fc * 1024 + mh * 512 + 512])
                        for fc in range(32)], pe[pid][:],
                       reads=["hsb", "w2sb"], writes=[f"l1f_pe{pid}"], stop=False)
                ctx.mm([(sb["ONESF"][0:1, :], sb["b2e"][0:1, mh * 512:mh * 512 + 512])],
                       pe[pid][:], reads=["ONESF", "b2e"], writes=[f"l1f_pe{pid}"],
                       start=False)
                op("vector", lambda e, mh=mh, pid=pid, r=r: e.tensor_copy(
                    eo[r][:, mh * 512:(mh + 1) * 512], pe[pid][:]),
                    reads=[f"l1f_pe{pid}"], writes=[f"l1f_eo{r}"])
            op("vector", lambda e, st=st, r=r: e.tensor_scalar(
                oc[r][:], eo[r][:], sb["GSL"][:, st:st + 1], None, ALU.mult),
                reads=[f"l1f_eo{r}", "GSL"], writes=[f"l1f_oc{r}"])
            tg = f"rsw{st}"
            etags.append(tg)
            op("gpsimd", lambda e, st=st, r=r: e.indirect_dma_start(
                out=dr["rsin"].ap(),
                out_offset=IndirectOffsetOnAxis(ap=sb["TOKP"][:, st:st + 1], axis=0),
                in_=oc[r][:], in_offset=None),
                reads=[f"l1f_oc{r}", "TOKP", "rzero"], writes=[tg], dma=True)
    ctx.barrier()
    p.free()
    return etags


def _l1_combine_final(ctx, nc, sb, w, out, dr, etags, taps):
    op = ctx.op
    p = Pool(nc)
    aos = p.sb("l1z_ao", [128, 8192], BF16)
    yf = p.sb("l1z_yf", [128, 1024], BF16)
    ytall = p.sb("l1z_ytall", [128, NTILE * D], BF16)
    x1t = [p.sb(f"l1z_x1{i}", [128, D], F32) for i in range(2)]
    xat = [p.sb(f"l1z_xa{i}", [128, D], F32) for i in range(2)]
    fa = [p.sb(f"l1z_fa{i}", [128, D], F32) for i in range(2)]
    rst = [p.sb(f"l1z_rs{i}", [128, D], BF16) for i in range(2)]
    ot = [p.sb(f"l1z_o{i}", [128, D], F32) for i in range(2)]
    pT = p.ps("l1z_pT", [128, 128], BF16)
    pF = p.ps("l1z_pF", [128, 512])
    src = bass.AP(dr["aotb"].tensor, 0, [[1024, 128], [1024 * 128, 8], [1, 1024]])
    op("sync", lambda e, src=src: e.dma_start(
        out=aos[:].rearrange("p (c f) -> p c f", f=1024), in_=src),
        reads=["aotb_d"], writes=["aos"], dma=True, dkey="aos")

    op("sync", lambda e: e.dma_start(out=ytall[:], in_=dr["ytm1d"]),
       reads=["ytm1d"], writes=["z_yta"], dma=True, dkey="z_yta")

    def proj(jj):
        q = jj % 2
        r0 = jj * 128
        for cc in range(8):
            op("tensor", lambda e, jj=jj, cc=cc: e.transpose(
                pT[:], ytall[:, jj * 1024 + cc * 128:jj * 1024 + (cc + 1) * 128],
                sb["IDENT"][:]),
               reads=["z_yta", "IDENT"], writes=["l1z_pT"])
            op("vector", lambda e, cc=cc: e.tensor_copy(
                yf[:, cc * 128:(cc + 1) * 128], pT[:]),
               reads=["l1z_pT"], writes=[f"zyf{cc}"])
        op("sync", lambda e, r0=r0, q=q: e.dma_start(
            out=x1t[q][:], in_=dr["x1"][r0:r0 + 128, :]),
            reads=[f"x1_{jj}"], writes=[f"z_x1{q}"], dma=True, dkey=f"z_x1{q}")
        for ih in range(2):
            ctx.mm([(yf[:, cc * 128:(cc + 1) * 128],
                     aos[:, cc * 1024 + ih * 512:cc * 1024 + ih * 512 + 512])
                    for cc in range(8)], pF[:],
                   reads=[f"zyf{cc}" for cc in range(8)] + ["aos"], writes=["z_pF"])
            op("vector", lambda e, ih=ih, q=q: e.tensor_add(
                xat[q][:, ih * 512:(ih + 1) * 512], pF[:],
                x1t[q][:, ih * 512:(ih + 1) * 512]),
                reads=["z_pF", f"z_x1{q}"], writes=[f"z_xa{q}"])
        op("sync", lambda e, r0=r0, q=q: e.dma_start(
            out=dr["xa"][r0:r0 + 128, :], in_=xat[q][:]),
            reads=[f"z_xa{q}"], writes=[f"xa_{jj}"], dma=True)

    rg = [list(range(NCORES))]

    def rs_q(q):
        r0 = q * 8192
        op("gpsimd", lambda e, q=q, r0=r0: e.collective_compute(
            "ReduceScatter", ALU.add, replica_groups=rg,
            ins=[dr["rsin"].ap()[r0:r0 + 8192, :].opt()],
            outs=[dr["rsout"].ap()[q * 1024:(q + 1) * 1024, :].opt()]),
            reads=etags + ["rzero"], writes=[f"rsout{q}"])

    def fadd(q, k):
        jj = q * 8 + k
        z = jj % 2
        r0 = jj * 128
        op("scalar", lambda e, r0=r0, z=z: e.dma_start(
            out=fa[z][:], in_=dr["xa"][r0:r0 + 128, :]),
            reads=[f"xa_{jj}"], writes=[f"z_fa{z}"], dma=True, dkey=f"z_fa{z}")
        op("scalar", lambda e, r0=r0, z=z, q=q: e.dma_start(
            out=rst[z][:], in_=dr["rsout"].ap()[r0:r0 + 128, :]),
            reads=[f"rsout{q}"], writes=[f"z_rs{z}"], dma=True, dkey=f"z_rs{z}")
        op("vector", lambda e, z=z: e.tensor_add(ot[z][:], fa[z][:], rst[z][:]),
           reads=[f"z_fa{z}", f"z_rs{z}"], writes=[f"z_ot{z}"])
        op("scalar", lambda e, r0=r0, z=z: e.dma_start(
            out=out[r0:r0 + 128, :], in_=ot[z][:]),
            reads=[f"z_ot{z}"], writes=["outd"], dma=True)

    for pj in range(NTILE):
        proj(pj)
    for q in range(4):
        rs_q(q)
        for k in range(8):
            fadd(q, k)
    ctx.barrier()
    p.free()



def _l1_attn_pre(ctx, nc, sb, dr, capd):
    op = ctx.op
    early = Pool(nc)
    eb = {"cs": early.sb("cos_sb", [128, T], BF16),
          "sn": early.sb("sin_sb", [128, T], BF16),
          "wqs": early.sb("l1t_wq", [128, 2048], BF16),
          "wks": early.sb("l1t_wk", [128, 2048], BF16),
          "wvs": early.sb("l1t_wv", [128, 2048], BF16)}
    op("sync", lambda e: e.dma_start(out=eb["cs"][:], in_=capd["COS"].ap()),
       writes=["COS"], dma=True)
    op("sync", lambda e: e.dma_start(out=eb["sn"][:], in_=capd["SIN"].ap()),
       writes=["SIN"], dma=True)
    for name, wt in (("wqs", "aqt"), ("wks", "akt"), ("wvs", "avt")):
        src = bass.AP(dr[wt + "b"].tensor, 0,
                      [[1024, 128], [1024 * 128, 8], [1, 256]])
        op("sync", lambda e, src=src, name=name: e.dma_start(
            out=eb[name][:].rearrange("p (c f) -> p c f", f=256), in_=src),
            reads=[f"{wt}b_d"], writes=[name], dma=True, dkey=name)
    return early, eb


def _l1_attn(ctx, nc, sb, w, dr, capd, taps, early, eb):
    op = ctx.op
    ypool = Pool(nc)
    ytm1 = ypool.sb("ytm1", [128, NTILE * D], BF16)
    cs = eb["cs"]
    sn = eb["sn"]

    p = Pool(nc)
    qq = p.sb("l1t_q", [128, 2 * T], BF16)
    kq = p.sb("l1t_k", [128, 2 * T], BF16)
    vex = p.sb("l1t_v", [128, NTILE * 260], BF16)
    xsq = p.sb("l1t_xs", [128, T], BF16)
    wqs, wks, wvs = eb["wqs"], eb["wks"], eb["wvs"]
    qraw = p.sb("l1t_qraw", [128, 512], BF16)
    t1 = p.sb("l1t_t1", [128, 512], BF16)
    t2 = p.sb("l1t_t2", [128, 512], BF16)
    Psb2 = [p.sb(f"l1t_P{i}", [128, 512], BF16) for i in range(2)]
    dcol = p.sb("l1t_dcol", [128, 1], F32)
    pP = p.ps("l1t_pP", [128, 512])
    pR = p.ps("l1t_pR", [128, 512])
    pA2 = [p.ps(f"l1t_pA{i}", [128, 512]) for i in range(2)]
    pO = [p.ps(f"l1t_pO{i}", [128, 72]) for i in range(4)]

    agxtags = [f"agxin{j}" for j in range(NTILE)]
    for qt in range(4):
        if qt > 0:
            for name, wt, buf in (("wqs", "aqt", wqs), ("wks", "akt", wks),
                                  ("wvs", "avt", wvs)):
                src = bass.AP(dr[wt + "b"].tensor, qt * 256,
                              [[1024, 128], [1024 * 128, 8], [1, 256]])
                op("sync", lambda e, src=src, buf=buf: e.dma_start(
                    out=buf[:].rearrange("p (c f) -> p c f", f=256), in_=src),
                    reads=[f"{wt}b_d"], writes=[name], dma=True, dkey=name)
        vones = bass.AP(vex, 64, [[NTILE * 260, 128], [260, NTILE], [65, 4], [1, 1]])
        op("vector", lambda e, vones=vones: e.memset(vones, 1.0),
           reads=["vex"], writes=["vex_ones"])
        for tb in range(8):
            xr = [f"xq{cc}" for cc in range(8)]
            for cc in range(8):
                op("sync", lambda e, cc=cc, tb=tb: e.dma_start_transpose(
                    xsq[:, cc * 512:(cc + 1) * 512],
                    dr["agx_in"][tb * 512:(tb + 1) * 512, cc * 128:(cc + 1) * 128]),
                    reads=agxtags[tb * 4:tb * 4 + 4], writes=[xr[cc]], dma=True,
                    dkey=xr[cc])
            for name, wbuf, dst, dtag in (("wqs", wqs, qq, "qqb"),
                                          ("wks", wks, kq, "kqb")):
                for it in range(2):
                    ctx.mm([(wbuf[:, cc * 256 + it * 128:cc * 256 + it * 128 + 128],
                             xsq[:, cc * 512:(cc + 1) * 512]) for cc in range(8)],
                           pP[:], reads=[name] + xr, writes=["l1t_pP"])
                    op("scalar", lambda e: e.copy(qraw[:], pP[:]),
                       reads=["l1t_pP"], writes=["qraw"])
                    ctx.mm([(sb["ROTT"][:], qraw[:])], pR[:],
                           reads=["ROTT", "qraw"], writes=["l1t_pR"])
                    c0 = tb * 512
                    op("vector", lambda e, c0=c0: e.tensor_mul(
                        t1[:], qraw[:], cs[:, c0:c0 + 512]),
                        reads=["qraw", "COS"], writes=["t1"])
                    op("vector", lambda e, c0=c0: e.tensor_mul(
                        t2[:], pR[:], sn[:, c0:c0 + 512]),
                        reads=["l1t_pR", "SIN"], writes=["t2"])
                    op("vector", lambda e, dst=dst, it=it, c0=c0: e.tensor_add(
                        dst[:, it * T + c0:it * T + c0 + 512], t1[:], t2[:]),
                        reads=["t1", "t2"], writes=[dtag])
            for ts_ in range(4):
                ctx.mm([(xsq[:, cc * 512 + ts_ * 128:cc * 512 + ts_ * 128 + 128],
                         wvs[:, cc * 256:(cc + 1) * 256]) for cc in range(8)],
                       pP[:, 0:256], reads=["wvs"] + xr, writes=["l1t_pP"])
                ti = tb * 4 + ts_
                for hh in range(4):
                    op("vector", lambda e, ti=ti, hh=hh: e.tensor_copy(
                        vex[:, ti * 260 + hh * 65:ti * 260 + hh * 65 + 64],
                        pP[:, hh * 64:hh * 64 + 64]),
                        reads=["l1t_pP"], writes=["vex"])
        for hl in range(4):
            h = qt * 4 + hl
            rb = (hl % 2) * 64
            itq = hl // 2
            for qb in range(8):
                q0 = qb * 512
                nkt = (qb + 1) * 4
                def emit_qk(kt):
                    pr = kt % 2
                    ctx.mm([(kq[rb:rb + 64, itq * T + kt * 128:itq * T + kt * 128 + 128],
                             qq[rb:rb + 64, itq * T + q0:itq * T + q0 + 512])],
                           pA2[pr][:], reads=["kqb", "qqb"], writes=[f"l1t_pA{pr}"])

                emit_qk(0)
                for kt in range(nkt):
                    pr = kt % 2
                    pA = pA2[pr]
                    Psb = Psb2[pr]
                    op("scalar", lambda e, pA=pA, Psb=Psb: e.activation(
                        Psb[:], pA[:], AF.Exp, bias=sb["cM12"][:], scale=0.125),
                       reads=[f"l1t_pA{pr}", "cM12"], writes=[f"Psb{pr}"])
                    jloc = kt * 128 - q0
                    if jloc >= 0:
                        if jloc > 0:
                            op("vector", lambda e, jloc=jloc, Psb=Psb: e.memset(
                                Psb[:, 0:jloc], 0.0), writes=[f"Psb{pr}"])
                        op("vector", lambda e, jloc=jloc, Psb=Psb: e.tensor_mul(
                            Psb[:, jloc:jloc + 128], Psb[:, jloc:jloc + 128],
                            sb["UTI"][:]),
                            reads=["UTI"], writes=[f"Psb{pr}"])
                    if kt + 1 < nkt:
                        emit_qk(kt + 1)
                    for s in range(4):
                        ctx.mm([(Psb[:, s * 128:(s + 1) * 128],
                                 vex[:, kt * 260 + hl * 65:kt * 260 + hl * 65 + 65])],
                               pO[s][:, 0:65], reads=[f"Psb{pr}", "vex", "vex_ones"],
                               writes=[f"l1t_pO{s}"],
                               start=(kt == 0), stop=(kt == nkt - 1))
                for s in range(4):
                    c = qb * 4 + s
                    op("vector", lambda e, s=s: e.reciprocal(
                        dcol[:], pO[s][:, 64:65]),
                        reads=[f"l1t_pO{s}"], writes=["l1t_dc"])
                    op("vector", lambda e, s=s, c=c, h=h: e.tensor_scalar(
                        ytm1[:, c * D + h * 64:c * D + h * 64 + 64],
                        pO[s][:, 0:64], dcol[:], None, ALU.mult),
                        reads=[f"l1t_pO{s}", "l1t_dc"], writes=["ytm1"])
    op("sync", lambda e: e.dma_start(out=dr["ytm1d"], in_=ytm1[:]),
       reads=["ytm1"], writes=["ytm1d"], dma=True)
    return [p, ypool]



def kernel(**inputs):
    ins = {k: np.ascontiguousarray(np.asarray(v)) for k, v in inputs.items()}
    x = ins["L0_attn_x"].astype(np.float32, copy=False)

    def tr(a):
        return np.ascontiguousarray(a.T)

    common = {
        "wqt": tr(ins["L0_attn_Wq"]), "wkt": tr(ins["L0_attn_Wk"]),
        "wvt": tr(ins["L0_attn_Wv"]), "wpt": tr(ins["L0_attn_Wp"]),
        "bp": ins["L0_attn_bp"].reshape(1, D),
        "rw": ins["L0_ffn_router_w"], "rb": ins["L0_ffn_router_b"].reshape(1, 8),
        "w1s": np.concatenate([ins["L0_ffn_routed_w1"], ins["L0_ffn_shared_w1"]], 0),
        "w3s": np.concatenate([ins["L0_ffn_routed_w3"], ins["L0_ffn_shared_w3"]], 0),
        "w2s": np.concatenate([ins["L0_ffn_routed_w2"], ins["L0_ffn_shared_w2"]], 0),
        "aqt": tr(ins["L1_attn_Wq"]), "akt": tr(ins["L1_attn_Wk"]),
        "avt": tr(ins["L1_attn_Wv"]), "aot": tr(ins["L1_attn_Wo"]),
        "wg": ins["L1_ffn_wg"],
    }
    common = {k: np.ascontiguousarray(v) for k, v in common.items()}
    nc = build_nc()
    in_maps = []
    for c in range(NCORES):
        m = dict(common)
        m["x0"] = x[c]
        m["w1e"] = ins["L1_ffn_w1"][c]
        m["b1e"] = np.ascontiguousarray(ins["L1_ffn_b1"][c].reshape(32, 128).T)
        m["w2e"] = ins["L1_ffn_w2"][c]
        m["b2e"] = ins["L1_ffn_b2"][c].reshape(1, D)
        es = np.zeros((1, 8), np.float32)
        es[0, c] = 1.0
        m["esel"] = es
        m["esel16"] = np.ascontiguousarray(np.tile(es, (1, 16)))
        in_maps.append(m)
    res = run_bass_kernel_spmd(nc, in_maps, core_ids=list(range(NCORES)))
    return np.stack([r["out"] for r in res.results], axis=0)


# version 26
# speedup vs baseline: 1.0823x; 1.0212x over previous
import numpy as np
import ml_dtypes
import concourse.bass as bass
import concourse.mybir as mybir
from concourse.bass_core_rust import IndirectOffsetOnAxis
from concourse.bass_utils import run_bass_kernel_spmd

F32 = mybir.dt.float32
BF16 = mybir.dt.bfloat16
I32 = mybir.dt.int32
AF = mybir.ActivationFunctionType
ALU = mybir.AluOpType
AX = mybir.AxisListType.X

NCORES = 8
T = 4096
D = 1024
CAP = 5120
NSLOT = 5632
TRASH = NSLOT - 1
NTILE = T // 128
GTILE = (NCORES * T) // 128

_L0D_SEL = list(range(10))
_BATCH_HALF = 99

ENGC = ("tensor", "vector", "scalar", "gpsimd", "sync")
DMAQ = ("sync", "gpsimd", "scalar")


class Ctx:

    def __init__(self, nc):
        self.nc = nc
        self.csem = {n: nc.alloc_semaphore(f"C_{n}") for n in ENGC}
        self.dsem = {n: nc.alloc_semaphore(f"D_{n}") for n in DMAQ}
        self.ccnt = {n: 0 for n in ENGC}
        self.dcnt = {n: 0 for n in DMAQ}
        self.wr = {}
        self.rd = {}
        self.mark = {n: {} for n in ENGC}
        self.sched = {n: [] for n in ENGC}
        self.dslot = {}

    def _sem(self, key):
        if key[0] == "c":
            return self.csem[key[1]]
        if key[0] == "s":
            return self.dslot[key[1]][0]
        return self.dsem[key[1]]

    def _need(self, waits, eng, key, cnt):
        if key == ("c", eng) and eng == "tensor":
            return
        m = self.mark[eng]
        if m.get(key, -1) >= cnt:
            return
        m[key] = cnt
        waits.append((key, cnt))

    def _bump(self, key, cnt):
        if key[0] == "d":
            return self.dcnt[key[1]]
        return cnt

    def _deps(self, eng, reads, writes):
        waits = []
        for t in reads:
            wv = self.wr.get(t)
            if wv:
                self._need(waits, eng, wv[0], self._bump(wv[0], wv[1]))
        for t in writes:
            wv = self.wr.get(t)
            if wv:
                self._need(waits, eng, wv[0], self._bump(wv[0], wv[1]))
            for (k, c) in self.rd.get(t, ()):
                self._need(waits, eng, k, self._bump(k, c))
        return waits

    def op(self, eng, fn, reads=(), writes=(), dma=False, dkey=None):
        waits = self._deps(eng, reads, writes)
        if dma:
            if dkey is not None:
                if dkey not in self.dslot:
                    self.dslot[dkey] = [self.nc.alloc_semaphore(f"S_{dkey}"), 0]
                ent = self.dslot[dkey]
                ent[1] += 16
                cnt = ent[1]
                key = ("s", dkey)
                iv = 16
            else:
                self.dcnt[eng] += 16
                cnt = self.dcnt[eng]
                key = ("d", eng)
                iv = 16
        else:
            self.ccnt[eng] += 1
            cnt = self.ccnt[eng]
            key = ("c", eng)
            iv = 1
        self.sched[eng].append((waits, fn, self._sem(key), iv))
        for t in writes:
            self.wr[t] = (key, cnt)
            self.rd[t] = []
        for t in reads:
            self.rd.setdefault(t, []).append((key, cnt))
        return cnt

    def mm(self, pairs, out_ap, reads=(), writes=(), start=True, stop=True):
        for lhsT, rhs in pairs:
            assert lhsT.base_partition() == rhs.base_partition(), \
                (lhsT.tensor.name, lhsT.base_partition(), rhs.tensor.name,
                 rhs.base_partition())
        def fn(e, pairs=tuple(pairs), out_ap=out_ap, start=start, stop=stop):
            n = len(pairs)
            ins = None
            for i, (lhsT, rhs) in enumerate(pairs):
                ins = e.matmul(out_ap, lhsT, rhs,
                               start=(start and i == 0),
                               stop=(stop and i == n - 1))
            return ins
        return self.op("tensor", fn, reads, writes)

    def barrier(self):
        targets = [(("c", n), self.ccnt[n]) for n in ENGC if self.ccnt[n] > 0]
        targets += [(("d", n), self.dcnt[n]) for n in DMAQ if self.dcnt[n] > 0]
        targets += [(("s", k), v[1]) for k, v in self.dslot.items() if v[1] > 0]
        for eng in ENGC:
            waits = []
            for key, cnt in targets:
                if key == ("c", eng):
                    continue
                self._need(waits, eng, key, cnt)
            if waits:
                self.sched[eng].append((waits, None, None, 0))

    def end_wait(self, eng, key_eng):
        key = ("d", key_eng)
        self.sched[eng].append(([(key, self.dcnt[key_eng])], None, None, 0))
        self.mark[eng][key] = self.dcnt[key_eng]

    def replay(self, block):
        def make(name):
            def run(eng):
                for waits, fn, sem, iv in self.sched[name]:
                    for key, cnt in waits:
                        eng.wait_ge(self._sem(key), cnt)
                    if fn is not None:
                        fn(eng).then_inc(sem, iv)
            return run
        block.tensor(make("tensor"))
        block.vector(make("vector"))
        block.scalar(make("scalar"))
        block.gpsimd(make("gpsimd"))
        block.sync(make("sync"))


class Pool:

    def __init__(self, nc):
        self.nc = nc
        self.guards = []

    def sb(self, name, shape, dt):
        g = self.nc.sbuf_tensor(name, shape, dt)
        h = g.__enter__()
        self.guards.append(g)
        return h

    def ps(self, name, shape, dt=F32):
        g = self.nc.psum_tensor(name, shape, dt)
        h = g.__enter__()
        self.guards.append(g)
        return h

    def free(self):
        for g in reversed(self.guards):
            g.__exit__(None, None, None)
        self.guards = []



def _consts():
    bf = ml_dtypes.bfloat16
    c = {}
    c["IDENT"] = np.eye(128, dtype=bf)
    c["IDENTF"] = np.eye(128, dtype=np.float32)
    k = np.arange(128)[:, None]
    i = np.arange(128)[None, :]
    c["UTI"] = (k <= i).astype(bf)
    c["UTS"] = (k < i).astype(bf)
    c["ONESB"] = np.ones((128, 128), dtype=bf)
    c["ONESF"] = np.ones((128, 128), dtype=np.float32)
    c["IOTAB"] = np.ascontiguousarray(
        np.broadcast_to(np.arange(128, dtype=np.float32)[None, :], (128, 128)))
    c["IOTAC"] = np.ascontiguousarray(np.arange(128, dtype=np.float32)[:, None])
    P = np.zeros((128, 128), dtype=np.float32)
    for r in range(128):
        if r % 64 < 32:
            P[r, r + 32] = -1.0
        else:
            P[r, r - 32] = 1.0
    c["ROTT"] = np.ascontiguousarray(P.T.astype(bf))
    inv = 1.0 / (10000.0 ** (np.arange(0, 64, 2, dtype=np.float64) / 64))
    t = np.arange(T, dtype=np.float64)
    fr = np.outer(inv, t)
    c["COS"] = np.ascontiguousarray(np.tile(np.cos(fr), (4, 1)).astype(bf))
    c["SIN"] = np.ascontiguousarray(np.tile(np.sin(fr), (4, 1)).astype(bf))
    cc = np.arange(256)
    pb = (cc % 32 // 8) * 8192 + (cc // 32) * 1024 + (cc % 8) * 128
    c["PBC"] = np.ascontiguousarray(pb.astype(np.float32)[:, None])
    return c


def build_nc(debug_taps=()):
    nc = bass.Bass("TRN2", target_bir_lowering=False)

    xin = nc.dram_tensor("x0", [T, D], F32, kind="ExternalInput").ap()
    w = {}
    for n in ("wqt", "wkt", "wvt", "wpt", "aqt", "akt", "avt", "aot"):
        w[n] = nc.dram_tensor(n, [D, D], F32, kind="ExternalInput").ap()
    w["bp"] = nc.dram_tensor("bp", [1, D], F32, kind="ExternalInput").ap()
    w["rw"] = nc.dram_tensor("rw", [D, 8], F32, kind="ExternalInput").ap()
    w["rb"] = nc.dram_tensor("rb", [1, 8], F32, kind="ExternalInput").ap()
    w["w1s"] = nc.dram_tensor("w1s", [10, D, 512], F32, kind="ExternalInput").ap()
    w["w3s"] = nc.dram_tensor("w3s", [10, D, 512], F32, kind="ExternalInput").ap()
    w["w2s"] = nc.dram_tensor("w2s", [10, 512, D], F32, kind="ExternalInput").ap()
    w["wg"] = nc.dram_tensor("wg", [D, 8], F32, kind="ExternalInput").ap()
    w["w1e"] = nc.dram_tensor("w1e", [D, 4096], F32, kind="ExternalInput").ap()
    w["b1e"] = nc.dram_tensor("b1e", [128, 32], F32, kind="ExternalInput").ap()
    w["w2e"] = nc.dram_tensor("w2e", [4096, D], F32, kind="ExternalInput").ap()
    w["b2e"] = nc.dram_tensor("b2e", [1, D], F32, kind="ExternalInput").ap()
    w["esel"] = nc.dram_tensor("esel", [1, 8], F32, kind="ExternalInput").ap()
    w["esel16"] = nc.dram_tensor("esel16", [1, 128], F32, kind="ExternalInput").ap()
    out = nc.dram_tensor("out", [T, D], F32, kind="ExternalOutput").ap()

    dr = {}
    dr["xn0"] = nc.dram_tensor("xn0_d", [T, D], BF16).ap()
    dr["x1"] = nc.dram_tensor("x1_d", [T, D], F32).ap()
    dr["agx_in"] = nc.dram_tensor("agx_in", [T, D], BF16)
    dr["agx_out"] = nc.dram_tensor("agx_out", [NCORES * T + 128, D], BF16)
    dr["aglg_in"] = nc.dram_tensor("aglg_in", [T, 8], F32)
    dr["aglg_out"] = nc.dram_tensor("aglg_out", [NCORES * T, 8], F32)
    dr["disp"] = nc.dram_tensor("disp_d", [NSLOT, D], BF16).ap()
    dr["eout"] = nc.dram_tensor("eout_d", [NSLOT, D], BF16).ap()
    dr["rsin"] = nc.dram_tensor("rsin_d", [NCORES * T + 128, D], BF16)
    dr["rsout"] = nc.dram_tensor("rsout_d", [T, D], BF16)
    dr["tbl"] = nc.dram_tensor("tbl_d", [256, 260], F32).ap()
    dr["bounce"] = nc.dram_tensor("bounce", [2, 256], F32).ap()
    dr["gfm"] = nc.dram_tensor("gfm_d", [8, T], F32).ap()
    dr["ytm1d"] = nc.dram_tensor("ytm1_d", [128, NTILE * D], BF16).ap()
    dr["xa"] = nc.dram_tensor("xa_d", [T, D], F32).ap()
    for n in ("aqt", "akt", "avt", "aot"):
        dr[n + "b"] = nc.dram_tensor(n + "_b16", [D, D], BF16).ap()

    consts = _consts()
    capd = {k: nc.inline_tensor(v, k) for k, v in consts.items()}

    taps = {}
    tapspec = {
        "xn0": ([T, D], BF16), "x1p": ([T, D], F32), "x1": ([T, D], F32),
        "gatesfm": ([8, T], F32), "logits": ([T, 8], F32),
        "mmat": ([128, GTILE], F32), "posmat": ([128, GTILE], F32),
        "slotmat": ([128, GTILE], F32), "gsmat": ([128, GTILE], F32),
        "toki": ([128, NSLOT // 128], I32), "tokp": ([128, NSLOT // 128], I32),
        "gsl": ([128, NSLOT // 128], F32), "tblm": ([256, 260], F32),
        "dbgm": ([128, 256], F32), "dbgc": ([128, 8], F32),
        "offsrow": ([1, GTILE], F32), "roffif": ([1, GTILE], F32),
        "regdump": ([1, GTILE], I32),
        "disp": ([NSLOT, D], BF16),
        "eoutd": ([NSLOT, D], BF16), "rsin": ([NCORES * T, D], BF16),
        "rsout": ([T, D], BF16), "ytm0": ([T, D], BF16),
        "ytm1": ([T, D], BF16), "agx": ([NCORES * T, D], BF16),
    }
    for name in debug_taps:
        shp, dt = tapspec[name]
        taps[name] = nc.dram_tensor("tap_" + name, shp, dt, kind="ExternalOutput").ap()

    ctx = Ctx(nc)
    with nc.Block() as block:
        _emit(ctx, nc, xin, w, out, dr, capd, taps)
        ctx.replay(block)
    return nc



def _emit(ctx, nc, xin, w, out, dr, capd, taps):
    op = ctx.op
    pers = Pool(nc)
    sb = {}
    for n, shp, dt in [("IDENT", [128, 128], BF16), ("IDENTF", [128, 128], F32),
                       ("UTI", [128, 128], BF16),
                       ("UTS", [128, 128], BF16), ("ONESB", [128, 128], BF16),
                       ("ONESF", [128, 128], F32), ("IOTAB", [128, 128], F32),
                       ("IOTAC", [128, 1], F32), ("ROTT", [128, 128], BF16)]:
        sb[n] = pers.sb(n.lower(), shp, dt)
        op("sync", lambda e, n=n: e.dma_start(out=sb[n][:], in_=capd[n].ap()),
           writes=[n], dma=True)
    for n, src, shp in [("bp", "bp", [1, D]), ("rb", "rb", [1, 8]),
                        ("b2e", "b2e", [1, D]), ("b1e", "b1e", [128, 32]),
                        ("esel", "esel", [1, 8])]:
        sb[n] = pers.sb(n + "_sb", shp, F32)
        op("sync", lambda e, n=n, src=src: e.dma_start(out=sb[n][:], in_=w[src]),
           writes=[n], dma=True)
    for n, src in [("rwsb", "rw"), ("wgsb", "wg")]:
        sb[n] = pers.sb(n, [128, 64], BF16)
        src_ap = bass.AP(w[src].tensor, 0, [[8, 128], [1024, 8], [1, 8]])
        op("gpsimd", lambda e, n=n, src_ap=src_ap: e.dma_start(
            out=sb[n][:].rearrange("p (c e) -> p c e", e=8), in_=src_ap),
            writes=[n], dma=True)
    for n in ("M", "G1", "POS", "SLOT", "GS", "CUMS", "KTMP"):
        sb[n] = pers.sb(n, [128, GTILE], F32)
    NST = NSLOT // 128
    sb["TOKI"] = pers.sb("TOKI", [128, NST], I32)
    sb["TOKP"] = pers.sb("TOKP", [128, NST], I32)
    sb["GSL"] = pers.sb("GSL", [128, NST], F32)
    sb["Mb"] = pers.sb("Mb", [128, GTILE], BF16)
    sb["tot"] = pers.sb("tot", [1, GTILE], F32)
    sb["offs"] = pers.sb("offs", [1, GTILE], F32)
    sb["rmin"] = pers.sb("rmin", [1, GTILE], F32)
    sb["roffi"] = pers.sb("roffi", [1, GTILE], I32)
    sb["tot2"] = pers.sb("tot2", [128, 2], BF16)
    sb["offs2"] = pers.sb("offs2", [128, 2], F32)
    sb["scs"] = pers.sb("scs", [1, 2], F32)
    sb["sc1"] = pers.sb("sc1", [128, 8], F32)
    sb["sc2"] = pers.sb("sc2", [128, 8], F32)
    sb["sc3"] = pers.sb("sc3", [128, 8], F32)
    sb["fc"] = pers.sb("fc", [128, 8], F32)
    sb["eselb"] = pers.sb("eselb", [128, 8], F32)
    sb["cEPS"] = pers.sb("cEPS", [128, 1], F32)
    sb["cM12"] = pers.sb("cM12", [128, 1], F32)
    op("vector", lambda e: e.memset(sb["cEPS"][:], 1e-5), writes=["cEPS"])
    op("vector", lambda e: e.memset(sb["cM12"][:], -12.0), writes=["cM12"])
    psepool = Pool(nc)
    pse = psepool.ps("pse", [128, 8])
    ctx.mm([(sb["ONESF"][0:1, :], sb["esel"][:])], pse[:],
           reads=["ONESF", "esel"], writes=["pse"])
    op("vector", lambda e: e.tensor_copy(sb["eselb"][:], pse[:]),
       reads=["pse"], writes=["eselb"])
    ctx.barrier()
    psepool.free()

    cvt = [pers.sb(f"cvt{i}", [128, D], BF16) for i in range(2)]
    for n in ("aqt", "akt", "avt", "aot"):
        for c in range(8):
            q = c % 2
            src = bass.AP(w[n].tensor, c * 128 * D, [[D, 128], [1, D]])
            dst = bass.AP(dr[n + "b"].tensor, c * 128 * D, [[D, 128], [1, D]])
            op("gpsimd", lambda e, src=src, q=q: e.dma_start(
                out=cvt[q][:], in_=src),
                writes=[f"cv{q}"], dma=True, dkey=f"cv{q}")
            op("sync", lambda e, dst=dst, q=q: e.dma_start(
                out=dst, in_=cvt[q][:]),
                reads=[f"cv{q}"], writes=[f"{n}b_d"], dma=True)

    _l0_ln(ctx, nc, sb, xin, dr, taps)
    ctx.barrier()
    _l0_attn(ctx, nc, sb, xin, w, dr, taps)
    ctx.barrier()
    _l0_moe(ctx, nc, sb, w, dr, taps)
    ctx.barrier()

    _l1_pre(ctx, nc, sb, w, dr, taps)
    early, eb = _l1_attn_pre(ctx, nc, sb, dr, capd)
    _l1_routing(ctx, nc, sb, w, dr, capd, taps)
    ctx.barrier()
    rg = [list(range(NCORES))]
    op("gpsimd", lambda e: e.collective_compute(
        "AllGather", ALU.bypass, replica_groups=rg,
        ins=[dr["agx_in"].ap().opt()],
        outs=[dr["agx_out"].ap()[0:NCORES * T, :].opt()]),
        reads=[f"agxin{j}" for j in range(NTILE)], writes=["agxout"])
    if "agx" in taps:
        op("sync", lambda e: e.dma_start(
            out=taps["agx"], in_=dr["agx_out"].ap()[0:NCORES * T, :]),
            reads=["agxout"], dma=True)
    pd = Pool(nc)
    dtags = _l1_dispatch(ctx, nc, sb, dr, taps, pd)
    apools = _l1_attn(ctx, nc, sb, w, dr, capd, taps, early, eb)
    ctx.barrier()
    for pp in apools:
        pp.free()
    pd.free()
    early.free()
    etags = _l1_ffn(ctx, nc, sb, w, dr, dtags, taps)
    _l1_combine_final(ctx, nc, sb, w, out, dr, etags, taps)

    ctx.end_wait("sync", "sync")
    ctx.end_wait("sync", "gpsimd")
    ctx.end_wait("sync", "scalar")


def _ln_tile(ctx, sb, xt, xnb, sq):
    op = ctx.op
    F = sb["fc"]
    op("vector", lambda e: e.reduce_sum(F[:, 0:1], xt[:], AX),
       reads=["xt"], writes=["f0"])
    op("vector", lambda e: e.tensor_scalar(F[:, 1:2], F[:, 0:1], 1.0 / D, None, ALU.mult),
       reads=["f0"], writes=["f1"])
    op("scalar", lambda e: e.activation(sq[:], xt[:], AF.Square, accum_out=F[:, 2:3]),
       reads=["xt"], writes=["sq", "f2"])
    op("vector", lambda e: e.tensor_mul(F[:, 3:4], F[:, 1:2], F[:, 1:2]),
       reads=["f1"], writes=["f3"])
    op("vector", lambda e: e.scalar_tensor_tensor(F[:, 3:4], F[:, 2:3], 1.0 / D,
                                                  F[:, 3:4], ALU.mult, ALU.subtract),
       reads=["f2", "f3"], writes=["f3"])
    op("scalar", lambda e: e.activation(F[:, 4:5], F[:, 3:4], AF.Sqrt,
                                        bias=sb["cEPS"][:]),
       reads=["f3", "cEPS"], writes=["f4"])
    op("vector", lambda e: e.reciprocal(F[:, 5:6], F[:, 4:5]),
       reads=["f4"], writes=["f5"])
    op("vector", lambda e: e.tensor_scalar(xnb[:], xt[:], F[:, 1:2], F[:, 5:6],
                                           ALU.subtract, ALU.mult),
       reads=["xt", "f1", "f5"], writes=["xnb"])



def _l0_ln(ctx, nc, sb, xin, dr, taps):
    op = ctx.op
    p = Pool(nc)
    NB = 2
    xt = [p.sb(f"l0a_xt{i}", [128, D], F32) for i in range(NB)]
    xnb = [p.sb(f"l0a_xnb{i}", [128, D], BF16) for i in range(NB)]
    xnf = [p.sb(f"l0a_xnf{i}", [128, D], BF16) for i in range(NB)]
    sq = [p.sb(f"l0a_sq{i}", [128, D], F32) for i in range(NB)]
    fcb = [p.sb(f"l0a_fc{i}", [128, 8], F32) for i in range(NB)]
    sc1 = [p.sb(f"l0a_sc1{i}", [128, 8], F32) for i in range(NB)]
    sc2 = [p.sb(f"l0a_sc2{i}", [128, 8], F32) for i in range(NB)]
    sc3 = [p.sb(f"l0a_sc3{i}", [128, 8], F32) for i in range(NB)]
    gst = [p.sb(f"l0a_gst{i}", [8, 128], F32) for i in range(NB)]
    pt = [p.ps(f"l0a_pt{i}", [128, 128], BF16) for i in range(NB)]
    plg = [p.ps(f"l0a_plg{i}", [128, 8]) for i in range(NB)]
    pg = [p.ps(f"l0a_pg{i}", [8, 128]) for i in range(NB)]

    def front(j):
        q = j % NB
        op("sync", lambda e, j=j, q=q: e.dma_start(
            out=xt[q][:], in_=xin[j * 128:(j + 1) * 128, :]),
            writes=[f"b_xt{q}"], dma=True, dkey=f"b_xt{q}")
        _ln_tile_r(ctx, sb, xt[q], xnb[q], sq[q], fcb[q], f"b{q}")

    def back(j):
        q = j % NB
        F = fcb[q]
        op("sync", lambda e, j=j, q=q: e.dma_start(
            out=dr["xn0"][j * 128:(j + 1) * 128, :], in_=xnb[q][:]),
            reads=[f"b_xnb{q}"], writes=[f"xn0_{j}"], dma=True)
        for cc in range(8):
            op("tensor", lambda e, cc=cc, q=q: e.transpose(
                pt[q][:], xnb[q][:, cc * 128:(cc + 1) * 128], sb["IDENT"][:]),
               reads=[f"b_xnb{q}", "IDENT"], writes=[f"b_pt{q}"])
            op("scalar", lambda e, cc=cc, q=q: e.copy(
                xnf[q][:, cc * 128:(cc + 1) * 128], pt[q][:]),
               reads=[f"b_pt{q}"], writes=[f"b_xnf{q}_{cc}"])
        ctx.mm([(xnf[q][:, cc * 128:(cc + 1) * 128], sb["rwsb"][:, cc * 8:(cc + 1) * 8])
                for cc in range(8)], plg[q][:],
               reads=[f"b_xnf{q}_{cc}" for cc in range(8)] + ["rwsb"],
               writes=[f"b_plg{q}"], stop=False)
        ctx.mm([(sb["ONESF"][0:1, :], sb["rb"][:])], plg[q][:],
               reads=["ONESF", "rb"], writes=[f"b_plg{q}"], start=False)
        op("vector", lambda e, q=q: e.reduce_max(F[:, 0:1], plg[q][:], AX),
           reads=[f"b_plg{q}"], writes=[f"b{q}_g0"])
        op("vector", lambda e, q=q: e.tensor_scalar(sc1[q][:], plg[q][:], F[:, 0:1],
                                                    None, ALU.is_ge),
           reads=[f"b_plg{q}", f"b{q}_g0"], writes=[f"b_sc1{q}"])
        op("vector", lambda e, q=q: e.scalar_tensor_tensor(
            sc2[q][:], sc1[q][:], -1e30, plg[q][:], ALU.mult, ALU.add),
           reads=[f"b_sc1{q}", f"b_plg{q}"], writes=[f"b_sc2{q}"])
        op("vector", lambda e, q=q: e.reduce_max(F[:, 1:2], sc2[q][:], AX),
           reads=[f"b_sc2{q}"], writes=[f"b{q}_g1"])
        op("vector", lambda e, q=q: e.tensor_scalar(sc1[q][:], plg[q][:], F[:, 1:2],
                                                    None, ALU.is_ge),
           reads=[f"b_plg{q}", f"b{q}_g1"], writes=[f"b_sc1{q}"])
        op("vector", lambda e, q=q: e.tensor_scalar(F[:, 2:3], F[:, 0:1], -1.0,
                                                    None, ALU.mult),
           reads=[f"b{q}_g0"], writes=[f"b{q}_g2"])
        op("scalar", lambda e, q=q: e.activation(sc2[q][:], plg[q][:], AF.Exp,
                                                 bias=F[:, 2:3]),
           reads=[f"b_plg{q}", f"b{q}_g2"], writes=[f"b_sc2{q}"])
        op("vector", lambda e, q=q: e.tensor_mul(sc3[q][:], sc2[q][:], sc1[q][:]),
           reads=[f"b_sc2{q}", f"b_sc1{q}"], writes=[f"b_sc3{q}"])
        op("vector", lambda e, q=q: e.reduce_sum(F[:, 3:4], sc3[q][:], AX),
           reads=[f"b_sc3{q}"], writes=[f"b{q}_g3"])
        op("vector", lambda e, q=q: e.reciprocal(F[:, 4:5], F[:, 3:4]),
           reads=[f"b{q}_g3"], writes=[f"b{q}_g4"])
        op("vector", lambda e, q=q: e.tensor_scalar(sc3[q][:], sc3[q][:], F[:, 4:5],
                                                    None, ALU.mult),
           reads=[f"b_sc3{q}", f"b{q}_g4"], writes=[f"b_sc3{q}"])
        op("tensor", lambda e, q=q: e.transpose(pg[q][:], sc3[q][:], sb["IDENTF"][:]),
           reads=[f"b_sc3{q}", "IDENTF"], writes=[f"b_pg{q}"])
        op("vector", lambda e, q=q: e.tensor_copy(gst[q][:], pg[q][:]),
           reads=[f"b_pg{q}"], writes=[f"b_gst{q}"])
        gap = bass.AP(dr["gfm"].tensor, j * 128, [[T, 8], [1, 128]])
        op("sync", lambda e, gap=gap, q=q: e.dma_start(out=gap, in_=gst[q][:]),
           reads=[f"b_gst{q}"], writes=["gfm_d"], dma=True)

    front(0)
    for j in range(NTILE):
        if j + 1 < NTILE:
            front(j + 1)
        back(j)
    if "xn0" in taps:
        op("sync", lambda e: e.dma_start(out=taps["xn0"], in_=dr["xn0"]),
           reads=[f"xn0_{j}" for j in range(NTILE)], dma=True)
    if "gatesfm" in taps:
        op("sync", lambda e: e.dma_start(out=taps["gatesfm"], in_=dr["gfm"]),
           reads=["gfm_d"], dma=True)
    ctx.barrier()
    p.free()



def _l0_attn(ctx, nc, sb, xin, w, dr, taps):
    op = ctx.op
    ypool = Pool(nc)
    ytm = ypool.sb("ytm", [128, NTILE * D], BF16)

    p = Pool(nc)
    qg = p.sb("l0b_q", [128, T], BF16)
    kg = p.sb("l0b_k", [128, T], BF16)
    ktg = p.sb("l0b_kt", [128, T], BF16)
    vex = p.sb("l0b_vex", [128, NTILE * 130], BF16)
    xst = p.sb("l0b_xst", [128, 8 * T], BF16)
    phs = p.sb("l0b_phs", [128, T], BF16)
    ssb = p.sb("l0b_s", [128, 72], BF16)
    asb2 = [p.sb(f"l0b_a{i}", [128, 128], BF16) for i in range(2)]
    wq = p.sb("l0b_wq", [128, 1024], BF16)
    wk = p.sb("l0b_wk", [128, 1024], BF16)
    wv = p.sb("l0b_wv", [128, 1024], BF16)
    dcol = p.sb("l0b_dcol", [128, 4], F32)
    pbank = [p.ps(f"l0b_pb{i}", [128, 512]) for i in range(4)]
    pS2 = [p.ps(f"l0b_pS{i}", [64, 72]) for i in range(2)]
    pT2 = [p.ps(f"l0b_pT{i}", [128, 128], BF16) for i in range(2)]

    xn0tags = [f"xn0_{j}" for j in range(NTILE)]
    for tb in range(8):
        for cc in range(8):
            op("sync", lambda e, cc=cc, tb=tb: e.dma_start_transpose(
                xst[:, cc * T + tb * 512:cc * T + tb * 512 + 512],
                dr["xn0"][tb * 512:(tb + 1) * 512, cc * 128:(cc + 1) * 128]),
                reads=xn0tags[tb * 4:tb * 4 + 4], writes=[f"xst{tb}"], dma=True)
    for g in range(8):
        for name, wt, buf in (("wq", "wqt", wq), ("wk", "wkt", wk), ("wv", "wvt", wv)):
            src = bass.AP(w[wt].tensor, g * 128, [[1024, 128], [1024 * 128, 8], [1, 128]])
            op("gpsimd", lambda e, src=src, buf=buf: e.dma_start(
                out=buf[:].rearrange("p (c f) -> p c f", f=128), in_=src),
                writes=[name], dma=True, dkey=name)
        vones = bass.AP(vex, 64, [[NTILE * 130, 128], [130, NTILE], [65, 2], [1, 1]])
        op("vector", lambda e, vones=vones: e.memset(vones, 1.0),
           reads=["vex"], writes=["vex_ones"])

        pj = 0
        for tb in range(8):
            xr = [f"xst{tb}"]
            for name, buf, dst, dtag in (("wq", wq, qg, "qg"), ("wk", wk, kg, "kg")):
                pb_ = pbank[pj % 4]
                ptag = f"pbank{pj % 4}"
                pj += 1
                ctx.mm([(buf[:, cc * 128:(cc + 1) * 128],
                         xst[:, cc * T + tb * 512:cc * T + tb * 512 + 512])
                        for cc in range(8)], pb_[:],
                       reads=[name] + xr, writes=[ptag])
                op("scalar", lambda e, dst=dst, tb=tb, pb_=pb_: e.copy(
                    dst[:, tb * 512:(tb + 1) * 512], pb_[:]),
                   reads=[ptag], writes=[dtag])
            for ts_ in range(4):
                pb_ = pbank[pj % 4]
                ptag = f"pbank{pj % 4}"
                pj += 1
                ctx.mm([(xst[:, cc * T + tb * 512 + ts_ * 128:
                             cc * T + tb * 512 + ts_ * 128 + 128],
                         wv[:, cc * 128:(cc + 1) * 128]) for cc in range(8)],
                       pb_[:, 0:128], reads=["wv"] + xr, writes=[ptag])
                ti = tb * 4 + ts_
                for hh in range(2):
                    op("scalar", lambda e, ti=ti, hh=hh, pb_=pb_: e.copy(
                        vex[:, ti * 130 + hh * 65:ti * 130 + hh * 65 + 64],
                        pb_[:, hh * 64:hh * 64 + 64]),
                        reads=[ptag], writes=["vex"])
        for dtag, buf in (("qg", qg), ("kg", kg)):
            op("vector", lambda e, buf=buf: e.tensor_scalar(phs[:], buf[:], 0.0,
                                                            None, ALU.min),
               reads=[dtag], writes=["phs"])
            op("scalar", lambda e: e.activation(phs[:], phs[:], AF.Exp),
               reads=["phs"], writes=["phs"])
            op("vector", lambda e, buf=buf: e.scalar_tensor_tensor(
                buf[:], buf[:], 0.0, phs[:], ALU.max, ALU.add),
               reads=[dtag, "phs"], writes=[dtag])
        for c in range(NTILE):
            tp_ = c % 2
            op("tensor", lambda e, c=c, tp_=tp_: e.transpose(
                pT2[tp_][:], kg[:, c * 128:(c + 1) * 128], sb["IDENT"][:]),
               reads=["kg", "IDENT"], writes=[f"pT{tp_}"])
            op("vector", lambda e, c=c, tp_=tp_: e.tensor_copy(
                ktg[:, c * 128:(c + 1) * 128], pT2[tp_][:]),
               reads=[f"pT{tp_}"], writes=["ktg"])
        for c in range(NTILE):
            for hh in range(2):
                b = hh * 64
                ctx.mm([(kg[b:b + 64, c * 128:(c + 1) * 128],
                         qg[b:b + 64, c * 128:(c + 1) * 128])],
                       pbank[hh][:, 0:128], reads=["kg", "qg"], writes=[f"pbank{hh}"])
            for hh in range(2):
                op("vector", lambda e, hh=hh: e.tensor_mul(
                    asb2[hh][:], pbank[hh][:, 0:128], sb["UTI"][:]),
                   reads=[f"pbank{hh}", "UTI"], writes=[f"asb{hh}"])
            for hh in range(2):
                b = hh * 64
                pairs = [(asb2[hh][:], vex[:, c * 130 + hh * 65:c * 130 + hh * 65 + 65])]
                rd = [f"asb{hh}", "vex", "vex_ones"]
                if c > 0:
                    pairs.append((qg[b:b + 64, c * 128:(c + 1) * 128],
                                  ssb[b:b + 64, 0:65]))
                    rd += ["qg", f"ssb{hh}"]
                ctx.mm(pairs, pbank[2 + hh][:, 0:65], reads=rd,
                       writes=[f"pbank{2 + hh}"])
                ctx.mm([(ktg[:, c * 128 + b:c * 128 + b + 64],
                         vex[:, c * 130 + hh * 65:c * 130 + hh * 65 + 65])],
                       pS2[hh][0:64, 0:65], reads=["ktg", "vex", "vex_ones"],
                       writes=[f"pS2{hh}"], start=(c == 0), stop=(c == NTILE - 1))
            for hh in range(2):
                b = hh * 64
                op("vector", lambda e, hh=hh: e.tensor_scalar(
                    dcol[:, 2 * hh:2 * hh + 1], pbank[2 + hh][:, 64:65],
                    1e-6, None, ALU.add),
                   reads=[f"pbank{2 + hh}"], writes=[f"dc0{hh}"])
                op("vector", lambda e, hh=hh: e.reciprocal(
                    dcol[:, 2 * hh + 1:2 * hh + 2], dcol[:, 2 * hh:2 * hh + 1]),
                   reads=[f"dc0{hh}"], writes=[f"dc1{hh}"])
                op("scalar", lambda e, c=c, g=g, hh=hh: e.mul(
                    ytm[:, c * D + g * 128 + hh * 64:c * D + g * 128 + hh * 64 + 64],
                    pbank[2 + hh][:, 0:64], dcol[:, 2 * hh + 1:2 * hh + 2]),
                    reads=[f"pbank{2 + hh}", f"dc1{hh}"], writes=["ytm"])
                if c < NTILE - 1:
                    op("vector", lambda e, hh=hh, b=b: e.tensor_copy(
                        ssb[b:b + 64, 0:65], pS2[hh][0:64, 0:65]),
                       reads=[f"pS2{hh}"], writes=[f"ssb{hh}"])
    if "ytm0" in taps:
        op("sync", lambda e: e.dma_start(
            out=taps["ytm0"].rearrange("(j p) d -> p (j d)", p=128), in_=ytm[:]),
            reads=["ytm"], dma=True)
    ctx.barrier()
    p.free()

    p = Pool(nc)
    yf = p.sb("l0c_yf", [128, 1024], BF16)
    x0t = p.sb("l0c_x0", [128, D], F32)
    x1t = p.sb("l0c_x1", [128, D], F32)
    wp = p.sb("l0c_wp", [128, 8192], BF16)
    pT = p.ps("l0c_pT", [128, 128], BF16)
    pU = p.ps("l0c_pU", [128, 512])
    src = bass.AP(w["wpt"].tensor, 0, [[1024, 128], [1024 * 128, 8], [1, 1024]])
    op("gpsimd", lambda e, src=src: e.dma_start(
        out=wp[:].rearrange("p (c f) -> p c f", f=1024), in_=src),
        writes=["wp"], dma=True, dkey="wp")
    for j in range(NTILE):
        r0 = j * 128
        for cc in range(8):
            op("tensor", lambda e, j=j, cc=cc: e.transpose(
                pT[:], ytm[:, j * D + cc * 128:j * D + (cc + 1) * 128], sb["IDENT"][:]),
               reads=["ytm", "IDENT"], writes=["l0c_pT"])
            op("vector", lambda e, cc=cc: e.tensor_copy(
                yf[:, cc * 128:(cc + 1) * 128], pT[:]),
               reads=["l0c_pT"], writes=[f"yf{cc}"])
        op("sync", lambda e, r0=r0: e.dma_start(out=x0t[:], in_=xin[r0:r0 + 128, :]),
           writes=["x0t"], dma=True)
        for ih in range(2):
            ctx.mm([(yf[:, cc * 128:(cc + 1) * 128],
                     wp[:, cc * 1024 + ih * 512:cc * 1024 + ih * 512 + 512])
                    for cc in range(8)], pU[:],
                   reads=[f"yf{cc}" for cc in range(8)] + ["wp"],
                   writes=["l0c_pU"], stop=False)
            ctx.mm([(sb["ONESF"][0:1, :], sb["bp"][0:1, ih * 512:ih * 512 + 512])], pU[:],
                   reads=["ONESF", "bp"], writes=["l0c_pU"], start=False)
            op("vector", lambda e, ih=ih: e.tensor_add(
                x1t[:, ih * 512:(ih + 1) * 512], pU[:], x0t[:, ih * 512:(ih + 1) * 512]),
                reads=["l0c_pU", "x0t"], writes=["x1t"])
        op("sync", lambda e, r0=r0: e.dma_start(out=dr["x1"][r0:r0 + 128, :], in_=x1t[:]),
           reads=["x1t"], writes=[f"x1_{j}"], dma=True)
    if "x1p" in taps:
        op("sync", lambda e: e.dma_start(out=taps["x1p"], in_=dr["x1"]),
           reads=[f"x1_{j}" for j in range(NTILE)], dma=True)
    ctx.barrier()
    p.free()
    ypool.free()



def _l0_moe(ctx, nc, sb, w, dr, taps):
    op = ctx.op
    p = Pool(nc)
    w1p = p.sb("l0d_w1", [128, 5 * 4096], BF16)
    w3p = p.sb("l0d_w3", [128, 5 * 4096], BF16)
    w2p = p.sb("l0d_w2", [128, 5 * 4096], BF16)
    xnd2 = [p.sb(f"l0d_xn{i}", [128, 2048], BF16) for i in range(2)]
    hsb2 = [p.sb(f"l0d_h{i}", [128, 1024], BF16) for i in range(2)]
    silu2 = [p.sb(f"l0d_si{i}", [128, 256], BF16) for i in range(2)]
    gG2 = [p.sb(f"l0d_g{i}", [128, 5 * 256], F32) for i in range(2)]
    x1t = p.sb("l0d_x1", [128, 2 * D], F32)
    o1 = [p.sb(f"l0d_o1{i}", [128, 512], F32) for i in range(2)]
    pa2 = [p.ps(f"l0d_pa{i}", [128, 256]) for i in range(2)]
    pb2 = [p.ps(f"l0d_pb{i}", [128, 256]) for i in range(2)]
    py = [p.ps(f"l0d_py{i}", [128, 512]) for i in range(4)]
    xn0tags = [f"xn0_{j}" for j in range(NTILE)]

    def load_xnd(tb):
        xb = tb % 2
        for cc in range(8):
            op("sync", lambda e, cc=cc, tb=tb, xb=xb: e.dma_start_transpose(
                xnd2[xb][:, cc * 256:(cc + 1) * 256],
                dr["xn0"][tb * 256:tb * 256 + 256, cc * 128:(cc + 1) * 128]),
                reads=xn0tags[tb * 2:tb * 2 + 2], writes=[f"xnd{xb}_{cc}"], dma=True,
                dkey=f"xnd{xb}")

    def load_gates(ps_, tb):
        xb = tb % 2
        nr = 5 if ps_ == 0 else 3
        for e5 in range(nr):
            ex = ps_ * 5 + e5
            src = bass.AP(dr["gfm"].tensor, ex * T + tb * 256,
                          [[0, 128], [1, 256]])
            wt = [f"gG{xb}_{e5}"] + ([f"gGt{xb}"] if e5 == nr - 1 else [])
            op("scalar", lambda e, src=src, xb=xb, e5=e5: e.dma_start(
                out=gG2[xb][:, e5 * 256:(e5 + 1) * 256], in_=src),
                reads=["gfm_d"], writes=wt, dma=True, dkey=f"gG{xb}")

    for ps_ in range(2):
        for e5 in range(5):
            ex = ps_ * 5 + e5
            for name, wsrc, buf, inner, nch in (("w1p", "w1s", w1p, 512, 8),
                                                ("w3p", "w3s", w3p, 512, 8),
                                                ("w2p", "w2s", w2p, 1024, 4)):
                src = bass.AP(w[wsrc].tensor, ex * D * 512,
                              [[inner, 128], [inner * 128, nch], [1, inner]])
                op("gpsimd", lambda e, src=src, buf=buf, e5=e5, inner=inner:
                   e.dma_start(out=buf[:, e5 * 4096:(e5 + 1) * 4096]
                               .rearrange("p (c f) -> p c f", f=inner), in_=src),
                   writes=[name + str(e5)], dma=True, dkey=name + str(e5))
        if ps_ == 0:
            load_xnd(0)
        load_gates(ps_, 0)
        for tb in range(16):
            t0 = tb * 256
            xb = tb % 2
            xr = [f"xnd{xb}_{cc}" for cc in range(8)]
            if tb + 1 < 16 or ps_ == 0:
                load_xnd(tb + 1 if tb + 1 < 16 else 0)
            if tb + 1 < 16:
                load_gates(ps_, tb + 1)
            elif ps_ == 0:
                load_gates(1, 0)
            for e5 in range(5):
                ex = ps_ * 5 + e5
                hp = e5 % 2
                hsb = hsb2[hp]
                for ft in range(4):
                    fp = ft % 2
                    ctx.mm([(w1p[:, e5 * 4096 + cc * 512 + ft * 128:
                             e5 * 4096 + cc * 512 + ft * 128 + 128],
                             xnd2[xb][:, cc * 256:(cc + 1) * 256]) for cc in range(8)],
                           pa2[fp][:], reads=[f"w1p{e5}"] + xr,
                           writes=[f"l0d_pa{fp}"])
                    ctx.mm([(w3p[:, e5 * 4096 + cc * 512 + ft * 128:
                             e5 * 4096 + cc * 512 + ft * 128 + 128],
                             xnd2[xb][:, cc * 256:(cc + 1) * 256]) for cc in range(8)],
                           pb2[fp][:], reads=[f"w3p{e5}"] + xr,
                           writes=[f"l0d_pb{fp}"])
                    op("scalar", lambda e, fp=fp: e.activation(
                        silu2[fp][:], pa2[fp][:], AF.Silu),
                       reads=[f"l0d_pa{fp}"], writes=[f"silu{fp}"])
                    if ex < 8:
                        op("vector", lambda e, ft=ft, e5=e5, fp=fp, hsb=hsb, xb=xb:
                            e.tensor_mul(
                                hsb[:, ft * 256:(ft + 1) * 256], pb2[fp][:],
                                gG2[xb][:, e5 * 256:(e5 + 1) * 256]),
                            reads=[f"l0d_pb{fp}", f"gG{xb}_{e5}", f"gGt{xb}"],
                            writes=[f"hsb{hp}"])
                        op("vector", lambda e, ft=ft, fp=fp, hsb=hsb: e.tensor_mul(
                            hsb[:, ft * 256:(ft + 1) * 256],
                            hsb[:, ft * 256:(ft + 1) * 256], silu2[fp][:]),
                            reads=[f"hsb{hp}", f"silu{fp}"], writes=[f"hsb{hp}"])
                    else:
                        op("vector", lambda e, ft=ft, fp=fp, hsb=hsb: e.tensor_mul(
                            hsb[:, ft * 256:(ft + 1) * 256], pb2[fp][:], silu2[fp][:]),
                            reads=[f"l0d_pb{fp}", f"silu{fp}"], writes=[f"hsb{hp}"])
                for tsub in range(2):
                    for mh in range(2):
                        ctx.mm([(hsb[:, fc * 256 + tsub * 128:fc * 256 + tsub * 128 + 128],
                                 w2p[:, e5 * 4096 + fc * 1024 + mh * 512:
                                     e5 * 4096 + fc * 1024 + mh * 512 + 512])
                                for fc in range(4)], py[tsub * 2 + mh][:],
                               reads=[f"hsb{hp}", f"w2p{e5}"],
                               writes=[f"l0d_py{tsub * 2 + mh}"],
                               start=(e5 == 0), stop=(e5 == 4))
            x1src = bass.AP(dr["x1"].tensor, t0 * D, [[D, 128], [128 * D, 2], [1, D]])
            op("sync", lambda e, x1src=x1src: e.dma_start(
                out=x1t[:].rearrange("p (k m) -> p k m", m=D), in_=x1src),
                reads=[f"x1_{tb * 2}", f"x1_{tb * 2 + 1}"], writes=["l0d_x1t"], dma=True, dkey="l0d_x1t")
            for tsub in range(2):
                for mh in range(2):
                    ob = mh
                    op("vector", lambda e, tsub=tsub, mh=mh, ob=ob: e.tensor_add(
                        o1[ob][:], py[tsub * 2 + mh][:],
                        x1t[:, tsub * D + mh * 512:tsub * D + mh * 512 + 512]),
                        reads=[f"l0d_py{tsub * 2 + mh}", "l0d_x1t"],
                        writes=[f"l0d_o1{ob}"])
                    op("sync", lambda e, t0=t0, tsub=tsub, mh=mh, ob=ob: e.dma_start(
                        out=dr["x1"][t0 + tsub * 128:t0 + tsub * 128 + 128,
                                     mh * 512:mh * 512 + 512], in_=o1[ob][:]),
                        reads=[f"l0d_o1{ob}"], writes=[f"x1_{tb * 2 + tsub}"], dma=True)
        ctx.barrier()
    if "x1" in taps:
        op("sync", lambda e: e.dma_start(out=taps["x1"], in_=dr["x1"]),
           reads=[f"x1_{j}" for j in range(NTILE)], dma=True)
    ctx.barrier()
    p.free()


def _l1_pre(ctx, nc, sb, w, dr, taps):
    op = ctx.op
    p = Pool(nc)
    NB = 2
    xt = [p.sb(f"l1a_xt{i}", [128, D], F32) for i in range(NB)]
    xnb = [p.sb(f"l1a_xnb{i}", [128, D], BF16) for i in range(NB)]
    xnf = [p.sb(f"l1a_xnf{i}", [128, D], BF16) for i in range(NB)]
    sq = [p.sb(f"l1a_sq{i}", [128, D], F32) for i in range(NB)]
    lg = [p.sb(f"l1a_lg{i}", [128, 8], F32) for i in range(NB)]
    fcb = [p.sb(f"l1a_fc{i}", [128, 8], F32) for i in range(NB)]
    pt = [p.ps(f"l1a_pt{i}", [128, 128], BF16) for i in range(NB)]
    plg = [p.ps(f"l1a_plg{i}", [128, 8]) for i in range(NB)]

    def front(j):
        q = j % NB
        op("sync", lambda e, j=j, q=q: e.dma_start(out=xt[q][:],
                                                   in_=dr["x1"][j * 128:(j + 1) * 128, :]),
           reads=[f"x1_{j}"], writes=[f"a_xt{q}"], dma=True, dkey=f"a_xt{q}")
        _ln_tile_r(ctx, sb, xt[q], xnb[q], sq[q], fcb[q], f"a{q}")

    def back(j):
        q = j % NB
        op("sync", lambda e, j=j, q=q: e.dma_start(
            out=dr["agx_in"][j * 128:(j + 1) * 128, :], in_=xnb[q][:]),
            reads=[f"a_xnb{q}"], writes=[f"agxin{j}"], dma=True)
        for cc in range(8):
            op("tensor", lambda e, cc=cc, q=q: e.transpose(
                pt[q][:], xnb[q][:, cc * 128:(cc + 1) * 128], sb["IDENT"][:]),
               reads=[f"a_xnb{q}", "IDENT"], writes=[f"a_pt{q}"])
            op("scalar", lambda e, cc=cc, q=q: e.copy(
                xnf[q][:, cc * 128:(cc + 1) * 128], pt[q][:]),
               reads=[f"a_pt{q}"], writes=[f"a_xnf{q}_{cc}"])
        ctx.mm([(xnf[q][:, cc * 128:(cc + 1) * 128], sb["wgsb"][:, cc * 8:(cc + 1) * 8])
                for cc in range(8)], plg[q][:],
               reads=[f"a_xnf{q}_{cc}" for cc in range(8)] + ["wgsb"],
               writes=[f"a_plg{q}"])
        op("vector", lambda e, q=q: e.tensor_copy(lg[q][:], plg[q][:]),
           reads=[f"a_plg{q}"], writes=[f"a_lg{q}"])
        op("sync", lambda e, j=j, q=q: e.dma_start(
            out=dr["aglg_in"][j * 128:(j + 1) * 128, :], in_=lg[q][:]),
            reads=[f"a_lg{q}"], writes=[f"aglgin{j}"], dma=True)

    front(0)
    for j in range(NTILE):
        if j + 1 < NTILE:
            front(j + 1)
        back(j)
    ctx.barrier()
    p.free()
    rg = [list(range(NCORES))]
    op("gpsimd", lambda e: e.collective_compute(
        "AllGather", ALU.bypass, replica_groups=rg,
        ins=[dr["aglg_in"].ap().opt()], outs=[dr["aglg_out"].ap().opt()]),
        reads=[f"aglgin{j}" for j in range(NTILE)], writes=["aglgout"])
    if "logits" in taps:
        op("sync", lambda e: e.dma_start(out=taps["logits"], in_=dr["aglg_in"].ap()),
           reads=[f"aglgin{j}" for j in range(NTILE)], dma=True)


def _ln_tile_r(ctx, sb, xt, xnb, sq, F, pfx):
    op = ctx.op
    op("vector", lambda e: e.reduce_sum(F[:, 0:1], xt[:], AX),
       reads=[f"{pfx}_xt" if False else f"{pfx[0]}_xt{pfx[1]}"], writes=[f"{pfx}_f0"])
    op("vector", lambda e: e.tensor_scalar(F[:, 1:2], F[:, 0:1], 1.0 / D, None,
                                           ALU.mult),
       reads=[f"{pfx}_f0"], writes=[f"{pfx}_f1"])
    op("scalar", lambda e: e.activation(sq[:], xt[:], AF.Square, accum_out=F[:, 2:3]),
       reads=[f"{pfx[0]}_xt{pfx[1]}"], writes=[f"{pfx}_sq", f"{pfx}_f2"])
    op("vector", lambda e: e.tensor_mul(F[:, 3:4], F[:, 1:2], F[:, 1:2]),
       reads=[f"{pfx}_f1"], writes=[f"{pfx}_f3"])
    op("vector", lambda e: e.scalar_tensor_tensor(F[:, 3:4], F[:, 2:3], 1.0 / D,
                                                  F[:, 3:4], ALU.mult, ALU.subtract),
       reads=[f"{pfx}_f2", f"{pfx}_f3"], writes=[f"{pfx}_f3"])
    op("scalar", lambda e: e.activation(F[:, 4:5], F[:, 3:4], AF.Sqrt,
                                        bias=sb["cEPS"][:]),
       reads=[f"{pfx}_f3", "cEPS"], writes=[f"{pfx}_f4"])
    op("vector", lambda e: e.reciprocal(F[:, 5:6], F[:, 4:5]),
       reads=[f"{pfx}_f4"], writes=[f"{pfx}_f5"])
    op("vector", lambda e: e.tensor_scalar(xnb[:], xt[:], F[:, 1:2], F[:, 5:6],
                                           ALU.subtract, ALU.mult),
       reads=[f"{pfx[0]}_xt{pfx[1]}", f"{pfx}_f1", f"{pfx}_f5"],
       writes=[f"{pfx[0]}_xnb{pfx[1]}"])



def _batch_done(ctx, nc, sb, dr, taps, p):
    ctx.barrier()
    p.free()


def _l1_routing(ctx, nc, sb, w, dr, capd, taps):
    op = ctx.op
    p = Pool(nc)
    lgt = p.sb("l1r_lg", [128, 128], F32)
    pc = p.ps("l1r_pc", [128, GTILE])
    pt1 = p.ps("l1r_pt1", [1, GTILE])
    pb2 = p.ps("l1r_pb2", [128, 2])
    pcs = p.ps("l1r_pcs", [1, 2])
    pbc = p.ps("l1r_pbc", [128, 1])
    F = sb["fc"]

    es16 = p.sb("l1r_es16", [1, 128], F32)
    esb16 = p.sb("l1r_esb16", [128, 128], F32)
    lgt2 = [p.sb(f"l1r_lg2{i}", [128, 128], F32) for i in range(2)]
    LL = [p.sb(f"l1r_LL{i}", [128, 128], F32) for i in range(2)]
    EX = [p.sb(f"l1r_EX{i}", [128, 128], F32) for i in range(2)]
    MYP = [p.sb(f"l1r_MYP{i}", [128, 128], F32) for i in range(2)]
    M1 = [p.sb(f"l1r_M1{i}", [128, 16], F32) for i in range(2)]
    S16 = [p.sb(f"l1r_S16{i}", [128, 16], F32) for i in range(2)]
    MY16 = [p.sb(f"l1r_MY16{i}", [128, 16], F32) for i in range(2)]
    peb = p.ps("l1r_peb", [128, 128])
    op("sync", lambda e: e.dma_start(out=es16[:], in_=w["esel16"]),
       writes=["es16"], dma=True)
    ctx.mm([(sb["ONESF"][0:1, :], es16[:])], peb[:], reads=["ONESF", "es16"],
           writes=["peb"])
    op("vector", lambda e: e.tensor_copy(esb16[:], peb[:]),
       reads=["peb"], writes=["esb16"])
    for jj in range(16):
        q = jj % 2
        src = bass.AP(dr["aglg_out"].ap().tensor, jj * 16 * 128 * 8,
                      [[8, 128], [128 * 8, 16], [1, 8]])
        op("sync", lambda e, src=src, q=q: e.dma_start(
            out=lgt2[q][:].rearrange("p (j e) -> p j e", e=8), in_=src),
            reads=["aglgout"], writes=[f"lgt{q}"], dma=True, dkey=f"lgt{q}")
        l3 = lgt2[q][:].rearrange("p (j e) -> p j e", e=8)
        op("vector", lambda e, q=q, l3=l3: e.reduce_max(M1[q][:], l3, AX),
           reads=[f"lgt{q}"], writes=[f"rM1{q}"])
        m1b = M1[q][:].rearrange("p (j o) -> p j o", o=1).to_broadcast([128, 16, 8])
        ll3 = LL[q][:].rearrange("p (j e) -> p j e", e=8)
        op("vector", lambda e, q=q, l3=l3, m1b=m1b, ll3=ll3: e.tensor_tensor(
            ll3, l3, m1b, ALU.subtract),
           reads=[f"lgt{q}", f"rM1{q}"], writes=[f"rLL{q}"])
        op("scalar", lambda e, q=q: e.activation(EX[q][:], LL[q][:], AF.Exp),
           reads=[f"rLL{q}"], writes=[f"rEX{q}"])
        e3 = EX[q][:].rearrange("p (j e) -> p j e", e=8)
        op("vector", lambda e, q=q, e3=e3: e.reduce_sum(S16[q][:], e3, AX),
           reads=[f"rEX{q}"], writes=[f"rS16{q}"])
        op("vector", lambda e, jj=jj, q=q: e.reciprocal(
            sb["G1"][:, jj * 16:(jj + 1) * 16], S16[q][:]),
           reads=[f"rS16{q}"], writes=["G1"])
        op("vector", lambda e, q=q: e.tensor_mul(MYP[q][:], lgt2[q][:], esb16[:]),
           reads=[f"lgt{q}", "esb16"], writes=[f"rMYP{q}"])
        p3 = MYP[q][:].rearrange("p (j e) -> p j e", e=8)
        op("vector", lambda e, q=q, p3=p3: e.reduce_sum(MY16[q][:], p3, AX),
           reads=[f"rMYP{q}"], writes=[f"rMY16{q}"])
        op("vector", lambda e, jj=jj, q=q: e.tensor_tensor(
            sb["M"][:, jj * 16:(jj + 1) * 16], MY16[q][:], M1[q][:], ALU.is_ge),
           reads=[f"rMY16{q}", f"rM1{q}"], writes=["M"])
        op("vector", lambda e: e.tensor_copy(sb["Mb"][:], sb["M"][:]),
       reads=["M"], writes=["Mb"])
    ctx.mm([(sb["UTI"][:], sb["Mb"][:])], pc[:], reads=["UTI", "Mb"], writes=["l1r_pc"])
    op("vector", lambda e: e.tensor_copy(sb["CUMS"][:], pc[:]),
       reads=["l1r_pc"], writes=["CUMS"])
    ctx.mm([(sb["ONESB"][:, 0:1], sb["Mb"][:])], pt1[:], reads=["ONESB", "Mb"],
           writes=["ptot"])
    op("vector", lambda e: e.tensor_copy(sb["tot"][:], pt1[:]),
       reads=["ptot"], writes=["tot"])
    op("sync", lambda e: e.dma_start(out=dr["bounce"][0:1, :], in_=sb["tot"][:]),
       reads=["tot"], writes=["bounce0"], dma=True)
    b_src = bass.AP(dr["bounce"].tensor, 0, [[1, 128], [128, 2]])
    op("gpsimd", lambda e, b_src=b_src: e.dma_start(
        out=sb["tot2"][:], in_=b_src, allow_slow_non_contiguous=True),
        reads=["bounce0"], writes=["tot2"], dma=True)
    ctx.mm([(sb["UTS"][:], sb["tot2"][:])], pb2[:], reads=["UTS", "tot2"],
           writes=["pb2"])
    ctx.mm([(sb["ONESB"][:, 0:1], sb["tot2"][:])], pcs[:], reads=["ONESB", "tot2"],
           writes=["pcs"])
    op("vector", lambda e: e.tensor_copy(sb["scs"][:], pcs[:]),
       reads=["pcs"], writes=["scs"])
    ctx.mm([(sb["ONESF"][0:1, :], sb["scs"][0:1, 0:1])], pbc[:],
           reads=["ONESF", "scs"], writes=["pbc"])
    op("vector", lambda e: e.tensor_copy(sb["offs2"][:], pb2[:]),
       reads=["pb2"], writes=["offs2"])
    op("vector", lambda e: e.tensor_add(sb["offs2"][:, 1:2], sb["offs2"][:, 1:2], pbc[:]),
       reads=["offs2", "pbc"], writes=["offs2"])
    b_dst = bass.AP(dr["bounce"].tensor, 256, [[1, 128], [128, 2]])
    op("gpsimd", lambda e, b_dst=b_dst: e.dma_start(
        out=b_dst, in_=sb["offs2"][:], allow_slow_non_contiguous=True),
        reads=["offs2"], writes=["bounce1"], dma=True)
    op("sync", lambda e: e.dma_start(out=sb["offs"][:], in_=dr["bounce"][1:2, :]),
       reads=["bounce1"], writes=["offs"], dma=True)
    ctx.mm([(sb["ONESF"][0:1, :], sb["offs"][:])], pc[:], reads=["ONESF", "offs"],
           writes=["l1r_pc"])
    op("vector", lambda e: e.tensor_add(sb["POS"][:], sb["CUMS"][:], pc[:]),
       reads=["CUMS", "l1r_pc"], writes=["POS"])
    op("vector", lambda e: e.scalar_tensor_tensor(sb["SLOT"][:], sb["M"][:], -33000.0,
                                                  sb["POS"][:], ALU.mult, ALU.add),
       reads=["M", "POS"], writes=["SLOT"])
    op("vector", lambda e: e.tensor_scalar(sb["SLOT"][:], sb["SLOT"][:], 32999.0,
                                           None, ALU.add),
       reads=["SLOT"], writes=["SLOT"])
    op("vector", lambda e: e.tensor_scalar(sb["SLOT"][:], sb["SLOT"][:], float(TRASH),
                                           None, ALU.min),
       reads=["SLOT"], writes=["SLOT"])
    op("vector", lambda e: e.tensor_scalar(sb["KTMP"][:], sb["POS"][:], float(CAP),
                                           None, ALU.is_le),
       reads=["POS"], writes=["KTMP"])
    op("vector", lambda e: e.tensor_mul(sb["GS"][:], sb["G1"][:], sb["M"][:]),
       reads=["G1", "M"], writes=["GS"])
    op("vector", lambda e: e.tensor_mul(sb["GS"][:], sb["GS"][:], sb["KTMP"][:]),
       reads=["GS", "KTMP"], writes=["GS"])
    op("vector", lambda e: e.tensor_scalar(sb["rmin"][:], sb["offs"][:], float(CAP),
                                           None, ALU.min),
       reads=["offs"], writes=["rmin"])
    op("vector", lambda e: e.tensor_copy(sb["roffi"][:], sb["rmin"][:]),
       reads=["rmin"], writes=["roffi"])

    NST = NSLOT // 128
    tpos = p.sb("l1r_tpos", [128, 128], F32)
    OFFSB = p.sb("l1r_offsb", [128, GTILE], F32)
    SCB = p.sb("l1r_scb", [128, NST], F32)
    TT2 = [p.sb(f"l1r_TT{i}", [128, 260], F32) for i in range(2)]
    tmA = [p.sb(f"l1r_tmA{i}", [128, GTILE], F32) for i in range(2)]
    tmB = [p.sb(f"l1r_tmB{i}", [128, 128], F32) for i in range(2)]
    tmC = [p.sb(f"l1r_tmC{i}", [128, 128], F32) for i in range(2)]
    cnt2 = [p.sb(f"l1r_cnt{i}", [128, 8], F32) for i in range(2)]
    coli = [p.sb(f"l1r_coli{i}", [128, 1], I32) for i in range(2)]
    for half, srcn in ((0, "POS"), (1, "GS")):
        for ct in range(2):
            op("tensor", lambda e, srcn=srcn, ct=ct: e.transpose(
                pc[:, 0:128], sb[srcn][:, ct * 128:(ct + 1) * 128], sb["IDENTF"][:]),
               reads=[srcn, "IDENTF"], writes=["l1r_pc"])
            op("vector", lambda e: e.tensor_copy(tpos[:], pc[:, 0:128]),
               reads=["l1r_pc"], writes=["l1r_tpos"])
            op("sync", lambda e, half=half, ct=ct: e.dma_start(
                out=dr["tbl"][ct * 128:(ct + 1) * 128, half * 128:(half + 1) * 128],
                in_=tpos[:]),
                reads=["l1r_tpos"], writes=["tbl_d"], dma=True)
    op("sync", lambda e: e.dma_start(out=dr["tbl"][:, 256:257], in_=capd["PBC"].ap(),
                                     allow_slow_non_contiguous=True),
       writes=["tbl_d"], dma=True)
    ctx.mm([(sb["ONESF"][0:1, :], sb["offs"][:])], pc[:], reads=["ONESF", "offs"],
           writes=["l1r_pc"])
    op("vector", lambda e: e.tensor_copy(OFFSB[:], pc[:]),
       reads=["l1r_pc"], writes=["l1r_offsb"])
    for st in range(NST):
        op("vector", lambda e, st=st: e.tensor_scalar(
            SCB[:, st:st + 1], sb["IOTAC"][:], float(st * 128), None, ALU.add),
           reads=["IOTAC"], writes=["l1r_scb"])
    for st in range(NST):
        q = st % 2
        cn = cnt2[q]
        op("vector", lambda e, st=st, q=q: e.tensor_scalar(
            tmA[q][:], OFFSB[:], SCB[:, st:st + 1], None, ALU.is_le),
           reads=["l1r_offsb", "l1r_scb"], writes=[f"l1r_A{q}"])
        op("vector", lambda e, q=q, cn=cn: e.reduce_sum(cn[:, 0:1], tmA[q][:], AX),
           reads=[f"l1r_A{q}"], writes=[f"l1r_c0{q}"])
        op("vector", lambda e, q=q, cn=cn: e.tensor_scalar(
            cn[:, 1:2], cn[:, 0:1], -1.0, None, ALU.add),
           reads=[f"l1r_c0{q}"], writes=[f"l1r_c1{q}"])
        op("vector", lambda e, q=q, cn=cn: e.tensor_copy(coli[q][:], cn[:, 1:2]),
           reads=[f"l1r_c1{q}"], writes=[f"l1r_coli{q}"])
        op("gpsimd", lambda e, q=q: e.indirect_dma_start(
            out=TT2[q][:], out_offset=None, in_=dr["tbl"],
            in_offset=IndirectOffsetOnAxis(ap=coli[q][:], axis=0)),
            reads=[f"l1r_coli{q}", "tbl_d"], writes=[f"l1r_TT{q}"], dma=True,
            dkey=f"ttg{q}")
        op("vector", lambda e, st=st, q=q: e.tensor_scalar(
            tmB[q][:], TT2[q][:, 0:128], SCB[:, st:st + 1], None, ALU.is_le),
           reads=[f"l1r_TT{q}", "l1r_scb"], writes=[f"l1r_B{q}"])
        op("vector", lambda e, q=q, cn=cn: e.reduce_sum(cn[:, 2:3], tmB[q][:], AX),
           reads=[f"l1r_B{q}"], writes=[f"l1r_c2{q}"])
        op("vector", lambda e, q=q, cn=cn: e.scalar_tensor_tensor(
            cn[:, 3:4], cn[:, 1:2], 128.0, cn[:, 2:3], ALU.mult, ALU.add),
           reads=[f"l1r_c1{q}", f"l1r_c2{q}"], writes=[f"l1r_c3{q}"])
        op("vector", lambda e, st=st, q=q, cn=cn: e.tensor_copy(
            sb["TOKI"][:, st:st + 1], cn[:, 3:4]),
           reads=[f"l1r_c3{q}"], writes=["TOKI"])
        op("vector", lambda e, q=q, cn=cn: e.tensor_add(
            cn[:, 4:5], TT2[q][:, 256:257], cn[:, 2:3]),
           reads=[f"l1r_TT{q}", f"l1r_c2{q}"], writes=[f"l1r_c4{q}"])
        op("vector", lambda e, st=st, q=q, cn=cn: e.tensor_copy(
            sb["TOKP"][:, st:st + 1], cn[:, 4:5]),
           reads=[f"l1r_c4{q}"], writes=["TOKP"])
        op("vector", lambda e, st=st, q=q, cn=cn: e.tensor_scalar(
            cn[:, 5:6], SCB[:, st:st + 1], 1.0, None, ALU.add),
           reads=["l1r_scb"], writes=[f"l1r_c5{q}"])
        op("vector", lambda e, q=q, cn=cn: e.tensor_scalar(
            tmC[q][:], TT2[q][:, 0:128], cn[:, 5:6], None, ALU.is_le),
           reads=[f"l1r_TT{q}", f"l1r_c5{q}"], writes=[f"l1r_C{q}"])
        op("vector", lambda e, q=q: e.tensor_sub(tmC[q][:], tmC[q][:], tmB[q][:]),
           reads=[f"l1r_C{q}", f"l1r_B{q}"], writes=[f"l1r_C{q}"])
        op("vector", lambda e, q=q: e.tensor_mul(
            tmC[q][:], tmC[q][:], TT2[q][:, 128:256]),
           reads=[f"l1r_C{q}", f"l1r_TT{q}"], writes=[f"l1r_C{q}"])
        op("vector", lambda e, st=st, q=q: e.reduce_sum(
            sb["GSL"][:, st:st + 1], tmC[q][:], AX),
           reads=[f"l1r_C{q}"], writes=["GSL"])
        if st == 0 and "dbgm" in taps:
            op("sync", lambda e, q=q: e.dma_start(
                out=taps["dbgm"][:, 0:128], in_=tmB[q][:]),
               reads=[f"l1r_B{q}"], dma=True)
            op("sync", lambda e, q=q: e.dma_start(
                out=taps["dbgm"][:, 128:256], in_=tmC[q][:]),
               reads=[f"l1r_C{q}"], dma=True)
            op("sync", lambda e, q=q, cn=cn: e.dma_start(
                out=taps["dbgc"], in_=cn[:]),
               reads=[f"l1r_c5{q}"], dma=True)
    for tn, buf in (("mmat", "M"), ("posmat", "POS"), ("slotmat", "SLOT"),
                    ("gsmat", "GS"), ("toki", "TOKI"), ("tokp", "TOKP"),
                    ("gsl", "GSL")):
        if tn in taps:
            op("sync", lambda e, tn=tn, buf=buf: e.dma_start(
                out=taps[tn], in_=sb[buf][:]), reads=[buf], dma=True)
    if "tblm" in taps:
        op("sync", lambda e: e.dma_start(out=taps["tblm"], in_=dr["tbl"]),
           reads=["tbl_d"], dma=True)
    if "offsrow" in taps:
        op("sync", lambda e: e.dma_start(out=taps["offsrow"], in_=sb["offs"][:]),
           reads=["offs"], dma=True)
    if "roffif" in taps:
        op("vector", lambda e: e.tensor_copy(sb["rmin"][:], sb["roffi"][:]),
           reads=["roffi"], writes=["rmin"])
        op("sync", lambda e: e.dma_start(out=taps["roffif"], in_=sb["rmin"][:]),
           reads=["rmin"], dma=True)
    ctx.barrier()
    p.free()



def _l1_dispatch(ctx, nc, sb, dr, taps, p):
    op = ctx.op
    zz = p.sb("l1m_zz", [128, 8], BF16)
    zt = p.sb("l1m_zero", [128, D], BF16)
    op("vector", lambda e: e.memset(zt[:], 0.0), writes=["l1m_zt"])
    op("gpsimd", lambda e: e.dma_start(
        out=dr["agx_out"].ap()[NCORES * T:NCORES * T + 128, :], in_=zt[:]),
        reads=["l1m_zt"], writes=["agz"], dma=True)
    xnt = [p.sb(f"l1m_xnt{i}", [128, D], BF16) for i in range(3)]
    dtags = []
    for st in range(NSLOT // 128):
        b = st % 3
        op("gpsimd", lambda e, st=st, b=b: e.indirect_dma_start(
            out=xnt[b][:], out_offset=None,
            in_=dr["agx_out"].ap(),
            in_offset=IndirectOffsetOnAxis(ap=sb["TOKI"][:, st:st + 1], axis=0)),
            reads=["agxout", "TOKI", "agz"], writes=[f"xnt{b}"], dma=True,
            dkey=f"xnt{b}")
        tg = f"dsc{st}"
        dtags.append(tg)
        op("gpsimd", lambda e, st=st, b=b: e.dma_start(
            out=dr["disp"][st * 128:(st + 1) * 128, :], in_=xnt[b][:]),
            reads=[f"xnt{b}"], writes=[tg], dma=True)
    op("gpsimd", lambda e: e.dma_start(out=zz[:, 0:8], in_=dr["disp"][0:128, 0:8]),
       reads=dtags, writes=["dfence"], dma=True)
    if "disp" in taps:
        op("sync", lambda e: e.dma_start(out=taps["disp"], in_=dr["disp"]),
           reads=dtags + ["dfence"], dma=True)
    for j in range(NCORES * T // 128 + 1):
        op("gpsimd", lambda e, j=j: e.dma_start(
            out=dr["rsin"].ap()[j * 128:(j + 1) * 128, :], in_=zt[:]),
            reads=["l1m_zt"], writes=["rzero"], dma=True)
    return dtags


def _l1_ffn(ctx, nc, sb, w, dr, dtags, taps):
    op = ctx.op
    p = Pool(nc)
    w1sb = p.sb("l1f_w1", [128, 8 * 4096], BF16)
    w2sb = p.sb("l1f_w2", [128, 32 * 1024], BF16)
    dspf = p.sb("l1f_dsp", [128, 8 * 512], BF16)
    hsb = p.sb("l1f_h", [128, 32 * 512], BF16)
    eo = [p.sb(f"l1f_eo{i}", [128, D], BF16) for i in range(3)]
    oc = [p.sb(f"l1f_oc{i}", [128, D], BF16) for i in range(3)]
    ph = p.ps("l1f_ph", [128, 512])
    pe = [p.ps(f"l1f_pe{i}", [128, 512]) for i in range(2)]
    src = bass.AP(w["w1e"].tensor, 0, [[4096, 128], [4096 * 128, 8], [1, 4096]])
    op("gpsimd", lambda e, src=src: e.dma_start(
        out=w1sb[:].rearrange("p (c f) -> p c f", f=4096), in_=src),
        writes=["w1sb"], dma=True, dkey="w1sb")
    src2 = bass.AP(w["w2e"].tensor, 0, [[1024, 128], [1024 * 128, 32], [1, 1024]])
    op("gpsimd", lambda e, src2=src2: e.dma_start(
        out=w2sb[:].rearrange("p (c f) -> p c f", f=1024), in_=src2),
        writes=["w2sb"], dma=True, dkey="w2sb")

    etags = []
    for cb in range(NSLOT // 512):
        c0 = cb * 512
        for mc in range(8):
            op("sync", lambda e, mc=mc, c0=c0: e.dma_start_transpose(
                dspf[:, mc * 512:(mc + 1) * 512],
                dr["disp"][c0:c0 + 512, mc * 128:(mc + 1) * 128]),
                reads=dtags + ["dfence"], writes=[f"dspf{mc}"], dma=True, dkey="dspf")
        for ft in range(32):
            ctx.mm([(w1sb[:, mc * 4096 + ft * 128:mc * 4096 + ft * 128 + 128],
                     dspf[:, mc * 512:(mc + 1) * 512]) for mc in range(8)],
                   ph[:], reads=["w1sb"] + [f"dspf{mc}" for mc in range(8)],
                   writes=["l1f_ph"])
            op("scalar", lambda e, ft=ft: e.activation(
                hsb[:, ft * 512:(ft + 1) * 512], ph[:], AF.Relu,
                bias=sb["b1e"][:, ft:ft + 1]),
                reads=["l1f_ph", "b1e"], writes=["hsb"])
        for ct in range(4):
            st = cb * 4 + ct
            r = st % 3
            for mh in range(2):
                pid = mh
                ctx.mm([(hsb[:, fc * 512 + ct * 128:fc * 512 + ct * 128 + 128],
                         w2sb[:, fc * 1024 + mh * 512:fc * 1024 + mh * 512 + 512])
                        for fc in range(32)], pe[pid][:],
                       reads=["hsb", "w2sb"], writes=[f"l1f_pe{pid}"], stop=False)
                ctx.mm([(sb["ONESF"][0:1, :], sb["b2e"][0:1, mh * 512:mh * 512 + 512])],
                       pe[pid][:], reads=["ONESF", "b2e"], writes=[f"l1f_pe{pid}"],
                       start=False)
                op("vector", lambda e, mh=mh, pid=pid, r=r: e.tensor_copy(
                    eo[r][:, mh * 512:(mh + 1) * 512], pe[pid][:]),
                    reads=[f"l1f_pe{pid}"], writes=[f"l1f_eo{r}"])
            op("vector", lambda e, st=st, r=r: e.tensor_scalar(
                oc[r][:], eo[r][:], sb["GSL"][:, st:st + 1], None, ALU.mult),
                reads=[f"l1f_eo{r}", "GSL"], writes=[f"l1f_oc{r}"])
            tg = f"rsw{st}"
            etags.append(tg)
            op("gpsimd", lambda e, st=st, r=r: e.indirect_dma_start(
                out=dr["rsin"].ap(),
                out_offset=IndirectOffsetOnAxis(ap=sb["TOKP"][:, st:st + 1], axis=0),
                in_=oc[r][:], in_offset=None),
                reads=[f"l1f_oc{r}", "TOKP", "rzero"], writes=[tg], dma=True)
    ctx.barrier()
    p.free()
    return etags


def _l1_combine_final(ctx, nc, sb, w, out, dr, etags, taps):
    op = ctx.op
    p = Pool(nc)
    aos = p.sb("l1z_ao", [128, 8192], BF16)
    yf = p.sb("l1z_yf", [128, 1024], BF16)
    ytall = p.sb("l1z_ytall", [128, NTILE * D], BF16)
    x1t = [p.sb(f"l1z_x1{i}", [128, D], F32) for i in range(2)]
    xat = [p.sb(f"l1z_xa{i}", [128, D], F32) for i in range(2)]
    fa = [p.sb(f"l1z_fa{i}", [128, D], F32) for i in range(2)]
    rst = [p.sb(f"l1z_rs{i}", [128, D], BF16) for i in range(2)]
    ot = [p.sb(f"l1z_o{i}", [128, D], F32) for i in range(2)]
    pT = p.ps("l1z_pT", [128, 128], BF16)
    pF = p.ps("l1z_pF", [128, 512])
    src = bass.AP(dr["aotb"].tensor, 0, [[1024, 128], [1024 * 128, 8], [1, 1024]])
    op("sync", lambda e, src=src: e.dma_start(
        out=aos[:].rearrange("p (c f) -> p c f", f=1024), in_=src),
        reads=["aotb_d"], writes=["aos"], dma=True, dkey="aos")

    op("sync", lambda e: e.dma_start(out=ytall[:], in_=dr["ytm1d"]),
       reads=["ytm1d"], writes=["z_yta"], dma=True, dkey="z_yta")

    def proj(jj):
        q = jj % 2
        r0 = jj * 128
        for cc in range(8):
            op("tensor", lambda e, jj=jj, cc=cc: e.transpose(
                pT[:], ytall[:, jj * 1024 + cc * 128:jj * 1024 + (cc + 1) * 128],
                sb["IDENT"][:]),
               reads=["z_yta", "IDENT"], writes=["l1z_pT"])
            op("vector", lambda e, cc=cc: e.tensor_copy(
                yf[:, cc * 128:(cc + 1) * 128], pT[:]),
               reads=["l1z_pT"], writes=[f"zyf{cc}"])
        op("sync", lambda e, r0=r0, q=q: e.dma_start(
            out=x1t[q][:], in_=dr["x1"][r0:r0 + 128, :]),
            reads=[f"x1_{jj}"], writes=[f"z_x1{q}"], dma=True, dkey=f"z_x1{q}")
        for ih in range(2):
            ctx.mm([(yf[:, cc * 128:(cc + 1) * 128],
                     aos[:, cc * 1024 + ih * 512:cc * 1024 + ih * 512 + 512])
                    for cc in range(8)], pF[:],
                   reads=[f"zyf{cc}" for cc in range(8)] + ["aos"], writes=["z_pF"])
            op("vector", lambda e, ih=ih, q=q: e.tensor_add(
                xat[q][:, ih * 512:(ih + 1) * 512], pF[:],
                x1t[q][:, ih * 512:(ih + 1) * 512]),
                reads=["z_pF", f"z_x1{q}"], writes=[f"z_xa{q}"])
        op("sync", lambda e, r0=r0, q=q: e.dma_start(
            out=dr["xa"][r0:r0 + 128, :], in_=xat[q][:]),
            reads=[f"z_xa{q}"], writes=[f"xa_{jj}"], dma=True)

    rg = [list(range(NCORES))]

    def rs_q(q):
        r0 = q * 8192
        op("gpsimd", lambda e, q=q, r0=r0: e.collective_compute(
            "ReduceScatter", ALU.add, replica_groups=rg,
            ins=[dr["rsin"].ap()[r0:r0 + 8192, :].opt()],
            outs=[dr["rsout"].ap()[q * 1024:(q + 1) * 1024, :].opt()]),
            reads=etags + ["rzero"], writes=[f"rsout{q}"])

    def fadd(q, k):
        jj = q * 8 + k
        z = jj % 2
        r0 = jj * 128
        op("scalar", lambda e, r0=r0, z=z: e.dma_start(
            out=fa[z][:], in_=dr["xa"][r0:r0 + 128, :]),
            reads=[f"xa_{jj}"], writes=[f"z_fa{z}"], dma=True, dkey=f"z_fa{z}")
        op("sync", lambda e, r0=r0, z=z, q=q: e.dma_start(
            out=rst[z][:], in_=dr["rsout"].ap()[r0:r0 + 128, :]),
            reads=[f"rsout{q}"], writes=[f"z_rs{z}"], dma=True, dkey=f"z_rs{z}")
        op("vector", lambda e, z=z: e.tensor_add(ot[z][:], fa[z][:], rst[z][:]),
           reads=[f"z_fa{z}", f"z_rs{z}"], writes=[f"z_ot{z}"])
        eng = "scalar" if jj % 2 == 0 else "sync"
        op(eng, lambda e, r0=r0, z=z: e.dma_start(
            out=out[r0:r0 + 128, :], in_=ot[z][:]),
            reads=[f"z_ot{z}"], writes=["outd"], dma=True)

    for pj in range(NTILE):
        proj(pj)
    for q in range(4):
        rs_q(q)
        for k in range(8):
            fadd(q, k)
    ctx.barrier()
    p.free()



def _l1_attn_pre(ctx, nc, sb, dr, capd):
    op = ctx.op
    early = Pool(nc)
    eb = {"cs": early.sb("cos_sb", [128, T], BF16),
          "sn": early.sb("sin_sb", [128, T], BF16),
          "wqs": early.sb("l1t_wq", [128, 2048], BF16),
          "wks": early.sb("l1t_wk", [128, 2048], BF16),
          "wvs": early.sb("l1t_wv", [128, 2048], BF16)}
    op("sync", lambda e: e.dma_start(out=eb["cs"][:], in_=capd["COS"].ap()),
       writes=["COS"], dma=True)
    op("sync", lambda e: e.dma_start(out=eb["sn"][:], in_=capd["SIN"].ap()),
       writes=["SIN"], dma=True)
    for name, wt in (("wqs", "aqt"), ("wks", "akt"), ("wvs", "avt")):
        src = bass.AP(dr[wt + "b"].tensor, 0,
                      [[1024, 128], [1024 * 128, 8], [1, 256]])
        op("sync", lambda e, src=src, name=name: e.dma_start(
            out=eb[name][:].rearrange("p (c f) -> p c f", f=256), in_=src),
            reads=[f"{wt}b_d"], writes=[name], dma=True, dkey=name)
    return early, eb


def _l1_attn(ctx, nc, sb, w, dr, capd, taps, early, eb):
    op = ctx.op
    ypool = Pool(nc)
    ytm1 = ypool.sb("ytm1", [128, NTILE * D], BF16)
    cs = eb["cs"]
    sn = eb["sn"]

    p = Pool(nc)
    qq = p.sb("l1t_q", [128, 2 * T], BF16)
    kq = p.sb("l1t_k", [128, 2 * T], BF16)
    vex = p.sb("l1t_v", [128, NTILE * 260], BF16)
    xsq = [p.sb(f"l1t_xs{i}", [128, T], BF16) for i in range(2)]
    wqs, wks, wvs = eb["wqs"], eb["wks"], eb["wvs"]
    qraw = p.sb("l1t_qraw", [128, 512], BF16)
    t1 = p.sb("l1t_t1", [128, 512], BF16)
    t2 = p.sb("l1t_t2", [128, 512], BF16)
    Psb2 = [p.sb(f"l1t_P{i}", [128, 512], BF16) for i in range(2)]
    dcol = p.sb("l1t_dcol", [128, 1], F32)
    pP = p.ps("l1t_pP", [128, 512])
    pR = p.ps("l1t_pR", [128, 512])
    pA2 = [p.ps(f"l1t_pA{i}", [128, 512]) for i in range(2)]
    pO = [p.ps(f"l1t_pO{i}", [128, 72]) for i in range(4)]

    agxtags = [f"agxin{j}" for j in range(NTILE)]
    for qt in range(4):
        if qt > 0:
            for name, wt, buf in (("wqs", "aqt", wqs), ("wks", "akt", wks),
                                  ("wvs", "avt", wvs)):
                src = bass.AP(dr[wt + "b"].tensor, qt * 256,
                              [[1024, 128], [1024 * 128, 8], [1, 256]])
                op("sync", lambda e, src=src, buf=buf: e.dma_start(
                    out=buf[:].rearrange("p (c f) -> p c f", f=256), in_=src),
                    reads=[f"{wt}b_d"], writes=[name], dma=True, dkey=name)
        vones = bass.AP(vex, 64, [[NTILE * 260, 128], [260, NTILE], [65, 4], [1, 1]])
        op("vector", lambda e, vones=vones: e.memset(vones, 1.0),
           reads=["vex"], writes=["vex_ones"])

        def ldx(tb):
            xb = tb % 2
            for cc in range(8):
                op("sync", lambda e, cc=cc, tb=tb, xb=xb: e.dma_start_transpose(
                    xsq[xb][:, cc * 512:(cc + 1) * 512],
                    dr["agx_in"][tb * 512:(tb + 1) * 512, cc * 128:(cc + 1) * 128]),
                    reads=agxtags[tb * 4:tb * 4 + 4], writes=[f"xq{xb}_{cc}"],
                    dma=True, dkey=f"xq{xb}_{cc}")

        if qt == 0:
            ldx(0)
        for tb in range(8):
            xb = tb % 2
            if tb + 1 < 8:
                ldx(tb + 1)
            elif qt + 1 < 4:
                ldx(0)
            xr = [f"xq{xb}_{cc}" for cc in range(8)]
            for name, wbuf, dst, dtag in (("wqs", wqs, qq, "qqb"),
                                          ("wks", wks, kq, "kqb")):
                for it in range(2):
                    ctx.mm([(wbuf[:, cc * 256 + it * 128:cc * 256 + it * 128 + 128],
                             xsq[xb][:, cc * 512:(cc + 1) * 512]) for cc in range(8)],
                           pP[:], reads=[name] + xr, writes=["l1t_pP"])
                    op("scalar", lambda e: e.copy(qraw[:], pP[:]),
                       reads=["l1t_pP"], writes=["qraw"])
                    ctx.mm([(sb["ROTT"][:], qraw[:])], pR[:],
                           reads=["ROTT", "qraw"], writes=["l1t_pR"])
                    c0 = tb * 512
                    op("vector", lambda e, c0=c0: e.tensor_mul(
                        t1[:], qraw[:], cs[:, c0:c0 + 512]),
                        reads=["qraw", "COS"], writes=["t1"])
                    op("vector", lambda e, c0=c0: e.tensor_mul(
                        t2[:], pR[:], sn[:, c0:c0 + 512]),
                        reads=["l1t_pR", "SIN"], writes=["t2"])
                    op("vector", lambda e, dst=dst, it=it, c0=c0: e.tensor_add(
                        dst[:, it * T + c0:it * T + c0 + 512], t1[:], t2[:]),
                        reads=["t1", "t2"], writes=[dtag])
            for ts_ in range(4):
                ctx.mm([(xsq[xb][:, cc * 512 + ts_ * 128:cc * 512 + ts_ * 128 + 128],
                         wvs[:, cc * 256:(cc + 1) * 256]) for cc in range(8)],
                       pP[:, 0:256], reads=["wvs"] + xr, writes=["l1t_pP"])
                ti = tb * 4 + ts_
                for hh in range(4):
                    op("vector", lambda e, ti=ti, hh=hh: e.tensor_copy(
                        vex[:, ti * 260 + hh * 65:ti * 260 + hh * 65 + 64],
                        pP[:, hh * 64:hh * 64 + 64]),
                        reads=["l1t_pP"], writes=["vex"])
        for hl in range(4):
            h = qt * 4 + hl
            rb = (hl % 2) * 64
            itq = hl // 2
            for qb in range(8):
                q0 = qb * 512
                nkt = (qb + 1) * 4
                def emit_qk(kt):
                    pr = kt % 2
                    ctx.mm([(kq[rb:rb + 64, itq * T + kt * 128:itq * T + kt * 128 + 128],
                             qq[rb:rb + 64, itq * T + q0:itq * T + q0 + 512])],
                           pA2[pr][:], reads=["kqb", "qqb"], writes=[f"l1t_pA{pr}"])

                emit_qk(0)
                for kt in range(nkt):
                    pr = kt % 2
                    pA = pA2[pr]
                    Psb = Psb2[pr]
                    op("scalar", lambda e, pA=pA, Psb=Psb: e.activation(
                        Psb[:], pA[:], AF.Exp, bias=sb["cM12"][:], scale=0.125),
                       reads=[f"l1t_pA{pr}", "cM12"], writes=[f"Psb{pr}"])
                    jloc = kt * 128 - q0
                    if jloc >= 0:
                        if jloc > 0:
                            op("vector", lambda e, jloc=jloc, Psb=Psb: e.memset(
                                Psb[:, 0:jloc], 0.0), writes=[f"Psb{pr}"])
                        op("vector", lambda e, jloc=jloc, Psb=Psb: e.tensor_mul(
                            Psb[:, jloc:jloc + 128], Psb[:, jloc:jloc + 128],
                            sb["UTI"][:]),
                            reads=["UTI"], writes=[f"Psb{pr}"])
                    if kt + 1 < nkt:
                        emit_qk(kt + 1)
                    for s in range(4):
                        ctx.mm([(Psb[:, s * 128:(s + 1) * 128],
                                 vex[:, kt * 260 + hl * 65:kt * 260 + hl * 65 + 65])],
                               pO[s][:, 0:65], reads=[f"Psb{pr}", "vex", "vex_ones"],
                               writes=[f"l1t_pO{s}"],
                               start=(kt == 0), stop=(kt == nkt - 1))
                for s in range(4):
                    c = qb * 4 + s
                    op("vector", lambda e, s=s: e.reciprocal(
                        dcol[:], pO[s][:, 64:65]),
                        reads=[f"l1t_pO{s}"], writes=["l1t_dc"])
                    op("vector", lambda e, s=s, c=c, h=h: e.tensor_scalar(
                        ytm1[:, c * D + h * 64:c * D + h * 64 + 64],
                        pO[s][:, 0:64], dcol[:], None, ALU.mult),
                        reads=[f"l1t_pO{s}", "l1t_dc"], writes=["ytm1"])
    op("sync", lambda e: e.dma_start(out=dr["ytm1d"], in_=ytm1[:]),
       reads=["ytm1"], writes=["ytm1d"], dma=True)
    return [p, ypool]



def kernel(**inputs):
    ins = {k: np.ascontiguousarray(np.asarray(v)) for k, v in inputs.items()}
    x = ins["L0_attn_x"].astype(np.float32, copy=False)

    def tr(a):
        return np.ascontiguousarray(a.T)

    common = {
        "wqt": tr(ins["L0_attn_Wq"]), "wkt": tr(ins["L0_attn_Wk"]),
        "wvt": tr(ins["L0_attn_Wv"]), "wpt": tr(ins["L0_attn_Wp"]),
        "bp": ins["L0_attn_bp"].reshape(1, D),
        "rw": ins["L0_ffn_router_w"], "rb": ins["L0_ffn_router_b"].reshape(1, 8),
        "w1s": np.concatenate([ins["L0_ffn_routed_w1"], ins["L0_ffn_shared_w1"]], 0),
        "w3s": np.concatenate([ins["L0_ffn_routed_w3"], ins["L0_ffn_shared_w3"]], 0),
        "w2s": np.concatenate([ins["L0_ffn_routed_w2"], ins["L0_ffn_shared_w2"]], 0),
        "aqt": tr(ins["L1_attn_Wq"]), "akt": tr(ins["L1_attn_Wk"]),
        "avt": tr(ins["L1_attn_Wv"]), "aot": tr(ins["L1_attn_Wo"]),
        "wg": ins["L1_ffn_wg"],
    }
    common = {k: np.ascontiguousarray(v) for k, v in common.items()}
    nc = build_nc()
    in_maps = []
    for c in range(NCORES):
        m = dict(common)
        m["x0"] = x[c]
        m["w1e"] = ins["L1_ffn_w1"][c]
        m["b1e"] = np.ascontiguousarray(ins["L1_ffn_b1"][c].reshape(32, 128).T)
        m["w2e"] = ins["L1_ffn_w2"][c]
        m["b2e"] = ins["L1_ffn_b2"][c].reshape(1, D)
        es = np.zeros((1, 8), np.float32)
        es[0, c] = 1.0
        m["esel"] = es
        m["esel16"] = np.ascontiguousarray(np.tile(es, (1, 16)))
        in_maps.append(m)
    res = run_bass_kernel_spmd(nc, in_maps, core_ids=list(range(NCORES)))
    return np.stack([r["out"] for r in res.results], axis=0)


# version 27 (final)
# speedup vs baseline: 1.1174x; 1.0324x over previous
import numpy as np
import ml_dtypes
import concourse.bass as bass
import concourse.mybir as mybir
from concourse.bass_core_rust import IndirectOffsetOnAxis
from concourse.bass_utils import run_bass_kernel_spmd

F32 = mybir.dt.float32
BF16 = mybir.dt.bfloat16
I32 = mybir.dt.int32
AF = mybir.ActivationFunctionType
ALU = mybir.AluOpType
AX = mybir.AxisListType.X

NCORES = 8
T = 4096
D = 1024
CAP = 5120
NSLOT = 5632
TRASH = NSLOT - 1
NTILE = T // 128
GTILE = (NCORES * T) // 128

_L0D_SEL = list(range(10))
_BATCH_HALF = 99

ENGC = ("tensor", "vector", "scalar", "gpsimd", "sync")
DMAQ = ("sync", "gpsimd", "scalar")


class Ctx:

    def __init__(self, nc):
        self.nc = nc
        self.csem = {n: nc.alloc_semaphore(f"C_{n}") for n in ENGC}
        self.dsem = {n: nc.alloc_semaphore(f"D_{n}") for n in DMAQ}
        self.ccnt = {n: 0 for n in ENGC}
        self.dcnt = {n: 0 for n in DMAQ}
        self.wr = {}
        self.rd = {}
        self.mark = {n: {} for n in ENGC}
        self.sched = {n: [] for n in ENGC}
        self.dslot = {}

    def _sem(self, key):
        if key[0] == "c":
            return self.csem[key[1]]
        if key[0] == "s":
            return self.dslot[key[1]][0]
        return self.dsem[key[1]]

    def _need(self, waits, eng, key, cnt):
        if key == ("c", eng) and eng == "tensor":
            return
        m = self.mark[eng]
        if m.get(key, -1) >= cnt:
            return
        m[key] = cnt
        waits.append((key, cnt))

    def _bump(self, key, cnt):
        if key[0] == "d":
            return self.dcnt[key[1]]
        return cnt

    def _deps(self, eng, reads, writes):
        waits = []
        for t in reads:
            wv = self.wr.get(t)
            if wv:
                self._need(waits, eng, wv[0], self._bump(wv[0], wv[1]))
        for t in writes:
            wv = self.wr.get(t)
            if wv:
                self._need(waits, eng, wv[0], self._bump(wv[0], wv[1]))
            for (k, c) in self.rd.get(t, ()):
                self._need(waits, eng, k, self._bump(k, c))
        return waits

    def op(self, eng, fn, reads=(), writes=(), dma=False, dkey=None):
        waits = self._deps(eng, reads, writes)
        if dma:
            if dkey is not None:
                if dkey not in self.dslot:
                    self.dslot[dkey] = [self.nc.alloc_semaphore(f"S_{dkey}"), 0]
                ent = self.dslot[dkey]
                ent[1] += 16
                cnt = ent[1]
                key = ("s", dkey)
                iv = 16
            else:
                self.dcnt[eng] += 16
                cnt = self.dcnt[eng]
                key = ("d", eng)
                iv = 16
        else:
            self.ccnt[eng] += 1
            cnt = self.ccnt[eng]
            key = ("c", eng)
            iv = 1
        self.sched[eng].append((waits, fn, self._sem(key), iv))
        for t in writes:
            self.wr[t] = (key, cnt)
            self.rd[t] = []
        for t in reads:
            self.rd.setdefault(t, []).append((key, cnt))
        return cnt

    def mm(self, pairs, out_ap, reads=(), writes=(), start=True, stop=True):
        for lhsT, rhs in pairs:
            assert lhsT.base_partition() == rhs.base_partition(), \
                (lhsT.tensor.name, lhsT.base_partition(), rhs.tensor.name,
                 rhs.base_partition())
        def fn(e, pairs=tuple(pairs), out_ap=out_ap, start=start, stop=stop):
            n = len(pairs)
            ins = None
            for i, (lhsT, rhs) in enumerate(pairs):
                ins = e.matmul(out_ap, lhsT, rhs,
                               start=(start and i == 0),
                               stop=(stop and i == n - 1))
            return ins
        return self.op("tensor", fn, reads, writes)

    def barrier(self):
        targets = [(("c", n), self.ccnt[n]) for n in ENGC if self.ccnt[n] > 0]
        targets += [(("d", n), self.dcnt[n]) for n in DMAQ if self.dcnt[n] > 0]
        targets += [(("s", k), v[1]) for k, v in self.dslot.items() if v[1] > 0]
        for eng in ENGC:
            waits = []
            for key, cnt in targets:
                if key == ("c", eng):
                    continue
                self._need(waits, eng, key, cnt)
            if waits:
                self.sched[eng].append((waits, None, None, 0))

    def end_wait(self, eng, key_eng):
        key = ("d", key_eng)
        self.sched[eng].append(([(key, self.dcnt[key_eng])], None, None, 0))
        self.mark[eng][key] = self.dcnt[key_eng]

    def replay(self, block):
        def make(name):
            def run(eng):
                for waits, fn, sem, iv in self.sched[name]:
                    for key, cnt in waits:
                        eng.wait_ge(self._sem(key), cnt)
                    if fn is not None:
                        fn(eng).then_inc(sem, iv)
            return run
        block.tensor(make("tensor"))
        block.vector(make("vector"))
        block.scalar(make("scalar"))
        block.gpsimd(make("gpsimd"))
        block.sync(make("sync"))


class Pool:

    def __init__(self, nc):
        self.nc = nc
        self.guards = []

    def sb(self, name, shape, dt):
        g = self.nc.sbuf_tensor(name, shape, dt)
        h = g.__enter__()
        self.guards.append(g)
        return h

    def ps(self, name, shape, dt=F32):
        g = self.nc.psum_tensor(name, shape, dt)
        h = g.__enter__()
        self.guards.append(g)
        return h

    def free(self):
        for g in reversed(self.guards):
            g.__exit__(None, None, None)
        self.guards = []



def _consts():
    bf = ml_dtypes.bfloat16
    c = {}
    c["IDENT"] = np.eye(128, dtype=bf)
    c["IDENTF"] = np.eye(128, dtype=np.float32)
    k = np.arange(128)[:, None]
    i = np.arange(128)[None, :]
    c["UTI"] = (k <= i).astype(bf)
    c["UTS"] = (k < i).astype(bf)
    c["ONESB"] = np.ones((128, 128), dtype=bf)
    c["ONESF"] = np.ones((128, 128), dtype=np.float32)
    c["IOTAB"] = np.ascontiguousarray(
        np.broadcast_to(np.arange(128, dtype=np.float32)[None, :], (128, 128)))
    c["IOTAC"] = np.ascontiguousarray(np.arange(128, dtype=np.float32)[:, None])
    P = np.zeros((128, 128), dtype=np.float32)
    for r in range(128):
        if r % 64 < 32:
            P[r, r + 32] = -1.0
        else:
            P[r, r - 32] = 1.0
    c["ROTT"] = np.ascontiguousarray(P.T.astype(bf))
    inv = 1.0 / (10000.0 ** (np.arange(0, 64, 2, dtype=np.float64) / 64))
    t = np.arange(T, dtype=np.float64)
    fr = np.outer(inv, t)
    c["COS"] = np.ascontiguousarray(np.tile(np.cos(fr), (4, 1)).astype(bf))
    c["SIN"] = np.ascontiguousarray(np.tile(np.sin(fr), (4, 1)).astype(bf))
    cc = np.arange(256)
    pb = (cc % 32 // 8) * 8192 + (cc // 32) * 1024 + (cc % 8) * 128
    c["PBC"] = np.ascontiguousarray(pb.astype(np.float32)[:, None])
    return c


def build_nc(debug_taps=()):
    nc = bass.Bass("TRN2", target_bir_lowering=False)

    xin = nc.dram_tensor("x0", [T, D], F32, kind="ExternalInput").ap()
    w = {}
    for n in ("wqt", "wkt", "wvt", "wpt", "aqt", "akt", "avt", "aot"):
        w[n] = nc.dram_tensor(n, [D, D], F32, kind="ExternalInput").ap()
    w["bp"] = nc.dram_tensor("bp", [1, D], F32, kind="ExternalInput").ap()
    w["rw"] = nc.dram_tensor("rw", [D, 8], F32, kind="ExternalInput").ap()
    w["rb"] = nc.dram_tensor("rb", [1, 8], F32, kind="ExternalInput").ap()
    w["w1s"] = nc.dram_tensor("w1s", [10, D, 512], F32, kind="ExternalInput").ap()
    w["w3s"] = nc.dram_tensor("w3s", [10, D, 512], F32, kind="ExternalInput").ap()
    w["w2s"] = nc.dram_tensor("w2s", [10, 512, D], F32, kind="ExternalInput").ap()
    w["wg"] = nc.dram_tensor("wg", [D, 8], F32, kind="ExternalInput").ap()
    w["w1e"] = nc.dram_tensor("w1e", [D, 4096], F32, kind="ExternalInput").ap()
    w["b1e"] = nc.dram_tensor("b1e", [128, 32], F32, kind="ExternalInput").ap()
    w["w2e"] = nc.dram_tensor("w2e", [4096, D], F32, kind="ExternalInput").ap()
    w["b2e"] = nc.dram_tensor("b2e", [1, D], F32, kind="ExternalInput").ap()
    w["esel"] = nc.dram_tensor("esel", [1, 8], F32, kind="ExternalInput").ap()
    w["esel16"] = nc.dram_tensor("esel16", [1, 128], F32, kind="ExternalInput").ap()
    out = nc.dram_tensor("out", [T, D], F32, kind="ExternalOutput").ap()

    dr = {}
    dr["xn0"] = nc.dram_tensor("xn0_d", [T, D], BF16).ap()
    dr["x1"] = nc.dram_tensor("x1_d", [T, D], F32).ap()
    dr["agx_in"] = nc.dram_tensor("agx_in", [T, D], BF16)
    dr["agx_out"] = nc.dram_tensor("agx_out", [NCORES * T + 128, D], BF16)
    dr["aglg_in"] = nc.dram_tensor("aglg_in", [T, 8], F32)
    dr["aglg_out"] = nc.dram_tensor("aglg_out", [NCORES * T, 8], F32)
    dr["disp"] = nc.dram_tensor("disp_d", [NSLOT, D], BF16).ap()
    dr["eout"] = nc.dram_tensor("eout_d", [NSLOT, D], BF16).ap()
    dr["rsin"] = nc.dram_tensor("rsin_d", [NCORES * T + 128, D], BF16)
    dr["rsout"] = nc.dram_tensor("rsout_d", [T, D], BF16)
    dr["tbl"] = nc.dram_tensor("tbl_d", [256, 260], F32).ap()
    dr["bounce"] = nc.dram_tensor("bounce", [2, 256], F32).ap()
    dr["gfm"] = nc.dram_tensor("gfm_d", [8, T], F32).ap()
    dr["ytm1d"] = nc.dram_tensor("ytm1_d", [128, NTILE * D], BF16).ap()
    dr["xa"] = nc.dram_tensor("xa_d", [T, D], F32).ap()
    for n in ("aqt", "akt", "avt", "aot"):
        dr[n + "b"] = nc.dram_tensor(n + "_b16", [D, D], BF16).ap()

    consts = _consts()
    capd = {k: nc.inline_tensor(v, k) for k, v in consts.items()}

    taps = {}
    tapspec = {
        "xn0": ([T, D], BF16), "x1p": ([T, D], F32), "x1": ([T, D], F32),
        "gatesfm": ([8, T], F32), "logits": ([T, 8], F32),
        "mmat": ([128, GTILE], F32), "posmat": ([128, GTILE], F32),
        "slotmat": ([128, GTILE], F32), "gsmat": ([128, GTILE], F32),
        "toki": ([128, NSLOT // 128], I32), "tokp": ([128, NSLOT // 128], I32),
        "gsl": ([128, NSLOT // 128], F32), "tblm": ([256, 260], F32),
        "dbgm": ([128, 256], F32), "dbgc": ([128, 8], F32),
        "offsrow": ([1, GTILE], F32), "roffif": ([1, GTILE], F32),
        "regdump": ([1, GTILE], I32),
        "disp": ([NSLOT, D], BF16),
        "eoutd": ([NSLOT, D], BF16), "rsin": ([NCORES * T, D], BF16),
        "rsout": ([T, D], BF16), "ytm0": ([T, D], BF16),
        "ytm1": ([T, D], BF16), "agx": ([NCORES * T, D], BF16),
    }
    for name in debug_taps:
        shp, dt = tapspec[name]
        taps[name] = nc.dram_tensor("tap_" + name, shp, dt, kind="ExternalOutput").ap()

    ctx = Ctx(nc)
    with nc.Block() as block:
        _emit(ctx, nc, xin, w, out, dr, capd, taps)
        ctx.replay(block)
    return nc



def _emit(ctx, nc, xin, w, out, dr, capd, taps):
    op = ctx.op
    pers = Pool(nc)
    sb = {}
    for n, shp, dt in [("IDENT", [128, 128], BF16), ("IDENTF", [128, 128], F32),
                       ("UTI", [128, 128], BF16),
                       ("UTS", [128, 128], BF16), ("ONESB", [128, 128], BF16),
                       ("ONESF", [128, 128], F32), ("IOTAB", [128, 128], F32),
                       ("IOTAC", [128, 1], F32), ("ROTT", [128, 128], BF16)]:
        sb[n] = pers.sb(n.lower(), shp, dt)
        op("sync", lambda e, n=n: e.dma_start(out=sb[n][:], in_=capd[n].ap()),
           writes=[n], dma=True)
    for n, src, shp in [("bp", "bp", [1, D]), ("rb", "rb", [1, 8]),
                        ("b2e", "b2e", [1, D]), ("b1e", "b1e", [128, 32]),
                        ("esel", "esel", [1, 8])]:
        sb[n] = pers.sb(n + "_sb", shp, F32)
        op("sync", lambda e, n=n, src=src: e.dma_start(out=sb[n][:], in_=w[src]),
           writes=[n], dma=True)
    for n, src in [("rwsb", "rw"), ("wgsb", "wg")]:
        sb[n] = pers.sb(n, [128, 64], BF16)
        src_ap = bass.AP(w[src].tensor, 0, [[8, 128], [1024, 8], [1, 8]])
        op("gpsimd", lambda e, n=n, src_ap=src_ap: e.dma_start(
            out=sb[n][:].rearrange("p (c e) -> p c e", e=8), in_=src_ap),
            writes=[n], dma=True)
    for n in ("M", "G1", "POS", "SLOT", "GS", "CUMS", "KTMP"):
        sb[n] = pers.sb(n, [128, GTILE], F32)
    NST = NSLOT // 128
    sb["TOKI"] = pers.sb("TOKI", [128, NST], I32)
    sb["TOKP"] = pers.sb("TOKP", [128, NST], I32)
    sb["GSL"] = pers.sb("GSL", [128, NST], F32)
    sb["Mb"] = pers.sb("Mb", [128, GTILE], BF16)
    sb["tot"] = pers.sb("tot", [1, GTILE], F32)
    sb["offs"] = pers.sb("offs", [1, GTILE], F32)
    sb["rmin"] = pers.sb("rmin", [1, GTILE], F32)
    sb["roffi"] = pers.sb("roffi", [1, GTILE], I32)
    sb["tot2"] = pers.sb("tot2", [128, 2], BF16)
    sb["offs2"] = pers.sb("offs2", [128, 2], F32)
    sb["scs"] = pers.sb("scs", [1, 2], F32)
    sb["sc1"] = pers.sb("sc1", [128, 8], F32)
    sb["sc2"] = pers.sb("sc2", [128, 8], F32)
    sb["sc3"] = pers.sb("sc3", [128, 8], F32)
    sb["fc"] = pers.sb("fc", [128, 8], F32)
    sb["eselb"] = pers.sb("eselb", [128, 8], F32)
    sb["cEPS"] = pers.sb("cEPS", [128, 1], F32)
    sb["cM12"] = pers.sb("cM12", [128, 1], F32)
    op("vector", lambda e: e.memset(sb["cEPS"][:], 1e-5), writes=["cEPS"])
    op("vector", lambda e: e.memset(sb["cM12"][:], -12.0), writes=["cM12"])
    psepool = Pool(nc)
    pse = psepool.ps("pse", [128, 8])
    ctx.mm([(sb["ONESF"][0:1, :], sb["esel"][:])], pse[:],
           reads=["ONESF", "esel"], writes=["pse"])
    op("vector", lambda e: e.tensor_copy(sb["eselb"][:], pse[:]),
       reads=["pse"], writes=["eselb"])
    ctx.barrier()
    psepool.free()

    cvt = [pers.sb(f"cvt{i}", [128, D], BF16) for i in range(2)]
    for n in ("aqt", "akt", "avt", "aot"):
        for c in range(8):
            q = c % 2
            src = bass.AP(w[n].tensor, c * 128 * D, [[D, 128], [1, D]])
            dst = bass.AP(dr[n + "b"].tensor, c * 128 * D, [[D, 128], [1, D]])
            op("gpsimd", lambda e, src=src, q=q: e.dma_start(
                out=cvt[q][:], in_=src),
                writes=[f"cv{q}"], dma=True, dkey=f"cv{q}")
            op("sync", lambda e, dst=dst, q=q: e.dma_start(
                out=dst, in_=cvt[q][:]),
                reads=[f"cv{q}"], writes=[f"{n}b_d"], dma=True)

    _l0_ln(ctx, nc, sb, xin, dr, taps)
    ctx.barrier()
    _l0_attn(ctx, nc, sb, xin, w, dr, taps)
    ctx.barrier()
    _l0_moe(ctx, nc, sb, w, dr, taps)
    ctx.barrier()

    _l1_pre(ctx, nc, sb, w, dr, taps)
    early, eb = _l1_attn_pre(ctx, nc, sb, dr, capd)
    _l1_routing(ctx, nc, sb, w, dr, capd, taps)
    ctx.barrier()
    rg = [list(range(NCORES))]
    op("gpsimd", lambda e: e.collective_compute(
        "AllGather", ALU.bypass, replica_groups=rg,
        ins=[dr["agx_in"].ap().opt()],
        outs=[dr["agx_out"].ap()[0:NCORES * T, :].opt()]),
        reads=[f"agxin{j}" for j in range(NTILE)], writes=["agxout"])
    if "agx" in taps:
        op("sync", lambda e: e.dma_start(
            out=taps["agx"], in_=dr["agx_out"].ap()[0:NCORES * T, :]),
            reads=["agxout"], dma=True)
    pd = Pool(nc)
    dtags = _l1_dispatch(ctx, nc, sb, dr, taps, pd)
    apools = _l1_attn(ctx, nc, sb, w, dr, capd, taps, early, eb)
    ctx.barrier()
    for pp in apools:
        pp.free()
    pd.free()
    early.free()
    etags = _l1_ffn(ctx, nc, sb, w, dr, dtags, taps)
    _l1_combine_final(ctx, nc, sb, w, out, dr, etags, taps)

    ctx.end_wait("sync", "sync")
    ctx.end_wait("sync", "gpsimd")
    ctx.end_wait("sync", "scalar")


def _ln_tile(ctx, sb, xt, xnb, sq):
    op = ctx.op
    F = sb["fc"]
    op("vector", lambda e: e.reduce_sum(F[:, 0:1], xt[:], AX),
       reads=["xt"], writes=["f0"])
    op("vector", lambda e: e.tensor_scalar(F[:, 1:2], F[:, 0:1], 1.0 / D, None, ALU.mult),
       reads=["f0"], writes=["f1"])
    op("scalar", lambda e: e.activation(sq[:], xt[:], AF.Square, accum_out=F[:, 2:3]),
       reads=["xt"], writes=["sq", "f2"])
    op("vector", lambda e: e.tensor_mul(F[:, 3:4], F[:, 1:2], F[:, 1:2]),
       reads=["f1"], writes=["f3"])
    op("vector", lambda e: e.scalar_tensor_tensor(F[:, 3:4], F[:, 2:3], 1.0 / D,
                                                  F[:, 3:4], ALU.mult, ALU.subtract),
       reads=["f2", "f3"], writes=["f3"])
    op("scalar", lambda e: e.activation(F[:, 4:5], F[:, 3:4], AF.Sqrt,
                                        bias=sb["cEPS"][:]),
       reads=["f3", "cEPS"], writes=["f4"])
    op("vector", lambda e: e.reciprocal(F[:, 5:6], F[:, 4:5]),
       reads=["f4"], writes=["f5"])
    op("vector", lambda e: e.tensor_scalar(xnb[:], xt[:], F[:, 1:2], F[:, 5:6],
                                           ALU.subtract, ALU.mult),
       reads=["xt", "f1", "f5"], writes=["xnb"])



def _l0_ln(ctx, nc, sb, xin, dr, taps):
    op = ctx.op
    p = Pool(nc)
    NB = 2
    xt = [p.sb(f"l0a_xt{i}", [128, D], F32) for i in range(NB)]
    xnb = [p.sb(f"l0a_xnb{i}", [128, D], BF16) for i in range(NB)]
    sq = [p.sb(f"l0a_sq{i}", [128, D], F32) for i in range(NB)]
    fcb = [p.sb(f"l0a_fc{i}", [128, 8], F32) for i in range(NB)]

    def front(j):
        q = j % NB
        op("sync", lambda e, j=j, q=q: e.dma_start(
            out=xt[q][:], in_=xin[j * 128:(j + 1) * 128, :]),
            writes=[f"b_xt{q}"], dma=True, dkey=f"b_xt{q}")
        _ln_tile_r(ctx, sb, xt[q], xnb[q], sq[q], fcb[q], f"b{q}")

    def back(j):
        q = j % NB
        op("sync", lambda e, j=j, q=q: e.dma_start(
            out=dr["xn0"][j * 128:(j + 1) * 128, :], in_=xnb[q][:]),
            reads=[f"b_xnb{q}"], writes=[f"xn0_{j}"], dma=True)

    front(0)
    for j in range(NTILE):
        if j + 1 < NTILE:
            front(j + 1)
        back(j)
    if "xn0" in taps:
        op("sync", lambda e: e.dma_start(out=taps["xn0"], in_=dr["xn0"]),
           reads=[f"xn0_{j}" for j in range(NTILE)], dma=True)
    if "gatesfm" in taps:
        op("sync", lambda e: e.dma_start(out=taps["gatesfm"], in_=dr["gfm"]),
           reads=["gfm_d"], dma=True)
    ctx.barrier()
    p.free()



def _l0_attn(ctx, nc, sb, xin, w, dr, taps):
    op = ctx.op
    ypool = Pool(nc)
    ytm = ypool.sb("ytm", [128, NTILE * D], BF16)
    xst = ypool.sb("l0b_xst", [128, 8 * T], BF16)

    p = Pool(nc)
    qg = p.sb("l0b_q", [128, T], BF16)
    kg = p.sb("l0b_k", [128, T], BF16)
    ktg = p.sb("l0b_kt", [128, T], BF16)
    vex = p.sb("l0b_vex", [128, NTILE * 130], BF16)
    phs = p.sb("l0b_phs", [128, T], BF16)
    ssb = p.sb("l0b_s", [128, 72], BF16)
    asb2 = [p.sb(f"l0b_a{i}", [128, 128], BF16) for i in range(2)]
    wq = p.sb("l0b_wq", [128, 1024], BF16)
    wk = p.sb("l0b_wk", [128, 1024], BF16)
    wv = p.sb("l0b_wv", [128, 1024], BF16)
    dcol = p.sb("l0b_dcol", [128, 4], F32)
    pbank = [p.ps(f"l0b_pb{i}", [128, 512]) for i in range(4)]
    pS2 = [p.ps(f"l0b_pS{i}", [64, 72]) for i in range(2)]
    pT2 = [p.ps(f"l0b_pT{i}", [128, 128], BF16) for i in range(2)]

    xn0tags = [f"xn0_{j}" for j in range(NTILE)]
    for tb in range(8):
        for cc in range(8):
            op("sync", lambda e, cc=cc, tb=tb: e.dma_start_transpose(
                xst[:, cc * T + tb * 512:cc * T + tb * 512 + 512],
                dr["xn0"][tb * 512:(tb + 1) * 512, cc * 128:(cc + 1) * 128]),
                reads=xn0tags[tb * 4:tb * 4 + 4], writes=[f"xst{tb}"], dma=True)
    for g in range(8):
        for name, wt, buf in (("wq", "wqt", wq), ("wk", "wkt", wk), ("wv", "wvt", wv)):
            src = bass.AP(w[wt].tensor, g * 128, [[1024, 128], [1024 * 128, 8], [1, 128]])
            op("gpsimd", lambda e, src=src, buf=buf: e.dma_start(
                out=buf[:].rearrange("p (c f) -> p c f", f=128), in_=src),
                writes=[name], dma=True, dkey=name)
        vones = bass.AP(vex, 64, [[NTILE * 130, 128], [130, NTILE], [65, 2], [1, 1]])
        op("vector", lambda e, vones=vones: e.memset(vones, 1.0),
           reads=["vex"], writes=["vex_ones"])

        pj = 0
        for tb in range(8):
            xr = [f"xst{tb}"]
            for name, buf, dst, dtag in (("wq", wq, qg, "qg"), ("wk", wk, kg, "kg")):
                pb_ = pbank[pj % 4]
                ptag = f"pbank{pj % 4}"
                pj += 1
                ctx.mm([(buf[:, cc * 128:(cc + 1) * 128],
                         xst[:, cc * T + tb * 512:cc * T + tb * 512 + 512])
                        for cc in range(8)], pb_[:],
                       reads=[name] + xr, writes=[ptag])
                op("scalar", lambda e, dst=dst, tb=tb, pb_=pb_: e.copy(
                    dst[:, tb * 512:(tb + 1) * 512], pb_[:]),
                   reads=[ptag], writes=[dtag])
            for ts_ in range(4):
                pb_ = pbank[pj % 4]
                ptag = f"pbank{pj % 4}"
                pj += 1
                ctx.mm([(xst[:, cc * T + tb * 512 + ts_ * 128:
                             cc * T + tb * 512 + ts_ * 128 + 128],
                         wv[:, cc * 128:(cc + 1) * 128]) for cc in range(8)],
                       pb_[:, 0:128], reads=["wv"] + xr, writes=[ptag])
                ti = tb * 4 + ts_
                for hh in range(2):
                    op("scalar", lambda e, ti=ti, hh=hh, pb_=pb_: e.copy(
                        vex[:, ti * 130 + hh * 65:ti * 130 + hh * 65 + 64],
                        pb_[:, hh * 64:hh * 64 + 64]),
                        reads=[ptag], writes=["vex"])
        for dtag, buf in (("qg", qg), ("kg", kg)):
            op("vector", lambda e, buf=buf: e.tensor_scalar(phs[:], buf[:], 0.0,
                                                            None, ALU.min),
               reads=[dtag], writes=["phs"])
            op("scalar", lambda e: e.activation(phs[:], phs[:], AF.Exp),
               reads=["phs"], writes=["phs"])
            op("vector", lambda e, buf=buf: e.scalar_tensor_tensor(
                buf[:], buf[:], 0.0, phs[:], ALU.max, ALU.add),
               reads=[dtag, "phs"], writes=[dtag])
        for c in range(NTILE):
            tp_ = c % 2
            op("tensor", lambda e, c=c, tp_=tp_: e.transpose(
                pT2[tp_][:], kg[:, c * 128:(c + 1) * 128], sb["IDENT"][:]),
               reads=["kg", "IDENT"], writes=[f"pT{tp_}"])
            op("vector", lambda e, c=c, tp_=tp_: e.tensor_copy(
                ktg[:, c * 128:(c + 1) * 128], pT2[tp_][:]),
               reads=[f"pT{tp_}"], writes=["ktg"])
        for c in range(NTILE):
            for hh in range(2):
                b = hh * 64
                ctx.mm([(kg[b:b + 64, c * 128:(c + 1) * 128],
                         qg[b:b + 64, c * 128:(c + 1) * 128])],
                       pbank[hh][:, 0:128], reads=["kg", "qg"], writes=[f"pbank{hh}"])
            for hh in range(2):
                op("vector", lambda e, hh=hh: e.tensor_mul(
                    asb2[hh][:], pbank[hh][:, 0:128], sb["UTI"][:]),
                   reads=[f"pbank{hh}", "UTI"], writes=[f"asb{hh}"])
            for hh in range(2):
                b = hh * 64
                pairs = [(asb2[hh][:], vex[:, c * 130 + hh * 65:c * 130 + hh * 65 + 65])]
                rd = [f"asb{hh}", "vex", "vex_ones"]
                if c > 0:
                    pairs.append((qg[b:b + 64, c * 128:(c + 1) * 128],
                                  ssb[b:b + 64, 0:65]))
                    rd += ["qg", f"ssb{hh}"]
                ctx.mm(pairs, pbank[2 + hh][:, 0:65], reads=rd,
                       writes=[f"pbank{2 + hh}"])
                ctx.mm([(ktg[:, c * 128 + b:c * 128 + b + 64],
                         vex[:, c * 130 + hh * 65:c * 130 + hh * 65 + 65])],
                       pS2[hh][0:64, 0:65], reads=["ktg", "vex", "vex_ones"],
                       writes=[f"pS2{hh}"], start=(c == 0), stop=(c == NTILE - 1))
            for hh in range(2):
                b = hh * 64
                op("vector", lambda e, hh=hh: e.tensor_scalar(
                    dcol[:, 2 * hh:2 * hh + 1], pbank[2 + hh][:, 64:65],
                    1e-6, None, ALU.add),
                   reads=[f"pbank{2 + hh}"], writes=[f"dc0{hh}"])
                op("vector", lambda e, hh=hh: e.reciprocal(
                    dcol[:, 2 * hh + 1:2 * hh + 2], dcol[:, 2 * hh:2 * hh + 1]),
                   reads=[f"dc0{hh}"], writes=[f"dc1{hh}"])
                op("scalar", lambda e, c=c, g=g, hh=hh: e.mul(
                    ytm[:, c * D + g * 128 + hh * 64:c * D + g * 128 + hh * 64 + 64],
                    pbank[2 + hh][:, 0:64], dcol[:, 2 * hh + 1:2 * hh + 2]),
                    reads=[f"pbank{2 + hh}", f"dc1{hh}"], writes=["ytm"])
                if c < NTILE - 1:
                    op("vector", lambda e, hh=hh, b=b: e.tensor_copy(
                        ssb[b:b + 64, 0:65], pS2[hh][0:64, 0:65]),
                       reads=[f"pS2{hh}"], writes=[f"ssb{hh}"])
    if "ytm0" in taps:
        op("sync", lambda e: e.dma_start(
            out=taps["ytm0"].rearrange("(j p) d -> p (j d)", p=128), in_=ytm[:]),
            reads=["ytm"], dma=True)
    ctx.barrier()
    p.free()

    p = Pool(nc)
    yf = p.sb("l0c_yf", [128, 1024], BF16)
    x0t = p.sb("l0c_x0", [128, D], F32)
    x1t = p.sb("l0c_x1", [128, D], F32)
    wp = p.sb("l0c_wp", [128, 8192], BF16)
    F = p.sb("l0c_F", [128, 8], F32)
    sc1 = p.sb("l0c_sc1", [128, 8], F32)
    sc2 = p.sb("l0c_sc2", [128, 8], F32)
    sc3 = p.sb("l0c_sc3", [128, 8], F32)
    gst = p.sb("l0c_gst", [8, 128], F32)
    pT = p.ps("l0c_pT", [128, 128], BF16)
    pU = p.ps("l0c_pU", [128, 512])
    plg = p.ps("l0c_plg", [128, 8])
    pg = p.ps("l0c_pg", [8, 128])
    src = bass.AP(w["wpt"].tensor, 0, [[1024, 128], [1024 * 128, 8], [1, 1024]])
    op("gpsimd", lambda e, src=src: e.dma_start(
        out=wp[:].rearrange("p (c f) -> p c f", f=1024), in_=src),
        writes=["wp"], dma=True, dkey="wp")

    def gates(j):
        ctx.mm([(xst[:, cc * T + j * 128:cc * T + j * 128 + 128],
                 sb["rwsb"][:, cc * 8:(cc + 1) * 8]) for cc in range(8)],
               plg[:], reads=[f"xst{j // 4}", "rwsb"],
               writes=["b_plg"], stop=False)
        ctx.mm([(sb["ONESF"][0:1, :], sb["rb"][:])], plg[:],
               reads=["ONESF", "rb"], writes=["b_plg"], start=False)
        op("vector", lambda e: e.reduce_max(F[:, 0:1], plg[:], AX),
           reads=["b_plg"], writes=["b_g0"])
        op("vector", lambda e: e.tensor_scalar(sc1[:], plg[:], F[:, 0:1],
                                               None, ALU.is_ge),
           reads=["b_plg", "b_g0"], writes=["b_sc1"])
        op("vector", lambda e: e.scalar_tensor_tensor(
            sc2[:], sc1[:], -1e30, plg[:], ALU.mult, ALU.add),
           reads=["b_sc1", "b_plg"], writes=["b_sc2"])
        op("vector", lambda e: e.reduce_max(F[:, 1:2], sc2[:], AX),
           reads=["b_sc2"], writes=["b_g1"])
        op("vector", lambda e: e.tensor_scalar(sc1[:], plg[:], F[:, 1:2],
                                               None, ALU.is_ge),
           reads=["b_plg", "b_g1"], writes=["b_sc1"])
        op("vector", lambda e: e.tensor_scalar(F[:, 2:3], F[:, 0:1], -1.0,
                                               None, ALU.mult),
           reads=["b_g0"], writes=["b_g2"])
        op("scalar", lambda e: e.activation(sc2[:], plg[:], AF.Exp,
                                            bias=F[:, 2:3]),
           reads=["b_plg", "b_g2"], writes=["b_sc2"])
        op("vector", lambda e: e.tensor_mul(sc3[:], sc2[:], sc1[:]),
           reads=["b_sc2", "b_sc1"], writes=["b_sc3"])
        op("vector", lambda e: e.reduce_sum(F[:, 3:4], sc3[:], AX),
           reads=["b_sc3"], writes=["b_g3"])
        op("vector", lambda e: e.reciprocal(F[:, 4:5], F[:, 3:4]),
           reads=["b_g3"], writes=["b_g4"])
        op("vector", lambda e: e.tensor_scalar(sc3[:], sc3[:], F[:, 4:5],
                                               None, ALU.mult),
           reads=["b_sc3", "b_g4"], writes=["b_sc3"])
        op("tensor", lambda e: e.transpose(pg[:], sc3[:], sb["IDENTF"][:]),
           reads=["b_sc3", "IDENTF"], writes=["b_pg"])
        op("vector", lambda e: e.tensor_copy(gst[:], pg[:]),
           reads=["b_pg"], writes=["b_gst"])
        gap = bass.AP(dr["gfm"].tensor, j * 128, [[T, 8], [1, 128]])
        op("sync", lambda e, gap=gap: e.dma_start(out=gap, in_=gst[:]),
           reads=["b_gst"], writes=["gfm_d"], dma=True)

    for j in range(NTILE):
        r0 = j * 128
        gates(j)
        for cc in range(8):
            op("tensor", lambda e, j=j, cc=cc: e.transpose(
                pT[:], ytm[:, j * D + cc * 128:j * D + (cc + 1) * 128], sb["IDENT"][:]),
               reads=["ytm", "IDENT"], writes=["l0c_pT"])
            op("vector", lambda e, cc=cc: e.tensor_copy(
                yf[:, cc * 128:(cc + 1) * 128], pT[:]),
               reads=["l0c_pT"], writes=[f"yf{cc}"])
        op("sync", lambda e, r0=r0: e.dma_start(out=x0t[:], in_=xin[r0:r0 + 128, :]),
           writes=["x0t"], dma=True)
        for ih in range(2):
            ctx.mm([(yf[:, cc * 128:(cc + 1) * 128],
                     wp[:, cc * 1024 + ih * 512:cc * 1024 + ih * 512 + 512])
                    for cc in range(8)], pU[:],
                   reads=[f"yf{cc}" for cc in range(8)] + ["wp"],
                   writes=["l0c_pU"], stop=False)
            ctx.mm([(sb["ONESF"][0:1, :], sb["bp"][0:1, ih * 512:ih * 512 + 512])], pU[:],
                   reads=["ONESF", "bp"], writes=["l0c_pU"], start=False)
            op("vector", lambda e, ih=ih: e.tensor_add(
                x1t[:, ih * 512:(ih + 1) * 512], pU[:], x0t[:, ih * 512:(ih + 1) * 512]),
                reads=["l0c_pU", "x0t"], writes=["x1t"])
        op("sync", lambda e, r0=r0: e.dma_start(out=dr["x1"][r0:r0 + 128, :], in_=x1t[:]),
           reads=["x1t"], writes=[f"x1_{j}"], dma=True)
    if "x1p" in taps:
        op("sync", lambda e: e.dma_start(out=taps["x1p"], in_=dr["x1"]),
           reads=[f"x1_{j}" for j in range(NTILE)], dma=True)
    ctx.barrier()
    p.free()
    ypool.free()



def _l0_moe(ctx, nc, sb, w, dr, taps):
    op = ctx.op
    p = Pool(nc)
    w1p = p.sb("l0d_w1", [128, 5 * 4096], BF16)
    w3p = p.sb("l0d_w3", [128, 5 * 4096], BF16)
    w2p = p.sb("l0d_w2", [128, 5 * 4096], BF16)
    xnd2 = [p.sb(f"l0d_xn{i}", [128, 2048], BF16) for i in range(2)]
    hsb2 = [p.sb(f"l0d_h{i}", [128, 1024], BF16) for i in range(2)]
    silu2 = [p.sb(f"l0d_si{i}", [128, 256], BF16) for i in range(2)]
    gG2 = [p.sb(f"l0d_g{i}", [128, 5 * 256], F32) for i in range(2)]
    x1t = p.sb("l0d_x1", [128, 2 * D], F32)
    o1 = [p.sb(f"l0d_o1{i}", [128, 512], F32) for i in range(2)]
    pa2 = [p.ps(f"l0d_pa{i}", [128, 256]) for i in range(2)]
    pb2 = [p.ps(f"l0d_pb{i}", [128, 256]) for i in range(2)]
    py = [p.ps(f"l0d_py{i}", [128, 512]) for i in range(4)]
    xn0tags = [f"xn0_{j}" for j in range(NTILE)]

    def load_xnd(tb):
        xb = tb % 2
        for cc in range(8):
            op("sync", lambda e, cc=cc, tb=tb, xb=xb: e.dma_start_transpose(
                xnd2[xb][:, cc * 256:(cc + 1) * 256],
                dr["xn0"][tb * 256:tb * 256 + 256, cc * 128:(cc + 1) * 128]),
                reads=xn0tags[tb * 2:tb * 2 + 2], writes=[f"xnd{xb}_{cc}"], dma=True,
                dkey=f"xnd{xb}")

    def load_gates(ps_, tb):
        xb = tb % 2
        nr = 5 if ps_ == 0 else 3
        for e5 in range(nr):
            ex = ps_ * 5 + e5
            src = bass.AP(dr["gfm"].tensor, ex * T + tb * 256,
                          [[0, 128], [1, 256]])
            wt = [f"gG{xb}_{e5}"] + ([f"gGt{xb}"] if e5 == nr - 1 else [])
            op("scalar", lambda e, src=src, xb=xb, e5=e5: e.dma_start(
                out=gG2[xb][:, e5 * 256:(e5 + 1) * 256], in_=src),
                reads=["gfm_d"], writes=wt, dma=True, dkey=f"gG{xb}")

    for ps_ in range(2):
        for e5 in range(5):
            ex = ps_ * 5 + e5
            for name, wsrc, buf, inner, nch in (("w1p", "w1s", w1p, 512, 8),
                                                ("w3p", "w3s", w3p, 512, 8),
                                                ("w2p", "w2s", w2p, 1024, 4)):
                src = bass.AP(w[wsrc].tensor, ex * D * 512,
                              [[inner, 128], [inner * 128, nch], [1, inner]])
                op("gpsimd", lambda e, src=src, buf=buf, e5=e5, inner=inner:
                   e.dma_start(out=buf[:, e5 * 4096:(e5 + 1) * 4096]
                               .rearrange("p (c f) -> p c f", f=inner), in_=src),
                   writes=[name + str(e5)], dma=True, dkey=name + str(e5))
        if ps_ == 0:
            load_xnd(0)
        load_gates(ps_, 0)
        for tb in range(16):
            t0 = tb * 256
            xb = tb % 2
            xr = [f"xnd{xb}_{cc}" for cc in range(8)]
            if tb + 1 < 16 or ps_ == 0:
                load_xnd(tb + 1 if tb + 1 < 16 else 0)
            if tb + 1 < 16:
                load_gates(ps_, tb + 1)
            elif ps_ == 0:
                load_gates(1, 0)
            for e5 in range(5):
                ex = ps_ * 5 + e5
                hp = e5 % 2
                hsb = hsb2[hp]
                for ft in range(4):
                    fp = ft % 2
                    ctx.mm([(w1p[:, e5 * 4096 + cc * 512 + ft * 128:
                             e5 * 4096 + cc * 512 + ft * 128 + 128],
                             xnd2[xb][:, cc * 256:(cc + 1) * 256]) for cc in range(8)],
                           pa2[fp][:], reads=[f"w1p{e5}"] + xr,
                           writes=[f"l0d_pa{fp}"])
                    ctx.mm([(w3p[:, e5 * 4096 + cc * 512 + ft * 128:
                             e5 * 4096 + cc * 512 + ft * 128 + 128],
                             xnd2[xb][:, cc * 256:(cc + 1) * 256]) for cc in range(8)],
                           pb2[fp][:], reads=[f"w3p{e5}"] + xr,
                           writes=[f"l0d_pb{fp}"])
                    op("scalar", lambda e, fp=fp: e.activation(
                        silu2[fp][:], pa2[fp][:], AF.Silu),
                       reads=[f"l0d_pa{fp}"], writes=[f"silu{fp}"])
                    if ex < 8:
                        op("vector", lambda e, ft=ft, e5=e5, fp=fp, hsb=hsb, xb=xb:
                            e.tensor_mul(
                                hsb[:, ft * 256:(ft + 1) * 256], pb2[fp][:],
                                gG2[xb][:, e5 * 256:(e5 + 1) * 256]),
                            reads=[f"l0d_pb{fp}", f"gG{xb}_{e5}", f"gGt{xb}"],
                            writes=[f"hsb{hp}"])
                        op("vector", lambda e, ft=ft, fp=fp, hsb=hsb: e.tensor_mul(
                            hsb[:, ft * 256:(ft + 1) * 256],
                            hsb[:, ft * 256:(ft + 1) * 256], silu2[fp][:]),
                            reads=[f"hsb{hp}", f"silu{fp}"], writes=[f"hsb{hp}"])
                    else:
                        op("vector", lambda e, ft=ft, fp=fp, hsb=hsb: e.tensor_mul(
                            hsb[:, ft * 256:(ft + 1) * 256], pb2[fp][:], silu2[fp][:]),
                            reads=[f"l0d_pb{fp}", f"silu{fp}"], writes=[f"hsb{hp}"])
                for tsub in range(2):
                    for mh in range(2):
                        ctx.mm([(hsb[:, fc * 256 + tsub * 128:fc * 256 + tsub * 128 + 128],
                                 w2p[:, e5 * 4096 + fc * 1024 + mh * 512:
                                     e5 * 4096 + fc * 1024 + mh * 512 + 512])
                                for fc in range(4)], py[tsub * 2 + mh][:],
                               reads=[f"hsb{hp}", f"w2p{e5}"],
                               writes=[f"l0d_py{tsub * 2 + mh}"],
                               start=(e5 == 0), stop=(e5 == 4))
            x1src = bass.AP(dr["x1"].tensor, t0 * D, [[D, 128], [128 * D, 2], [1, D]])
            op("sync", lambda e, x1src=x1src: e.dma_start(
                out=x1t[:].rearrange("p (k m) -> p k m", m=D), in_=x1src),
                reads=[f"x1_{tb * 2}", f"x1_{tb * 2 + 1}"], writes=["l0d_x1t"], dma=True, dkey="l0d_x1t")
            for tsub in range(2):
                for mh in range(2):
                    ob = mh
                    op("vector", lambda e, tsub=tsub, mh=mh, ob=ob: e.tensor_add(
                        o1[ob][:], py[tsub * 2 + mh][:],
                        x1t[:, tsub * D + mh * 512:tsub * D + mh * 512 + 512]),
                        reads=[f"l0d_py{tsub * 2 + mh}", "l0d_x1t"],
                        writes=[f"l0d_o1{ob}"])
                    op("sync", lambda e, t0=t0, tsub=tsub, mh=mh, ob=ob: e.dma_start(
                        out=dr["x1"][t0 + tsub * 128:t0 + tsub * 128 + 128,
                                     mh * 512:mh * 512 + 512], in_=o1[ob][:]),
                        reads=[f"l0d_o1{ob}"], writes=[f"x1_{tb * 2 + tsub}"], dma=True)
        ctx.barrier()
    if "x1" in taps:
        op("sync", lambda e: e.dma_start(out=taps["x1"], in_=dr["x1"]),
           reads=[f"x1_{j}" for j in range(NTILE)], dma=True)
    ctx.barrier()
    p.free()


def _l1_pre(ctx, nc, sb, w, dr, taps):
    op = ctx.op
    p = Pool(nc)
    NB = 2
    xt = [p.sb(f"l1a_xt{i}", [128, D], F32) for i in range(NB)]
    xnb = [p.sb(f"l1a_xnb{i}", [128, D], BF16) for i in range(NB)]
    xnf = [p.sb(f"l1a_xnf{i}", [128, D], BF16) for i in range(NB)]
    sq = [p.sb(f"l1a_sq{i}", [128, D], F32) for i in range(NB)]
    lg = [p.sb(f"l1a_lg{i}", [128, 8], F32) for i in range(NB)]
    fcb = [p.sb(f"l1a_fc{i}", [128, 8], F32) for i in range(NB)]
    pt = [p.ps(f"l1a_pt{i}", [128, 128], BF16) for i in range(NB)]
    plg = [p.ps(f"l1a_plg{i}", [128, 8]) for i in range(NB)]

    def front(j):
        q = j % NB
        op("sync", lambda e, j=j, q=q: e.dma_start(out=xt[q][:],
                                                   in_=dr["x1"][j * 128:(j + 1) * 128, :]),
           reads=[f"x1_{j}"], writes=[f"a_xt{q}"], dma=True, dkey=f"a_xt{q}")
        _ln_tile_r(ctx, sb, xt[q], xnb[q], sq[q], fcb[q], f"a{q}")

    def back(j):
        q = j % NB
        op("sync", lambda e, j=j, q=q: e.dma_start(
            out=dr["agx_in"][j * 128:(j + 1) * 128, :], in_=xnb[q][:]),
            reads=[f"a_xnb{q}"], writes=[f"agxin{j}"], dma=True)
        for cc in range(8):
            op("tensor", lambda e, cc=cc, q=q: e.transpose(
                pt[q][:], xnb[q][:, cc * 128:(cc + 1) * 128], sb["IDENT"][:]),
               reads=[f"a_xnb{q}", "IDENT"], writes=[f"a_pt{q}"])
            op("scalar", lambda e, cc=cc, q=q: e.copy(
                xnf[q][:, cc * 128:(cc + 1) * 128], pt[q][:]),
               reads=[f"a_pt{q}"], writes=[f"a_xnf{q}_{cc}"])
        ctx.mm([(xnf[q][:, cc * 128:(cc + 1) * 128], sb["wgsb"][:, cc * 8:(cc + 1) * 8])
                for cc in range(8)], plg[q][:],
               reads=[f"a_xnf{q}_{cc}" for cc in range(8)] + ["wgsb"],
               writes=[f"a_plg{q}"])
        op("vector", lambda e, q=q: e.tensor_copy(lg[q][:], plg[q][:]),
           reads=[f"a_plg{q}"], writes=[f"a_lg{q}"])
        op("sync", lambda e, j=j, q=q: e.dma_start(
            out=dr["aglg_in"][j * 128:(j + 1) * 128, :], in_=lg[q][:]),
            reads=[f"a_lg{q}"], writes=[f"aglgin{j}"], dma=True)

    front(0)
    for j in range(NTILE):
        if j + 1 < NTILE:
            front(j + 1)
        back(j)
    ctx.barrier()
    p.free()
    rg = [list(range(NCORES))]
    op("gpsimd", lambda e: e.collective_compute(
        "AllGather", ALU.bypass, replica_groups=rg,
        ins=[dr["aglg_in"].ap().opt()], outs=[dr["aglg_out"].ap().opt()]),
        reads=[f"aglgin{j}" for j in range(NTILE)], writes=["aglgout"])
    if "logits" in taps:
        op("sync", lambda e: e.dma_start(out=taps["logits"], in_=dr["aglg_in"].ap()),
           reads=[f"aglgin{j}" for j in range(NTILE)], dma=True)


def _ln_tile_r(ctx, sb, xt, xnb, sq, F, pfx):
    op = ctx.op
    op("vector", lambda e: e.reduce_sum(F[:, 0:1], xt[:], AX),
       reads=[f"{pfx}_xt" if False else f"{pfx[0]}_xt{pfx[1]}"], writes=[f"{pfx}_f0"])
    op("vector", lambda e: e.tensor_scalar(F[:, 1:2], F[:, 0:1], 1.0 / D, None,
                                           ALU.mult),
       reads=[f"{pfx}_f0"], writes=[f"{pfx}_f1"])
    op("scalar", lambda e: e.activation(sq[:], xt[:], AF.Square, accum_out=F[:, 2:3]),
       reads=[f"{pfx[0]}_xt{pfx[1]}"], writes=[f"{pfx}_sq", f"{pfx}_f2"])
    op("vector", lambda e: e.tensor_mul(F[:, 3:4], F[:, 1:2], F[:, 1:2]),
       reads=[f"{pfx}_f1"], writes=[f"{pfx}_f3"])
    op("vector", lambda e: e.scalar_tensor_tensor(F[:, 3:4], F[:, 2:3], 1.0 / D,
                                                  F[:, 3:4], ALU.mult, ALU.subtract),
       reads=[f"{pfx}_f2", f"{pfx}_f3"], writes=[f"{pfx}_f3"])
    op("scalar", lambda e: e.activation(F[:, 4:5], F[:, 3:4], AF.Sqrt,
                                        bias=sb["cEPS"][:]),
       reads=[f"{pfx}_f3", "cEPS"], writes=[f"{pfx}_f4"])
    op("vector", lambda e: e.reciprocal(F[:, 5:6], F[:, 4:5]),
       reads=[f"{pfx}_f4"], writes=[f"{pfx}_f5"])
    op("vector", lambda e: e.tensor_scalar(xnb[:], xt[:], F[:, 1:2], F[:, 5:6],
                                           ALU.subtract, ALU.mult),
       reads=[f"{pfx[0]}_xt{pfx[1]}", f"{pfx}_f1", f"{pfx}_f5"],
       writes=[f"{pfx[0]}_xnb{pfx[1]}"])



def _batch_done(ctx, nc, sb, dr, taps, p):
    ctx.barrier()
    p.free()


def _l1_routing(ctx, nc, sb, w, dr, capd, taps):
    op = ctx.op
    p = Pool(nc)
    lgt = p.sb("l1r_lg", [128, 128], F32)
    pc = p.ps("l1r_pc", [128, GTILE])
    pt1 = p.ps("l1r_pt1", [1, GTILE])
    pb2 = p.ps("l1r_pb2", [128, 2])
    pcs = p.ps("l1r_pcs", [1, 2])
    pbc = p.ps("l1r_pbc", [128, 1])
    F = sb["fc"]

    es16 = p.sb("l1r_es16", [1, 128], F32)
    esb16 = p.sb("l1r_esb16", [128, 128], F32)
    lgt2 = [p.sb(f"l1r_lg2{i}", [128, 128], F32) for i in range(2)]
    LL = [p.sb(f"l1r_LL{i}", [128, 128], F32) for i in range(2)]
    EX = [p.sb(f"l1r_EX{i}", [128, 128], F32) for i in range(2)]
    MYP = [p.sb(f"l1r_MYP{i}", [128, 128], F32) for i in range(2)]
    M1 = [p.sb(f"l1r_M1{i}", [128, 16], F32) for i in range(2)]
    S16 = [p.sb(f"l1r_S16{i}", [128, 16], F32) for i in range(2)]
    MY16 = [p.sb(f"l1r_MY16{i}", [128, 16], F32) for i in range(2)]
    peb = p.ps("l1r_peb", [128, 128])
    op("sync", lambda e: e.dma_start(out=es16[:], in_=w["esel16"]),
       writes=["es16"], dma=True)
    ctx.mm([(sb["ONESF"][0:1, :], es16[:])], peb[:], reads=["ONESF", "es16"],
           writes=["peb"])
    op("vector", lambda e: e.tensor_copy(esb16[:], peb[:]),
       reads=["peb"], writes=["esb16"])
    for jj in range(16):
        q = jj % 2
        src = bass.AP(dr["aglg_out"].ap().tensor, jj * 16 * 128 * 8,
                      [[8, 128], [128 * 8, 16], [1, 8]])
        op("sync", lambda e, src=src, q=q: e.dma_start(
            out=lgt2[q][:].rearrange("p (j e) -> p j e", e=8), in_=src),
            reads=["aglgout"], writes=[f"lgt{q}"], dma=True, dkey=f"lgt{q}")
        l3 = lgt2[q][:].rearrange("p (j e) -> p j e", e=8)
        op("vector", lambda e, q=q, l3=l3: e.reduce_max(M1[q][:], l3, AX),
           reads=[f"lgt{q}"], writes=[f"rM1{q}"])
        m1b = M1[q][:].rearrange("p (j o) -> p j o", o=1).to_broadcast([128, 16, 8])
        ll3 = LL[q][:].rearrange("p (j e) -> p j e", e=8)
        op("vector", lambda e, q=q, l3=l3, m1b=m1b, ll3=ll3: e.tensor_tensor(
            ll3, l3, m1b, ALU.subtract),
           reads=[f"lgt{q}", f"rM1{q}"], writes=[f"rLL{q}"])
        op("scalar", lambda e, q=q: e.activation(EX[q][:], LL[q][:], AF.Exp),
           reads=[f"rLL{q}"], writes=[f"rEX{q}"])
        e3 = EX[q][:].rearrange("p (j e) -> p j e", e=8)
        op("vector", lambda e, q=q, e3=e3: e.reduce_sum(S16[q][:], e3, AX),
           reads=[f"rEX{q}"], writes=[f"rS16{q}"])
        op("vector", lambda e, jj=jj, q=q: e.reciprocal(
            sb["G1"][:, jj * 16:(jj + 1) * 16], S16[q][:]),
           reads=[f"rS16{q}"], writes=["G1"])
        op("vector", lambda e, q=q: e.tensor_mul(MYP[q][:], lgt2[q][:], esb16[:]),
           reads=[f"lgt{q}", "esb16"], writes=[f"rMYP{q}"])
        p3 = MYP[q][:].rearrange("p (j e) -> p j e", e=8)
        op("vector", lambda e, q=q, p3=p3: e.reduce_sum(MY16[q][:], p3, AX),
           reads=[f"rMYP{q}"], writes=[f"rMY16{q}"])
        op("vector", lambda e, jj=jj, q=q: e.tensor_tensor(
            sb["M"][:, jj * 16:(jj + 1) * 16], MY16[q][:], M1[q][:], ALU.is_ge),
           reads=[f"rMY16{q}", f"rM1{q}"], writes=["M"])
        op("vector", lambda e: e.tensor_copy(sb["Mb"][:], sb["M"][:]),
       reads=["M"], writes=["Mb"])
    ctx.mm([(sb["UTI"][:], sb["Mb"][:])], pc[:], reads=["UTI", "Mb"], writes=["l1r_pc"])
    op("vector", lambda e: e.tensor_copy(sb["CUMS"][:], pc[:]),
       reads=["l1r_pc"], writes=["CUMS"])
    ctx.mm([(sb["ONESB"][:, 0:1], sb["Mb"][:])], pt1[:], reads=["ONESB", "Mb"],
           writes=["ptot"])
    op("vector", lambda e: e.tensor_copy(sb["tot"][:], pt1[:]),
       reads=["ptot"], writes=["tot"])
    op("sync", lambda e: e.dma_start(out=dr["bounce"][0:1, :], in_=sb["tot"][:]),
       reads=["tot"], writes=["bounce0"], dma=True)
    b_src = bass.AP(dr["bounce"].tensor, 0, [[1, 128], [128, 2]])
    op("gpsimd", lambda e, b_src=b_src: e.dma_start(
        out=sb["tot2"][:], in_=b_src, allow_slow_non_contiguous=True),
        reads=["bounce0"], writes=["tot2"], dma=True)
    ctx.mm([(sb["UTS"][:], sb["tot2"][:])], pb2[:], reads=["UTS", "tot2"],
           writes=["pb2"])
    ctx.mm([(sb["ONESB"][:, 0:1], sb["tot2"][:])], pcs[:], reads=["ONESB", "tot2"],
           writes=["pcs"])
    op("vector", lambda e: e.tensor_copy(sb["scs"][:], pcs[:]),
       reads=["pcs"], writes=["scs"])
    ctx.mm([(sb["ONESF"][0:1, :], sb["scs"][0:1, 0:1])], pbc[:],
           reads=["ONESF", "scs"], writes=["pbc"])
    op("vector", lambda e: e.tensor_copy(sb["offs2"][:], pb2[:]),
       reads=["pb2"], writes=["offs2"])
    op("vector", lambda e: e.tensor_add(sb["offs2"][:, 1:2], sb["offs2"][:, 1:2], pbc[:]),
       reads=["offs2", "pbc"], writes=["offs2"])
    b_dst = bass.AP(dr["bounce"].tensor, 256, [[1, 128], [128, 2]])
    op("gpsimd", lambda e, b_dst=b_dst: e.dma_start(
        out=b_dst, in_=sb["offs2"][:], allow_slow_non_contiguous=True),
        reads=["offs2"], writes=["bounce1"], dma=True)
    op("sync", lambda e: e.dma_start(out=sb["offs"][:], in_=dr["bounce"][1:2, :]),
       reads=["bounce1"], writes=["offs"], dma=True)
    ctx.mm([(sb["ONESF"][0:1, :], sb["offs"][:])], pc[:], reads=["ONESF", "offs"],
           writes=["l1r_pc"])
    op("vector", lambda e: e.tensor_add(sb["POS"][:], sb["CUMS"][:], pc[:]),
       reads=["CUMS", "l1r_pc"], writes=["POS"])
    op("vector", lambda e: e.scalar_tensor_tensor(sb["SLOT"][:], sb["M"][:], -33000.0,
                                                  sb["POS"][:], ALU.mult, ALU.add),
       reads=["M", "POS"], writes=["SLOT"])
    op("vector", lambda e: e.tensor_scalar(sb["SLOT"][:], sb["SLOT"][:], 32999.0,
                                           None, ALU.add),
       reads=["SLOT"], writes=["SLOT"])
    op("vector", lambda e: e.tensor_scalar(sb["SLOT"][:], sb["SLOT"][:], float(TRASH),
                                           None, ALU.min),
       reads=["SLOT"], writes=["SLOT"])
    op("vector", lambda e: e.tensor_scalar(sb["KTMP"][:], sb["POS"][:], float(CAP),
                                           None, ALU.is_le),
       reads=["POS"], writes=["KTMP"])
    op("vector", lambda e: e.tensor_mul(sb["GS"][:], sb["G1"][:], sb["M"][:]),
       reads=["G1", "M"], writes=["GS"])
    op("vector", lambda e: e.tensor_mul(sb["GS"][:], sb["GS"][:], sb["KTMP"][:]),
       reads=["GS", "KTMP"], writes=["GS"])
    op("vector", lambda e: e.tensor_scalar(sb["rmin"][:], sb["offs"][:], float(CAP),
                                           None, ALU.min),
       reads=["offs"], writes=["rmin"])
    op("vector", lambda e: e.tensor_copy(sb["roffi"][:], sb["rmin"][:]),
       reads=["rmin"], writes=["roffi"])

    NST = NSLOT // 128
    tpos = p.sb("l1r_tpos", [128, 128], F32)
    OFFSB = p.sb("l1r_offsb", [128, GTILE], F32)
    SCB = p.sb("l1r_scb", [128, NST], F32)
    TT2 = [p.sb(f"l1r_TT{i}", [128, 260], F32) for i in range(2)]
    tmA = [p.sb(f"l1r_tmA{i}", [128, GTILE], F32) for i in range(2)]
    tmB = [p.sb(f"l1r_tmB{i}", [128, 128], F32) for i in range(2)]
    tmC = [p.sb(f"l1r_tmC{i}", [128, 128], F32) for i in range(2)]
    cnt2 = [p.sb(f"l1r_cnt{i}", [128, 8], F32) for i in range(2)]
    coli = [p.sb(f"l1r_coli{i}", [128, 1], I32) for i in range(2)]
    for half, srcn in ((0, "POS"), (1, "GS")):
        for ct in range(2):
            op("tensor", lambda e, srcn=srcn, ct=ct: e.transpose(
                pc[:, 0:128], sb[srcn][:, ct * 128:(ct + 1) * 128], sb["IDENTF"][:]),
               reads=[srcn, "IDENTF"], writes=["l1r_pc"])
            op("vector", lambda e: e.tensor_copy(tpos[:], pc[:, 0:128]),
               reads=["l1r_pc"], writes=["l1r_tpos"])
            op("sync", lambda e, half=half, ct=ct: e.dma_start(
                out=dr["tbl"][ct * 128:(ct + 1) * 128, half * 128:(half + 1) * 128],
                in_=tpos[:]),
                reads=["l1r_tpos"], writes=["tbl_d"], dma=True)
    op("sync", lambda e: e.dma_start(out=dr["tbl"][:, 256:257], in_=capd["PBC"].ap(),
                                     allow_slow_non_contiguous=True),
       writes=["tbl_d"], dma=True)
    ctx.mm([(sb["ONESF"][0:1, :], sb["offs"][:])], pc[:], reads=["ONESF", "offs"],
           writes=["l1r_pc"])
    op("vector", lambda e: e.tensor_copy(OFFSB[:], pc[:]),
       reads=["l1r_pc"], writes=["l1r_offsb"])
    for st in range(NST):
        op("vector", lambda e, st=st: e.tensor_scalar(
            SCB[:, st:st + 1], sb["IOTAC"][:], float(st * 128), None, ALU.add),
           reads=["IOTAC"], writes=["l1r_scb"])
    for st in range(NST):
        q = st % 2
        cn = cnt2[q]
        op("vector", lambda e, st=st, q=q: e.tensor_scalar(
            tmA[q][:], OFFSB[:], SCB[:, st:st + 1], None, ALU.is_le),
           reads=["l1r_offsb", "l1r_scb"], writes=[f"l1r_A{q}"])
        op("vector", lambda e, q=q, cn=cn: e.reduce_sum(cn[:, 0:1], tmA[q][:], AX),
           reads=[f"l1r_A{q}"], writes=[f"l1r_c0{q}"])
        op("vector", lambda e, q=q, cn=cn: e.tensor_scalar(
            cn[:, 1:2], cn[:, 0:1], -1.0, None, ALU.add),
           reads=[f"l1r_c0{q}"], writes=[f"l1r_c1{q}"])
        op("vector", lambda e, q=q, cn=cn: e.tensor_copy(coli[q][:], cn[:, 1:2]),
           reads=[f"l1r_c1{q}"], writes=[f"l1r_coli{q}"])
        op("gpsimd", lambda e, q=q: e.indirect_dma_start(
            out=TT2[q][:], out_offset=None, in_=dr["tbl"],
            in_offset=IndirectOffsetOnAxis(ap=coli[q][:], axis=0)),
            reads=[f"l1r_coli{q}", "tbl_d"], writes=[f"l1r_TT{q}"], dma=True,
            dkey=f"ttg{q}")
        op("vector", lambda e, st=st, q=q: e.tensor_scalar(
            tmB[q][:], TT2[q][:, 0:128], SCB[:, st:st + 1], None, ALU.is_le),
           reads=[f"l1r_TT{q}", "l1r_scb"], writes=[f"l1r_B{q}"])
        op("vector", lambda e, q=q, cn=cn: e.reduce_sum(cn[:, 2:3], tmB[q][:], AX),
           reads=[f"l1r_B{q}"], writes=[f"l1r_c2{q}"])
        op("vector", lambda e, q=q, cn=cn: e.scalar_tensor_tensor(
            cn[:, 3:4], cn[:, 1:2], 128.0, cn[:, 2:3], ALU.mult, ALU.add),
           reads=[f"l1r_c1{q}", f"l1r_c2{q}"], writes=[f"l1r_c3{q}"])
        op("vector", lambda e, st=st, q=q, cn=cn: e.tensor_copy(
            sb["TOKI"][:, st:st + 1], cn[:, 3:4]),
           reads=[f"l1r_c3{q}"], writes=["TOKI"])
        op("vector", lambda e, q=q, cn=cn: e.tensor_add(
            cn[:, 4:5], TT2[q][:, 256:257], cn[:, 2:3]),
           reads=[f"l1r_TT{q}", f"l1r_c2{q}"], writes=[f"l1r_c4{q}"])
        op("vector", lambda e, st=st, q=q, cn=cn: e.tensor_copy(
            sb["TOKP"][:, st:st + 1], cn[:, 4:5]),
           reads=[f"l1r_c4{q}"], writes=["TOKP"])
        op("vector", lambda e, st=st, q=q, cn=cn: e.tensor_scalar(
            cn[:, 5:6], SCB[:, st:st + 1], 1.0, None, ALU.add),
           reads=["l1r_scb"], writes=[f"l1r_c5{q}"])
        op("vector", lambda e, q=q, cn=cn: e.tensor_scalar(
            tmC[q][:], TT2[q][:, 0:128], cn[:, 5:6], None, ALU.is_le),
           reads=[f"l1r_TT{q}", f"l1r_c5{q}"], writes=[f"l1r_C{q}"])
        op("vector", lambda e, q=q: e.tensor_sub(tmC[q][:], tmC[q][:], tmB[q][:]),
           reads=[f"l1r_C{q}", f"l1r_B{q}"], writes=[f"l1r_C{q}"])
        op("vector", lambda e, q=q: e.tensor_mul(
            tmC[q][:], tmC[q][:], TT2[q][:, 128:256]),
           reads=[f"l1r_C{q}", f"l1r_TT{q}"], writes=[f"l1r_C{q}"])
        op("vector", lambda e, st=st, q=q: e.reduce_sum(
            sb["GSL"][:, st:st + 1], tmC[q][:], AX),
           reads=[f"l1r_C{q}"], writes=["GSL"])
        if st == 0 and "dbgm" in taps:
            op("sync", lambda e, q=q: e.dma_start(
                out=taps["dbgm"][:, 0:128], in_=tmB[q][:]),
               reads=[f"l1r_B{q}"], dma=True)
            op("sync", lambda e, q=q: e.dma_start(
                out=taps["dbgm"][:, 128:256], in_=tmC[q][:]),
               reads=[f"l1r_C{q}"], dma=True)
            op("sync", lambda e, q=q, cn=cn: e.dma_start(
                out=taps["dbgc"], in_=cn[:]),
               reads=[f"l1r_c5{q}"], dma=True)
    for tn, buf in (("mmat", "M"), ("posmat", "POS"), ("slotmat", "SLOT"),
                    ("gsmat", "GS"), ("toki", "TOKI"), ("tokp", "TOKP"),
                    ("gsl", "GSL")):
        if tn in taps:
            op("sync", lambda e, tn=tn, buf=buf: e.dma_start(
                out=taps[tn], in_=sb[buf][:]), reads=[buf], dma=True)
    if "tblm" in taps:
        op("sync", lambda e: e.dma_start(out=taps["tblm"], in_=dr["tbl"]),
           reads=["tbl_d"], dma=True)
    if "offsrow" in taps:
        op("sync", lambda e: e.dma_start(out=taps["offsrow"], in_=sb["offs"][:]),
           reads=["offs"], dma=True)
    if "roffif" in taps:
        op("vector", lambda e: e.tensor_copy(sb["rmin"][:], sb["roffi"][:]),
           reads=["roffi"], writes=["rmin"])
        op("sync", lambda e: e.dma_start(out=taps["roffif"], in_=sb["rmin"][:]),
           reads=["rmin"], dma=True)
    ctx.barrier()
    p.free()



def _l1_dispatch(ctx, nc, sb, dr, taps, p):
    op = ctx.op
    zz = p.sb("l1m_zz", [128, 8], BF16)
    zt = p.sb("l1m_zero", [128, D], BF16)
    op("vector", lambda e: e.memset(zt[:], 0.0), writes=["l1m_zt"])
    op("gpsimd", lambda e: e.dma_start(
        out=dr["agx_out"].ap()[NCORES * T:NCORES * T + 128, :], in_=zt[:]),
        reads=["l1m_zt"], writes=["agz"], dma=True)
    xnt = [p.sb(f"l1m_xnt{i}", [128, D], BF16) for i in range(3)]
    dtags = []
    for st in range(NSLOT // 128):
        b = st % 3
        op("gpsimd", lambda e, st=st, b=b: e.indirect_dma_start(
            out=xnt[b][:], out_offset=None,
            in_=dr["agx_out"].ap(),
            in_offset=IndirectOffsetOnAxis(ap=sb["TOKI"][:, st:st + 1], axis=0)),
            reads=["agxout", "TOKI", "agz"], writes=[f"xnt{b}"], dma=True,
            dkey=f"xnt{b}")
        tg = f"dsc{st}"
        dtags.append(tg)
        op("gpsimd", lambda e, st=st, b=b: e.dma_start(
            out=dr["disp"][st * 128:(st + 1) * 128, :], in_=xnt[b][:]),
            reads=[f"xnt{b}"], writes=[tg], dma=True)
    op("gpsimd", lambda e: e.dma_start(out=zz[:, 0:8], in_=dr["disp"][0:128, 0:8]),
       reads=dtags, writes=["dfence"], dma=True)
    if "disp" in taps:
        op("sync", lambda e: e.dma_start(out=taps["disp"], in_=dr["disp"]),
           reads=dtags + ["dfence"], dma=True)
    for j in range(NCORES * T // 128 + 1):
        op("gpsimd", lambda e, j=j: e.dma_start(
            out=dr["rsin"].ap()[j * 128:(j + 1) * 128, :], in_=zt[:]),
            reads=["l1m_zt"], writes=["rzero"], dma=True)
    return dtags


def _l1_ffn(ctx, nc, sb, w, dr, dtags, taps):
    op = ctx.op
    p = Pool(nc)
    w1sb = p.sb("l1f_w1", [128, 8 * 4096], BF16)
    w2sb = p.sb("l1f_w2", [128, 32 * 1024], BF16)
    dspf = p.sb("l1f_dsp", [128, 8 * 512], BF16)
    hsb = p.sb("l1f_h", [128, 32 * 512], BF16)
    eo = [p.sb(f"l1f_eo{i}", [128, D], BF16) for i in range(3)]
    oc = [p.sb(f"l1f_oc{i}", [128, D], BF16) for i in range(3)]
    ph = p.ps("l1f_ph", [128, 512])
    pe = [p.ps(f"l1f_pe{i}", [128, 512]) for i in range(2)]
    src = bass.AP(w["w1e"].tensor, 0, [[4096, 128], [4096 * 128, 8], [1, 4096]])
    op("gpsimd", lambda e, src=src: e.dma_start(
        out=w1sb[:].rearrange("p (c f) -> p c f", f=4096), in_=src),
        writes=["w1sb"], dma=True, dkey="w1sb")
    src2 = bass.AP(w["w2e"].tensor, 0, [[1024, 128], [1024 * 128, 32], [1, 1024]])
    op("gpsimd", lambda e, src2=src2: e.dma_start(
        out=w2sb[:].rearrange("p (c f) -> p c f", f=1024), in_=src2),
        writes=["w2sb"], dma=True, dkey="w2sb")

    etags = []
    for cb in range(NSLOT // 512):
        c0 = cb * 512
        for mc in range(8):
            op("sync", lambda e, mc=mc, c0=c0: e.dma_start_transpose(
                dspf[:, mc * 512:(mc + 1) * 512],
                dr["disp"][c0:c0 + 512, mc * 128:(mc + 1) * 128]),
                reads=dtags + ["dfence"], writes=[f"dspf{mc}"], dma=True, dkey="dspf")
        for ft in range(32):
            ctx.mm([(w1sb[:, mc * 4096 + ft * 128:mc * 4096 + ft * 128 + 128],
                     dspf[:, mc * 512:(mc + 1) * 512]) for mc in range(8)],
                   ph[:], reads=["w1sb"] + [f"dspf{mc}" for mc in range(8)],
                   writes=["l1f_ph"])
            op("scalar", lambda e, ft=ft: e.activation(
                hsb[:, ft * 512:(ft + 1) * 512], ph[:], AF.Relu,
                bias=sb["b1e"][:, ft:ft + 1]),
                reads=["l1f_ph", "b1e"], writes=["hsb"])
        for ct in range(4):
            st = cb * 4 + ct
            r = st % 3
            for mh in range(2):
                pid = mh
                ctx.mm([(hsb[:, fc * 512 + ct * 128:fc * 512 + ct * 128 + 128],
                         w2sb[:, fc * 1024 + mh * 512:fc * 1024 + mh * 512 + 512])
                        for fc in range(32)], pe[pid][:],
                       reads=["hsb", "w2sb"], writes=[f"l1f_pe{pid}"], stop=False)
                ctx.mm([(sb["ONESF"][0:1, :], sb["b2e"][0:1, mh * 512:mh * 512 + 512])],
                       pe[pid][:], reads=["ONESF", "b2e"], writes=[f"l1f_pe{pid}"],
                       start=False)
                op("vector", lambda e, mh=mh, pid=pid, r=r: e.tensor_copy(
                    eo[r][:, mh * 512:(mh + 1) * 512], pe[pid][:]),
                    reads=[f"l1f_pe{pid}"], writes=[f"l1f_eo{r}"])
            op("vector", lambda e, st=st, r=r: e.tensor_scalar(
                oc[r][:], eo[r][:], sb["GSL"][:, st:st + 1], None, ALU.mult),
                reads=[f"l1f_eo{r}", "GSL"], writes=[f"l1f_oc{r}"])
            tg = f"rsw{st}"
            etags.append(tg)
            op("gpsimd", lambda e, st=st, r=r: e.indirect_dma_start(
                out=dr["rsin"].ap(),
                out_offset=IndirectOffsetOnAxis(ap=sb["TOKP"][:, st:st + 1], axis=0),
                in_=oc[r][:], in_offset=None),
                reads=[f"l1f_oc{r}", "TOKP", "rzero"], writes=[tg], dma=True)
    ctx.barrier()
    p.free()
    return etags


def _l1_combine_final(ctx, nc, sb, w, out, dr, etags, taps):
    op = ctx.op
    p = Pool(nc)
    aos = p.sb("l1z_ao", [128, 8192], BF16)
    yf = p.sb("l1z_yf", [128, 1024], BF16)
    ytall = p.sb("l1z_ytall", [128, NTILE * D], BF16)
    x1t = [p.sb(f"l1z_x1{i}", [128, D], F32) for i in range(2)]
    xat = [p.sb(f"l1z_xa{i}", [128, D], F32) for i in range(2)]
    fa = [p.sb(f"l1z_fa{i}", [128, D], F32) for i in range(2)]
    rst = [p.sb(f"l1z_rs{i}", [128, D], BF16) for i in range(2)]
    ot = [p.sb(f"l1z_o{i}", [128, D], F32) for i in range(2)]
    pT = p.ps("l1z_pT", [128, 128], BF16)
    pF = p.ps("l1z_pF", [128, 512])
    src = bass.AP(dr["aotb"].tensor, 0, [[1024, 128], [1024 * 128, 8], [1, 1024]])
    op("sync", lambda e, src=src: e.dma_start(
        out=aos[:].rearrange("p (c f) -> p c f", f=1024), in_=src),
        reads=["aotb_d"], writes=["aos"], dma=True, dkey="aos")

    op("sync", lambda e: e.dma_start(out=ytall[:], in_=dr["ytm1d"]),
       reads=["ytm1d"], writes=["z_yta"], dma=True, dkey="z_yta")

    def proj(jj):
        q = jj % 2
        r0 = jj * 128
        for cc in range(8):
            op("tensor", lambda e, jj=jj, cc=cc: e.transpose(
                pT[:], ytall[:, jj * 1024 + cc * 128:jj * 1024 + (cc + 1) * 128],
                sb["IDENT"][:]),
               reads=["z_yta", "IDENT"], writes=["l1z_pT"])
            op("vector", lambda e, cc=cc: e.tensor_copy(
                yf[:, cc * 128:(cc + 1) * 128], pT[:]),
               reads=["l1z_pT"], writes=[f"zyf{cc}"])
        op("sync", lambda e, r0=r0, q=q: e.dma_start(
            out=x1t[q][:], in_=dr["x1"][r0:r0 + 128, :]),
            reads=[f"x1_{jj}"], writes=[f"z_x1{q}"], dma=True, dkey=f"z_x1{q}")
        for ih in range(2):
            ctx.mm([(yf[:, cc * 128:(cc + 1) * 128],
                     aos[:, cc * 1024 + ih * 512:cc * 1024 + ih * 512 + 512])
                    for cc in range(8)], pF[:],
                   reads=[f"zyf{cc}" for cc in range(8)] + ["aos"], writes=["z_pF"])
            op("vector", lambda e, ih=ih, q=q: e.tensor_add(
                xat[q][:, ih * 512:(ih + 1) * 512], pF[:],
                x1t[q][:, ih * 512:(ih + 1) * 512]),
                reads=["z_pF", f"z_x1{q}"], writes=[f"z_xa{q}"])
        op("sync", lambda e, r0=r0, q=q: e.dma_start(
            out=dr["xa"][r0:r0 + 128, :], in_=xat[q][:]),
            reads=[f"z_xa{q}"], writes=[f"xa_{jj}"], dma=True)

    rg = [list(range(NCORES))]

    def rs_q(q):
        r0 = q * 8192
        op("gpsimd", lambda e, q=q, r0=r0: e.collective_compute(
            "ReduceScatter", ALU.add, replica_groups=rg,
            ins=[dr["rsin"].ap()[r0:r0 + 8192, :].opt()],
            outs=[dr["rsout"].ap()[q * 1024:(q + 1) * 1024, :].opt()]),
            reads=etags + ["rzero"], writes=[f"rsout{q}"])

    def fadd(q, k):
        jj = q * 8 + k
        z = jj % 2
        r0 = jj * 128
        op("scalar", lambda e, r0=r0, z=z: e.dma_start(
            out=fa[z][:], in_=dr["xa"][r0:r0 + 128, :]),
            reads=[f"xa_{jj}"], writes=[f"z_fa{z}"], dma=True, dkey=f"z_fa{z}")
        op("sync", lambda e, r0=r0, z=z, q=q: e.dma_start(
            out=rst[z][:], in_=dr["rsout"].ap()[r0:r0 + 128, :]),
            reads=[f"rsout{q}"], writes=[f"z_rs{z}"], dma=True, dkey=f"z_rs{z}")
        op("vector", lambda e, z=z: e.tensor_add(ot[z][:], fa[z][:], rst[z][:]),
           reads=[f"z_fa{z}", f"z_rs{z}"], writes=[f"z_ot{z}"])
        eng = "scalar" if jj % 2 == 0 else "sync"
        op(eng, lambda e, r0=r0, z=z: e.dma_start(
            out=out[r0:r0 + 128, :], in_=ot[z][:]),
            reads=[f"z_ot{z}"], writes=["outd"], dma=True)

    for pj in range(NTILE):
        proj(pj)
    for q in range(4):
        rs_q(q)
        for k in range(8):
            fadd(q, k)
    ctx.barrier()
    p.free()



def _l1_attn_pre(ctx, nc, sb, dr, capd):
    op = ctx.op
    early = Pool(nc)
    eb = {"cs": early.sb("cos_sb", [128, T], BF16),
          "sn": early.sb("sin_sb", [128, T], BF16),
          "wqs": early.sb("l1t_wq", [128, 2048], BF16),
          "wks": early.sb("l1t_wk", [128, 2048], BF16),
          "wvs": early.sb("l1t_wv", [128, 2048], BF16)}
    op("sync", lambda e: e.dma_start(out=eb["cs"][:], in_=capd["COS"].ap()),
       writes=["COS"], dma=True)
    op("sync", lambda e: e.dma_start(out=eb["sn"][:], in_=capd["SIN"].ap()),
       writes=["SIN"], dma=True)
    for name, wt in (("wqs", "aqt"), ("wks", "akt"), ("wvs", "avt")):
        src = bass.AP(dr[wt + "b"].tensor, 0,
                      [[1024, 128], [1024 * 128, 8], [1, 256]])
        op("sync", lambda e, src=src, name=name: e.dma_start(
            out=eb[name][:].rearrange("p (c f) -> p c f", f=256), in_=src),
            reads=[f"{wt}b_d"], writes=[name], dma=True, dkey=name)
    return early, eb


def _l1_attn(ctx, nc, sb, w, dr, capd, taps, early, eb):
    op = ctx.op
    ypool = Pool(nc)
    ytm1 = ypool.sb("ytm1", [128, NTILE * D], BF16)
    cs = eb["cs"]
    sn = eb["sn"]

    p = Pool(nc)
    qq = p.sb("l1t_q", [128, 2 * T], BF16)
    kq = p.sb("l1t_k", [128, 2 * T], BF16)
    vex = p.sb("l1t_v", [128, NTILE * 260], BF16)
    xsq = [p.sb(f"l1t_xs{i}", [128, T], BF16) for i in range(2)]
    wqs, wks, wvs = eb["wqs"], eb["wks"], eb["wvs"]
    qraw = p.sb("l1t_qraw", [128, 512], BF16)
    t1 = p.sb("l1t_t1", [128, 512], BF16)
    t2 = p.sb("l1t_t2", [128, 512], BF16)
    Psb2 = [p.sb(f"l1t_P{i}", [128, 512], BF16) for i in range(2)]
    dcol = p.sb("l1t_dcol", [128, 1], F32)
    pP = p.ps("l1t_pP", [128, 512])
    pR = p.ps("l1t_pR", [128, 512])
    pA2 = [p.ps(f"l1t_pA{i}", [128, 512]) for i in range(2)]
    pO = [p.ps(f"l1t_pO{i}", [128, 72]) for i in range(4)]

    agxtags = [f"agxin{j}" for j in range(NTILE)]
    for qt in range(4):
        if qt > 0:
            for name, wt, buf in (("wqs", "aqt", wqs), ("wks", "akt", wks),
                                  ("wvs", "avt", wvs)):
                src = bass.AP(dr[wt + "b"].tensor, qt * 256,
                              [[1024, 128], [1024 * 128, 8], [1, 256]])
                op("sync", lambda e, src=src, buf=buf: e.dma_start(
                    out=buf[:].rearrange("p (c f) -> p c f", f=256), in_=src),
                    reads=[f"{wt}b_d"], writes=[name], dma=True, dkey=name)
        vones = bass.AP(vex, 64, [[NTILE * 260, 128], [260, NTILE], [65, 4], [1, 1]])
        op("vector", lambda e, vones=vones: e.memset(vones, 1.0),
           reads=["vex"], writes=["vex_ones"])

        def ldx(tb):
            xb = tb % 2
            for cc in range(8):
                op("sync", lambda e, cc=cc, tb=tb, xb=xb: e.dma_start_transpose(
                    xsq[xb][:, cc * 512:(cc + 1) * 512],
                    dr["agx_in"][tb * 512:(tb + 1) * 512, cc * 128:(cc + 1) * 128]),
                    reads=agxtags[tb * 4:tb * 4 + 4], writes=[f"xq{xb}_{cc}"],
                    dma=True, dkey=f"xq{xb}_{cc}")

        if qt == 0:
            ldx(0)
        for tb in range(8):
            xb = tb % 2
            if tb + 1 < 8:
                ldx(tb + 1)
            elif qt + 1 < 4:
                ldx(0)
            xr = [f"xq{xb}_{cc}" for cc in range(8)]
            for name, wbuf, dst, dtag in (("wqs", wqs, qq, "qqb"),
                                          ("wks", wks, kq, "kqb")):
                for it in range(2):
                    ctx.mm([(wbuf[:, cc * 256 + it * 128:cc * 256 + it * 128 + 128],
                             xsq[xb][:, cc * 512:(cc + 1) * 512]) for cc in range(8)],
                           pP[:], reads=[name] + xr, writes=["l1t_pP"])
                    op("scalar", lambda e: e.copy(qraw[:], pP[:]),
                       reads=["l1t_pP"], writes=["qraw"])
                    ctx.mm([(sb["ROTT"][:], qraw[:])], pR[:],
                           reads=["ROTT", "qraw"], writes=["l1t_pR"])
                    c0 = tb * 512
                    op("vector", lambda e, c0=c0: e.tensor_mul(
                        t1[:], qraw[:], cs[:, c0:c0 + 512]),
                        reads=["qraw", "COS"], writes=["t1"])
                    op("vector", lambda e, c0=c0: e.tensor_mul(
                        t2[:], pR[:], sn[:, c0:c0 + 512]),
                        reads=["l1t_pR", "SIN"], writes=["t2"])
                    op("vector", lambda e, dst=dst, it=it, c0=c0: e.tensor_add(
                        dst[:, it * T + c0:it * T + c0 + 512], t1[:], t2[:]),
                        reads=["t1", "t2"], writes=[dtag])
            for ts_ in range(4):
                ctx.mm([(xsq[xb][:, cc * 512 + ts_ * 128:cc * 512 + ts_ * 128 + 128],
                         wvs[:, cc * 256:(cc + 1) * 256]) for cc in range(8)],
                       pP[:, 0:256], reads=["wvs"] + xr, writes=["l1t_pP"])
                ti = tb * 4 + ts_
                for hh in range(4):
                    op("vector", lambda e, ti=ti, hh=hh: e.tensor_copy(
                        vex[:, ti * 260 + hh * 65:ti * 260 + hh * 65 + 64],
                        pP[:, hh * 64:hh * 64 + 64]),
                        reads=["l1t_pP"], writes=["vex"])
        for hl in range(4):
            h = qt * 4 + hl
            rb = (hl % 2) * 64
            itq = hl // 2
            for qb in range(8):
                q0 = qb * 512
                nkt = (qb + 1) * 4
                def emit_qk(kt):
                    pr = kt % 2
                    ctx.mm([(kq[rb:rb + 64, itq * T + kt * 128:itq * T + kt * 128 + 128],
                             qq[rb:rb + 64, itq * T + q0:itq * T + q0 + 512])],
                           pA2[pr][:], reads=["kqb", "qqb"], writes=[f"l1t_pA{pr}"])

                emit_qk(0)
                for kt in range(nkt):
                    pr = kt % 2
                    pA = pA2[pr]
                    Psb = Psb2[pr]
                    op("scalar", lambda e, pA=pA, Psb=Psb: e.activation(
                        Psb[:], pA[:], AF.Exp, bias=sb["cM12"][:], scale=0.125),
                       reads=[f"l1t_pA{pr}", "cM12"], writes=[f"Psb{pr}"])
                    jloc = kt * 128 - q0
                    if jloc >= 0:
                        if jloc > 0:
                            op("vector", lambda e, jloc=jloc, Psb=Psb: e.memset(
                                Psb[:, 0:jloc], 0.0), writes=[f"Psb{pr}"])
                        op("vector", lambda e, jloc=jloc, Psb=Psb: e.tensor_mul(
                            Psb[:, jloc:jloc + 128], Psb[:, jloc:jloc + 128],
                            sb["UTI"][:]),
                            reads=["UTI"], writes=[f"Psb{pr}"])
                    if kt + 1 < nkt:
                        emit_qk(kt + 1)
                    for s in range(4):
                        ctx.mm([(Psb[:, s * 128:(s + 1) * 128],
                                 vex[:, kt * 260 + hl * 65:kt * 260 + hl * 65 + 65])],
                               pO[s][:, 0:65], reads=[f"Psb{pr}", "vex", "vex_ones"],
                               writes=[f"l1t_pO{s}"],
                               start=(kt == 0), stop=(kt == nkt - 1))
                for s in range(4):
                    c = qb * 4 + s
                    op("vector", lambda e, s=s: e.reciprocal(
                        dcol[:], pO[s][:, 64:65]),
                        reads=[f"l1t_pO{s}"], writes=["l1t_dc"])
                    op("vector", lambda e, s=s, c=c, h=h: e.tensor_scalar(
                        ytm1[:, c * D + h * 64:c * D + h * 64 + 64],
                        pO[s][:, 0:64], dcol[:], None, ALU.mult),
                        reads=[f"l1t_pO{s}", "l1t_dc"], writes=["ytm1"])
    op("sync", lambda e: e.dma_start(out=dr["ytm1d"], in_=ytm1[:]),
       reads=["ytm1"], writes=["ytm1d"], dma=True)
    return [p, ypool]



def kernel(**inputs):
    ins = {k: np.ascontiguousarray(np.asarray(v)) for k, v in inputs.items()}
    x = ins["L0_attn_x"].astype(np.float32, copy=False)

    def tr(a):
        return np.ascontiguousarray(a.T)

    common = {
        "wqt": tr(ins["L0_attn_Wq"]), "wkt": tr(ins["L0_attn_Wk"]),
        "wvt": tr(ins["L0_attn_Wv"]), "wpt": tr(ins["L0_attn_Wp"]),
        "bp": ins["L0_attn_bp"].reshape(1, D),
        "rw": ins["L0_ffn_router_w"], "rb": ins["L0_ffn_router_b"].reshape(1, 8),
        "w1s": np.concatenate([ins["L0_ffn_routed_w1"], ins["L0_ffn_shared_w1"]], 0),
        "w3s": np.concatenate([ins["L0_ffn_routed_w3"], ins["L0_ffn_shared_w3"]], 0),
        "w2s": np.concatenate([ins["L0_ffn_routed_w2"], ins["L0_ffn_shared_w2"]], 0),
        "aqt": tr(ins["L1_attn_Wq"]), "akt": tr(ins["L1_attn_Wk"]),
        "avt": tr(ins["L1_attn_Wv"]), "aot": tr(ins["L1_attn_Wo"]),
        "wg": ins["L1_ffn_wg"],
    }
    common = {k: np.ascontiguousarray(v) for k, v in common.items()}
    nc = build_nc()
    in_maps = []
    for c in range(NCORES):
        m = dict(common)
        m["x0"] = x[c]
        m["w1e"] = ins["L1_ffn_w1"][c]
        m["b1e"] = np.ascontiguousarray(ins["L1_ffn_b1"][c].reshape(32, 128).T)
        m["w2e"] = ins["L1_ffn_w2"][c]
        m["b2e"] = ins["L1_ffn_b2"][c].reshape(1, D)
        es = np.zeros((1, 8), np.float32)
        es[0, c] = 1.0
        m["esel"] = es
        m["esel16"] = np.ascontiguousarray(np.tile(es, (1, 16)))
        in_maps.append(m)
    res = run_bass_kernel_spmd(nc, in_maps, core_ids=list(range(NCORES)))
    return np.stack([r["out"] for r in res.results], axis=0)


# version 28
# speedup vs baseline: 1.1198x; 1.0021x over previous
import numpy as np
import ml_dtypes
import concourse.bass as bass
import concourse.mybir as mybir
from concourse.bass_core_rust import IndirectOffsetOnAxis
from concourse.bass_utils import run_bass_kernel_spmd

F32 = mybir.dt.float32
BF16 = mybir.dt.bfloat16
I32 = mybir.dt.int32
AF = mybir.ActivationFunctionType
ALU = mybir.AluOpType
AX = mybir.AxisListType.X

NCORES = 8
T = 4096
D = 1024
CAP = 5120
NSLOT = 5632
TRASH = NSLOT - 1
NTILE = T // 128
GTILE = (NCORES * T) // 128

_L0D_SEL = list(range(10))
_BATCH_HALF = 99

ENGC = ("tensor", "vector", "scalar", "gpsimd", "sync")
DMAQ = ("sync", "gpsimd", "scalar")


class Ctx:

    def __init__(self, nc):
        self.nc = nc
        self.csem = {n: nc.alloc_semaphore(f"C_{n}") for n in ENGC}
        self.dsem = {n: nc.alloc_semaphore(f"D_{n}") for n in DMAQ}
        self.ccnt = {n: 0 for n in ENGC}
        self.dcnt = {n: 0 for n in DMAQ}
        self.wr = {}
        self.rd = {}
        self.mark = {n: {} for n in ENGC}
        self.sched = {n: [] for n in ENGC}
        self.dslot = {}

    def _sem(self, key):
        if key[0] == "c":
            return self.csem[key[1]]
        if key[0] == "s":
            return self.dslot[key[1]][0]
        return self.dsem[key[1]]

    def _need(self, waits, eng, key, cnt):
        if key == ("c", eng) and eng == "tensor":
            return
        m = self.mark[eng]
        if m.get(key, -1) >= cnt:
            return
        m[key] = cnt
        waits.append((key, cnt))

    def _bump(self, key, cnt):
        if key[0] == "d":
            return self.dcnt[key[1]]
        return cnt

    def _deps(self, eng, reads, writes):
        waits = []
        for t in reads:
            wv = self.wr.get(t)
            if wv:
                self._need(waits, eng, wv[0], self._bump(wv[0], wv[1]))
        for t in writes:
            wv = self.wr.get(t)
            if wv:
                self._need(waits, eng, wv[0], self._bump(wv[0], wv[1]))
            for (k, c) in self.rd.get(t, ()):
                self._need(waits, eng, k, self._bump(k, c))
        return waits

    def op(self, eng, fn, reads=(), writes=(), dma=False, dkey=None):
        waits = self._deps(eng, reads, writes)
        if dma:
            if dkey is not None:
                if dkey not in self.dslot:
                    self.dslot[dkey] = [self.nc.alloc_semaphore(f"S_{dkey}"), 0]
                ent = self.dslot[dkey]
                ent[1] += 16
                cnt = ent[1]
                key = ("s", dkey)
                iv = 16
            else:
                self.dcnt[eng] += 16
                cnt = self.dcnt[eng]
                key = ("d", eng)
                iv = 16
        else:
            self.ccnt[eng] += 1
            cnt = self.ccnt[eng]
            key = ("c", eng)
            iv = 1
        self.sched[eng].append((waits, fn, self._sem(key), iv))
        for t in writes:
            self.wr[t] = (key, cnt)
            self.rd[t] = []
        for t in reads:
            self.rd.setdefault(t, []).append((key, cnt))
        return cnt

    def mm(self, pairs, out_ap, reads=(), writes=(), start=True, stop=True):
        for lhsT, rhs in pairs:
            assert lhsT.base_partition() == rhs.base_partition(), \
                (lhsT.tensor.name, lhsT.base_partition(), rhs.tensor.name,
                 rhs.base_partition())
        def fn(e, pairs=tuple(pairs), out_ap=out_ap, start=start, stop=stop):
            n = len(pairs)
            ins = None
            for i, (lhsT, rhs) in enumerate(pairs):
                ins = e.matmul(out_ap, lhsT, rhs,
                               start=(start and i == 0),
                               stop=(stop and i == n - 1))
            return ins
        return self.op("tensor", fn, reads, writes)

    def barrier(self):
        targets = [(("c", n), self.ccnt[n]) for n in ENGC if self.ccnt[n] > 0]
        targets += [(("d", n), self.dcnt[n]) for n in DMAQ if self.dcnt[n] > 0]
        targets += [(("s", k), v[1]) for k, v in self.dslot.items() if v[1] > 0]
        for eng in ENGC:
            waits = []
            for key, cnt in targets:
                if key == ("c", eng):
                    continue
                self._need(waits, eng, key, cnt)
            if waits:
                self.sched[eng].append((waits, None, None, 0))

    def end_wait(self, eng, key_eng):
        key = ("d", key_eng)
        self.sched[eng].append(([(key, self.dcnt[key_eng])], None, None, 0))
        self.mark[eng][key] = self.dcnt[key_eng]

    def replay(self, block):
        def make(name):
            def run(eng):
                for waits, fn, sem, iv in self.sched[name]:
                    for key, cnt in waits:
                        eng.wait_ge(self._sem(key), cnt)
                    if fn is not None:
                        fn(eng).then_inc(sem, iv)
            return run
        block.tensor(make("tensor"))
        block.vector(make("vector"))
        block.scalar(make("scalar"))
        block.gpsimd(make("gpsimd"))
        block.sync(make("sync"))


class Pool:

    def __init__(self, nc):
        self.nc = nc
        self.guards = []

    def sb(self, name, shape, dt):
        g = self.nc.sbuf_tensor(name, shape, dt)
        h = g.__enter__()
        self.guards.append(g)
        return h

    def ps(self, name, shape, dt=F32):
        g = self.nc.psum_tensor(name, shape, dt)
        h = g.__enter__()
        self.guards.append(g)
        return h

    def free(self):
        for g in reversed(self.guards):
            g.__exit__(None, None, None)
        self.guards = []



def _consts():
    bf = ml_dtypes.bfloat16
    c = {}
    c["IDENT"] = np.eye(128, dtype=bf)
    c["IDENTF"] = np.eye(128, dtype=np.float32)
    k = np.arange(128)[:, None]
    i = np.arange(128)[None, :]
    c["UTI"] = (k <= i).astype(bf)
    c["UTS"] = (k < i).astype(bf)
    c["ONESB"] = np.ones((128, 128), dtype=bf)
    c["ONESF"] = np.ones((128, 128), dtype=np.float32)
    c["IOTAB"] = np.ascontiguousarray(
        np.broadcast_to(np.arange(128, dtype=np.float32)[None, :], (128, 128)))
    c["IOTAC"] = np.ascontiguousarray(np.arange(128, dtype=np.float32)[:, None])
    P = np.zeros((128, 128), dtype=np.float32)
    for r in range(128):
        if r % 64 < 32:
            P[r, r + 32] = -1.0
        else:
            P[r, r - 32] = 1.0
    c["ROTT"] = np.ascontiguousarray(P.T.astype(bf))
    inv = 1.0 / (10000.0 ** (np.arange(0, 64, 2, dtype=np.float64) / 64))
    t = np.arange(T, dtype=np.float64)
    fr = np.outer(inv, t)
    c["COS"] = np.ascontiguousarray(np.tile(np.cos(fr), (4, 1)).astype(bf))
    c["SIN"] = np.ascontiguousarray(np.tile(np.sin(fr), (4, 1)).astype(bf))
    cc = np.arange(256)
    pb = (cc % 32 // 8) * 8192 + (cc // 32) * 1024 + (cc % 8) * 128
    c["PBC"] = np.ascontiguousarray(pb.astype(np.float32)[:, None])
    return c


def build_nc(debug_taps=()):
    nc = bass.Bass("TRN2", target_bir_lowering=False)

    xin = nc.dram_tensor("x0", [T, D], F32, kind="ExternalInput").ap()
    w = {}
    for n in ("wqt", "wkt", "wvt", "wpt", "aqt", "akt", "avt", "aot"):
        w[n] = nc.dram_tensor(n, [D, D], F32, kind="ExternalInput").ap()
    w["bp"] = nc.dram_tensor("bp", [1, D], F32, kind="ExternalInput").ap()
    w["rw"] = nc.dram_tensor("rw", [D, 8], F32, kind="ExternalInput").ap()
    w["rb"] = nc.dram_tensor("rb", [1, 8], F32, kind="ExternalInput").ap()
    w["w1s"] = nc.dram_tensor("w1s", [10, D, 512], F32, kind="ExternalInput").ap()
    w["w3s"] = nc.dram_tensor("w3s", [10, D, 512], F32, kind="ExternalInput").ap()
    w["w2s"] = nc.dram_tensor("w2s", [10, 512, D], F32, kind="ExternalInput").ap()
    w["wg"] = nc.dram_tensor("wg", [D, 8], F32, kind="ExternalInput").ap()
    w["w1e"] = nc.dram_tensor("w1e", [D, 4096], F32, kind="ExternalInput").ap()
    w["b1e"] = nc.dram_tensor("b1e", [128, 32], F32, kind="ExternalInput").ap()
    w["w2e"] = nc.dram_tensor("w2e", [4096, D], F32, kind="ExternalInput").ap()
    w["b2e"] = nc.dram_tensor("b2e", [1, D], F32, kind="ExternalInput").ap()
    w["esel"] = nc.dram_tensor("esel", [1, 8], F32, kind="ExternalInput").ap()
    w["esel16"] = nc.dram_tensor("esel16", [1, 128], F32, kind="ExternalInput").ap()
    out = nc.dram_tensor("out", [T, D], F32, kind="ExternalOutput").ap()

    dr = {}
    dr["xn0"] = nc.dram_tensor("xn0_d", [T, D], BF16).ap()
    dr["x1"] = nc.dram_tensor("x1_d", [T, D], F32).ap()
    dr["agx_in"] = nc.dram_tensor("agx_in", [T, D], BF16)
    dr["agx_out"] = nc.dram_tensor("agx_out", [NCORES * T + 128, D], BF16)
    dr["aglg_in"] = nc.dram_tensor("aglg_in", [T, 8], F32)
    dr["aglg_out"] = nc.dram_tensor("aglg_out", [NCORES * T, 8], F32)
    dr["disp"] = nc.dram_tensor("disp_d", [NSLOT, D], BF16).ap()
    dr["eout"] = nc.dram_tensor("eout_d", [NSLOT, D], BF16).ap()
    dr["rsin"] = nc.dram_tensor("rsin_d", [NCORES * T + 128, D], BF16)
    dr["rsout"] = nc.dram_tensor("rsout_d", [T, D], BF16)
    dr["tbl"] = nc.dram_tensor("tbl_d", [256, 260], F32).ap()
    dr["w1eb"] = nc.dram_tensor("w1e_b16", [D, 4096], BF16).ap()
    dr["w2eb"] = nc.dram_tensor("w2e_b16", [4096, D], BF16).ap()
    dr["bounce"] = nc.dram_tensor("bounce", [2, 256], F32).ap()
    dr["gfm"] = nc.dram_tensor("gfm_d", [8, T], F32).ap()
    dr["ytm1d"] = nc.dram_tensor("ytm1_d", [128, NTILE * D], BF16).ap()
    dr["xa"] = nc.dram_tensor("xa_d", [T, D], F32).ap()
    for n in ("aqt", "akt", "avt", "aot"):
        dr[n + "b"] = nc.dram_tensor(n + "_b16", [D, D], BF16).ap()

    consts = _consts()
    capd = {k: nc.inline_tensor(v, k) for k, v in consts.items()}

    taps = {}
    tapspec = {
        "xn0": ([T, D], BF16), "x1p": ([T, D], F32), "x1": ([T, D], F32),
        "gatesfm": ([8, T], F32), "logits": ([T, 8], F32),
        "mmat": ([128, GTILE], F32), "posmat": ([128, GTILE], F32),
        "slotmat": ([128, GTILE], F32), "gsmat": ([128, GTILE], F32),
        "toki": ([128, NSLOT // 128], I32), "tokp": ([128, NSLOT // 128], I32),
        "gsl": ([128, NSLOT // 128], F32), "tblm": ([256, 260], F32),
        "dbgm": ([128, 256], F32), "dbgc": ([128, 8], F32),
        "offsrow": ([1, GTILE], F32), "roffif": ([1, GTILE], F32),
        "regdump": ([1, GTILE], I32),
        "disp": ([NSLOT, D], BF16),
        "eoutd": ([NSLOT, D], BF16), "rsin": ([NCORES * T, D], BF16),
        "rsout": ([T, D], BF16), "ytm0": ([T, D], BF16),
        "ytm1": ([T, D], BF16), "agx": ([NCORES * T, D], BF16),
    }
    for name in debug_taps:
        shp, dt = tapspec[name]
        taps[name] = nc.dram_tensor("tap_" + name, shp, dt, kind="ExternalOutput").ap()

    ctx = Ctx(nc)
    with nc.Block() as block:
        _emit(ctx, nc, xin, w, out, dr, capd, taps)
        ctx.replay(block)
    return nc



def _emit(ctx, nc, xin, w, out, dr, capd, taps):
    op = ctx.op
    pers = Pool(nc)
    sb = {}
    for n, shp, dt in [("IDENT", [128, 128], BF16), ("IDENTF", [128, 128], F32),
                       ("UTI", [128, 128], BF16),
                       ("UTS", [128, 128], BF16), ("ONESB", [128, 128], BF16),
                       ("ONESF", [128, 128], F32), ("IOTAB", [128, 128], F32),
                       ("IOTAC", [128, 1], F32), ("ROTT", [128, 128], BF16)]:
        sb[n] = pers.sb(n.lower(), shp, dt)
        op("sync", lambda e, n=n: e.dma_start(out=sb[n][:], in_=capd[n].ap()),
           writes=[n], dma=True)
    for n, src, shp in [("bp", "bp", [1, D]), ("rb", "rb", [1, 8]),
                        ("b2e", "b2e", [1, D]), ("b1e", "b1e", [128, 32]),
                        ("esel", "esel", [1, 8])]:
        sb[n] = pers.sb(n + "_sb", shp, F32)
        op("sync", lambda e, n=n, src=src: e.dma_start(out=sb[n][:], in_=w[src]),
           writes=[n], dma=True)
    for n, src in [("rwsb", "rw"), ("wgsb", "wg")]:
        sb[n] = pers.sb(n, [128, 64], BF16)
        src_ap = bass.AP(w[src].tensor, 0, [[8, 128], [1024, 8], [1, 8]])
        op("gpsimd", lambda e, n=n, src_ap=src_ap: e.dma_start(
            out=sb[n][:].rearrange("p (c e) -> p c e", e=8), in_=src_ap),
            writes=[n], dma=True)
    for n in ("M", "G1", "POS", "SLOT", "GS", "CUMS", "KTMP"):
        sb[n] = pers.sb(n, [128, GTILE], F32)
    NST = NSLOT // 128
    sb["TOKI"] = pers.sb("TOKI", [128, NST], I32)
    sb["TOKP"] = pers.sb("TOKP", [128, NST], I32)
    sb["GSL"] = pers.sb("GSL", [128, NST], F32)
    sb["Mb"] = pers.sb("Mb", [128, GTILE], BF16)
    sb["tot"] = pers.sb("tot", [1, GTILE], F32)
    sb["offs"] = pers.sb("offs", [1, GTILE], F32)
    sb["rmin"] = pers.sb("rmin", [1, GTILE], F32)
    sb["roffi"] = pers.sb("roffi", [1, GTILE], I32)
    sb["tot2"] = pers.sb("tot2", [128, 2], BF16)
    sb["offs2"] = pers.sb("offs2", [128, 2], F32)
    sb["scs"] = pers.sb("scs", [1, 2], F32)
    sb["sc1"] = pers.sb("sc1", [128, 8], F32)
    sb["sc2"] = pers.sb("sc2", [128, 8], F32)
    sb["sc3"] = pers.sb("sc3", [128, 8], F32)
    sb["fc"] = pers.sb("fc", [128, 8], F32)
    sb["eselb"] = pers.sb("eselb", [128, 8], F32)
    sb["cEPS"] = pers.sb("cEPS", [128, 1], F32)
    sb["cM12"] = pers.sb("cM12", [128, 1], F32)
    op("vector", lambda e: e.memset(sb["cEPS"][:], 1e-5), writes=["cEPS"])
    op("vector", lambda e: e.memset(sb["cM12"][:], -12.0), writes=["cM12"])
    psepool = Pool(nc)
    pse = psepool.ps("pse", [128, 8])
    ctx.mm([(sb["ONESF"][0:1, :], sb["esel"][:])], pse[:],
           reads=["ONESF", "esel"], writes=["pse"])
    op("vector", lambda e: e.tensor_copy(sb["eselb"][:], pse[:]),
       reads=["pse"], writes=["eselb"])
    ctx.barrier()
    psepool.free()

    cvt = [pers.sb(f"cvt{i}", [128, D], BF16) for i in range(2)]
    sb["cvt0"], sb["cvt1"] = cvt
    for n in ("aqt", "akt", "avt", "aot"):
        for c in range(8):
            q = c % 2
            src = bass.AP(w[n].tensor, c * 128 * D, [[D, 128], [1, D]])
            dst = bass.AP(dr[n + "b"].tensor, c * 128 * D, [[D, 128], [1, D]])
            op("gpsimd", lambda e, src=src, q=q: e.dma_start(
                out=cvt[q][:], in_=src),
                writes=[f"cv{q}"], dma=True, dkey=f"cv{q}")
            op("sync", lambda e, dst=dst, q=q: e.dma_start(
                out=dst, in_=cvt[q][:]),
                reads=[f"cv{q}"], writes=[f"{n}b_d"], dma=True)

    _l0_ln(ctx, nc, sb, xin, dr, taps)
    ctx.barrier()
    _l0_attn(ctx, nc, sb, xin, w, dr, taps)
    ctx.barrier()
    _l0_moe(ctx, nc, sb, w, dr, taps)
    ctx.barrier()

    _l1_pre(ctx, nc, sb, w, dr, taps)
    early, eb = _l1_attn_pre(ctx, nc, sb, dr, capd)
    _l1_routing(ctx, nc, sb, w, dr, capd, taps)
    ctx.barrier()
    rg = [list(range(NCORES))]
    op("gpsimd", lambda e: e.collective_compute(
        "AllGather", ALU.bypass, replica_groups=rg,
        ins=[dr["agx_in"].ap().opt()],
        outs=[dr["agx_out"].ap()[0:NCORES * T, :].opt()]),
        reads=[f"agxin{j}" for j in range(NTILE)], writes=["agxout"])
    if "agx" in taps:
        op("sync", lambda e: e.dma_start(
            out=taps["agx"], in_=dr["agx_out"].ap()[0:NCORES * T, :]),
            reads=["agxout"], dma=True)
    pd = Pool(nc)
    dtags = _l1_dispatch(ctx, nc, sb, w, dr, taps, pd)
    apools = _l1_attn(ctx, nc, sb, w, dr, capd, taps, early, eb)
    ctx.barrier()
    for pp in apools:
        pp.free()
    pd.free()
    early.free()
    etags = _l1_ffn(ctx, nc, sb, w, dr, dtags, taps)
    _l1_combine_final(ctx, nc, sb, w, out, dr, etags, taps)

    ctx.end_wait("sync", "sync")
    ctx.end_wait("sync", "gpsimd")
    ctx.end_wait("sync", "scalar")


def _ln_tile(ctx, sb, xt, xnb, sq):
    op = ctx.op
    F = sb["fc"]
    op("vector", lambda e: e.reduce_sum(F[:, 0:1], xt[:], AX),
       reads=["xt"], writes=["f0"])
    op("vector", lambda e: e.tensor_scalar(F[:, 1:2], F[:, 0:1], 1.0 / D, None, ALU.mult),
       reads=["f0"], writes=["f1"])
    op("scalar", lambda e: e.activation(sq[:], xt[:], AF.Square, accum_out=F[:, 2:3]),
       reads=["xt"], writes=["sq", "f2"])
    op("vector", lambda e: e.tensor_mul(F[:, 3:4], F[:, 1:2], F[:, 1:2]),
       reads=["f1"], writes=["f3"])
    op("vector", lambda e: e.scalar_tensor_tensor(F[:, 3:4], F[:, 2:3], 1.0 / D,
                                                  F[:, 3:4], ALU.mult, ALU.subtract),
       reads=["f2", "f3"], writes=["f3"])
    op("scalar", lambda e: e.activation(F[:, 4:5], F[:, 3:4], AF.Sqrt,
                                        bias=sb["cEPS"][:]),
       reads=["f3", "cEPS"], writes=["f4"])
    op("vector", lambda e: e.reciprocal(F[:, 5:6], F[:, 4:5]),
       reads=["f4"], writes=["f5"])
    op("vector", lambda e: e.tensor_scalar(xnb[:], xt[:], F[:, 1:2], F[:, 5:6],
                                           ALU.subtract, ALU.mult),
       reads=["xt", "f1", "f5"], writes=["xnb"])



def _l0_ln(ctx, nc, sb, xin, dr, taps):
    op = ctx.op
    p = Pool(nc)
    NB = 2
    xt = [p.sb(f"l0a_xt{i}", [128, D], F32) for i in range(NB)]
    xnb = [p.sb(f"l0a_xnb{i}", [128, D], BF16) for i in range(NB)]
    sq = [p.sb(f"l0a_sq{i}", [128, D], F32) for i in range(NB)]
    fcb = [p.sb(f"l0a_fc{i}", [128, 8], F32) for i in range(NB)]

    def front(j):
        q = j % NB
        op("sync", lambda e, j=j, q=q: e.dma_start(
            out=xt[q][:], in_=xin[j * 128:(j + 1) * 128, :]),
            writes=[f"b_xt{q}"], dma=True, dkey=f"b_xt{q}")
        _ln_tile_r(ctx, sb, xt[q], xnb[q], sq[q], fcb[q], f"b{q}")

    def back(j):
        q = j % NB
        op("sync", lambda e, j=j, q=q: e.dma_start(
            out=dr["xn0"][j * 128:(j + 1) * 128, :], in_=xnb[q][:]),
            reads=[f"b_xnb{q}"], writes=[f"xn0_{j}"], dma=True)

    front(0)
    for j in range(NTILE):
        if j + 1 < NTILE:
            front(j + 1)
        back(j)
    if "xn0" in taps:
        op("sync", lambda e: e.dma_start(out=taps["xn0"], in_=dr["xn0"]),
           reads=[f"xn0_{j}" for j in range(NTILE)], dma=True)
    if "gatesfm" in taps:
        op("sync", lambda e: e.dma_start(out=taps["gatesfm"], in_=dr["gfm"]),
           reads=["gfm_d"], dma=True)
    ctx.barrier()
    p.free()



def _l0_attn(ctx, nc, sb, xin, w, dr, taps):
    op = ctx.op
    ypool = Pool(nc)
    ytm = ypool.sb("ytm", [128, NTILE * D], BF16)
    xst = ypool.sb("l0b_xst", [128, 8 * T], BF16)

    p = Pool(nc)
    qg = p.sb("l0b_q", [128, T], BF16)
    kg = p.sb("l0b_k", [128, T], BF16)
    ktg = p.sb("l0b_kt", [128, T], BF16)
    vex = p.sb("l0b_vex", [128, NTILE * 130], BF16)
    phs = p.sb("l0b_phs", [128, T], BF16)
    ssb = p.sb("l0b_s", [128, 72], BF16)
    asb2 = [p.sb(f"l0b_a{i}", [128, 128], BF16) for i in range(2)]
    wq = p.sb("l0b_wq", [128, 1024], BF16)
    wk = p.sb("l0b_wk", [128, 1024], BF16)
    wv = p.sb("l0b_wv", [128, 1024], BF16)
    dcol = p.sb("l0b_dcol", [128, 4], F32)
    pbank = [p.ps(f"l0b_pb{i}", [128, 512]) for i in range(4)]
    pS2 = [p.ps(f"l0b_pS{i}", [64, 72]) for i in range(2)]
    pT2 = [p.ps(f"l0b_pT{i}", [128, 128], BF16) for i in range(2)]

    xn0tags = [f"xn0_{j}" for j in range(NTILE)]
    for tb in range(8):
        for cc in range(8):
            op("sync", lambda e, cc=cc, tb=tb: e.dma_start_transpose(
                xst[:, cc * T + tb * 512:cc * T + tb * 512 + 512],
                dr["xn0"][tb * 512:(tb + 1) * 512, cc * 128:(cc + 1) * 128]),
                reads=xn0tags[tb * 4:tb * 4 + 4], writes=[f"xst{tb}"], dma=True)
    for g in range(8):
        for name, wt, buf in (("wq", "wqt", wq), ("wk", "wkt", wk), ("wv", "wvt", wv)):
            src = bass.AP(w[wt].tensor, g * 128, [[1024, 128], [1024 * 128, 8], [1, 128]])
            op("gpsimd", lambda e, src=src, buf=buf: e.dma_start(
                out=buf[:].rearrange("p (c f) -> p c f", f=128), in_=src),
                writes=[name], dma=True, dkey=name)
        vones = bass.AP(vex, 64, [[NTILE * 130, 128], [130, NTILE], [65, 2], [1, 1]])
        op("vector", lambda e, vones=vones: e.memset(vones, 1.0),
           reads=["vex"], writes=["vex_ones"])

        pj = 0
        for tb in range(8):
            xr = [f"xst{tb}"]
            for name, buf, dst, dtag in (("wq", wq, qg, "qg"), ("wk", wk, kg, "kg")):
                pb_ = pbank[pj % 4]
                ptag = f"pbank{pj % 4}"
                pj += 1
                ctx.mm([(buf[:, cc * 128:(cc + 1) * 128],
                         xst[:, cc * T + tb * 512:cc * T + tb * 512 + 512])
                        for cc in range(8)], pb_[:],
                       reads=[name] + xr, writes=[ptag])
                op("scalar", lambda e, dst=dst, tb=tb, pb_=pb_: e.copy(
                    dst[:, tb * 512:(tb + 1) * 512], pb_[:]),
                   reads=[ptag], writes=[dtag])
            for ts_ in range(4):
                pb_ = pbank[pj % 4]
                ptag = f"pbank{pj % 4}"
                pj += 1
                ctx.mm([(xst[:, cc * T + tb * 512 + ts_ * 128:
                             cc * T + tb * 512 + ts_ * 128 + 128],
                         wv[:, cc * 128:(cc + 1) * 128]) for cc in range(8)],
                       pb_[:, 0:128], reads=["wv"] + xr, writes=[ptag])
                ti = tb * 4 + ts_
                for hh in range(2):
                    op("scalar", lambda e, ti=ti, hh=hh, pb_=pb_: e.copy(
                        vex[:, ti * 130 + hh * 65:ti * 130 + hh * 65 + 64],
                        pb_[:, hh * 64:hh * 64 + 64]),
                        reads=[ptag], writes=["vex"])
        for dtag, buf in (("qg", qg), ("kg", kg)):
            op("vector", lambda e, buf=buf: e.tensor_scalar(phs[:], buf[:], 0.0,
                                                            None, ALU.min),
               reads=[dtag], writes=["phs"])
            op("scalar", lambda e: e.activation(phs[:], phs[:], AF.Exp),
               reads=["phs"], writes=["phs"])
            op("vector", lambda e, buf=buf: e.scalar_tensor_tensor(
                buf[:], buf[:], 0.0, phs[:], ALU.max, ALU.add),
               reads=[dtag, "phs"], writes=[dtag])
        for c in range(NTILE):
            tp_ = c % 2
            op("tensor", lambda e, c=c, tp_=tp_: e.transpose(
                pT2[tp_][:], kg[:, c * 128:(c + 1) * 128], sb["IDENT"][:]),
               reads=["kg", "IDENT"], writes=[f"pT{tp_}"])
            op("vector", lambda e, c=c, tp_=tp_: e.tensor_copy(
                ktg[:, c * 128:(c + 1) * 128], pT2[tp_][:]),
               reads=[f"pT{tp_}"], writes=["ktg"])
        for c in range(NTILE):
            for hh in range(2):
                b = hh * 64
                ctx.mm([(kg[b:b + 64, c * 128:(c + 1) * 128],
                         qg[b:b + 64, c * 128:(c + 1) * 128])],
                       pbank[hh][:, 0:128], reads=["kg", "qg"], writes=[f"pbank{hh}"])
            for hh in range(2):
                op("vector", lambda e, hh=hh: e.tensor_mul(
                    asb2[hh][:], pbank[hh][:, 0:128], sb["UTI"][:]),
                   reads=[f"pbank{hh}", "UTI"], writes=[f"asb{hh}"])
            for hh in range(2):
                b = hh * 64
                pairs = [(asb2[hh][:], vex[:, c * 130 + hh * 65:c * 130 + hh * 65 + 65])]
                rd = [f"asb{hh}", "vex", "vex_ones"]
                if c > 0:
                    pairs.append((qg[b:b + 64, c * 128:(c + 1) * 128],
                                  ssb[b:b + 64, 0:65]))
                    rd += ["qg", f"ssb{hh}"]
                ctx.mm(pairs, pbank[2 + hh][:, 0:65], reads=rd,
                       writes=[f"pbank{2 + hh}"])
                ctx.mm([(ktg[:, c * 128 + b:c * 128 + b + 64],
                         vex[:, c * 130 + hh * 65:c * 130 + hh * 65 + 65])],
                       pS2[hh][0:64, 0:65], reads=["ktg", "vex", "vex_ones"],
                       writes=[f"pS2{hh}"], start=(c == 0), stop=(c == NTILE - 1))
            for hh in range(2):
                b = hh * 64
                op("vector", lambda e, hh=hh: e.tensor_scalar(
                    dcol[:, 2 * hh:2 * hh + 1], pbank[2 + hh][:, 64:65],
                    1e-6, None, ALU.add),
                   reads=[f"pbank{2 + hh}"], writes=[f"dc0{hh}"])
                op("vector", lambda e, hh=hh: e.reciprocal(
                    dcol[:, 2 * hh + 1:2 * hh + 2], dcol[:, 2 * hh:2 * hh + 1]),
                   reads=[f"dc0{hh}"], writes=[f"dc1{hh}"])
                op("scalar", lambda e, c=c, g=g, hh=hh: e.mul(
                    ytm[:, c * D + g * 128 + hh * 64:c * D + g * 128 + hh * 64 + 64],
                    pbank[2 + hh][:, 0:64], dcol[:, 2 * hh + 1:2 * hh + 2]),
                    reads=[f"pbank{2 + hh}", f"dc1{hh}"], writes=["ytm"])
                if c < NTILE - 1:
                    op("vector", lambda e, hh=hh, b=b: e.tensor_copy(
                        ssb[b:b + 64, 0:65], pS2[hh][0:64, 0:65]),
                       reads=[f"pS2{hh}"], writes=[f"ssb{hh}"])
    if "ytm0" in taps:
        op("sync", lambda e: e.dma_start(
            out=taps["ytm0"].rearrange("(j p) d -> p (j d)", p=128), in_=ytm[:]),
            reads=["ytm"], dma=True)
    ctx.barrier()
    p.free()

    p = Pool(nc)
    yf = p.sb("l0c_yf", [128, 1024], BF16)
    x0t = p.sb("l0c_x0", [128, D], F32)
    x1t = p.sb("l0c_x1", [128, D], F32)
    wp = p.sb("l0c_wp", [128, 8192], BF16)
    F = p.sb("l0c_F", [128, 8], F32)
    sc1 = p.sb("l0c_sc1", [128, 8], F32)
    sc2 = p.sb("l0c_sc2", [128, 8], F32)
    sc3 = p.sb("l0c_sc3", [128, 8], F32)
    gst = p.sb("l0c_gst", [8, 128], F32)
    pT = p.ps("l0c_pT", [128, 128], BF16)
    pU = p.ps("l0c_pU", [128, 512])
    plg = p.ps("l0c_plg", [128, 8])
    pg = p.ps("l0c_pg", [8, 128])
    src = bass.AP(w["wpt"].tensor, 0, [[1024, 128], [1024 * 128, 8], [1, 1024]])
    op("gpsimd", lambda e, src=src: e.dma_start(
        out=wp[:].rearrange("p (c f) -> p c f", f=1024), in_=src),
        writes=["wp"], dma=True, dkey="wp")

    def gates(j):
        ctx.mm([(xst[:, cc * T + j * 128:cc * T + j * 128 + 128],
                 sb["rwsb"][:, cc * 8:(cc + 1) * 8]) for cc in range(8)],
               plg[:], reads=[f"xst{j // 4}", "rwsb"],
               writes=["b_plg"], stop=False)
        ctx.mm([(sb["ONESF"][0:1, :], sb["rb"][:])], plg[:],
               reads=["ONESF", "rb"], writes=["b_plg"], start=False)
        op("vector", lambda e: e.reduce_max(F[:, 0:1], plg[:], AX),
           reads=["b_plg"], writes=["b_g0"])
        op("vector", lambda e: e.tensor_scalar(sc1[:], plg[:], F[:, 0:1],
                                               None, ALU.is_ge),
           reads=["b_plg", "b_g0"], writes=["b_sc1"])
        op("vector", lambda e: e.scalar_tensor_tensor(
            sc2[:], sc1[:], -1e30, plg[:], ALU.mult, ALU.add),
           reads=["b_sc1", "b_plg"], writes=["b_sc2"])
        op("vector", lambda e: e.reduce_max(F[:, 1:2], sc2[:], AX),
           reads=["b_sc2"], writes=["b_g1"])
        op("vector", lambda e: e.tensor_scalar(sc1[:], plg[:], F[:, 1:2],
                                               None, ALU.is_ge),
           reads=["b_plg", "b_g1"], writes=["b_sc1"])
        op("vector", lambda e: e.tensor_scalar(F[:, 2:3], F[:, 0:1], -1.0,
                                               None, ALU.mult),
           reads=["b_g0"], writes=["b_g2"])
        op("scalar", lambda e: e.activation(sc2[:], plg[:], AF.Exp,
                                            bias=F[:, 2:3]),
           reads=["b_plg", "b_g2"], writes=["b_sc2"])
        op("vector", lambda e: e.tensor_mul(sc3[:], sc2[:], sc1[:]),
           reads=["b_sc2", "b_sc1"], writes=["b_sc3"])
        op("vector", lambda e: e.reduce_sum(F[:, 3:4], sc3[:], AX),
           reads=["b_sc3"], writes=["b_g3"])
        op("vector", lambda e: e.reciprocal(F[:, 4:5], F[:, 3:4]),
           reads=["b_g3"], writes=["b_g4"])
        op("vector", lambda e: e.tensor_scalar(sc3[:], sc3[:], F[:, 4:5],
                                               None, ALU.mult),
           reads=["b_sc3", "b_g4"], writes=["b_sc3"])
        op("tensor", lambda e: e.transpose(pg[:], sc3[:], sb["IDENTF"][:]),
           reads=["b_sc3", "IDENTF"], writes=["b_pg"])
        op("vector", lambda e: e.tensor_copy(gst[:], pg[:]),
           reads=["b_pg"], writes=["b_gst"])
        gap = bass.AP(dr["gfm"].tensor, j * 128, [[T, 8], [1, 128]])
        op("sync", lambda e, gap=gap: e.dma_start(out=gap, in_=gst[:]),
           reads=["b_gst"], writes=["gfm_d"], dma=True)

    for j in range(NTILE):
        r0 = j * 128
        gates(j)
        for cc in range(8):
            op("tensor", lambda e, j=j, cc=cc: e.transpose(
                pT[:], ytm[:, j * D + cc * 128:j * D + (cc + 1) * 128], sb["IDENT"][:]),
               reads=["ytm", "IDENT"], writes=["l0c_pT"])
            op("vector", lambda e, cc=cc: e.tensor_copy(
                yf[:, cc * 128:(cc + 1) * 128], pT[:]),
               reads=["l0c_pT"], writes=[f"yf{cc}"])
        op("sync", lambda e, r0=r0: e.dma_start(out=x0t[:], in_=xin[r0:r0 + 128, :]),
           writes=["x0t"], dma=True)
        for ih in range(2):
            ctx.mm([(yf[:, cc * 128:(cc + 1) * 128],
                     wp[:, cc * 1024 + ih * 512:cc * 1024 + ih * 512 + 512])
                    for cc in range(8)], pU[:],
                   reads=[f"yf{cc}" for cc in range(8)] + ["wp"],
                   writes=["l0c_pU"], stop=False)
            ctx.mm([(sb["ONESF"][0:1, :], sb["bp"][0:1, ih * 512:ih * 512 + 512])], pU[:],
                   reads=["ONESF", "bp"], writes=["l0c_pU"], start=False)
            op("vector", lambda e, ih=ih: e.tensor_add(
                x1t[:, ih * 512:(ih + 1) * 512], pU[:], x0t[:, ih * 512:(ih + 1) * 512]),
                reads=["l0c_pU", "x0t"], writes=["x1t"])
        op("sync", lambda e, r0=r0: e.dma_start(out=dr["x1"][r0:r0 + 128, :], in_=x1t[:]),
           reads=["x1t"], writes=[f"x1_{j}"], dma=True)
    if "x1p" in taps:
        op("sync", lambda e: e.dma_start(out=taps["x1p"], in_=dr["x1"]),
           reads=[f"x1_{j}" for j in range(NTILE)], dma=True)
    ctx.barrier()
    p.free()
    ypool.free()



def _l0_moe(ctx, nc, sb, w, dr, taps):
    op = ctx.op
    p = Pool(nc)
    w1p = p.sb("l0d_w1", [128, 5 * 4096], BF16)
    w3p = p.sb("l0d_w3", [128, 5 * 4096], BF16)
    w2p = p.sb("l0d_w2", [128, 5 * 4096], BF16)
    xnd2 = [p.sb(f"l0d_xn{i}", [128, 2048], BF16) for i in range(2)]
    hsb2 = [p.sb(f"l0d_h{i}", [128, 1024], BF16) for i in range(2)]
    silu2 = [p.sb(f"l0d_si{i}", [128, 256], BF16) for i in range(2)]
    gG2 = [p.sb(f"l0d_g{i}", [128, 5 * 256], F32) for i in range(2)]
    x1t = p.sb("l0d_x1", [128, 2 * D], F32)
    o1 = [p.sb(f"l0d_o1{i}", [128, 512], F32) for i in range(2)]
    pa2 = [p.ps(f"l0d_pa{i}", [128, 256]) for i in range(2)]
    pb2 = [p.ps(f"l0d_pb{i}", [128, 256]) for i in range(2)]
    py = [p.ps(f"l0d_py{i}", [128, 512]) for i in range(4)]
    xn0tags = [f"xn0_{j}" for j in range(NTILE)]

    def load_xnd(tb):
        xb = tb % 2
        for cc in range(8):
            op("sync", lambda e, cc=cc, tb=tb, xb=xb: e.dma_start_transpose(
                xnd2[xb][:, cc * 256:(cc + 1) * 256],
                dr["xn0"][tb * 256:tb * 256 + 256, cc * 128:(cc + 1) * 128]),
                reads=xn0tags[tb * 2:tb * 2 + 2], writes=[f"xnd{xb}_{cc}"], dma=True,
                dkey=f"xnd{xb}")

    def load_gates(ps_, tb):
        xb = tb % 2
        nr = 5 if ps_ == 0 else 3
        for e5 in range(nr):
            ex = ps_ * 5 + e5
            src = bass.AP(dr["gfm"].tensor, ex * T + tb * 256,
                          [[0, 128], [1, 256]])
            wt = [f"gG{xb}_{e5}"] + ([f"gGt{xb}"] if e5 == nr - 1 else [])
            op("scalar", lambda e, src=src, xb=xb, e5=e5: e.dma_start(
                out=gG2[xb][:, e5 * 256:(e5 + 1) * 256], in_=src),
                reads=["gfm_d"], writes=wt, dma=True, dkey=f"gG{xb}")

    for ps_ in range(2):
        for e5 in range(5):
            ex = ps_ * 5 + e5
            for name, wsrc, buf, inner, nch in (("w1p", "w1s", w1p, 512, 8),
                                                ("w3p", "w3s", w3p, 512, 8),
                                                ("w2p", "w2s", w2p, 1024, 4)):
                src = bass.AP(w[wsrc].tensor, ex * D * 512,
                              [[inner, 128], [inner * 128, nch], [1, inner]])
                op("gpsimd", lambda e, src=src, buf=buf, e5=e5, inner=inner:
                   e.dma_start(out=buf[:, e5 * 4096:(e5 + 1) * 4096]
                               .rearrange("p (c f) -> p c f", f=inner), in_=src),
                   writes=[name + str(e5)], dma=True, dkey=name + str(e5))
        if ps_ == 0:
            load_xnd(0)
        load_gates(ps_, 0)
        for tb in range(16):
            t0 = tb * 256
            xb = tb % 2
            xr = [f"xnd{xb}_{cc}" for cc in range(8)]
            if tb + 1 < 16 or ps_ == 0:
                load_xnd(tb + 1 if tb + 1 < 16 else 0)
            if tb + 1 < 16:
                load_gates(ps_, tb + 1)
            elif ps_ == 0:
                load_gates(1, 0)
            for e5 in range(5):
                ex = ps_ * 5 + e5
                hp = e5 % 2
                hsb = hsb2[hp]
                for ft in range(4):
                    fp = ft % 2
                    ctx.mm([(w1p[:, e5 * 4096 + cc * 512 + ft * 128:
                             e5 * 4096 + cc * 512 + ft * 128 + 128],
                             xnd2[xb][:, cc * 256:(cc + 1) * 256]) for cc in range(8)],
                           pa2[fp][:], reads=[f"w1p{e5}"] + xr,
                           writes=[f"l0d_pa{fp}"])
                    ctx.mm([(w3p[:, e5 * 4096 + cc * 512 + ft * 128:
                             e5 * 4096 + cc * 512 + ft * 128 + 128],
                             xnd2[xb][:, cc * 256:(cc + 1) * 256]) for cc in range(8)],
                           pb2[fp][:], reads=[f"w3p{e5}"] + xr,
                           writes=[f"l0d_pb{fp}"])
                    op("scalar", lambda e, fp=fp: e.activation(
                        silu2[fp][:], pa2[fp][:], AF.Silu),
                       reads=[f"l0d_pa{fp}"], writes=[f"silu{fp}"])
                    if ex < 8:
                        op("vector", lambda e, ft=ft, e5=e5, fp=fp, hsb=hsb, xb=xb:
                            e.tensor_mul(
                                hsb[:, ft * 256:(ft + 1) * 256], pb2[fp][:],
                                gG2[xb][:, e5 * 256:(e5 + 1) * 256]),
                            reads=[f"l0d_pb{fp}", f"gG{xb}_{e5}", f"gGt{xb}"],
                            writes=[f"hsb{hp}"])
                        op("vector", lambda e, ft=ft, fp=fp, hsb=hsb: e.tensor_mul(
                            hsb[:, ft * 256:(ft + 1) * 256],
                            hsb[:, ft * 256:(ft + 1) * 256], silu2[fp][:]),
                            reads=[f"hsb{hp}", f"silu{fp}"], writes=[f"hsb{hp}"])
                    else:
                        op("vector", lambda e, ft=ft, fp=fp, hsb=hsb: e.tensor_mul(
                            hsb[:, ft * 256:(ft + 1) * 256], pb2[fp][:], silu2[fp][:]),
                            reads=[f"l0d_pb{fp}", f"silu{fp}"], writes=[f"hsb{hp}"])
                for tsub in range(2):
                    for mh in range(2):
                        ctx.mm([(hsb[:, fc * 256 + tsub * 128:fc * 256 + tsub * 128 + 128],
                                 w2p[:, e5 * 4096 + fc * 1024 + mh * 512:
                                     e5 * 4096 + fc * 1024 + mh * 512 + 512])
                                for fc in range(4)], py[tsub * 2 + mh][:],
                               reads=[f"hsb{hp}", f"w2p{e5}"],
                               writes=[f"l0d_py{tsub * 2 + mh}"],
                               start=(e5 == 0), stop=(e5 == 4))
            x1src = bass.AP(dr["x1"].tensor, t0 * D, [[D, 128], [128 * D, 2], [1, D]])
            op("sync", lambda e, x1src=x1src: e.dma_start(
                out=x1t[:].rearrange("p (k m) -> p k m", m=D), in_=x1src),
                reads=[f"x1_{tb * 2}", f"x1_{tb * 2 + 1}"], writes=["l0d_x1t"], dma=True, dkey="l0d_x1t")
            for tsub in range(2):
                for mh in range(2):
                    ob = mh
                    op("vector", lambda e, tsub=tsub, mh=mh, ob=ob: e.tensor_add(
                        o1[ob][:], py[tsub * 2 + mh][:],
                        x1t[:, tsub * D + mh * 512:tsub * D + mh * 512 + 512]),
                        reads=[f"l0d_py{tsub * 2 + mh}", "l0d_x1t"],
                        writes=[f"l0d_o1{ob}"])
                    op("sync", lambda e, t0=t0, tsub=tsub, mh=mh, ob=ob: e.dma_start(
                        out=dr["x1"][t0 + tsub * 128:t0 + tsub * 128 + 128,
                                     mh * 512:mh * 512 + 512], in_=o1[ob][:]),
                        reads=[f"l0d_o1{ob}"], writes=[f"x1_{tb * 2 + tsub}"], dma=True)
        ctx.barrier()
    if "x1" in taps:
        op("sync", lambda e: e.dma_start(out=taps["x1"], in_=dr["x1"]),
           reads=[f"x1_{j}" for j in range(NTILE)], dma=True)
    ctx.barrier()
    p.free()


def _l1_pre(ctx, nc, sb, w, dr, taps):
    op = ctx.op
    p = Pool(nc)
    NB = 2
    xt = [p.sb(f"l1a_xt{i}", [128, D], F32) for i in range(NB)]
    xnb = [p.sb(f"l1a_xnb{i}", [128, D], BF16) for i in range(NB)]
    xnf = [p.sb(f"l1a_xnf{i}", [128, D], BF16) for i in range(NB)]
    sq = [p.sb(f"l1a_sq{i}", [128, D], F32) for i in range(NB)]
    lg = [p.sb(f"l1a_lg{i}", [128, 8], F32) for i in range(NB)]
    fcb = [p.sb(f"l1a_fc{i}", [128, 8], F32) for i in range(NB)]
    pt = [p.ps(f"l1a_pt{i}", [128, 128], BF16) for i in range(NB)]
    plg = [p.ps(f"l1a_plg{i}", [128, 8]) for i in range(NB)]

    def front(j):
        q = j % NB
        op("sync", lambda e, j=j, q=q: e.dma_start(out=xt[q][:],
                                                   in_=dr["x1"][j * 128:(j + 1) * 128, :]),
           reads=[f"x1_{j}"], writes=[f"a_xt{q}"], dma=True, dkey=f"a_xt{q}")
        _ln_tile_r(ctx, sb, xt[q], xnb[q], sq[q], fcb[q], f"a{q}")

    def back(j):
        q = j % NB
        op("sync", lambda e, j=j, q=q: e.dma_start(
            out=dr["agx_in"][j * 128:(j + 1) * 128, :], in_=xnb[q][:]),
            reads=[f"a_xnb{q}"], writes=[f"agxin{j}"], dma=True)
        for cc in range(8):
            op("tensor", lambda e, cc=cc, q=q: e.transpose(
                pt[q][:], xnb[q][:, cc * 128:(cc + 1) * 128], sb["IDENT"][:]),
               reads=[f"a_xnb{q}", "IDENT"], writes=[f"a_pt{q}"])
            op("scalar", lambda e, cc=cc, q=q: e.copy(
                xnf[q][:, cc * 128:(cc + 1) * 128], pt[q][:]),
               reads=[f"a_pt{q}"], writes=[f"a_xnf{q}_{cc}"])
        ctx.mm([(xnf[q][:, cc * 128:(cc + 1) * 128], sb["wgsb"][:, cc * 8:(cc + 1) * 8])
                for cc in range(8)], plg[q][:],
               reads=[f"a_xnf{q}_{cc}" for cc in range(8)] + ["wgsb"],
               writes=[f"a_plg{q}"])
        op("vector", lambda e, q=q: e.tensor_copy(lg[q][:], plg[q][:]),
           reads=[f"a_plg{q}"], writes=[f"a_lg{q}"])
        op("sync", lambda e, j=j, q=q: e.dma_start(
            out=dr["aglg_in"][j * 128:(j + 1) * 128, :], in_=lg[q][:]),
            reads=[f"a_lg{q}"], writes=[f"aglgin{j}"], dma=True)

    front(0)
    for j in range(NTILE):
        if j + 1 < NTILE:
            front(j + 1)
        back(j)
    ctx.barrier()
    p.free()
    rg = [list(range(NCORES))]
    op("gpsimd", lambda e: e.collective_compute(
        "AllGather", ALU.bypass, replica_groups=rg,
        ins=[dr["aglg_in"].ap().opt()], outs=[dr["aglg_out"].ap().opt()]),
        reads=[f"aglgin{j}" for j in range(NTILE)], writes=["aglgout"])
    if "logits" in taps:
        op("sync", lambda e: e.dma_start(out=taps["logits"], in_=dr["aglg_in"].ap()),
           reads=[f"aglgin{j}" for j in range(NTILE)], dma=True)


def _ln_tile_r(ctx, sb, xt, xnb, sq, F, pfx):
    op = ctx.op
    op("vector", lambda e: e.reduce_sum(F[:, 0:1], xt[:], AX),
       reads=[f"{pfx}_xt" if False else f"{pfx[0]}_xt{pfx[1]}"], writes=[f"{pfx}_f0"])
    op("vector", lambda e: e.tensor_scalar(F[:, 1:2], F[:, 0:1], 1.0 / D, None,
                                           ALU.mult),
       reads=[f"{pfx}_f0"], writes=[f"{pfx}_f1"])
    op("scalar", lambda e: e.activation(sq[:], xt[:], AF.Square, accum_out=F[:, 2:3]),
       reads=[f"{pfx[0]}_xt{pfx[1]}"], writes=[f"{pfx}_sq", f"{pfx}_f2"])
    op("vector", lambda e: e.tensor_mul(F[:, 3:4], F[:, 1:2], F[:, 1:2]),
       reads=[f"{pfx}_f1"], writes=[f"{pfx}_f3"])
    op("vector", lambda e: e.scalar_tensor_tensor(F[:, 3:4], F[:, 2:3], 1.0 / D,
                                                  F[:, 3:4], ALU.mult, ALU.subtract),
       reads=[f"{pfx}_f2", f"{pfx}_f3"], writes=[f"{pfx}_f3"])
    op("scalar", lambda e: e.activation(F[:, 4:5], F[:, 3:4], AF.Sqrt,
                                        bias=sb["cEPS"][:]),
       reads=[f"{pfx}_f3", "cEPS"], writes=[f"{pfx}_f4"])
    op("vector", lambda e: e.reciprocal(F[:, 5:6], F[:, 4:5]),
       reads=[f"{pfx}_f4"], writes=[f"{pfx}_f5"])
    op("vector", lambda e: e.tensor_scalar(xnb[:], xt[:], F[:, 1:2], F[:, 5:6],
                                           ALU.subtract, ALU.mult),
       reads=[f"{pfx[0]}_xt{pfx[1]}", f"{pfx}_f1", f"{pfx}_f5"],
       writes=[f"{pfx[0]}_xnb{pfx[1]}"])



def _batch_done(ctx, nc, sb, dr, taps, p):
    ctx.barrier()
    p.free()


def _l1_routing(ctx, nc, sb, w, dr, capd, taps):
    op = ctx.op
    p = Pool(nc)
    lgt = p.sb("l1r_lg", [128, 128], F32)
    pc = p.ps("l1r_pc", [128, GTILE])
    pt1 = p.ps("l1r_pt1", [1, GTILE])
    pb2 = p.ps("l1r_pb2", [128, 2])
    pcs = p.ps("l1r_pcs", [1, 2])
    pbc = p.ps("l1r_pbc", [128, 1])
    F = sb["fc"]

    es16 = p.sb("l1r_es16", [1, 128], F32)
    esb16 = p.sb("l1r_esb16", [128, 128], F32)
    lgt2 = [p.sb(f"l1r_lg2{i}", [128, 128], F32) for i in range(2)]
    LL = [p.sb(f"l1r_LL{i}", [128, 128], F32) for i in range(2)]
    EX = [p.sb(f"l1r_EX{i}", [128, 128], F32) for i in range(2)]
    MYP = [p.sb(f"l1r_MYP{i}", [128, 128], F32) for i in range(2)]
    M1 = [p.sb(f"l1r_M1{i}", [128, 16], F32) for i in range(2)]
    S16 = [p.sb(f"l1r_S16{i}", [128, 16], F32) for i in range(2)]
    MY16 = [p.sb(f"l1r_MY16{i}", [128, 16], F32) for i in range(2)]
    peb = p.ps("l1r_peb", [128, 128])
    op("sync", lambda e: e.dma_start(out=es16[:], in_=w["esel16"]),
       writes=["es16"], dma=True)
    ctx.mm([(sb["ONESF"][0:1, :], es16[:])], peb[:], reads=["ONESF", "es16"],
           writes=["peb"])
    op("vector", lambda e: e.tensor_copy(esb16[:], peb[:]),
       reads=["peb"], writes=["esb16"])
    for jj in range(16):
        q = jj % 2
        src = bass.AP(dr["aglg_out"].ap().tensor, jj * 16 * 128 * 8,
                      [[8, 128], [128 * 8, 16], [1, 8]])
        op("sync", lambda e, src=src, q=q: e.dma_start(
            out=lgt2[q][:].rearrange("p (j e) -> p j e", e=8), in_=src),
            reads=["aglgout"], writes=[f"lgt{q}"], dma=True, dkey=f"lgt{q}")
        l3 = lgt2[q][:].rearrange("p (j e) -> p j e", e=8)
        op("vector", lambda e, q=q, l3=l3: e.reduce_max(M1[q][:], l3, AX),
           reads=[f"lgt{q}"], writes=[f"rM1{q}"])
        m1b = M1[q][:].rearrange("p (j o) -> p j o", o=1).to_broadcast([128, 16, 8])
        ll3 = LL[q][:].rearrange("p (j e) -> p j e", e=8)
        op("vector", lambda e, q=q, l3=l3, m1b=m1b, ll3=ll3: e.tensor_tensor(
            ll3, l3, m1b, ALU.subtract),
           reads=[f"lgt{q}", f"rM1{q}"], writes=[f"rLL{q}"])
        op("scalar", lambda e, q=q: e.activation(EX[q][:], LL[q][:], AF.Exp),
           reads=[f"rLL{q}"], writes=[f"rEX{q}"])
        e3 = EX[q][:].rearrange("p (j e) -> p j e", e=8)
        op("vector", lambda e, q=q, e3=e3: e.reduce_sum(S16[q][:], e3, AX),
           reads=[f"rEX{q}"], writes=[f"rS16{q}"])
        op("vector", lambda e, jj=jj, q=q: e.reciprocal(
            sb["G1"][:, jj * 16:(jj + 1) * 16], S16[q][:]),
           reads=[f"rS16{q}"], writes=["G1"])
        op("vector", lambda e, q=q: e.tensor_mul(MYP[q][:], lgt2[q][:], esb16[:]),
           reads=[f"lgt{q}", "esb16"], writes=[f"rMYP{q}"])
        p3 = MYP[q][:].rearrange("p (j e) -> p j e", e=8)
        op("vector", lambda e, q=q, p3=p3: e.reduce_sum(MY16[q][:], p3, AX),
           reads=[f"rMYP{q}"], writes=[f"rMY16{q}"])
        op("vector", lambda e, jj=jj, q=q: e.tensor_tensor(
            sb["M"][:, jj * 16:(jj + 1) * 16], MY16[q][:], M1[q][:], ALU.is_ge),
           reads=[f"rMY16{q}", f"rM1{q}"], writes=["M"])
        op("vector", lambda e: e.tensor_copy(sb["Mb"][:], sb["M"][:]),
       reads=["M"], writes=["Mb"])
    ctx.mm([(sb["UTI"][:], sb["Mb"][:])], pc[:], reads=["UTI", "Mb"], writes=["l1r_pc"])
    op("vector", lambda e: e.tensor_copy(sb["CUMS"][:], pc[:]),
       reads=["l1r_pc"], writes=["CUMS"])
    ctx.mm([(sb["ONESB"][:, 0:1], sb["Mb"][:])], pt1[:], reads=["ONESB", "Mb"],
           writes=["ptot"])
    op("vector", lambda e: e.tensor_copy(sb["tot"][:], pt1[:]),
       reads=["ptot"], writes=["tot"])
    op("sync", lambda e: e.dma_start(out=dr["bounce"][0:1, :], in_=sb["tot"][:]),
       reads=["tot"], writes=["bounce0"], dma=True)
    b_src = bass.AP(dr["bounce"].tensor, 0, [[1, 128], [128, 2]])
    op("gpsimd", lambda e, b_src=b_src: e.dma_start(
        out=sb["tot2"][:], in_=b_src, allow_slow_non_contiguous=True),
        reads=["bounce0"], writes=["tot2"], dma=True)
    ctx.mm([(sb["UTS"][:], sb["tot2"][:])], pb2[:], reads=["UTS", "tot2"],
           writes=["pb2"])
    ctx.mm([(sb["ONESB"][:, 0:1], sb["tot2"][:])], pcs[:], reads=["ONESB", "tot2"],
           writes=["pcs"])
    op("vector", lambda e: e.tensor_copy(sb["scs"][:], pcs[:]),
       reads=["pcs"], writes=["scs"])
    ctx.mm([(sb["ONESF"][0:1, :], sb["scs"][0:1, 0:1])], pbc[:],
           reads=["ONESF", "scs"], writes=["pbc"])
    op("vector", lambda e: e.tensor_copy(sb["offs2"][:], pb2[:]),
       reads=["pb2"], writes=["offs2"])
    op("vector", lambda e: e.tensor_add(sb["offs2"][:, 1:2], sb["offs2"][:, 1:2], pbc[:]),
       reads=["offs2", "pbc"], writes=["offs2"])
    b_dst = bass.AP(dr["bounce"].tensor, 256, [[1, 128], [128, 2]])
    op("gpsimd", lambda e, b_dst=b_dst: e.dma_start(
        out=b_dst, in_=sb["offs2"][:], allow_slow_non_contiguous=True),
        reads=["offs2"], writes=["bounce1"], dma=True)
    op("sync", lambda e: e.dma_start(out=sb["offs"][:], in_=dr["bounce"][1:2, :]),
       reads=["bounce1"], writes=["offs"], dma=True)
    ctx.mm([(sb["ONESF"][0:1, :], sb["offs"][:])], pc[:], reads=["ONESF", "offs"],
           writes=["l1r_pc"])
    op("vector", lambda e: e.tensor_add(sb["POS"][:], sb["CUMS"][:], pc[:]),
       reads=["CUMS", "l1r_pc"], writes=["POS"])
    op("vector", lambda e: e.scalar_tensor_tensor(sb["SLOT"][:], sb["M"][:], -33000.0,
                                                  sb["POS"][:], ALU.mult, ALU.add),
       reads=["M", "POS"], writes=["SLOT"])
    op("vector", lambda e: e.tensor_scalar(sb["SLOT"][:], sb["SLOT"][:], 32999.0,
                                           None, ALU.add),
       reads=["SLOT"], writes=["SLOT"])
    op("vector", lambda e: e.tensor_scalar(sb["SLOT"][:], sb["SLOT"][:], float(TRASH),
                                           None, ALU.min),
       reads=["SLOT"], writes=["SLOT"])
    op("vector", lambda e: e.tensor_scalar(sb["KTMP"][:], sb["POS"][:], float(CAP),
                                           None, ALU.is_le),
       reads=["POS"], writes=["KTMP"])
    op("vector", lambda e: e.tensor_mul(sb["GS"][:], sb["G1"][:], sb["M"][:]),
       reads=["G1", "M"], writes=["GS"])
    op("vector", lambda e: e.tensor_mul(sb["GS"][:], sb["GS"][:], sb["KTMP"][:]),
       reads=["GS", "KTMP"], writes=["GS"])
    op("vector", lambda e: e.tensor_scalar(sb["rmin"][:], sb["offs"][:], float(CAP),
                                           None, ALU.min),
       reads=["offs"], writes=["rmin"])
    op("vector", lambda e: e.tensor_copy(sb["roffi"][:], sb["rmin"][:]),
       reads=["rmin"], writes=["roffi"])

    NST = NSLOT // 128
    tpos = p.sb("l1r_tpos", [128, 128], F32)
    OFFSB = p.sb("l1r_offsb", [128, GTILE], F32)
    SCB = p.sb("l1r_scb", [128, NST], F32)
    TT2 = [p.sb(f"l1r_TT{i}", [128, 260], F32) for i in range(2)]
    tmA = [p.sb(f"l1r_tmA{i}", [128, GTILE], F32) for i in range(2)]
    tmB = [p.sb(f"l1r_tmB{i}", [128, 128], F32) for i in range(2)]
    tmC = [p.sb(f"l1r_tmC{i}", [128, 128], F32) for i in range(2)]
    cnt2 = [p.sb(f"l1r_cnt{i}", [128, 8], F32) for i in range(2)]
    coli = [p.sb(f"l1r_coli{i}", [128, 1], I32) for i in range(2)]
    for half, srcn in ((0, "POS"), (1, "GS")):
        for ct in range(2):
            op("tensor", lambda e, srcn=srcn, ct=ct: e.transpose(
                pc[:, 0:128], sb[srcn][:, ct * 128:(ct + 1) * 128], sb["IDENTF"][:]),
               reads=[srcn, "IDENTF"], writes=["l1r_pc"])
            op("vector", lambda e: e.tensor_copy(tpos[:], pc[:, 0:128]),
               reads=["l1r_pc"], writes=["l1r_tpos"])
            op("sync", lambda e, half=half, ct=ct: e.dma_start(
                out=dr["tbl"][ct * 128:(ct + 1) * 128, half * 128:(half + 1) * 128],
                in_=tpos[:]),
                reads=["l1r_tpos"], writes=["tbl_d"], dma=True)
    op("sync", lambda e: e.dma_start(out=dr["tbl"][:, 256:257], in_=capd["PBC"].ap(),
                                     allow_slow_non_contiguous=True),
       writes=["tbl_d"], dma=True)
    ctx.mm([(sb["ONESF"][0:1, :], sb["offs"][:])], pc[:], reads=["ONESF", "offs"],
           writes=["l1r_pc"])
    op("vector", lambda e: e.tensor_copy(OFFSB[:], pc[:]),
       reads=["l1r_pc"], writes=["l1r_offsb"])
    for st in range(NST):
        op("vector", lambda e, st=st: e.tensor_scalar(
            SCB[:, st:st + 1], sb["IOTAC"][:], float(st * 128), None, ALU.add),
           reads=["IOTAC"], writes=["l1r_scb"])
    for st in range(NST):
        q = st % 2
        cn = cnt2[q]
        op("vector", lambda e, st=st, q=q: e.tensor_scalar(
            tmA[q][:], OFFSB[:], SCB[:, st:st + 1], None, ALU.is_le),
           reads=["l1r_offsb", "l1r_scb"], writes=[f"l1r_A{q}"])
        op("vector", lambda e, q=q, cn=cn: e.reduce_sum(cn[:, 0:1], tmA[q][:], AX),
           reads=[f"l1r_A{q}"], writes=[f"l1r_c0{q}"])
        op("vector", lambda e, q=q, cn=cn: e.tensor_scalar(
            cn[:, 1:2], cn[:, 0:1], -1.0, None, ALU.add),
           reads=[f"l1r_c0{q}"], writes=[f"l1r_c1{q}"])
        op("vector", lambda e, q=q, cn=cn: e.tensor_copy(coli[q][:], cn[:, 1:2]),
           reads=[f"l1r_c1{q}"], writes=[f"l1r_coli{q}"])
        op("gpsimd", lambda e, q=q: e.indirect_dma_start(
            out=TT2[q][:], out_offset=None, in_=dr["tbl"],
            in_offset=IndirectOffsetOnAxis(ap=coli[q][:], axis=0)),
            reads=[f"l1r_coli{q}", "tbl_d"], writes=[f"l1r_TT{q}"], dma=True,
            dkey=f"ttg{q}")
        op("vector", lambda e, st=st, q=q: e.tensor_scalar(
            tmB[q][:], TT2[q][:, 0:128], SCB[:, st:st + 1], None, ALU.is_le),
           reads=[f"l1r_TT{q}", "l1r_scb"], writes=[f"l1r_B{q}"])
        op("vector", lambda e, q=q, cn=cn: e.reduce_sum(cn[:, 2:3], tmB[q][:], AX),
           reads=[f"l1r_B{q}"], writes=[f"l1r_c2{q}"])
        op("vector", lambda e, q=q, cn=cn: e.scalar_tensor_tensor(
            cn[:, 3:4], cn[:, 1:2], 128.0, cn[:, 2:3], ALU.mult, ALU.add),
           reads=[f"l1r_c1{q}", f"l1r_c2{q}"], writes=[f"l1r_c3{q}"])
        op("vector", lambda e, st=st, q=q, cn=cn: e.tensor_copy(
            sb["TOKI"][:, st:st + 1], cn[:, 3:4]),
           reads=[f"l1r_c3{q}"], writes=["TOKI"])
        op("vector", lambda e, q=q, cn=cn: e.tensor_add(
            cn[:, 4:5], TT2[q][:, 256:257], cn[:, 2:3]),
           reads=[f"l1r_TT{q}", f"l1r_c2{q}"], writes=[f"l1r_c4{q}"])
        op("vector", lambda e, st=st, q=q, cn=cn: e.tensor_copy(
            sb["TOKP"][:, st:st + 1], cn[:, 4:5]),
           reads=[f"l1r_c4{q}"], writes=["TOKP"])
        op("vector", lambda e, st=st, q=q, cn=cn: e.tensor_scalar(
            cn[:, 5:6], SCB[:, st:st + 1], 1.0, None, ALU.add),
           reads=["l1r_scb"], writes=[f"l1r_c5{q}"])
        op("vector", lambda e, q=q, cn=cn: e.tensor_scalar(
            tmC[q][:], TT2[q][:, 0:128], cn[:, 5:6], None, ALU.is_le),
           reads=[f"l1r_TT{q}", f"l1r_c5{q}"], writes=[f"l1r_C{q}"])
        op("vector", lambda e, q=q: e.tensor_sub(tmC[q][:], tmC[q][:], tmB[q][:]),
           reads=[f"l1r_C{q}", f"l1r_B{q}"], writes=[f"l1r_C{q}"])
        op("vector", lambda e, q=q: e.tensor_mul(
            tmC[q][:], tmC[q][:], TT2[q][:, 128:256]),
           reads=[f"l1r_C{q}", f"l1r_TT{q}"], writes=[f"l1r_C{q}"])
        op("vector", lambda e, st=st, q=q: e.reduce_sum(
            sb["GSL"][:, st:st + 1], tmC[q][:], AX),
           reads=[f"l1r_C{q}"], writes=["GSL"])
        if st == 0 and "dbgm" in taps:
            op("sync", lambda e, q=q: e.dma_start(
                out=taps["dbgm"][:, 0:128], in_=tmB[q][:]),
               reads=[f"l1r_B{q}"], dma=True)
            op("sync", lambda e, q=q: e.dma_start(
                out=taps["dbgm"][:, 128:256], in_=tmC[q][:]),
               reads=[f"l1r_C{q}"], dma=True)
            op("sync", lambda e, q=q, cn=cn: e.dma_start(
                out=taps["dbgc"], in_=cn[:]),
               reads=[f"l1r_c5{q}"], dma=True)
    for tn, buf in (("mmat", "M"), ("posmat", "POS"), ("slotmat", "SLOT"),
                    ("gsmat", "GS"), ("toki", "TOKI"), ("tokp", "TOKP"),
                    ("gsl", "GSL")):
        if tn in taps:
            op("sync", lambda e, tn=tn, buf=buf: e.dma_start(
                out=taps[tn], in_=sb[buf][:]), reads=[buf], dma=True)
    if "tblm" in taps:
        op("sync", lambda e: e.dma_start(out=taps["tblm"], in_=dr["tbl"]),
           reads=["tbl_d"], dma=True)
    if "offsrow" in taps:
        op("sync", lambda e: e.dma_start(out=taps["offsrow"], in_=sb["offs"][:]),
           reads=["offs"], dma=True)
    if "roffif" in taps:
        op("vector", lambda e: e.tensor_copy(sb["rmin"][:], sb["roffi"][:]),
           reads=["roffi"], writes=["rmin"])
        op("sync", lambda e: e.dma_start(out=taps["roffif"], in_=sb["rmin"][:]),
           reads=["rmin"], dma=True)
    ctx.barrier()
    p.free()



def _l1_dispatch(ctx, nc, sb, w, dr, taps, p):
    op = ctx.op
    zz = p.sb("l1m_zz", [128, 8], BF16)
    zt = p.sb("l1m_zero", [128, D], BF16)
    op("vector", lambda e: e.memset(zt[:], 0.0), writes=["l1m_zt"])
    op("gpsimd", lambda e: e.dma_start(
        out=dr["agx_out"].ap()[NCORES * T:NCORES * T + 128, :], in_=zt[:]),
        reads=["l1m_zt"], writes=["agz"], dma=True)
    xnt = [p.sb(f"l1m_xnt{i}", [128, D], BF16) for i in range(3)]
    dtags = []
    for st in range(NSLOT // 128):
        b = st % 3
        op("gpsimd", lambda e, st=st, b=b: e.indirect_dma_start(
            out=xnt[b][:], out_offset=None,
            in_=dr["agx_out"].ap(),
            in_offset=IndirectOffsetOnAxis(ap=sb["TOKI"][:, st:st + 1], axis=0)),
            reads=["agxout", "TOKI", "agz"], writes=[f"xnt{b}"], dma=True,
            dkey=f"xnt{b}")
        tg = f"dsc{st}"
        dtags.append(tg)
        op("gpsimd", lambda e, st=st, b=b: e.dma_start(
            out=dr["disp"][st * 128:(st + 1) * 128, :], in_=xnt[b][:]),
            reads=[f"xnt{b}"], writes=[tg], dma=True)
    op("gpsimd", lambda e: e.dma_start(out=zz[:, 0:8], in_=dr["disp"][0:128, 0:8]),
       reads=dtags, writes=["dfence"], dma=True)
    if "disp" in taps:
        op("sync", lambda e: e.dma_start(out=taps["disp"], in_=dr["disp"]),
           reads=dtags + ["dfence"], dma=True)
    for j in range(NCORES * T // 128 + 1):
        op("gpsimd", lambda e, j=j: e.dma_start(
            out=dr["rsin"].ap()[j * 128:(j + 1) * 128, :], in_=zt[:]),
            reads=["l1m_zt"], writes=["rzero"], dma=True)
    cvt = [sb["cvt0"], sb["cvt1"]]
    for n, rows, cols in (("w1e", D, 4096), ("w2e", 4096, D)):
        for rc in range(rows // 128):
            for c2 in range(cols // D):
                q = (rc * (cols // D) + c2) % 2
                off = rc * 128 * cols + c2 * D
                src = bass.AP(w[n].tensor, off, [[cols, 128], [1, D]])
                dst = bass.AP(dr[n + "b"].tensor, off, [[cols, 128], [1, D]])
                op("gpsimd", lambda e, src=src, q=q: e.dma_start(
                    out=cvt[q][:], in_=src),
                    writes=[f"cv{q}"], dma=True, dkey=f"cv{q}")
                op("gpsimd", lambda e, dst=dst, q=q: e.dma_start(
                    out=dst, in_=cvt[q][:]),
                    reads=[f"cv{q}"], writes=[f"{n}b_d"], dma=True)
    return dtags


def _l1_ffn(ctx, nc, sb, w, dr, dtags, taps):
    op = ctx.op
    p = Pool(nc)
    w1sb = p.sb("l1f_w1", [128, 8 * 4096], BF16)
    w2sb = p.sb("l1f_w2", [128, 32 * 1024], BF16)
    dspf = p.sb("l1f_dsp", [128, 8 * 512], BF16)
    hsb = p.sb("l1f_h", [128, 32 * 512], BF16)
    eo = [p.sb(f"l1f_eo{i}", [128, D], BF16) for i in range(3)]
    oc = [p.sb(f"l1f_oc{i}", [128, D], BF16) for i in range(3)]
    ph = p.ps("l1f_ph", [128, 512])
    pe = [p.ps(f"l1f_pe{i}", [128, 512]) for i in range(2)]
    src = bass.AP(dr["w1eb"].tensor, 0, [[4096, 128], [4096 * 128, 8], [1, 4096]])
    op("sync", lambda e, src=src: e.dma_start(
        out=w1sb[:].rearrange("p (c f) -> p c f", f=4096), in_=src),
        reads=["w1eb_d"], writes=["w1sb"], dma=True, dkey="w1sb")
    src2 = bass.AP(dr["w2eb"].tensor, 0, [[1024, 128], [1024 * 128, 32], [1, 1024]])
    op("sync", lambda e, src2=src2: e.dma_start(
        out=w2sb[:].rearrange("p (c f) -> p c f", f=1024), in_=src2),
        reads=["w2eb_d"], writes=["w2sb"], dma=True, dkey="w2sb")

    etags = []
    for cb in range(NSLOT // 512):
        c0 = cb * 512
        for mc in range(8):
            op("sync", lambda e, mc=mc, c0=c0: e.dma_start_transpose(
                dspf[:, mc * 512:(mc + 1) * 512],
                dr["disp"][c0:c0 + 512, mc * 128:(mc + 1) * 128]),
                reads=dtags + ["dfence"], writes=[f"dspf{mc}"], dma=True, dkey="dspf")
        for ft in range(32):
            ctx.mm([(w1sb[:, mc * 4096 + ft * 128:mc * 4096 + ft * 128 + 128],
                     dspf[:, mc * 512:(mc + 1) * 512]) for mc in range(8)],
                   ph[:], reads=["w1sb"] + [f"dspf{mc}" for mc in range(8)],
                   writes=["l1f_ph"])
            op("scalar", lambda e, ft=ft: e.activation(
                hsb[:, ft * 512:(ft + 1) * 512], ph[:], AF.Relu,
                bias=sb["b1e"][:, ft:ft + 1]),
                reads=["l1f_ph", "b1e"], writes=["hsb"])
        for ct in range(4):
            st = cb * 4 + ct
            r = st % 3
            for mh in range(2):
                pid = mh
                ctx.mm([(hsb[:, fc * 512 + ct * 128:fc * 512 + ct * 128 + 128],
                         w2sb[:, fc * 1024 + mh * 512:fc * 1024 + mh * 512 + 512])
                        for fc in range(32)], pe[pid][:],
                       reads=["hsb", "w2sb"], writes=[f"l1f_pe{pid}"], stop=False)
                ctx.mm([(sb["ONESF"][0:1, :], sb["b2e"][0:1, mh * 512:mh * 512 + 512])],
                       pe[pid][:], reads=["ONESF", "b2e"], writes=[f"l1f_pe{pid}"],
                       start=False)
                op("vector", lambda e, mh=mh, pid=pid, r=r: e.tensor_copy(
                    eo[r][:, mh * 512:(mh + 1) * 512], pe[pid][:]),
                    reads=[f"l1f_pe{pid}"], writes=[f"l1f_eo{r}"])
            op("vector", lambda e, st=st, r=r: e.tensor_scalar(
                oc[r][:], eo[r][:], sb["GSL"][:, st:st + 1], None, ALU.mult),
                reads=[f"l1f_eo{r}", "GSL"], writes=[f"l1f_oc{r}"])
            tg = f"rsw{st}"
            etags.append(tg)
            op("gpsimd", lambda e, st=st, r=r: e.indirect_dma_start(
                out=dr["rsin"].ap(),
                out_offset=IndirectOffsetOnAxis(ap=sb["TOKP"][:, st:st + 1], axis=0),
                in_=oc[r][:], in_offset=None),
                reads=[f"l1f_oc{r}", "TOKP", "rzero"], writes=[tg], dma=True)
    ctx.barrier()
    p.free()
    return etags


def _l1_combine_final(ctx, nc, sb, w, out, dr, etags, taps):
    op = ctx.op
    p = Pool(nc)
    aos = p.sb("l1z_ao", [128, 8192], BF16)
    yf = p.sb("l1z_yf", [128, 1024], BF16)
    ytall = p.sb("l1z_ytall", [128, NTILE * D], BF16)
    x1t = [p.sb(f"l1z_x1{i}", [128, D], F32) for i in range(2)]
    xat = [p.sb(f"l1z_xa{i}", [128, D], F32) for i in range(2)]
    fa = [p.sb(f"l1z_fa{i}", [128, D], F32) for i in range(2)]
    rst = [p.sb(f"l1z_rs{i}", [128, D], BF16) for i in range(2)]
    ot = [p.sb(f"l1z_o{i}", [128, D], F32) for i in range(2)]
    pT = p.ps("l1z_pT", [128, 128], BF16)
    pF = p.ps("l1z_pF", [128, 512])
    src = bass.AP(dr["aotb"].tensor, 0, [[1024, 128], [1024 * 128, 8], [1, 1024]])
    op("sync", lambda e, src=src: e.dma_start(
        out=aos[:].rearrange("p (c f) -> p c f", f=1024), in_=src),
        reads=["aotb_d"], writes=["aos"], dma=True, dkey="aos")

    op("sync", lambda e: e.dma_start(out=ytall[:], in_=dr["ytm1d"]),
       reads=["ytm1d"], writes=["z_yta"], dma=True, dkey="z_yta")

    def proj(jj):
        q = jj % 2
        r0 = jj * 128
        for cc in range(8):
            op("tensor", lambda e, jj=jj, cc=cc: e.transpose(
                pT[:], ytall[:, jj * 1024 + cc * 128:jj * 1024 + (cc + 1) * 128],
                sb["IDENT"][:]),
               reads=["z_yta", "IDENT"], writes=["l1z_pT"])
            op("vector", lambda e, cc=cc: e.tensor_copy(
                yf[:, cc * 128:(cc + 1) * 128], pT[:]),
               reads=["l1z_pT"], writes=[f"zyf{cc}"])
        op("sync", lambda e, r0=r0, q=q: e.dma_start(
            out=x1t[q][:], in_=dr["x1"][r0:r0 + 128, :]),
            reads=[f"x1_{jj}"], writes=[f"z_x1{q}"], dma=True, dkey=f"z_x1{q}")
        for ih in range(2):
            ctx.mm([(yf[:, cc * 128:(cc + 1) * 128],
                     aos[:, cc * 1024 + ih * 512:cc * 1024 + ih * 512 + 512])
                    for cc in range(8)], pF[:],
                   reads=[f"zyf{cc}" for cc in range(8)] + ["aos"], writes=["z_pF"])
            op("vector", lambda e, ih=ih, q=q: e.tensor_add(
                xat[q][:, ih * 512:(ih + 1) * 512], pF[:],
                x1t[q][:, ih * 512:(ih + 1) * 512]),
                reads=["z_pF", f"z_x1{q}"], writes=[f"z_xa{q}"])
        op("sync", lambda e, r0=r0, q=q: e.dma_start(
            out=dr["xa"][r0:r0 + 128, :], in_=xat[q][:]),
            reads=[f"z_xa{q}"], writes=[f"xa_{jj}"], dma=True)

    rg = [list(range(NCORES))]

    def rs_q(q):
        r0 = q * 8192
        op("gpsimd", lambda e, q=q, r0=r0: e.collective_compute(
            "ReduceScatter", ALU.add, replica_groups=rg,
            ins=[dr["rsin"].ap()[r0:r0 + 8192, :].opt()],
            outs=[dr["rsout"].ap()[q * 1024:(q + 1) * 1024, :].opt()]),
            reads=etags + ["rzero"], writes=[f"rsout{q}"])

    def fadd(q, k):
        jj = q * 8 + k
        z = jj % 2
        r0 = jj * 128
        op("scalar", lambda e, r0=r0, z=z: e.dma_start(
            out=fa[z][:], in_=dr["xa"][r0:r0 + 128, :]),
            reads=[f"xa_{jj}"], writes=[f"z_fa{z}"], dma=True, dkey=f"z_fa{z}")
        op("sync", lambda e, r0=r0, z=z, q=q: e.dma_start(
            out=rst[z][:], in_=dr["rsout"].ap()[r0:r0 + 128, :]),
            reads=[f"rsout{q}"], writes=[f"z_rs{z}"], dma=True, dkey=f"z_rs{z}")
        op("vector", lambda e, z=z: e.tensor_add(ot[z][:], fa[z][:], rst[z][:]),
           reads=[f"z_fa{z}", f"z_rs{z}"], writes=[f"z_ot{z}"])
        eng = "scalar" if jj % 2 == 0 else "sync"
        op(eng, lambda e, r0=r0, z=z: e.dma_start(
            out=out[r0:r0 + 128, :], in_=ot[z][:]),
            reads=[f"z_ot{z}"], writes=["outd"], dma=True)

    for pj in range(NTILE):
        proj(pj)
    for q in range(4):
        rs_q(q)
        for k in range(8):
            fadd(q, k)
    ctx.barrier()
    p.free()



def _l1_attn_pre(ctx, nc, sb, dr, capd):
    op = ctx.op
    early = Pool(nc)
    eb = {"cs": early.sb("cos_sb", [128, T], BF16),
          "sn": early.sb("sin_sb", [128, T], BF16),
          "wqs": early.sb("l1t_wq", [128, 2048], BF16),
          "wks": early.sb("l1t_wk", [128, 2048], BF16),
          "wvs": early.sb("l1t_wv", [128, 2048], BF16)}
    op("sync", lambda e: e.dma_start(out=eb["cs"][:], in_=capd["COS"].ap()),
       writes=["COS"], dma=True)
    op("sync", lambda e: e.dma_start(out=eb["sn"][:], in_=capd["SIN"].ap()),
       writes=["SIN"], dma=True)
    for name, wt in (("wqs", "aqt"), ("wks", "akt"), ("wvs", "avt")):
        src = bass.AP(dr[wt + "b"].tensor, 0,
                      [[1024, 128], [1024 * 128, 8], [1, 256]])
        op("sync", lambda e, src=src, name=name: e.dma_start(
            out=eb[name][:].rearrange("p (c f) -> p c f", f=256), in_=src),
            reads=[f"{wt}b_d"], writes=[name], dma=True, dkey=name)
    return early, eb


def _l1_attn(ctx, nc, sb, w, dr, capd, taps, early, eb):
    op = ctx.op
    ypool = Pool(nc)
    ytm1 = ypool.sb("ytm1", [128, NTILE * D], BF16)
    cs = eb["cs"]
    sn = eb["sn"]

    p = Pool(nc)
    qq = p.sb("l1t_q", [128, 2 * T], BF16)
    kq = p.sb("l1t_k", [128, 2 * T], BF16)
    vex = p.sb("l1t_v", [128, NTILE * 260], BF16)
    xsq = [p.sb(f"l1t_xs{i}", [128, T], BF16) for i in range(2)]
    wqs, wks, wvs = eb["wqs"], eb["wks"], eb["wvs"]
    qraw = p.sb("l1t_qraw", [128, 512], BF16)
    t1 = p.sb("l1t_t1", [128, 512], BF16)
    t2 = p.sb("l1t_t2", [128, 512], BF16)
    Psb2 = [p.sb(f"l1t_P{i}", [128, 512], BF16) for i in range(2)]
    dcol = p.sb("l1t_dcol", [128, 1], F32)
    pP = p.ps("l1t_pP", [128, 512])
    pR = p.ps("l1t_pR", [128, 512])
    pA2 = [p.ps(f"l1t_pA{i}", [128, 512]) for i in range(2)]
    pO = [p.ps(f"l1t_pO{i}", [128, 72]) for i in range(4)]

    agxtags = [f"agxin{j}" for j in range(NTILE)]
    for qt in range(4):
        if qt > 0:
            for name, wt, buf in (("wqs", "aqt", wqs), ("wks", "akt", wks),
                                  ("wvs", "avt", wvs)):
                src = bass.AP(dr[wt + "b"].tensor, qt * 256,
                              [[1024, 128], [1024 * 128, 8], [1, 256]])
                op("sync", lambda e, src=src, buf=buf: e.dma_start(
                    out=buf[:].rearrange("p (c f) -> p c f", f=256), in_=src),
                    reads=[f"{wt}b_d"], writes=[name], dma=True, dkey=name)
        vones = bass.AP(vex, 64, [[NTILE * 260, 128], [260, NTILE], [65, 4], [1, 1]])
        op("vector", lambda e, vones=vones: e.memset(vones, 1.0),
           reads=["vex"], writes=["vex_ones"])

        def ldx(tb):
            xb = tb % 2
            for cc in range(8):
                op("sync", lambda e, cc=cc, tb=tb, xb=xb: e.dma_start_transpose(
                    xsq[xb][:, cc * 512:(cc + 1) * 512],
                    dr["agx_in"][tb * 512:(tb + 1) * 512, cc * 128:(cc + 1) * 128]),
                    reads=agxtags[tb * 4:tb * 4 + 4], writes=[f"xq{xb}_{cc}"],
                    dma=True, dkey=f"xq{xb}_{cc}")

        if qt == 0:
            ldx(0)
        for tb in range(8):
            xb = tb % 2
            if tb + 1 < 8:
                ldx(tb + 1)
            elif qt + 1 < 4:
                ldx(0)
            xr = [f"xq{xb}_{cc}" for cc in range(8)]
            for name, wbuf, dst, dtag in (("wqs", wqs, qq, "qqb"),
                                          ("wks", wks, kq, "kqb")):
                for it in range(2):
                    ctx.mm([(wbuf[:, cc * 256 + it * 128:cc * 256 + it * 128 + 128],
                             xsq[xb][:, cc * 512:(cc + 1) * 512]) for cc in range(8)],
                           pP[:], reads=[name] + xr, writes=["l1t_pP"])
                    op("scalar", lambda e: e.copy(qraw[:], pP[:]),
                       reads=["l1t_pP"], writes=["qraw"])
                    ctx.mm([(sb["ROTT"][:], qraw[:])], pR[:],
                           reads=["ROTT", "qraw"], writes=["l1t_pR"])
                    c0 = tb * 512
                    op("vector", lambda e, c0=c0: e.tensor_mul(
                        t1[:], qraw[:], cs[:, c0:c0 + 512]),
                        reads=["qraw", "COS"], writes=["t1"])
                    op("vector", lambda e, c0=c0: e.tensor_mul(
                        t2[:], pR[:], sn[:, c0:c0 + 512]),
                        reads=["l1t_pR", "SIN"], writes=["t2"])
                    op("vector", lambda e, dst=dst, it=it, c0=c0: e.tensor_add(
                        dst[:, it * T + c0:it * T + c0 + 512], t1[:], t2[:]),
                        reads=["t1", "t2"], writes=[dtag])
            for ts_ in range(4):
                ctx.mm([(xsq[xb][:, cc * 512 + ts_ * 128:cc * 512 + ts_ * 128 + 128],
                         wvs[:, cc * 256:(cc + 1) * 256]) for cc in range(8)],
                       pP[:, 0:256], reads=["wvs"] + xr, writes=["l1t_pP"])
                ti = tb * 4 + ts_
                for hh in range(4):
                    op("vector", lambda e, ti=ti, hh=hh: e.tensor_copy(
                        vex[:, ti * 260 + hh * 65:ti * 260 + hh * 65 + 64],
                        pP[:, hh * 64:hh * 64 + 64]),
                        reads=["l1t_pP"], writes=["vex"])
        for hl in range(4):
            h = qt * 4 + hl
            rb = (hl % 2) * 64
            itq = hl // 2
            for qb in range(8):
                q0 = qb * 512
                nkt = (qb + 1) * 4
                def emit_qk(kt):
                    pr = kt % 2
                    ctx.mm([(kq[rb:rb + 64, itq * T + kt * 128:itq * T + kt * 128 + 128],
                             qq[rb:rb + 64, itq * T + q0:itq * T + q0 + 512])],
                           pA2[pr][:], reads=["kqb", "qqb"], writes=[f"l1t_pA{pr}"])

                emit_qk(0)
                for kt in range(nkt):
                    pr = kt % 2
                    pA = pA2[pr]
                    Psb = Psb2[pr]
                    op("scalar", lambda e, pA=pA, Psb=Psb: e.activation(
                        Psb[:], pA[:], AF.Exp, bias=sb["cM12"][:], scale=0.125),
                       reads=[f"l1t_pA{pr}", "cM12"], writes=[f"Psb{pr}"])
                    jloc = kt * 128 - q0
                    if jloc >= 0:
                        if jloc > 0:
                            op("vector", lambda e, jloc=jloc, Psb=Psb: e.memset(
                                Psb[:, 0:jloc], 0.0), writes=[f"Psb{pr}"])
                        op("vector", lambda e, jloc=jloc, Psb=Psb: e.tensor_mul(
                            Psb[:, jloc:jloc + 128], Psb[:, jloc:jloc + 128],
                            sb["UTI"][:]),
                            reads=["UTI"], writes=[f"Psb{pr}"])
                    if kt + 1 < nkt:
                        emit_qk(kt + 1)
                    for s in range(4):
                        ctx.mm([(Psb[:, s * 128:(s + 1) * 128],
                                 vex[:, kt * 260 + hl * 65:kt * 260 + hl * 65 + 65])],
                               pO[s][:, 0:65], reads=[f"Psb{pr}", "vex", "vex_ones"],
                               writes=[f"l1t_pO{s}"],
                               start=(kt == 0), stop=(kt == nkt - 1))
                for s in range(4):
                    c = qb * 4 + s
                    op("vector", lambda e, s=s: e.reciprocal(
                        dcol[:], pO[s][:, 64:65]),
                        reads=[f"l1t_pO{s}"], writes=["l1t_dc"])
                    op("vector", lambda e, s=s, c=c, h=h: e.tensor_scalar(
                        ytm1[:, c * D + h * 64:c * D + h * 64 + 64],
                        pO[s][:, 0:64], dcol[:], None, ALU.mult),
                        reads=[f"l1t_pO{s}", "l1t_dc"], writes=["ytm1"])
    op("sync", lambda e: e.dma_start(out=dr["ytm1d"], in_=ytm1[:]),
       reads=["ytm1"], writes=["ytm1d"], dma=True)
    return [p, ypool]



def kernel(**inputs):
    ins = {k: np.ascontiguousarray(np.asarray(v)) for k, v in inputs.items()}
    x = ins["L0_attn_x"].astype(np.float32, copy=False)

    def tr(a):
        return np.ascontiguousarray(a.T)

    common = {
        "wqt": tr(ins["L0_attn_Wq"]), "wkt": tr(ins["L0_attn_Wk"]),
        "wvt": tr(ins["L0_attn_Wv"]), "wpt": tr(ins["L0_attn_Wp"]),
        "bp": ins["L0_attn_bp"].reshape(1, D),
        "rw": ins["L0_ffn_router_w"], "rb": ins["L0_ffn_router_b"].reshape(1, 8),
        "w1s": np.concatenate([ins["L0_ffn_routed_w1"], ins["L0_ffn_shared_w1"]], 0),
        "w3s": np.concatenate([ins["L0_ffn_routed_w3"], ins["L0_ffn_shared_w3"]], 0),
        "w2s": np.concatenate([ins["L0_ffn_routed_w2"], ins["L0_ffn_shared_w2"]], 0),
        "aqt": tr(ins["L1_attn_Wq"]), "akt": tr(ins["L1_attn_Wk"]),
        "avt": tr(ins["L1_attn_Wv"]), "aot": tr(ins["L1_attn_Wo"]),
        "wg": ins["L1_ffn_wg"],
    }
    common = {k: np.ascontiguousarray(v) for k, v in common.items()}
    nc = build_nc()
    in_maps = []
    for c in range(NCORES):
        m = dict(common)
        m["x0"] = x[c]
        m["w1e"] = ins["L1_ffn_w1"][c]
        m["b1e"] = np.ascontiguousarray(ins["L1_ffn_b1"][c].reshape(32, 128).T)
        m["w2e"] = ins["L1_ffn_w2"][c]
        m["b2e"] = ins["L1_ffn_b2"][c].reshape(1, D)
        es = np.zeros((1, 8), np.float32)
        es[0, c] = 1.0
        m["esel"] = es
        m["esel16"] = np.ascontiguousarray(np.tile(es, (1, 16)))
        in_maps.append(m)
    res = run_bass_kernel_spmd(nc, in_maps, core_ids=list(range(NCORES)))
    return np.stack([r["out"] for r in res.results], axis=0)
